# version 01
# baseline (speedup 1.0000x reference)
import math
from contextlib import ExitStack

import numpy as np
import ml_dtypes
import concourse.bass as bass
import concourse.mybir as mybir
from concourse.bass_utils import run_bass_kernel_spmd

F32 = mybir.dt.float32
BF16 = mybir.dt.bfloat16
I32 = mybir.dt.int32
AF = mybir.ActivationFunctionType
ALU = mybir.AluOpType
AX = mybir.AxisListType
bf16 = ml_dtypes.bfloat16

T = 2048
TG = T + 4
D = 1024
NCHUNK = 8
NSEQ = 4
TJ = 512
NTJ = T // TJ
EPS = 1e-5

C_L0A = 20.0
C_L0B = 30.0
C_L1A = 25.0
C_L1B = 25.0



def _rope_flavA():
    nb = 16
    inv_freq = 1.0 / (10000.0 ** (np.arange(nb, dtype=np.float64) / nb))
    ang = np.arange(T, dtype=np.float64)[:, None] * inv_freq[None, :]
    sin_h = np.repeat(np.sin(ang), 2, axis=-1)
    cos_h = np.repeat(np.cos(ang), 2, axis=-1)
    return (np.concatenate([cos_h, cos_h], axis=-1),
            np.concatenate([sin_h, sin_h], axis=-1))


def _rope_halves(dim=64, theta=10000.0):
    inv_freq = 1.0 / (theta ** (np.arange(0, dim, 2, dtype=np.float64) / dim))
    freqs = np.arange(T, dtype=np.float64)[:, None] * inv_freq[None, :]
    emb = np.concatenate([freqs, freqs], axis=-1)
    return np.cos(emb), np.sin(emb)


def _rope_interleaved(d_k=64, theta=10000.0):
    half = d_k // 2
    inv_freq = theta ** (-np.arange(half, dtype=np.float64) * 2.0 / d_k)
    ang = np.arange(T, dtype=np.float64)[:, None] * inv_freq[None, :]
    return np.repeat(np.cos(ang), 2, axis=-1), np.repeat(np.sin(ang), 2, axis=-1)


def _xpos_tables():
    DIM_HEAD, WINDOW = 64, 512
    half_idx = np.arange(0, DIM_HEAD, 2, dtype=np.float64)
    inv_freq = 1.0 / (10000.0 ** (half_idx / DIM_HEAD))
    t = np.arange(T, dtype=np.float64)
    freqs = t[:, None] * inv_freq[None, :]
    freqs = np.concatenate([freqs, freqs], axis=-1)
    base = (half_idx + 0.4 * DIM_HEAD) / (1.4 * DIM_HEAD)
    power = (t - T // 2) / float(WINDOW // 2)
    scale = base[None, :] ** power[:, None]
    scale = np.concatenate([scale, scale], axis=-1)
    return np.cos(freqs), np.sin(freqs), scale


def _sign_halves(dim=64):
    s = np.ones(dim)
    s[dim // 2:] = 1.0
    s[: dim // 2] = -1.0
    return s


def _sign_interleaved(dim=64):
    s = np.ones(dim)
    s[0::2] = -1.0
    return s


def _permA_idx(n):
    h = n % 64
    return n - h + ((h + 32) % 64)


def _permI_idx(n):
    return n ^ 1


def _tile2(a_t64):
    at = np.ascontiguousarray(a_t64.T)
    return np.concatenate([at, at], axis=0)


def host_constants():
    C = {}
    cosA, sinA = _rope_flavA()
    sgA = _sign_halves()
    C["cst_cosA"] = _tile2(cosA).astype(bf16)
    C["cst_sinA"] = _tile2(sinA * sgA[None, :]).astype(bf16)
    cosB, sinB = _rope_halves(64)
    C["cst_cosB_q"] = np.ascontiguousarray(cosB.T).astype(bf16)
    C["cst_sinB_q"] = np.ascontiguousarray((sinB * sgA[None, :]).T).astype(bf16)
    C["cst_cosB_k"] = np.ascontiguousarray((cosB / 16.0).T).astype(bf16)
    C["cst_sinB_k"] = np.ascontiguousarray(
        (sinB * sgA[None, :] / 16.0).T).astype(bf16)
    cosI, sinI = _rope_interleaved()
    sgI = _sign_interleaved()
    C["cst_cosI_q"] = _tile2(cosI / 8.0).astype(bf16)
    C["cst_sinI_q"] = _tile2(sinI * sgI[None, :] / 8.0).astype(bf16)
    C["cst_cosI_k"] = _tile2(cosI).astype(bf16)
    C["cst_sinI_k"] = _tile2(sinI * sgI[None, :]).astype(bf16)
    C["cst_cosH_q"] = _tile2(cosB / 8.0).astype(bf16)
    C["cst_sinH_q"] = _tile2(sinB * sgA[None, :] / 8.0).astype(bf16)
    C["cst_cosH_k"] = _tile2(cosB).astype(bf16)
    C["cst_sinH_k"] = _tile2(sinB * sgA[None, :]).astype(bf16)
    cosX, sinX, scX = _xpos_tables()
    C["cst_xq_cos"] = _tile2(cosX * scX * 8.0).astype(bf16)
    C["cst_xq_sin"] = _tile2(sinX * scX * sgA[None, :] * 8.0).astype(bf16)
    C["cst_xk_cos"] = _tile2(cosX / scX).astype(bf16)
    C["cst_xk_sin"] = _tile2(sinX / scX * sgA[None, :]).astype(bf16)
    return C



class _Op:
    __slots__ = ("eng", "emit", "reads", "writes", "dma", "idx", "deps",
                 "marked", "inc_total", "tag", "pe_base")

    def __init__(self, eng, emit, reads, writes, dma, idx, tag, pe_base=0):
        self.eng, self.emit, self.dma, self.idx = eng, emit, dma, idx
        self.reads, self.writes = list(reads), list(writes)
        self.deps = set()
        self.marked = False
        self.inc_total = None
        self.tag = tag
        self.pe_base = pe_base


class Sched:
    def __init__(self, nc, strict=False):
        self.nc = nc
        self.strict = strict
        self.ops = []
        self.last_writer = {}
        self.readers = {}

    def op(self, eng, emit, reads=(), writes=(), dma=False, tag="", pe_base=0):
        o = _Op(eng, emit, reads, writes, dma, len(self.ops), tag, pe_base)
        for k in o.reads:
            if k in self.last_writer:
                o.deps.add(self.last_writer[k])
        for k in o.writes:
            if k in self.last_writer:
                o.deps.add(self.last_writer[k])
            for r in self.readers.get(k, ()):
                o.deps.add(r)
        for k in o.reads:
            self.readers.setdefault(k, []).append(o.idx)
        for k in o.writes:
            self.last_writer[k] = o.idx
            self.readers[k] = []
        self.ops.append(o)
        return o.idx

    def dma(self, eng, out, in_, reads=(), writes=(), tag="", **kw):
        return self.op(eng, (lambda e: e.dma_start(out=out, in_=in_, **kw)),
                       reads=reads, writes=writes, dma=True, tag=tag)

    NSEM_DMA = 6

    def finalize(self, final_wait_engine="sync"):
        nc = self.nc
        ops = self.ops
        N = self.NSEM_DMA
        if not self.strict:
            for o in ops:
                o.deps = {d for d in o.deps
                          if not (ops[d].eng == o.eng == "tensor"
                                  and not ops[d].dma and not o.dma)}
        prev_t = None
        for o in ops:
            if o.eng == "tensor" and not o.dma:
                if prev_t is not None and ops[prev_t].pe_base != o.pe_base:
                    o.deps.add(prev_t)
                prev_t = o.idx
        dma_idx = {}
        for o in ops:
            if o.dma:
                i = dma_idx.get(o.eng, 0)
                dma_idx[o.eng] = i + 1
                o.inc_total = i
        for o in ops:
            best = {}
            for d in o.deps:
                od = ops[d]
                key = (od.eng, od.dma, od.inc_total % N if od.dma else 0)
                if key not in best or d > best[key]:
                    best[key] = d
            o.deps = set(best.values())
        for o in ops:
            for d in o.deps:
                ops[d].marked = True
        lastof = {}
        for o in ops:
            if not o.dma:
                lastof[o.eng] = o
        for o in lastof.values():
            o.marked = True
        counters = {}
        for o in ops:
            if o.dma:
                continue
            if o.marked:
                key = (o.eng, False)
                counters[key] = counters.get(key, 0) + 1
                o.inc_total = counters[key]

        sems = {}
        cms = []

        def get_sem(key, name):
            if key not in sems:
                cm = nc.semaphore(name)
                sems[key] = cm.__enter__()
                cms.append(cm)
            return sems[key]

        for key in counters:
            get_sem((key[0], False, 0), f"sem_{key[0]}_ex")
        for eng, cnt in dma_idx.items():
            for s in range(min(N, cnt)):
                get_sem((eng, True, s), f"sem_{eng}_dma{s}")

        watermark = {}
        blk_cm = nc.Block()
        block = blk_cm.__enter__()

        def wait_for(eng, eng_name, od):
            if od.dma:
                skey = (od.eng, True, od.inc_total % N)
                val = 16 * (od.inc_total // N + 1)
            else:
                skey = (od.eng, False, 0)
                val = od.inc_total
            w = (eng_name, skey)
            if watermark.get(w, -1) >= val:
                return
            eng.wait_ge(sems[skey], val)
            watermark[w] = val

        def mk_section(eng_name):
            def section(eng):
                for o in ops:
                    if o.eng != eng_name:
                        continue
                    for d in sorted(o.deps):
                        wait_for(eng, eng_name, ops[d])
                    if o.dma:
                        i = o.inc_total
                        if i >= N:
                            skey = (o.eng, True, i % N)
                            val = 16 * (i // N)
                            w = (eng_name, skey)
                            if watermark.get(w, -1) < val:
                                eng.wait_ge(sems[skey], val)
                                watermark[w] = val
                    inst = o.emit(eng)
                    if o.dma:
                        inst.then_inc(sems[(o.eng, True, o.inc_total % N)], 16)
                    elif o.marked:
                        inst.then_inc(sems[(o.eng, False, 0)], 1)
                if eng_name == final_wait_engine:
                    for (ename, isdma, slot), sem in sems.items():
                        if (ename, isdma) == (eng_name, False):
                            continue
                        if isdma:
                            cnt = dma_idx[ename]
                            rounds = (cnt - slot + N - 1) // N
                            eng.wait_ge(sem, 16 * rounds)
                        else:
                            eng.wait_ge(sem, counters[(ename, False)])
            return section

        for name in sorted({o.eng for o in ops}):
            getattr(block, name)(mk_section(name))
        blk_cm.__exit__(None, None, None)
        for cm in cms:
            cm.__exit__(None, None, None)
        self.n_ops = len(ops)


def r_(name, *idx):
    return (name,) + tuple(idx)


def tok(ap, t0, n):
    return ap[:, 1 + t0:1 + t0 + n]



class Bufs:
    pass


def alloc_common(nc, ctx):
    B = Bufs()
    ent = ctx.enter_context
    B.x = [ent(nc.sbuf_tensor(f"x{c}", [128, TG], F32)) for c in range(NCHUNK)]
    B.h = [ent(nc.sbuf_tensor(f"h{c}", [128, TG], BF16)) for c in range(NCHUNK)]
    B.b1bc = ent(nc.sbuf_tensor("b1bc", [128, T], BF16))
    B.b2bc = ent(nc.sbuf_tensor("b2bc", [128, T], BF16))
    B.b3bc = None
    B.b4bc = None
    B.rowf = ent(nc.sbuf_tensor("rowf", [1, 5 * TJ], F32))
    B.rowh = ent(nc.sbuf_tensor("rowh", [1, TJ], BF16))
    B.pb = [ent(nc.psum_tensor(f"pb{i}", [128, 512], F32)) for i in range(8)]
    B.ones_row = ent(nc.sbuf_tensor("ones_row", [1, 128], BF16))
    B.onecol = ent(nc.sbuf_tensor("onecol", [128, 1], BF16))
    B.sq = [ent(nc.sbuf_tensor(f"sq{i}", [128, TJ], BF16)) for i in range(2)]
    B.consts = ent(nc.sbuf_tensor("consts", [128, 16], F32))
    B.const_cols = {}
    return B


def cvec(S, B, val, p0=0, p=128):
    if val not in B.const_cols:
        col = len(B.const_cols)
        assert col < 16
        B.const_cols[val] = col
        S.op("vector", lambda e, col=col, val=val: e.memset(
            B.consts[:, col:col + 1], float(val)),
            writes=[r_("consts", col)], tag="cset")
    col = B.const_cols[val]
    return B.consts[p0:p0 + p, col:col + 1], r_("consts", col)


def emit_init_consts(S, B):
    S.op("vector", lambda e: e.memset(B.ones_row[:], 1.0), writes=[r_("ones_row")])
    S.op("vector", lambda e: e.memset(B.onecol[:], 1.0), writes=[r_("onecol")])
    for c in range(NCHUNK):
        S.op("vector", lambda e, c=c: e.memset(B.x[c][:, 0:1], 0.0),
             writes=[r_("x", c)])
        S.op("vector", lambda e, c=c: e.memset(B.h[c][:, 0:1], 0.0),
             writes=[r_("h", c)])



def emit_ln(S, B, st, src=None, src_key="x", out_h=None, out_key="h",
            gamma_vec=None, beta_vec=None, gb_key=None, l0ffn=False,
            shifted=False):
    src = src or B.x
    out_h = out_h or B.h
    pb_s0, pb_s1, pb_b = B.pb[5], B.pb[6], B.pb[7]

    def skey(c):
        return r_(src_key, c)

    def hkey(c):
        return r_(out_key, c)

    def sslice(c, t0, n):
        if shifted and c >= 4:
            return src[c][:, t0:t0 + n]
        return tok(src[c], t0, n)

    for c in range(NCHUNK):
        S.op("vector",
             lambda e, c=c: e.tensor_copy(tok(out_h[c], 0, T), sslice(c, 0, T)),
             reads=[skey(c)], writes=[hkey(c)], tag="ln_cast")

    for tj in range(NTJ):
        t0 = tj * TJ
        for c in range(NCHUNK):
            sqb = B.sq[c % 2]
            S.op("scalar",
                 lambda e, c=c, t0=t0, sqb=sqb: e.activation(
                     sqb[:], tok(out_h[c], t0, TJ), AF.Square),
                 reads=[hkey(c)], writes=[r_("sq", c % 2)], tag="ln_sq")
            S.op("tensor",
                 lambda e, c=c, t0=t0: e.matmul(
                     pb_s0[0:1, :], B.onecol[:], tok(out_h[c], t0, TJ),
                     start=(c == 0), stop=(c == NCHUNK - 1)),
                 reads=[hkey(c), r_("onecol")], writes=[r_("pb", 5)], tag="ln_mm0")
            S.op("tensor",
                 lambda e, sqb=sqb, c=c: e.matmul(
                     pb_s1[0:1, :], B.onecol[:], sqb[:],
                     start=(c == 0), stop=(c == NCHUNK - 1)),
                 reads=[r_("sq", c % 2), r_("onecol")], writes=[r_("pb", 6)],
                 tag="ln_mm1")
        def slot(i):
            return B.rowf[:, i * TJ:(i + 1) * TJ]

        mu, b1r, b2r, var_s = (slot(i) for i in range(4))
        ex = tmp = slot(4)
        K_mu, K_b1, K_b2, K_var = (r_("rowf", i) for i in range(4))
        K_ex = K_tmp = r_("rowf", 4)
        eps_ap, eps_k = cvec(S, B, EPS, 0, 1)
        mh_ap, mh_k = cvec(S, B, -0.5, 0, 1)
        S.op("vector", lambda e, mu=mu: e.tensor_scalar(
            mu, pb_s0[0:1, :], 1.0 / D, None, ALU.mult),
            reads=[r_("pb", 5)], writes=[K_mu], tag="ln_mu")
        S.op("vector", lambda e: e.tensor_scalar(
            pb_s1[0:1, :], pb_s1[0:1, :], 1.0 / D, None, ALU.mult),
            reads=[r_("pb", 6)], writes=[r_("pb", 6)], tag="ln_ms")
        S.op("vector", lambda e, mu=mu, tmp=tmp: e.tensor_mul(tmp, mu, mu),
             reads=[K_mu], writes=[K_tmp], tag="ln_mu2")
        S.op("vector", lambda e, tmp=tmp: e.tensor_sub(
            pb_s1[0:1, :], pb_s1[0:1, :], tmp),
            reads=[K_tmp, r_("pb", 6)], writes=[r_("pb", 6)], tag="ln_var")
        if l0ffn:
            S.op("vector", lambda e, var_s=var_s: e.tensor_copy(
                var_s, pb_s1[0:1, :]),
                reads=[r_("pb", 6)], writes=[K_var], tag="lf_var")
        S.op("scalar", lambda e, a=eps_ap: e.activation(
            pb_s1[0:1, :], pb_s1[0:1, :], AF.Ln, bias=a),
            reads=[r_("pb", 6), eps_k], writes=[r_("pb", 6)], tag="ln_ln")
        S.op("scalar", lambda e, b1r=b1r, a=mh_ap: e.activation(
            b1r, pb_s1[0:1, :], AF.Exp, scale=a),
            reads=[r_("pb", 6), mh_k], writes=[K_b1], tag="ln_exp")
        S.op("vector", lambda e, mu=mu, b1r=b1r, b2r=b2r: e.scalar_tensor_tensor(
            b2r, mu, -1.0, b1r, ALU.mult, ALU.mult),
            reads=[K_mu, K_b1], writes=[K_b2], tag="ln_b2")
        if l0ffn:
            S.op("vector", lambda e, var_s=var_s, b1r=b1r, ex=ex: e.tensor_mul(
                ex, var_s, b1r), reads=[K_var, K_b1], writes=[K_ex], tag="lf_t")
            S.op("vector", lambda e, ex=ex, b1r=b1r: e.tensor_mul(ex, ex, b1r),
                 reads=[K_ex, K_b1], writes=[K_ex], tag="lf_t2")
            S.op("scalar", lambda e, ex=ex, a=eps_ap: e.activation(
                ex, ex, AF.Ln, bias=a),
                reads=[K_ex, eps_k], writes=[K_ex], tag="lf_ln")
            S.op("scalar", lambda e, ex=ex, a=mh_ap: e.activation(
                ex, ex, AF.Exp, scale=a),
                reads=[K_ex, mh_k], writes=[K_ex], tag="lf_exp")
            _bc_row(S, B, b1r, K_b1, B.b3bc, r_("b3bc"), t0)
            _bc_row(S, B, b2r, K_b2, B.b4bc, r_("b4bc"), t0)
            S.op("vector", lambda e, b1r=b1r, ex=ex: e.tensor_mul(b1r, b1r, ex),
                 reads=[K_ex, K_b1], writes=[K_b1], tag="lf_f1")
            S.op("vector", lambda e, b2r=b2r, ex=ex: e.tensor_mul(b2r, b2r, ex),
                 reads=[K_ex, K_b2], writes=[K_b2], tag="lf_f2")
        _bc_row(S, B, b1r, K_b1, B.b1bc, r_("b1bc"), t0)
        _bc_row(S, B, b2r, K_b2, B.b2bc, r_("b2bc"), t0)

    for c in range(NCHUNK):
        g = gamma_vec[c] if gamma_vec is not None else None
        bta = beta_vec[c] if beta_vec is not None else None
        if g is None:
            S.op("vector", lambda e, c=c: e.tensor_mul(
                tok(out_h[c], 0, T), tok(out_h[c], 0, T), B.b1bc[:]),
                reads=[hkey(c), r_("b1bc")], writes=[hkey(c)], tag="ln_ap1")
            S.op("vector", lambda e, c=c: e.tensor_add(
                tok(out_h[c], 0, T), tok(out_h[c], 0, T), B.b2bc[:]),
                reads=[hkey(c), r_("b2bc")], writes=[hkey(c)], tag="ln_ap2")
        else:
            S.op("vector", lambda e, c=c, g=g: e.scalar_tensor_tensor(
                tok(out_h[c], 0, T), tok(out_h[c], 0, T), g, B.b1bc[:],
                ALU.mult, ALU.mult),
                reads=[hkey(c), r_("b1bc"), gb_key], writes=[hkey(c)],
                tag="ln_ap1g")
            S.op("vector", lambda e, c=c, g=g: e.scalar_tensor_tensor(
                tok(out_h[c], 0, T), B.b2bc[:], g, tok(out_h[c], 0, T),
                ALU.mult, ALU.add),
                reads=[hkey(c), r_("b2bc"), gb_key], writes=[hkey(c)],
                tag="ln_ap2g")
            if bta is not None:
                S.op("vector", lambda e, c=c, bta=bta: e.tensor_scalar(
                    tok(out_h[c], 0, T), tok(out_h[c], 0, T), bta, None, ALU.add),
                    reads=[hkey(c), gb_key], writes=[hkey(c)], tag="ln_ap3g")


def _bc_row(S, B, row_ap, row_key, dst, dst_key, t0):
    pb_b = B.pb[7]
    rh = B.rowh[:]
    K_rh = r_("rowh")
    S.op("vector", lambda e, rh=rh, row_ap=row_ap: e.tensor_copy(rh, row_ap),
         reads=[row_key], writes=[K_rh], tag="bc_cast")
    S.op("tensor", lambda e, rh=rh: e.matmul(
        pb_b[:, 0:TJ], B.ones_row[:], rh, start=True, stop=True),
        reads=[K_rh, r_("ones_row")], writes=[r_("pb", 7)], tag="bc_mm")
    S.op("scalar", lambda e, dst=dst, t0=t0: e.copy(dst[:, t0:t0 + TJ], pb_b[:, 0:TJ]),
         reads=[r_("pb", 7)], writes=[dst_key], tag="bc_evac")



def _l0a_ranges():
    out = []
    for r in (-1, 0, 1, 2, 3, 4):
        qlo = max(0, 128 * r - 12)
        qhi = min(512, 128 * r + 140)
        out.append((r, qlo, qhi))
    return out


def _l0a_masks():
    INF = np.float32(np.inf)
    packed = []
    offs = {}
    pos = 0
    for r, qlo, qhi in _l0a_ranges():
        w = qhi - qlo
        jj = np.arange(128)[:, None]
        qq = np.arange(qlo, qhi)[None, :]
        allowed = np.abs(-128 * r + qq - jj) <= 12
        packed.append(np.where(allowed, INF, 0.0).astype(bf16))
        offs[r] = (pos, w)
        pos += w
    band = np.concatenate(packed, axis=1)
    jj = np.arange(128)[:, None]
    qq = np.arange(512)[None, :]
    allowed0 = (np.abs(qq - jj) <= 12) | (jj < 8) | (qq < 8)
    m0 = np.where(allowed0, INF, 0.0).astype(bf16)
    return band, offs, m0


def alloc_l0a(nc, ctx):
    A = Bufs()
    ent = ctx.enter_context
    A.kh = [ent(nc.sbuf_tensor(f"l0a_kh{i}", [128, T], BF16)) for i in range(4)]
    A.qh = [ent(nc.sbuf_tensor("l0a_qh0", [128, T], BF16))]
    A.vt = [ent(nc.sbuf_tensor(f"l0a_vt{t}", [128, 260], BF16)) for t in range(16)]
    A.ao = [ent(nc.sbuf_tensor(f"l0a_ao{c}", [128, T], BF16)) for c in range(NCHUNK)]
    A.tmp = [ent(nc.sbuf_tensor(f"l0a_tmp{i}", [128, TJ], BF16)) for i in range(2)]
    A.pt = [ent(nc.sbuf_tensor(f"l0a_pt{i}", [128, TJ], BF16)) for i in range(3)]
    A.recbc = ent(nc.sbuf_tensor("l0a_recbc", [64, TJ], BF16))
    A.wa = [ent(nc.sbuf_tensor(f"l0a_wa{c}", [128, 128], BF16)) for c in range(NCHUNK)]
    A.wb = [ent(nc.sbuf_tensor(f"l0a_wb{c}", [128, 128], BF16)) for c in range(NCHUNK)]
    A.wv = [ent(nc.sbuf_tensor(f"l0a_wv{c}", [128, 256], BF16)) for c in range(NCHUNK)]
    A.bias = ent(nc.sbuf_tensor("l0a_bias", [128, 8], F32))
    A.biasp = ent(nc.sbuf_tensor("l0a_biasp", [128, 8], F32))
    A.bk = ent(nc.sbuf_tensor("l0a_bk", [64, 4], F32))
    A.bkp = ent(nc.sbuf_tensor("l0a_bkp", [64, 4], F32))
    A.bo = ent(nc.sbuf_tensor("l0a_bo", [128, 8], F32))
    A.bvrow = ent(nc.sbuf_tensor("l0a_bvrow", [1, 256], BF16))
    A.hw8 = ent(nc.sbuf_tensor("l0a_hw8", [128, 8], F32))
    A.cos = ent(nc.sbuf_tensor("l0a_cos", [128, T], BF16))
    A.sin = ent(nc.sbuf_tensor("l0a_sin", [128, T], BF16))
    bw = sum(qhi - qlo for _, qlo, qhi in _l0a_ranges())
    A.band = ent(nc.sbuf_tensor("l0a_band", [128, bw], BF16))
    A.m0 = ent(nc.sbuf_tensor("l0a_m0", [128, 512], BF16))
    return A


def emit_l0a_setup(S, nc, B, A, W):
    S.dma("sync", A.cos[:], W["cst_cosA"], writes=[r_("l0a_cos")])
    S.dma("sync", A.sin[:], W["cst_sinA"], writes=[r_("l0a_sin")])
    S.dma("sync", A.band[:], W["cst_l0a_band"], writes=[r_("l0a_band")])
    S.dma("sync", A.m0[:], W["cst_l0a_m0"], writes=[r_("l0a_m0")])
    S.dma("sync", A.bias[:], W["p_l0a_bq8"], writes=[r_("l0a_bias")])
    S.dma("sync", A.biasp[:], W["p_l0a_bqp8"], writes=[r_("l0a_biasp")])
    S.dma("sync", A.bk[:], W["p_l0a_bk4"], writes=[r_("l0a_bk")])
    S.dma("sync", A.bkp[:], W["p_l0a_bkp4"], writes=[r_("l0a_bkp")])
    S.dma("sync", A.bo[:], W["p_l0a_bo8"], writes=[r_("l0a_bo")])
    S.dma("gpsimd", A.bvrow[:], W["p_l0a_bvrow"], writes=[r_("l0a_bvrow")])
    S.dma("sync", A.hw8[:], W["p_l0a_hw8"], writes=[r_("l0a_hw8")])
    S.op("vector", lambda e: e.tensor_scalar(
        A.hw8[:], A.hw8[:], 1.0 / 8.0, None, ALU.mult),
        reads=[r_("l0a_hw8")], writes=[r_("l0a_hw8")], tag="hw8")


def _load_w_tiles(S, tiles, physkey, dram, n0, w, nk=NCHUNK, queue="gpsimd",
                  coloff=0):
    for c in range(nk):
        S.dma(queue, tiles[c][:, coloff:coloff + w],
              dram[c * 128:(c + 1) * 128, n0:n0 + w],
              writes=[r_(physkey, c)], tag="wload")


def emit_dual_rope_proj(S, B, A, wA, wAk, wB, wBk, bA, bB, kbA, kbB,
                        cos, sin, kcos, ksin, psA, psB, out_tile, out_key,
                        hw_ap=None, hwkey=None, n_k=NCHUNK, hsrc=None):
    hsrc = hsrc or B.h
    for tj in range(NTJ):
        t0 = tj * TJ
        for c in range(n_k):
            S.op("tensor", lambda e, c=c, t0=t0: e.matmul(
                B.pb[psA][:], wA[c][:, 0:128], tok(hsrc[c], t0, TJ),
                start=(c == 0), stop=(c == n_k - 1)),
                reads=[r_(wAk, c), r_("h", c)], writes=[r_("pb", psA)],
                tag="qp_a")
            S.op("tensor", lambda e, c=c, t0=t0: e.matmul(
                B.pb[psB][:], wB[c][:, 0:128], tok(hsrc[c], t0, TJ),
                start=(c == 0), stop=(c == n_k - 1)),
                reads=[r_(wBk, c), r_("h", c)], writes=[r_("pb", psB)],
                tag="qp_b")
        t1, t2 = A.tmp[0], A.tmp[1]
        S.op("vector", lambda e, t0=t0, t1=t1: e.scalar_tensor_tensor(
            t1[:], B.pb[psA][:], bA, cos[:, t0:t0 + TJ], ALU.add, ALU.mult),
            reads=[r_("pb", psA), kbA, kcos], writes=[r_("l0a_t", 0)], tag="rope1")
        S.op("vector", lambda e, t0=t0, t2=t2: e.scalar_tensor_tensor(
            t2[:], B.pb[psB][:], bB, sin[:, t0:t0 + TJ], ALU.add, ALU.mult),
            reads=[r_("pb", psB), kbB, ksin], writes=[r_("l0a_t", 1)], tag="rope2")
        if hw_ap is None:
            S.op("vector", lambda e, t0=t0, t1=t1, t2=t2: e.tensor_add(
                out_tile[:, t0:t0 + TJ], t1[:], t2[:]),
                reads=[r_("l0a_t", 0), r_("l0a_t", 1)], writes=[out_key],
                tag="rope3")
        else:
            S.op("vector", lambda e, t1=t1, t2=t2: e.tensor_add(
                t1[:], t1[:], t2[:]),
                reads=[r_("l0a_t", 0), r_("l0a_t", 1)], writes=[r_("l0a_t", 0)],
                tag="rope3")
            S.op("vector", lambda e, t0=t0, t1=t1: e.tensor_scalar(
                out_tile[:, t0:t0 + TJ], t1[:], hw_ap, None, ALU.mult),
                reads=[r_("l0a_t", 0), hwkey], writes=[out_key], tag="rope4")


def emit_l0_attn_a(S, nc, B, A, W):
    band_offs = {r: None for r, _, _ in _l0a_ranges()}
    pos = 0
    for r, qlo, qhi in _l0a_ranges():
        band_offs[r] = (pos, qlo, qhi)
        pos += qhi - qlo
    negC, negC_k = cvec(S, B, -C_L0A)

    _load_w_tiles(S, A.wv, "l0a_wv", W["L0_attn_a_Wv"], 0, 256)
    for tt in range(16):
        vt = A.vt[tt]
        S.op("vector", lambda e, vt=vt: e.memset(
            vt.ap().rearrange("p (a b) -> p a b", b=65)[:, :, 64:65], 1.0),
            writes=[r_("l0a_vt", tt)], tag="vones")
        for c in range(NCHUNK):
            S.op("tensor", lambda e, c=c, tt=tt: e.matmul(
                B.pb[6][:, 0:256], tok(B.h[c], tt * 128, 128), A.wv[c][:, 0:256],
                start=(c == 0), stop=False),
                reads=[r_("l0a_wv", c), r_("h", c)], writes=[r_("pb", 6)],
                tag="vproj")
        S.op("tensor", lambda e: e.matmul(
            B.pb[6][:, 0:256], B.ones_row[:], A.bvrow[:], start=False, stop=True),
            reads=[r_("l0a_bvrow"), r_("ones_row")], writes=[r_("pb", 6)],
            tag="vbias")
        S.op("scalar", lambda e, vt=vt: e.copy(
            vt.ap().rearrange("p (a b) -> p a b", b=65)[:, :, 0:64],
            B.pb[6][:, 0:256].rearrange("p (a b) -> p a b", b=64)),
            reads=[r_("pb", 6)], writes=[r_("l0a_vt", tt)], tag="vevac")

    for kv in range(4):
        _load_w_tiles(S, A.wa, "l0a_wa", W["L0_attn_a_Wk"], kv * 64, 64)
        _load_w_tiles(S, A.wb, "l0a_wb", W["p_l0a_wkp"], kv * 64, 64)
        kt_tile = A.kh[kv]
        kkey = r_("l0a_kh", kv)
        for tj in range(NTJ):
            t0 = tj * TJ
            for c in range(NCHUNK):
                S.op("tensor", lambda e, c=c, t0=t0: e.matmul(
                    B.pb[0][0:64, :], A.wa[c][:, 0:64], tok(B.h[c], t0, TJ),
                    start=(c == 0), stop=(c == NCHUNK - 1)),
                    reads=[r_("l0a_wa", c), r_("h", c)], writes=[r_("pb", 0)],
                    tag="kp_a")
                S.op("tensor", lambda e, c=c, t0=t0: e.matmul(
                    B.pb[1][0:64, :], A.wb[c][:, 0:64], tok(B.h[c], t0, TJ),
                    start=(c == 0), stop=(c == NCHUNK - 1)),
                    reads=[r_("l0a_wb", c), r_("h", c)], writes=[r_("pb", 1)],
                    tag="kp_b")
            t1, t2 = A.tmp[0], A.tmp[1]
            S.op("vector", lambda e, t0=t0, kv=kv: e.scalar_tensor_tensor(
                t1[0:64, :], B.pb[0][0:64, :], A.bk[0:64, kv:kv + 1],
                A.cos[0:64, t0:t0 + TJ], ALU.add, ALU.mult),
                reads=[r_("pb", 0), r_("l0a_bk"), r_("l0a_cos")],
                writes=[r_("l0a_t", 0)], tag="krope1")
            S.op("vector", lambda e, t0=t0, kv=kv: e.scalar_tensor_tensor(
                t2[0:64, :], B.pb[1][0:64, :], A.bkp[0:64, kv:kv + 1],
                A.sin[0:64, t0:t0 + TJ], ALU.add, ALU.mult),
                reads=[r_("pb", 1), r_("l0a_bkp"), r_("l0a_sin")],
                writes=[r_("l0a_t", 1)], tag="krope2")
            S.op("vector", lambda e, t0=t0, kt_tile=kt_tile: e.tensor_add(
                kt_tile[0:64, t0:t0 + TJ], t1[0:64, :], t2[0:64, :]),
                reads=[r_("l0a_t", 0), r_("l0a_t", 1)], writes=[kkey],
                tag="krope3")
        S.dma("gpsimd", kt_tile[64:128, :], kt_tile[0:64, :],
              reads=[kkey], writes=[kkey], tag="kdup")

    for c in range(NCHUNK):
        qb = A.qh[0]
        qbk = r_("l0a_qh", 0)
        _load_w_tiles(S, A.wa, "l0a_wa", W["L0_attn_a_Wq"], c * 128, 128)
        _load_w_tiles(S, A.wb, "l0a_wb", W["p_l0a_wqp"], c * 128, 128)
        emit_dual_rope_proj(
            S, B, A, A.wa, "l0a_wa", A.wb, "l0a_wb",
            A.bias[:, c:c + 1], A.biasp[:, c:c + 1], r_("l0a_bias"), r_("l0a_biasp"),
            A.cos, A.sin, r_("l0a_cos"), r_("l0a_sin"),
            0, 1, qb, qbk,
            hw_ap=A.hw8[:, c:c + 1], hwkey=r_("l0a_hw8"))
        for hh in range(2):
            h = 2 * c + hh
            kv = h // 4
            krow = 64 * hh
            klhs = A.kh[kv]
            kkey = r_("l0a_kh", kv)
            qrows = qb[64 * hh:64 * hh + 64, :]
            for a in range(4):
                q0 = a * TJ
                emit_l0a_head_chunk(S, B, A, h, a, q0, qrows, qbk, klhs, kkey,
                                    krow, kv, band_offs, negC, negC_k, c, hh)

    for oc in range(NCHUNK):
        _load_w_tiles(S, A.wa, "l0a_wa", W["L0_attn_a_Wo"], oc * 128, 128)
        for tj in range(NTJ):
            t0 = tj * TJ
            for c in range(NCHUNK):
                S.op("tensor", lambda e, c=c, t0=t0: e.matmul(
                    B.pb[4][:], A.wa[c][:, 0:128], A.ao[c][:, t0:t0 + TJ],
                    start=(c == 0), stop=(c == NCHUNK - 1)),
                    reads=[r_("l0a_wa", c), r_("l0a_ao", c)],
                    writes=[r_("pb", 4)], tag="wo_mm")
            S.op("vector", lambda e, oc=oc, t0=t0: e.scalar_tensor_tensor(
                tok(B.x[oc], t0, TJ), B.pb[4][:], A.bo[:, oc:oc + 1],
                tok(B.x[oc], t0, TJ), ALU.add, ALU.add),
                reads=[r_("pb", 4), r_("l0a_bo"), r_("x", oc)],
                writes=[r_("x", oc)], tag="wo_res")


def emit_l0a_head_chunk(S, B, A, h, a, q0, qrows, qbk, klhs, kkey, krow, kv,
                        band_offs, negC, negC_k, c, hh):
    ps_S, ps_O, ps_G, ps_B = 2, 3, 5, 7
    ktiles = []
    if a == 0:
        for kt in range(0, 5):
            ktiles.append((kt, kt))
    else:
        for r in (-1, 0, 1, 2, 3, 4):
            kt = 4 * a + r
            if 0 <= kt < 16:
                ktiles.append((kt, r))
    n_pv = len(ktiles) + (11 if a == 0 else 1)
    pv_i = [0]

    def pv(lhsT_ap, lhs_key, rhs_ap, rhs_key, qlo, w):
        st = pv_i[0] == 0
        sp = pv_i[0] == n_pv - 1
        pv_i[0] += 1
        S.op("tensor", lambda e: e.matmul(
            B.pb[ps_O][0:65, qlo:qlo + w],
            lhsT_ap, rhs_ap, start=st, stop=sp),
            reads=[lhs_key, rhs_key], writes=[r_("pb", ps_O)], tag="pv")

    if a != 0:
        pt = A.pt[2]
        ptk = r_("l0a_pt", 2)
        S.op("tensor", lambda e: e.matmul(
            B.pb[ps_G][0:8, 0:TJ], klhs[krow:krow + 64, 0:8],
            qrows[:, q0:q0 + TJ], start=True, stop=True),
            reads=[kkey, qbk], writes=[r_("pb", ps_G)], tag="sk_mm",
            pe_base=krow)
        S.op("scalar", lambda e, pt=pt: e.activation(
            pt[0:8, 0:TJ], B.pb[ps_G][0:8, 0:TJ], AF.Exp, bias=negC[0:8, :]),
            reads=[r_("pb", ps_G), negC_k], writes=[ptk], tag="sk_exp")
        pv(A.vt[0][0:8, 65 * kv:65 * kv + 65], r_("l0a_vt", 0),
           pt[0:8, 0:TJ], ptk, 0, TJ)

    for i, (kt, r) in enumerate(ktiles):
        if a == 0:
            qlo, qhi = (0, 512) if kt == 0 else band_offs[r][1:]
        else:
            qlo, qhi = band_offs[r][1:]
        w = qhi - qlo
        pt = A.pt[i % 2]
        ptk = r_("l0a_pt", i % 2)
        S.op("tensor", lambda e, kt=kt, qlo=qlo, w=w, pt=pt: e.matmul(
            B.pb[ps_S][:, 0:w], klhs[krow:krow + 64, kt * 128:(kt + 1) * 128],
            qrows[:, q0 + qlo:q0 + qlo + w], start=True, stop=True),
            reads=[kkey, qbk], writes=[r_("pb", ps_S)], tag="s_mm",
            pe_base=krow)
        S.op("scalar", lambda e, w=w, pt=pt: e.activation(
            pt[:, 0:w], B.pb[ps_S][:, 0:w], AF.Exp, bias=negC),
            reads=[r_("pb", ps_S), negC_k], writes=[ptk], tag="s_exp")
        if a == 0 and kt == 0:
            S.op("vector", lambda e, w=w, pt=pt: e.tensor_tensor(
                pt[:, 0:w], pt[:, 0:w], A.m0[:, 0:w], ALU.min),
                reads=[ptk, r_("l0a_m0")], writes=[ptk], tag="s_mask")
        else:
            moff = band_offs[r][0]
            S.op("vector", lambda e, w=w, moff=moff, pt=pt: e.tensor_tensor(
                pt[:, 0:w], pt[:, 0:w], A.band[:, moff:moff + w], ALU.min),
                reads=[ptk, r_("l0a_band")], writes=[ptk], tag="s_mask")
        pv(A.vt[kt][:, 65 * kv:65 * kv + 65], r_("l0a_vt", kt),
           pt[:, 0:w], ptk, qlo, w)

    if a == 0:
        for kt in range(5, 16):
            pt = A.pt[2]
            ptk = r_("l0a_pt", 2)
            S.op("tensor", lambda e, kt=kt: e.matmul(
                B.pb[ps_G][:, 0:8], klhs[krow:krow + 64, kt * 128:(kt + 1) * 128],
                qrows[:, q0:q0 + 8], start=True, stop=True),
                reads=[kkey, qbk], writes=[r_("pb", ps_G)], tag="sg_mm",
                pe_base=krow)
            S.op("scalar", lambda e: e.activation(
                pt[:, 0:8], B.pb[ps_G][:, 0:8], AF.Exp, bias=negC),
                reads=[r_("pb", ps_G), negC_k], writes=[ptk], tag="sg_exp")
            pv(A.vt[kt][:, 65 * kv:65 * kv + 65], r_("l0a_vt", kt),
               pt[:, 0:8], ptk, 0, 8)

    rec = B.rowf[:, 3 * TJ:4 * TJ]
    rec_k = r_("rowf", 3)
    S.op("vector", lambda e, rec=rec: e.reciprocal(rec, B.pb[ps_O][64:65, 0:TJ]),
         reads=[r_("pb", ps_O)], writes=[rec_k], tag="n_rec")
    S.op("vector", lambda e, rec=rec: e.tensor_copy(B.rowh[:], rec),
         reads=[rec_k], writes=[r_("rowh")], tag="n_cast")
    S.op("tensor", lambda e: e.matmul(
        B.pb[ps_B][0:64, 0:TJ], B.ones_row[:, 0:64], B.rowh[:],
        start=True, stop=True),
        reads=[r_("rowh"), r_("ones_row")], writes=[r_("pb", ps_B)],
        tag="n_bcmm")
    S.op("scalar", lambda e: e.copy(A.recbc[:], B.pb[ps_B][0:64, 0:TJ]),
         reads=[r_("pb", ps_B)], writes=[r_("l0a_recbc")], tag="n_bcevac")
    S.op("vector", lambda e, c=c, hh=hh: e.tensor_tensor(
        A.ao[c][64 * hh:64 * hh + 64, q0:q0 + TJ], B.pb[ps_O][0:64, 0:TJ],
        A.recbc[:], ALU.mult),
        reads=[r_("pb", ps_O), r_("l0a_recbc")], writes=[r_("l0a_ao", c)],
        tag="n_mul")



def _declare_x_io(nc):
    x_in = nc.dram_tensor("x_io_in", [NSEQ, NCHUNK, 128, T], F32,
                          kind="ExternalInput").ap()
    x_out = nc.dram_tensor("x_io_out", [NSEQ, NCHUNK, 128, T], F32,
                           kind="ExternalOutput").ap()
    return x_in, x_out


def emit_load_x(S, B, x_in, seq):
    for c in range(NCHUNK):
        S.dma("sync", tok(B.x[c], 0, T), x_in[seq, c], writes=[r_("x", c)],
              tag="ldx")


def emit_store_x(S, B, x_out, seq):
    for c in range(NCHUNK):
        S.dma("sync", x_out[seq, c], tok(B.x[c], 0, T), reads=[r_("x", c)],
              tag="stx")


def _np_dt(a):
    if a.dtype == np.float32:
        return F32
    if a.dtype == np.int32:
        return I32
    if a.dtype == bf16:
        return BF16
    raise ValueError(a.dtype)


def declare_inputs(nc, arrays, names):
    W = {}
    for n in names:
        a = arrays[n]
        W[n] = nc.dram_tensor(n, list(a.shape), _np_dt(a), kind="ExternalInput").ap()
    return W


L0A_NAMES = ["L0_attn_a_Wq", "L0_attn_a_Wk", "L0_attn_a_Wv", "L0_attn_a_Wo",
             "p_l0a_bvrow", "p_l0a_wqp", "p_l0a_wkp", "p_l0a_bq8",
             "p_l0a_bqp8", "p_l0a_bk4", "p_l0a_bkp4", "p_l0a_bo8", "p_l0a_hw8",
             "cst_cosA", "cst_sinA", "cst_l0a_band", "cst_l0a_m0"]


def host_prep(inputs):
    P = dict(host_constants())
    permA1024 = np.array([_permA_idx(n) for n in range(1024)])
    permA256 = permA1024[:256]
    ii = inputs
    P["p_l0a_wqp"] = np.ascontiguousarray(ii["L0_attn_a_Wq"][:, permA1024])
    P["p_l0a_wkp"] = np.ascontiguousarray(ii["L0_attn_a_Wk"][:, permA256])
    P["p_l0a_bq8"] = np.ascontiguousarray(ii["L0_attn_a_bq"].reshape(8, 128).T)
    P["p_l0a_bqp8"] = np.ascontiguousarray(
        ii["L0_attn_a_bq"][permA1024].reshape(8, 128).T)
    P["p_l0a_bk4"] = np.ascontiguousarray(ii["L0_attn_a_bk"].reshape(4, 64).T)
    P["p_l0a_bkp4"] = np.ascontiguousarray(
        ii["L0_attn_a_bk"][permA256].reshape(4, 64).T)
    P["p_l0a_bo8"] = np.ascontiguousarray(ii["L0_attn_a_bo"].reshape(8, 128).T)
    P["p_l0a_bvrow"] = np.ascontiguousarray(ii["L0_attn_a_bv"].reshape(1, 256))
    hw = ii["L0_attn_a_head_weights"]
    P["p_l0a_hw8"] = np.ascontiguousarray(
        np.repeat(hw.reshape(8, 2), 64, axis=1).reshape(8, 128).T)
    band, offs, m0 = _l0a_masks()
    P["cst_l0a_band"] = band
    P["cst_l0a_m0"] = m0
    rot64 = np.array([((d + 32) % 64) for d in range(64)])
    qrot_cols = np.concatenate([256 * h + rot64 for h in range(8)])
    krot_cols = np.concatenate([256 * kv + rot64 for kv in range(2)])
    P["p_l0b_wqrot"] = np.ascontiguousarray(ii["L0_attn_b_Wq"][:, qrot_cols])
    P["p_l0b_wkrot"] = np.ascontiguousarray(ii["L0_attn_b_Wk"][:, krot_cols])
    P["p_l0b_wox"] = np.ascontiguousarray(
        ii["L0_attn_b_Wo"].reshape(16, 128, 8, 128).transpose(2, 1, 0, 3))
    permI1024 = np.arange(1024) ^ 1
    wqT = np.ascontiguousarray(ii["L1_attn_a_Wq"].T)
    wkT = np.ascontiguousarray(ii["L1_attn_a_Wk"].T)
    P["p_l1a_wqT"] = wqT
    P["p_l1a_wqTp"] = np.ascontiguousarray(wqT[:, permI1024])
    P["p_l1a_wkT"] = wkT
    P["p_l1a_wkTp"] = np.ascontiguousarray(wkT[:, permI1024])
    P["p_l1a_wvT"] = np.ascontiguousarray(ii["L1_attn_a_Wv"].T)
    P["p_l1a_wox"] = np.ascontiguousarray(
        ii["L1_attn_a_Wo"].T.reshape(8, 128, 8, 128).transpose(2, 1, 0, 3))
    INF = np.float32(np.inf)
    jj = np.arange(128)[:, None]
    qq = np.arange(128)[None, :]
    P["cst_l1_tri"] = np.where(jj <= qq, INF, 0.0).astype(bf16)
    tri = np.where(jj <= qq, INF, 0.0).astype(bf16)
    low = np.where(jj > qq, INF, 0.0).astype(bf16)
    P["cst_l1b_tri4"] = np.ascontiguousarray(np.tile(tri, (1, 4)))
    P["cst_l1b_low4"] = np.ascontiguousarray(np.tile(low, (1, 4)))
    permA1024b = np.array([_permA_idx(n) for n in range(1024)])
    wqTb = np.ascontiguousarray(ii["L1_attn_b_Wq"].T)
    wkTb = np.ascontiguousarray(ii["L1_attn_b_Wk"].T)
    P["p_l1b_wqT"] = wqTb
    P["p_l1b_wqTp"] = np.ascontiguousarray(wqTb[:, permA1024b])
    P["p_l1b_wkT"] = wkTb
    P["p_l1b_wkTp"] = np.ascontiguousarray(wkTb[:, permA1024b])
    P["p_l1b_wvT"] = np.ascontiguousarray(ii["L1_attn_b_Wv"].T)
    P["p_l1b_wox"] = np.ascontiguousarray(
        ii["L1_attn_b_Wo"].T.reshape(8, 128, 8, 128).transpose(2, 1, 0, 3))
    P["p_l1b_bo8"] = np.ascontiguousarray(ii["L1_attn_b_bo"].reshape(8, 128).T)
    rot64b = np.array([(d + 32) % 64 for d in range(64)])
    permqk = np.concatenate([64 * hh + rot64b for hh in range(16)])
    P["p_l1f_wqkvp"] = np.ascontiguousarray(ii["L1_ffn_wqkv"][:, :, permqk])
    P["p_l1f_wox"] = np.ascontiguousarray(
        ii["L1_ffn_wo"].reshape(2, 4, 128, 8, 128).transpose(0, 3, 2, 1, 4))
    qsv = np.zeros((128, 8), np.float32)
    for l in range(2):
        qsv[:, 4 * l + 0] = np.tile(ii["L1_ffn_q_scale"][l], 2)
        qsv[:, 4 * l + 1] = np.tile(ii["L1_ffn_q_scale"][l][rot64b], 2)
        qsv[:, 4 * l + 2] = np.tile(ii["L1_ffn_k_scale"][l], 2)
        qsv[:, 4 * l + 3] = np.tile(ii["L1_ffn_k_scale"][l][rot64b], 2)
    P["p_l1f_qsv"] = qsv
    P["p_l1f_lg"] = np.ascontiguousarray(
        ii["L1_ffn_ln_g"].reshape(2, 8, 128).transpose(0, 2, 1))
    P["p_l1f_lb"] = np.ascontiguousarray(
        ii["L1_ffn_ln_b"].reshape(2, 8, 128).transpose(0, 2, 1))
    b1p = np.zeros((2, 128, 22), np.float32)
    b1gp = np.zeros((2, 128, 22), np.float32)
    for l in range(2):
        a = np.zeros(22 * 128, np.float32)
        a[:2730] = ii["L1_ffn_b1"][l][:2730]
        b1p[l] = a.reshape(22, 128).T
        gpart = np.zeros(22 * 128, np.float32)
        gpart[:2730] = ii["L1_ffn_b1"][l][2730:]
        b1gp[l] = gpart.reshape(22, 128).T
    P["p_l1f_b1a"] = b1p
    P["p_l1f_b1g"] = b1gp
    P["p_l1f_b2"] = np.ascontiguousarray(
        ii["L1_ffn_b2"].reshape(2, 8, 128).transpose(0, 2, 1))
    P["cst_geq"] = np.where(jj >= qq, INF, 0.0).astype(bf16)
    P["p_l0f_g8"] = np.ascontiguousarray(ii["L0_ffn_ln_gamma"].reshape(8, 128).T)
    P["p_l0f_be8"] = np.ascontiguousarray(ii["L0_ffn_ln_beta"].reshape(8, 128).T)
    P["p_l0f_b1"] = np.ascontiguousarray(ii["L0_ffn_fc1_bias"].reshape(32, 128).T)
    P["p_l0f_b2"] = np.ascontiguousarray(ii["L0_ffn_fc2_bias"].reshape(8, 128).T)
    return P


def shard_x(x):
    shards = []
    for ci in range(8):
        xs = x[4 * ci:4 * ci + 4]
        xt = np.ascontiguousarray(xs.transpose(0, 2, 1))
        shards.append(xt.reshape(NSEQ, NCHUNK, 128, T))
    return shards



L0F_NAMES = ["L0_ffn_fc1_q", "L0_ffn_fc1_scales", "L0_ffn_fc2_q",
             "L0_ffn_fc2_scales", "p_l0f_g8", "p_l0f_be8", "p_l0f_b1",
             "p_l0f_b2"]


def alloc_l0f(nc, ctx):
    F = Bufs()
    ent = ctx.enter_context
    B3 = ent(nc.sbuf_tensor("b3bc", [128, T], BF16))
    B4 = ent(nc.sbuf_tensor("b4bc", [128, T], BF16))
    F.b3bc, F.b4bc = B3, B4
    F.qi = ent(nc.sbuf_tensor("l0f_qi", [128, 512], I32))
    F.srow = ent(nc.sbuf_tensor("l0f_srow", [1, 512], F32))
    F.srowh = ent(nc.sbuf_tensor("l0f_srowh", [1, 512], BF16))
    F.scbc = ent(nc.sbuf_tensor("l0f_scbc", [128, 512], F32))
    F.wout = ent(nc.sbuf_tensor("l0f_wout", [128, 512], BF16))
    F.wacat = ent(nc.sbuf_tensor("l0f_wacat", [128, 1024], BF16))
    F.w2cat = ent(nc.sbuf_tensor("l0f_w2cat", [128, 4096], BF16))
    F.z = [ent(nc.sbuf_tensor(f"l0f_z{i}", [128, 512], BF16)) for i in range(32)]
    F.u = ent(nc.sbuf_tensor("l0f_u", [128, 512], F32))
    F.g8 = ent(nc.sbuf_tensor("l0f_g8", [128, 8], F32))
    F.be8 = ent(nc.sbuf_tensor("l0f_be8", [128, 8], F32))
    F.b1 = ent(nc.sbuf_tensor("l0f_b1", [128, 32], F32))
    F.b2h = ent(nc.sbuf_tensor("l0f_b2h", [128, 8], F32))
    return F


def emit_l0f_setup(S, nc, B, F, W):
    S.dma("sync", F.g8[:], W["p_l0f_g8"], writes=[r_("l0f_g8")])
    S.dma("sync", F.be8[:], W["p_l0f_be8"], writes=[r_("l0f_be8")])
    S.dma("sync", F.b1[:], W["p_l0f_b1"], writes=[r_("l0f_b1")])
    S.dma("sync", F.b2h[:], W["p_l0f_b2"], writes=[r_("l0f_b2h")])
    S.op("vector", lambda e: e.tensor_scalar(
        F.b2h[:], F.b2h[:], 0.5, None, ALU.mult),
        reads=[r_("l0f_b2h")], writes=[r_("l0f_b2h")], tag="b2half")
    scr_w1 = nc.dram_tensor("scr_w1", [32, 128, 8, 128], BF16, kind="Internal").ap()
    scr_w2 = nc.dram_tensor("scr_w2", [8, 128, 32, 128], BF16, kind="Internal").ap()
    F.scr_w1, F.scr_w2 = scr_w1, scr_w2
    for wi, (qd, sd, KK, NN) in enumerate([
            (W["L0_ffn_fc1_q"], W["L0_ffn_fc1_scales"], 1024, 4096),
            (W["L0_ffn_fc2_q"], W["L0_ffn_fc2_scales"], 4096, 1024)]):
        for c in range(KK // 128):
            for nb in range(NN // 512):
                n0 = nb * 512
                S.dma("sync", F.qi[:], qd[c * 128:(c + 1) * 128, n0:n0 + 512],
                      writes=[r_("l0f_qi")], tag="dq_q")
                S.dma("sync", F.srow[:], sd[c:c + 1, n0:n0 + 512],
                      writes=[r_("l0f_srow")], tag="dq_s")
                S.op("vector", lambda e: e.tensor_copy(F.srowh[:], F.srow[:]),
                     reads=[r_("l0f_srow")], writes=[r_("l0f_srowh")], tag="dq_c")
                S.op("tensor", lambda e: e.matmul(
                    B.pb[7][:], B.ones_row[:], F.srowh[:], start=True, stop=True),
                    reads=[r_("l0f_srowh"), r_("ones_row")], writes=[r_("pb", 7)],
                    tag="dq_mm")
                S.op("scalar", lambda e: e.copy(F.scbc[:], B.pb[7][:]),
                     reads=[r_("pb", 7)], writes=[r_("l0f_scbc")], tag="dq_ev")
                S.op("vector", lambda e: e.scalar_tensor_tensor(
                    F.wout[:], F.qi[:], -128.0, F.scbc[:], ALU.add, ALU.mult),
                    reads=[r_("l0f_qi"), r_("l0f_scbc")], writes=[r_("l0f_wout")],
                    tag="dq_stt")
                if wi == 0:
                    dst = scr_w1[4 * nb:4 * nb + 4, :, c, :].rearrange(
                        "a p j -> p a j")
                else:
                    dst = scr_w2[4 * nb:4 * nb + 4, :, c, :].rearrange(
                        "a p j -> p a j")
                S.dma("gpsimd", dst,
                      F.wout[:].rearrange("p (a j) -> p a j", a=4),
                      reads=[r_("l0f_wout")], writes=[r_("scr", wi, c, nb)],
                      tag="dq_st")


def emit_l0_ffn(S, nc, B, F, W):
    B.b3bc, B.b4bc = F.b3bc, F.b4bc
    gvec = [F.g8[:, c:c + 1] for c in range(NCHUNK)]
    bvec = [F.be8[:, c:c + 1] for c in range(NCHUNK)]
    emit_ln(S, B, "lf", gamma_vec=gvec, beta_vec=bvec, gb_key=r_("l0f_g8"),
            l0ffn=True)
    for tj in range(NTJ):
        t0 = tj * TJ
        for ns in range(32):
            zb = F.z[ns]
            S.dma("sync", F.wacat[:],
                  F.scr_w1[ns].rearrange("p c j -> p (c j)"),
                  reads=[r_("scr", 0, c_, ns // 4) for c_ in range(8)],
                  writes=[r_("l0f_wacat")], tag="w1load")
            for c in range(NCHUNK):
                S.op("tensor", lambda e, c=c, t0=t0, ns=ns: e.matmul(
                    B.pb[ns % 2][:], F.wacat[:, c * 128:(c + 1) * 128],
                    tok(B.h[c], t0, TJ),
                    start=(c == 0), stop=(c == NCHUNK - 1)),
                    reads=[r_("l0f_wacat"), r_("h", c)], writes=[r_("pb", ns % 2)],
                    tag="f1_mm")
            S.op("scalar", lambda e, ns=ns, zb=zb: e.activation(
                zb[:], B.pb[ns % 2][:], AF.Silu, bias=F.b1[:, ns:ns + 1]),
                reads=[r_("pb", ns % 2), r_("l0f_b1")], writes=[r_("l0f_z", ns)],
                tag="f1_silu")
        for oc in range(NCHUNK):
            S.dma("sync", F.w2cat[:],
                  F.scr_w2[oc].rearrange("p i j -> p (i j)"),
                  reads=[r_("scr", 1, c_, oc // 4) for c_ in range(32)],
                  writes=[r_("l0f_w2cat")], tag="w2load")
            for i in range(32):
                S.op("tensor", lambda e, i=i, oc=oc: e.matmul(
                    B.pb[2 + oc % 2][:], F.w2cat[:, i * 128:(i + 1) * 128],
                    F.z[i][:],
                    start=(i == 0), stop=(i == 31)),
                    reads=[r_("l0f_w2cat"), r_("l0f_z", i)],
                    writes=[r_("pb", 2 + oc % 2)], tag="f2_mm")
            xx = tok(B.x[oc], t0, TJ)
            S.op("vector", lambda e, oc=oc, t0=t0: e.tensor_mul(
                F.u[:], tok(B.x[oc], t0, TJ), F.b3bc[:, t0:t0 + TJ]),
                reads=[r_("x", oc), r_("b3bc")], writes=[r_("l0f_u")], tag="res_u")
            S.op("vector", lambda e, oc=oc, t0=t0: e.scalar_tensor_tensor(
                tok(B.x[oc], t0, TJ), B.pb[2 + oc % 2][:], 0.5,
                tok(B.x[oc], t0, TJ), ALU.mult, ALU.add),
                reads=[r_("pb", 2 + oc % 2), r_("x", oc)], writes=[r_("x", oc)],
                tag="res_z")
            S.op("vector", lambda e, oc=oc, t0=t0: e.tensor_add(
                tok(B.x[oc], t0, TJ), tok(B.x[oc], t0, TJ), F.u[:]),
                reads=[r_("x", oc), r_("l0f_u")], writes=[r_("x", oc)], tag="res_a")
            S.op("vector", lambda e, oc=oc, t0=t0: e.scalar_tensor_tensor(
                tok(B.x[oc], t0, TJ), F.b4bc[:, t0:t0 + TJ], F.b2h[:, oc:oc + 1],
                tok(B.x[oc], t0, TJ), ALU.add, ALU.add),
                reads=[r_("x", oc), r_("b4bc"), r_("l0f_b2h")],
                writes=[r_("x", oc)], tag="res_b")



L0B_NAMES = ["L0_attn_b_Wq", "L0_attn_b_Wk", "L0_attn_b_Wv",
             "p_l0b_wqrot", "p_l0b_wkrot", "p_l0b_wox",
             "cst_cosB_q", "cst_sinB_q"]


def alloc_l0b(nc, ctx):
    A = Bufs()
    ent = ctx.enter_context
    A.kh = [ent(nc.sbuf_tensor(f"l0b_kh{i}", [128, T], BF16)) for i in range(4)]
    A.vt = [ent(nc.sbuf_tensor(f"l0b_vt{t}", [128, 512], BF16)) for t in range(16)]
    A.qh = [ent(nc.sbuf_tensor(f"l0b_qh{i}", [128, TJ], BF16)) for i in range(2)]
    A.gs = [ent(nc.sbuf_tensor(f"l0b_gs{i}", [128, TJ], BF16)) for i in range(2)]
    A.pt = [ent(nc.sbuf_tensor(f"l0b_pt{i}", [128, TJ], BF16)) for i in range(3)]
    A.recbc = ent(nc.sbuf_tensor("l0b_recbc", [128, TJ], BF16))
    A.gm = ent(nc.sbuf_tensor("l0b_gm", [128, TJ], BF16))
    A.ao = [ent(nc.sbuf_tensor(f"l0b_ao{i}", [128, TJ], BF16)) for i in range(16)]
    A.wq = [ent(nc.sbuf_tensor(f"l0b_wq{c}", [128, 256], BF16)) for c in range(NCHUNK)]
    A.wg = [ent(nc.sbuf_tensor(f"l0b_wg{c}", [128, 256], BF16)) for c in range(NCHUNK)]
    A.wr = [ent(nc.sbuf_tensor(f"l0b_wr{c}", [128, 64], BF16)) for c in range(NCHUNK)]
    A.wo = ent(nc.sbuf_tensor("l0b_wo", [128, 2048], BF16))
    A.tmp = [ent(nc.sbuf_tensor(f"l0b_tmp{i}", [128, TJ], BF16)) for i in range(2)]
    A.cq = ent(nc.sbuf_tensor("l0b_cq", [64, T], BF16))
    A.sq_ = ent(nc.sbuf_tensor("l0b_sq", [64, T], BF16))
    return A


def emit_l0b_setup(S, nc, B, A, W):
    S.dma("sync", A.cq[:], W["cst_cosB_q"], writes=[r_("l0b_cq")])
    S.dma("sync", A.sq_[:], W["cst_sinB_q"], writes=[r_("l0b_sq")])


def emit_l0_attn_b(S, nc, B, A, W):
    emit_ln(S, B, "ln")
    negC, negC_k = cvec(S, B, -C_L0B)

    _load_w_tiles(S, A.wq, "l0b_wq", W["L0_attn_b_Wv"], 0, 256)
    _load_w_tiles(S, A.wg, "l0b_wg", W["L0_attn_b_Wv"], 256, 256)
    for tt in range(16):
        vt = A.vt[tt]
        for c in range(NCHUNK):
            S.op("tensor", lambda e, c=c, tt=tt: e.matmul(
                B.pb[6][:, 0:256], tok(B.h[c], tt * 128, 128), A.wq[c][:, 0:256],
                start=(c == 0), stop=False),
                reads=[r_("l0b_wq", c), r_("h", c)], writes=[r_("pb", 6)],
                tag="bvproj")
            S.op("tensor", lambda e, c=c, tt=tt: e.matmul(
                B.pb[6][:, 256:512], tok(B.h[c], tt * 128, 128), A.wg[c][:, 0:256],
                start=False, stop=(c == NCHUNK - 1)),
                reads=[r_("l0b_wg", c), r_("h", c)], writes=[r_("pb", 6)],
                tag="bvproj2")
        S.op("scalar", lambda e, vt=vt: e.copy(vt[:], B.pb[6][:]),
             reads=[r_("pb", 6)], writes=[r_("l0b_vt", tt)], tag="bvevac")

    for m in range(4):
        kb = A.kh[m]
        kkey = r_("l0b_kh", m)
        _load_w_tiles(S, A.wq, "l0b_wq", W["L0_attn_b_Wk"], m * 128, 128)
        if m % 2 == 0:
            _load_w_tiles(S, A.wr, "l0b_wr", W["p_l0b_wkrot"], (m // 2) * 64, 64)
        for tj in range(NTJ):
            t0 = tj * TJ
            for c in range(NCHUNK):
                S.op("tensor", lambda e, c=c, t0=t0: e.matmul(
                    B.pb[0][:], A.wq[c][:, 0:128], tok(B.h[c], t0, TJ),
                    start=(c == 0), stop=(c == NCHUNK - 1)),
                    reads=[r_("l0b_wq", c), r_("h", c)], writes=[r_("pb", 0)],
                    tag="bk_mm")
                if m % 2 == 0:
                    S.op("tensor", lambda e, c=c, t0=t0: e.matmul(
                        B.pb[1][0:64, :], A.wr[c][:, 0:64], tok(B.h[c], t0, TJ),
                        start=(c == 0), stop=(c == NCHUNK - 1)),
                        reads=[r_("l0b_wr", c), r_("h", c)], writes=[r_("pb", 1)],
                        tag="bk_mmr")
            if m % 2 == 0:
                t1, t2 = A.tmp[0], A.tmp[1]
                S.op("vector", lambda e, t0=t0, t1=t1: e.scalar_tensor_tensor(
                    t1[0:64, :], B.pb[0][0:64, :], 1.0 / 16.0,
                    A.cq[:, t0:t0 + TJ], ALU.mult, ALU.mult),
                    reads=[r_("pb", 0), r_("l0b_cq")], writes=[r_("l0b_t", 0)],
                    tag="bk_r1")
                S.op("vector", lambda e, t0=t0, t2=t2: e.scalar_tensor_tensor(
                    t2[0:64, :], B.pb[1][0:64, :], 1.0 / 16.0,
                    A.sq_[:, t0:t0 + TJ], ALU.mult, ALU.mult),
                    reads=[r_("pb", 1), r_("l0b_sq")], writes=[r_("l0b_t", 1)],
                    tag="bk_r2")
                S.op("vector", lambda e, t0=t0, kb=kb, t1=t1, t2=t2: e.tensor_add(
                    kb[0:64, t0:t0 + TJ], t1[0:64, :], t2[0:64, :]),
                    reads=[r_("l0b_t", 0), r_("l0b_t", 1)], writes=[kkey],
                    tag="bk_r3")
                S.op("scalar", lambda e, t0=t0, kb=kb: e.activation(
                    kb[64:128, t0:t0 + TJ], B.pb[0][64:128, :], AF.Copy,
                    scale=1.0 / 16.0),
                    reads=[r_("pb", 0)], writes=[kkey], tag="bk_pass")
            else:
                S.op("scalar", lambda e, t0=t0, kb=kb: e.activation(
                    kb[:, t0:t0 + TJ], B.pb[0][:], AF.Copy, scale=1.0 / 16.0),
                    reads=[r_("pb", 0)], writes=[kkey], tag="bk_pass2")

    for tj in range(NTJ):
        t0 = tj * TJ
        for h in range(8):
            kv = h // 4
            _load_w_tiles(S, A.wq, "l0b_wq", W["L0_attn_b_Wq"], h * 256, 256)
            _load_w_tiles(S, A.wr, "l0b_wr", W["p_l0b_wqrot"], h * 64, 64)
            _load_w_tiles(S, A.wg, "l0b_wg", W["L0_attn_b_Wq"], 2048 + h * 256, 256)
            for cc in range(2):
                qb = A.qh[cc]
                for c in range(NCHUNK):
                    S.op("tensor", lambda e, c=c, cc=cc: e.matmul(
                        B.pb[0][:], A.wq[c][:, cc * 128:cc * 128 + 128],
                        tok(B.h[c], t0, TJ),
                        start=(c == 0), stop=(c == NCHUNK - 1)),
                        reads=[r_("l0b_wq", c), r_("h", c)], writes=[r_("pb", 0)],
                        tag="bq_mm")
                if cc == 0:
                    for c in range(NCHUNK):
                        S.op("tensor", lambda e, c=c: e.matmul(
                            B.pb[1][0:64, :], A.wr[c][:, 0:64],
                            tok(B.h[c], t0, TJ),
                            start=(c == 0), stop=(c == NCHUNK - 1)),
                            reads=[r_("l0b_wr", c), r_("h", c)],
                            writes=[r_("pb", 1)], tag="bq_mmr")
                    t1, t2 = A.tmp[0], A.tmp[1]
                    S.op("vector", lambda e, t1=t1: e.tensor_mul(
                        t1[0:64, :], B.pb[0][0:64, :], A.cq[:, t0:t0 + TJ]),
                        reads=[r_("pb", 0), r_("l0b_cq")], writes=[r_("l0b_t", 0)],
                        tag="bq_r1")
                    S.op("vector", lambda e, t2=t2: e.tensor_mul(
                        t2[0:64, :], B.pb[1][0:64, :], A.sq_[:, t0:t0 + TJ]),
                        reads=[r_("pb", 1), r_("l0b_sq")], writes=[r_("l0b_t", 1)],
                        tag="bq_r2")
                    S.op("vector", lambda e, qb=qb, t1=t1, t2=t2: e.tensor_add(
                        qb[0:64, :], t1[0:64, :], t2[0:64, :]),
                        reads=[r_("l0b_t", 0), r_("l0b_t", 1)],
                        writes=[r_("l0b_qh", cc)], tag="bq_r3")
                    S.op("scalar", lambda e, qb=qb: e.copy(
                        qb[64:128, :], B.pb[0][64:128, :]),
                        reads=[r_("pb", 0)], writes=[r_("l0b_qh", cc)],
                        tag="bq_pass")
                else:
                    S.op("scalar", lambda e, qb=qb: e.copy(qb[:], B.pb[0][:]),
                         reads=[r_("pb", 0)], writes=[r_("l0b_qh", cc)],
                         tag="bq_pass2")
                gb = A.gs[cc]
                for c in range(NCHUNK):
                    S.op("tensor", lambda e, c=c, cc=cc: e.matmul(
                        B.pb[1][:], A.wg[c][:, cc * 128:cc * 128 + 128],
                        tok(B.h[c], t0, TJ),
                        start=(c == 0), stop=(c == NCHUNK - 1)),
                        reads=[r_("l0b_wg", c), r_("h", c)], writes=[r_("pb", 1)],
                        tag="bg_mm")
                S.op("scalar", lambda e, gb=gb, cc=cc: e.activation(
                    gb[:], B.pb[1][:], AF.Silu),
                    reads=[r_("pb", 1)], writes=[r_("l0b_gs", cc)], tag="bg_silu")
            n_pv = 16 * 2
            for kt in range(16):
                pt = A.pt[kt % 2]
                ptk = r_("l0b_pt", kt % 2)
                ps_S = 2 + (kt % 2) * 2
                for cc in range(2):
                    S.op("tensor", lambda e, kt=kt, cc=cc, ps_S=ps_S: e.matmul(
                        B.pb[ps_S][:], A.kh[2 * kv + cc][:, kt * 128:(kt + 1) * 128],
                        A.qh[cc][:], start=(cc == 0), stop=(cc == 1)),
                        reads=[r_("l0b_kh", 2 * kv + cc), r_("l0b_qh", cc)],
                        writes=[r_("pb", ps_S)], tag="bs_mm")
                S.op("scalar", lambda e, pt=pt, ps_S=ps_S: e.activation(
                    pt[:], B.pb[ps_S][:], AF.Exp, bias=negC),
                    reads=[r_("pb", ps_S), negC_k], writes=[ptk], tag="bs_exp")
                for cc in range(2):
                    S.op("tensor", lambda e, kt=kt, cc=cc, pt=pt: e.matmul(
                        B.pb[3 + 2 * cc][:], A.vt[kt][:, kv * 256 + cc * 128:
                                                      kv * 256 + cc * 128 + 128],
                        pt[:], start=(kt == 0), stop=(kt == 15)),
                        reads=[r_("l0b_vt", kt), ptk],
                        writes=[r_("pb", 3 + 2 * cc)], tag="bpv")
                S.op("tensor", lambda e, pt=pt, kt=kt: e.matmul(
                    B.pb[6][0:1, :], B.onecol[:], pt[:],
                    start=(kt == 0), stop=(kt == 15)),
                    reads=[r_("onecol"), ptk], writes=[r_("pb", 6)], tag="bden")
            rec = B.rowf[:, 3 * TJ:4 * TJ]
            rec_k = r_("rowf", 3)
            S.op("vector", lambda e, rec=rec: e.reciprocal(rec, B.pb[6][0:1, :]),
                 reads=[r_("pb", 6)], writes=[rec_k], tag="bn_rec")
            S.op("vector", lambda e, rec=rec: e.tensor_copy(B.rowh[:], rec),
                 reads=[rec_k], writes=[r_("rowh")], tag="bn_cast")
            S.op("tensor", lambda e: e.matmul(
                B.pb[7][:], B.ones_row[:], B.rowh[:], start=True, stop=True),
                reads=[r_("rowh"), r_("ones_row")], writes=[r_("pb", 7)],
                tag="bn_bcmm")
            S.op("scalar", lambda e: e.copy(A.recbc[:], B.pb[7][:]),
                 reads=[r_("pb", 7)], writes=[r_("l0b_recbc")], tag="bn_bcev")
            for cc in range(2):
                S.op("vector", lambda e, cc=cc: e.tensor_mul(
                    A.gm[:], A.gs[cc][:], A.recbc[:]),
                    reads=[r_("l0b_gs", cc), r_("l0b_recbc")], writes=[r_("l0b_gm")],
                    tag="bn_gm")
                S.op("vector", lambda e, cc=cc, h=h: e.tensor_mul(
                    A.ao[2 * h + cc][:], B.pb[3 + 2 * cc][:], A.gm[:]),
                    reads=[r_("pb", 3 + 2 * cc), r_("l0b_gm")],
                    writes=[r_("l0b_ao", 2 * h + cc)], tag="bn_mul")
        for oc in range(NCHUNK):
            S.dma("gpsimd", A.wo[:],
                  W["p_l0b_wox"][oc].rearrange("p a j -> p (a j)"),
                  writes=[r_("l0b_wo")], tag="bwo_ld")
            for i in range(16):
                S.op("tensor", lambda e, i=i, oc=oc: e.matmul(
                    B.pb[0 + oc % 2][:], A.wo[:, i * 128:(i + 1) * 128],
                    A.ao[i][:], start=(i == 0), stop=(i == 15)),
                    reads=[r_("l0b_wo"), r_("l0b_ao", i)],
                    writes=[r_("pb", 0 + oc % 2)], tag="bwo_mm")
            S.op("vector", lambda e, oc=oc, t0=t0: e.tensor_add(
                tok(B.x[oc], t0, TJ), tok(B.x[oc], t0, TJ),
                B.pb[0 + oc % 2][:]),
                reads=[r_("pb", 0 + oc % 2), r_("x", oc)], writes=[r_("x", oc)],
                tag="bwo_res")



L1A_NAMES = ["p_l1a_wqT", "p_l1a_wqTp", "p_l1a_wkT", "p_l1a_wkTp",
             "p_l1a_wvT", "p_l1a_wox", "cst_cosI_q", "cst_sinI_q",
             "cst_l1_tri"]


def alloc_l1a(nc, ctx):
    A = Bufs()
    ent = ctx.enter_context
    A.kh = [ent(nc.sbuf_tensor(f"l1a_kh{c}", [128, T], BF16)) for c in range(4)]
    A.vt = [ent(nc.sbuf_tensor(f"l1a_vt{t}", [128, 520], BF16)) for t in range(16)]
    A.qh = ent(nc.sbuf_tensor("l1a_qh", [128, TJ], BF16))
    A.pt = [ent(nc.sbuf_tensor(f"l1a_pt{i}", [128, TJ], BF16)) for i in range(2)]
    A.recbc = ent(nc.sbuf_tensor("l1a_recbc", [64, TJ], BF16))
    A.ao = [ent(nc.sbuf_tensor(f"l1a_ao{c}", [128, TJ], BF16)) for c in range(4)]
    A.wv = [ent(nc.sbuf_tensor(f"l1a_wv{c}", [128, 512], BF16)) for c in range(NCHUNK)]
    A.wo = ent(nc.sbuf_tensor("l1a_wo", [128, 512], BF16))
    A.tmp = [ent(nc.sbuf_tensor(f"l1a_tmp{i}", [128, TJ], BF16)) for i in range(2)]
    A.cq = ent(nc.sbuf_tensor("l1a_cq", [128, T], BF16))
    A.sq_ = ent(nc.sbuf_tensor("l1a_sq", [128, T], BF16))
    A.tri = ent(nc.sbuf_tensor("l1a_tri", [128, 128], BF16))
    return A


def emit_l1a_setup(S, nc, B, A, W):
    S.dma("sync", A.cq[:], W["cst_cosI_q"], writes=[r_("l1a_cq")])
    S.dma("sync", A.sq_[:], W["cst_sinI_q"], writes=[r_("l1a_sq")])
    S.dma("sync", A.tri[:], W["cst_l1_tri"], writes=[r_("l1a_tri")])


def emit_l1_attn_a(S, nc, B, A, W):
    emit_ln(S, B, "ln")
    negC, negC_k = cvec(S, B, -C_L1A)
    for half in range(2):
        emit_l1a_half(S, nc, B, A, W, half, negC, negC_k)


def emit_l1a_half(S, nc, B, A, W, half, negC, negC_k):
    hbase = 4 * half

    for c in range(NCHUNK):
        S.dma("gpsimd", A.wv[c][:, 0:512],
              W["p_l1a_wvT"][c * 128:(c + 1) * 128, 512 * half:512 * half + 512],
              writes=[r_("l1a_wv", c), r_("l1a_wv2", c)], tag="wloadv")
    for tt in range(16):
        vt = A.vt[tt]
        S.op("vector", lambda e, vt=vt: e.memset(
            vt.ap().rearrange("p (a b) -> p a b", b=65)[:, :, 64:65], 1.0),
            writes=[r_("l1a_vt", tt)], tag="avones")
        for c in range(NCHUNK):
            S.op("tensor", lambda e, c=c, tt=tt: e.matmul(
                B.pb[6][:, 0:512], tok(B.h[c], tt * 128, 128),
                A.wv[c][:, 0:512],
                start=(c == 0), stop=(c == NCHUNK - 1)),
                reads=[r_("l1a_wv", c), r_("h", c)], writes=[r_("pb", 6)],
                tag="avproj")
        S.op("scalar", lambda e, vt=vt: e.copy(
            vt.ap().rearrange("p (a b) -> p a b", b=65)[:, :, 0:64],
            B.pb[6][:, 0:512].rearrange("p (a b) -> p a b", b=64)),
            reads=[r_("pb", 6)], writes=[r_("l1a_vt", tt)], tag="avevac")

    for ci in range(4):
        c = hbase + ci
        kb = A.kh[ci]
        kkey = r_("l1a_kh", ci)
        _load_w_tiles(S, A.wv, "l1a_wv", W["p_l1a_wkT"], c * 128, 128)
        _load_w_tiles(S, A.wv, "l1a_wv2", W["p_l1a_wkTp"], c * 128, 128,
                      coloff=128)
        for tj in range(NTJ):
            t0 = tj * TJ
            for cc in range(NCHUNK):
                S.op("tensor", lambda e, cc=cc, t0=t0: e.matmul(
                    B.pb[0][:], A.wv[cc][:, 0:128], tok(B.h[cc], t0, TJ),
                    start=(cc == 0), stop=(cc == NCHUNK - 1)),
                    reads=[r_("l1a_wv", cc), r_("h", cc)], writes=[r_("pb", 0)],
                    tag="ak_mm")
                S.op("tensor", lambda e, cc=cc, t0=t0: e.matmul(
                    B.pb[1][:], A.wv[cc][:, 128:256], tok(B.h[cc], t0, TJ),
                    start=(cc == 0), stop=(cc == NCHUNK - 1)),
                    reads=[r_("l1a_wv2", cc), r_("h", cc)], writes=[r_("pb", 1)],
                    tag="ak_mmp")
            t1, t2 = A.tmp[0], A.tmp[1]
            S.op("vector", lambda e, t0=t0, t1=t1: e.scalar_tensor_tensor(
                t1[:], B.pb[0][:], 8.0, A.cq[:, t0:t0 + TJ], ALU.mult, ALU.mult),
                reads=[r_("pb", 0), r_("l1a_cq")], writes=[r_("l1a_t", 0)],
                tag="ak_r1")
            S.op("vector", lambda e, t0=t0, t2=t2: e.scalar_tensor_tensor(
                t2[:], B.pb[1][:], 8.0, A.sq_[:, t0:t0 + TJ], ALU.mult, ALU.mult),
                reads=[r_("pb", 1), r_("l1a_sq")], writes=[r_("l1a_t", 1)],
                tag="ak_r2")
            S.op("vector", lambda e, t0=t0, kb=kb, t1=t1, t2=t2: e.tensor_add(
                kb[:, t0:t0 + TJ], t1[:], t2[:]),
                reads=[r_("l1a_t", 0), r_("l1a_t", 1)], writes=[kkey], tag="ak_r3")

    for tj in range(NTJ):
        t0 = tj * TJ
        for ci in range(4):
            c = hbase + ci
            qb = A.qh
            qbk = r_("l1a_qh")
            _load_w_tiles(S, A.wv, "l1a_wv", W["p_l1a_wqT"], c * 128, 128)
            _load_w_tiles(S, A.wv, "l1a_wv2", W["p_l1a_wqTp"], c * 128, 128,
                          coloff=128)
            for cc in range(NCHUNK):
                S.op("tensor", lambda e, cc=cc, t0=t0: e.matmul(
                    B.pb[0][:], A.wv[cc][:, 0:128], tok(B.h[cc], t0, TJ),
                    start=(cc == 0), stop=(cc == NCHUNK - 1)),
                    reads=[r_("l1a_wv", cc), r_("h", cc)], writes=[r_("pb", 0)],
                    tag="aq_mm")
                S.op("tensor", lambda e, cc=cc, t0=t0: e.matmul(
                    B.pb[1][:], A.wv[cc][:, 128:256], tok(B.h[cc], t0, TJ),
                    start=(cc == 0), stop=(cc == NCHUNK - 1)),
                    reads=[r_("l1a_wv2", cc), r_("h", cc)], writes=[r_("pb", 1)],
                    tag="aq_mmp")
            t1, t2 = A.tmp[0], A.tmp[1]
            S.op("vector", lambda e, t0=t0, t1=t1: e.tensor_mul(
                t1[:], B.pb[0][:], A.cq[:, t0:t0 + TJ]),
                reads=[r_("pb", 0), r_("l1a_cq")], writes=[r_("l1a_t", 0)],
                tag="aq_r1")
            S.op("vector", lambda e, t0=t0, t2=t2: e.tensor_mul(
                t2[:], B.pb[1][:], A.sq_[:, t0:t0 + TJ]),
                reads=[r_("pb", 1), r_("l1a_sq")], writes=[r_("l1a_t", 1)],
                tag="aq_r2")
            S.op("vector", lambda e, qb=qb, t1=t1, t2=t2: e.tensor_add(
                qb[:], t1[:], t2[:]),
                reads=[r_("l1a_t", 0), r_("l1a_t", 1)], writes=[qbk], tag="aq_r3")
            for hh in range(2):
                h = 8 * half + 2 * ci + hh
                rs = 64 * hh
                n_pv = 4 * tj + 4
                pvi = [0]

                def pv(lhs, lk, rhs, rk, qlo, w):
                    st = pvi[0] == 0
                    sp = pvi[0] == n_pv - 1
                    pvi[0] += 1
                    S.op("tensor", lambda e, lhs=lhs, rhs=rhs, qlo=qlo, w=w,
                         st=st, sp=sp: e.matmul(
                             B.pb[3][0:65, qlo:qlo + w], lhs, rhs,
                             start=st, stop=sp),
                         reads=[lk, rk], writes=[r_("pb", 3)], tag="apv")

                hg = h % 8
                for kt in range(4 * tj + 4):
                    r = kt - 4 * tj
                    qlo = max(0, 128 * r)
                    w = TJ - qlo
                    pt = A.pt[kt % 2]
                    ptk = r_("l1a_pt", kt % 2)
                    ps_S = 2 + (kt % 2) * 2
                    S.op("tensor", lambda e, kt=kt, qlo=qlo, w=w, rs=rs,
                         ps_S=ps_S, ci=ci: e.matmul(
                             B.pb[ps_S][:, 0:w],
                             A.kh[ci][rs:rs + 64, kt * 128:(kt + 1) * 128],
                             qb[rs:rs + 64, qlo:qlo + w], start=True, stop=True),
                        reads=[r_("l1a_kh", ci), qbk], writes=[r_("pb", ps_S)],
                        tag="as_mm", pe_base=rs)
                    S.op("scalar", lambda e, w=w, pt=pt, ps_S=ps_S: e.activation(
                        pt[:, 0:w], B.pb[ps_S][:, 0:w], AF.Exp, bias=negC),
                        reads=[r_("pb", ps_S), negC_k], writes=[ptk], tag="as_exp")
                    if r >= 0:
                        S.op("vector", lambda e, pt=pt: e.tensor_tensor(
                            pt[:, 0:128], pt[:, 0:128], A.tri[:], ALU.min),
                            reads=[ptk, r_("l1a_tri")], writes=[ptk], tag="as_msk")
                    pv(A.vt[kt][:, 65 * hg:65 * hg + 65], r_("l1a_vt", kt),
                       pt[:, 0:w], ptk, qlo, w)
                rec = B.rowf[:, 3 * TJ:4 * TJ]
                rec_k = r_("rowf", 3)
                S.op("vector", lambda e, rec=rec: e.reciprocal(
                    rec, B.pb[3][64:65, 0:TJ]),
                    reads=[r_("pb", 3)], writes=[rec_k], tag="an_rec")
                S.op("vector", lambda e, rec=rec: e.tensor_copy(B.rowh[:], rec),
                     reads=[rec_k], writes=[r_("rowh")], tag="an_cast")
                S.op("tensor", lambda e: e.matmul(
                    B.pb[7][0:64, 0:TJ], B.ones_row[:, 0:64], B.rowh[:],
                    start=True, stop=True),
                    reads=[r_("rowh"), r_("ones_row")], writes=[r_("pb", 7)],
                    tag="an_bcmm")
                S.op("scalar", lambda e: e.copy(A.recbc[:], B.pb[7][0:64, 0:TJ]),
                     reads=[r_("pb", 7)], writes=[r_("l1a_recbc")], tag="an_bcev")
                S.op("vector", lambda e, ci=ci, rs=rs: e.tensor_mul(
                    A.ao[ci][rs:rs + 64, :], B.pb[3][0:64, 0:TJ], A.recbc[:]),
                    reads=[r_("pb", 3), r_("l1a_recbc")], writes=[r_("l1a_ao", ci)],
                    tag="an_mul")
        for oc in range(NCHUNK):
            S.dma("gpsimd", A.wo[:],
                  W["p_l1a_wox"][oc, :, 4 * half:4 * half + 4, :].rearrange(
                      "p a j -> p (a j)"),
                  writes=[r_("l1a_wo")], tag="awo_ld")
            for i in range(4):
                S.op("tensor", lambda e, i=i, oc=oc: e.matmul(
                    B.pb[0 + oc % 2][:], A.wo[:, i * 128:(i + 1) * 128],
                    A.ao[i][:], start=(i == 0), stop=(i == 3)),
                    reads=[r_("l1a_wo"), r_("l1a_ao", i)],
                    writes=[r_("pb", 0 + oc % 2)], tag="awo_mm")
            S.op("vector", lambda e, oc=oc, t0=t0: e.tensor_add(
                tok(B.x[oc], t0, TJ), tok(B.x[oc], t0, TJ), B.pb[0 + oc % 2][:]),
                reads=[r_("pb", 0 + oc % 2), r_("x", oc)], writes=[r_("x", oc)],
                tag="awo_res")



L1B_NAMES = ["p_l1b_wqT", "p_l1b_wqTp", "p_l1b_wkT", "p_l1b_wkTp",
             "p_l1b_wvT", "p_l1b_wox", "p_l1b_bo8", "cst_cosH_q",
             "cst_sinH_q", "cst_l1b_tri4", "cst_l1b_low4"]


def alloc_l1b(nc, ctx):
    A = Bufs()
    ent = ctx.enter_context
    A.kh = [ent(nc.sbuf_tensor(f"l1b_kh{c}", [128, T], BF16)) for c in range(4)]
    A.vt = [ent(nc.sbuf_tensor(f"l1b_vt{t}", [128, 520], BF16)) for t in range(16)]
    A.qh2 = [ent(nc.sbuf_tensor(f"l1b_qh{i}", [128, TJ], BF16)) for i in range(2)]
    A.pt = [ent(nc.sbuf_tensor(f"l1b_pt{i}", [128, TJ], BF16)) for i in range(2)]
    A.recbc = ent(nc.sbuf_tensor("l1b_recbc", [64, TJ], BF16))
    A.ao = [ent(nc.sbuf_tensor(f"l1b_ao{c}", [128, T], BF16)) for c in range(4)]
    A.wv = [ent(nc.sbuf_tensor(f"l1b_wv{c}", [128, 512], BF16)) for c in range(NCHUNK)]
    A.wo = ent(nc.sbuf_tensor("l1b_wo", [128, 512], BF16))
    A.tmp = [ent(nc.sbuf_tensor(f"l1b_tmp{i}", [128, TJ], BF16)) for i in range(2)]
    A.cq = ent(nc.sbuf_tensor("l1b_cq", [128, T], BF16))
    A.sq_ = ent(nc.sbuf_tensor("l1b_sq", [128, T], BF16))
    A.tri4 = ent(nc.sbuf_tensor("l1b_tri4", [128, 512], BF16))
    A.low4 = ent(nc.sbuf_tensor("l1b_low4", [128, 512], BF16))
    A.bo = ent(nc.sbuf_tensor("l1b_bo", [128, 8], F32))
    return A


def emit_l1b_setup(S, nc, B, A, W):
    S.dma("sync", A.cq[:], W["cst_cosH_q"], writes=[r_("l1b_cq")])
    S.dma("sync", A.sq_[:], W["cst_sinH_q"], writes=[r_("l1b_sq")])
    S.dma("sync", A.tri4[:], W["cst_l1b_tri4"], writes=[r_("l1b_tri4")])
    S.dma("sync", A.low4[:], W["cst_l1b_low4"], writes=[r_("l1b_low4")])
    S.dma("sync", A.bo[:], W["p_l1b_bo8"], writes=[r_("l1b_bo")])


def emit_l1_attn_b(S, nc, B, A, W):
    emit_ln(S, B, "ln")
    negC, negC_k = cvec(S, B, -C_L1B)
    for half in range(2):
        emit_l1b_half(S, nc, B, A, W, half, negC, negC_k)


L1B_STAGE = 4


def emit_l1b_half(S, nc, B, A, W, half, negC, negC_k):
    hbase = 4 * half

    for c in range(NCHUNK):
        S.dma("gpsimd", A.wv[c][:, 0:512],
              W["p_l1b_wvT"][c * 128:(c + 1) * 128, 512 * half:512 * half + 512],
              writes=[r_("l1b_wv", c), r_("l1b_wv2", c)], tag="wloadv")
    for tt in range(16):
        vt = A.vt[tt]
        S.op("vector", lambda e, vt=vt: e.memset(
            vt.ap().rearrange("p (a b) -> p a b", b=65)[:, :, 64:65], 1.0),
            writes=[r_("l1b_vt", tt)], tag="bvones")
        for c in range(NCHUNK):
            S.op("tensor", lambda e, c=c, tt=tt: e.matmul(
                B.pb[6][:, 0:512], tok(B.h[c], tt * 128, 128), A.wv[c][:, 0:512],
                start=(c == 0), stop=(c == NCHUNK - 1)),
                reads=[r_("l1b_wv", c), r_("h", c)], writes=[r_("pb", 6)],
                tag="bvproj")
        S.op("scalar", lambda e, vt=vt: e.copy(
            vt.ap().rearrange("p (a b) -> p a b", b=65)[:, :, 0:64],
            B.pb[6][:, 0:512].rearrange("p (a b) -> p a b", b=64)),
            reads=[r_("pb", 6)], writes=[r_("l1b_vt", tt)], tag="bvevac")

    for ci in range(4):
        c = hbase + ci
        kb = A.kh[ci]
        kkey = r_("l1b_kh", ci)
        _load_w_tiles(S, A.wv, "l1b_wv", W["p_l1b_wkT"], c * 128, 128)
        _load_w_tiles(S, A.wv, "l1b_wv2", W["p_l1b_wkTp"], c * 128, 128,
                      coloff=128)
        for tj in range(NTJ):
            t0 = tj * TJ
            for cc in range(NCHUNK):
                S.op("tensor", lambda e, cc=cc, t0=t0: e.matmul(
                    B.pb[0][:], A.wv[cc][:, 0:128], tok(B.h[cc], t0, TJ),
                    start=(cc == 0), stop=(cc == NCHUNK - 1)),
                    reads=[r_("l1b_wv", cc), r_("h", cc)], writes=[r_("pb", 0)],
                    tag="bk_mm")
                S.op("tensor", lambda e, cc=cc, t0=t0: e.matmul(
                    B.pb[1][:], A.wv[cc][:, 128:256], tok(B.h[cc], t0, TJ),
                    start=(cc == 0), stop=(cc == NCHUNK - 1)),
                    reads=[r_("l1b_wv2", cc), r_("h", cc)], writes=[r_("pb", 1)],
                    tag="bk_mmp")
            t1, t2 = A.tmp[0], A.tmp[1]
            S.op("vector", lambda e, t0=t0, t1=t1: e.scalar_tensor_tensor(
                t1[:], B.pb[0][:], 8.0, A.cq[:, t0:t0 + TJ], ALU.mult, ALU.mult),
                reads=[r_("pb", 0), r_("l1b_cq")], writes=[r_("l1b_t", 0)],
                tag="bk_r1")
            S.op("vector", lambda e, t0=t0, t2=t2: e.scalar_tensor_tensor(
                t2[:], B.pb[1][:], 8.0, A.sq_[:, t0:t0 + TJ], ALU.mult, ALU.mult),
                reads=[r_("pb", 1), r_("l1b_sq")], writes=[r_("l1b_t", 1)],
                tag="bk_r2")
            S.op("vector", lambda e, t0=t0, kb=kb, t1=t1, t2=t2: e.tensor_add(
                kb[:, t0:t0 + TJ], t1[:], t2[:]),
                reads=[r_("l1b_t", 0), r_("l1b_t", 1)], writes=[kkey], tag="bk_r3")

    if L1B_STAGE < 1:
        return
    for tj in range(NTJ):
        t0 = tj * TJ
        for grp in range(2):
            for gg in range(2):
                ci = 2 * grp + gg
                c = hbase + ci
                qb = A.qh2[gg]
                qbk = r_("l1b_qh", gg)
                _load_w_tiles(S, A.wv, "l1b_wv", W["p_l1b_wqT"], c * 128, 128)
                _load_w_tiles(S, A.wv, "l1b_wv2", W["p_l1b_wqTp"], c * 128, 128,
                              coloff=128)
                for cc in range(NCHUNK):
                    S.op("tensor", lambda e, cc=cc, t0=t0: e.matmul(
                        B.pb[0][:], A.wv[cc][:, 0:128], tok(B.h[cc], t0, TJ),
                        start=(cc == 0), stop=(cc == NCHUNK - 1)),
                        reads=[r_("l1b_wv", cc), r_("h", cc)], writes=[r_("pb", 0)],
                        tag="bq_mm")
                    S.op("tensor", lambda e, cc=cc, t0=t0: e.matmul(
                        B.pb[1][:], A.wv[cc][:, 128:256], tok(B.h[cc], t0, TJ),
                        start=(cc == 0), stop=(cc == NCHUNK - 1)),
                        reads=[r_("l1b_wv2", cc), r_("h", cc)],
                        writes=[r_("pb", 1)], tag="bq_mmp")
                t1, t2 = A.tmp[0], A.tmp[1]
                S.op("vector", lambda e, t0=t0, t1=t1: e.tensor_mul(
                    t1[:], B.pb[0][:], A.cq[:, t0:t0 + TJ]),
                    reads=[r_("pb", 0), r_("l1b_cq")], writes=[r_("l1b_t", 0)],
                    tag="bq_r1")
                S.op("vector", lambda e, t0=t0, t2=t2: e.tensor_mul(
                    t2[:], B.pb[1][:], A.sq_[:, t0:t0 + TJ]),
                    reads=[r_("pb", 1), r_("l1b_sq")], writes=[r_("l1b_t", 1)],
                    tag="bq_r2")
                S.op("vector", lambda e, qb=qb, t1=t1, t2=t2: e.tensor_add(
                    qb[:], t1[:], t2[:]),
                    reads=[r_("l1b_t", 0), r_("l1b_t", 1)], writes=[qbk],
                    tag="bq_r3")
            if L1B_STAGE < 2:
                continue
            for qt in range(4):
                qt_abs = 4 * tj + qt
                kts = [qt_abs] + ([qt_abs - 1] if qt_abs > 0 else [])
                n_pv = 4 * len(kts)
                pvi = [0]
                for ki, kt in enumerate(kts):
                    pt = A.pt[ki]
                    ptk = r_("l1b_pt", ki)
                    ps_S = 2 + ki * 2
                    for j in range(4):
                        ci, rs = 2 * grp + j // 2, 64 * (j % 2)
                        S.op("tensor", lambda e, kt=kt, j=j, ci=ci, rs=rs,
                             qt=qt, ps_S=ps_S: e.matmul(
                                 B.pb[ps_S][:, 128 * j:128 * j + 128],
                                 A.kh[ci][rs:rs + 64, kt * 128:(kt + 1) * 128],
                                 A.qh2[j // 2][rs:rs + 64,
                                               128 * qt:128 * qt + 128],
                                 start=True, stop=True),
                            reads=[r_("l1b_kh", ci), r_("l1b_qh", j // 2)],
                            writes=[r_("pb", ps_S)], tag="bs_mm", pe_base=rs)
                    S.op("scalar", lambda e, pt=pt, ps_S=ps_S: e.activation(
                        pt[:], B.pb[ps_S][:], AF.Exp, bias=negC),
                        reads=[r_("pb", ps_S), negC_k], writes=[ptk], tag="bs_exp")
                    if L1B_STAGE != 9:
                        msk = A.tri4 if kt == qt_abs else A.low4
                        mkey = r_("l1b_tri4") if kt == qt_abs else r_("l1b_low4")
                        S.op("vector", lambda e, pt=pt, msk=msk: e.tensor_tensor(
                            pt[:], pt[:], msk[:], ALU.min),
                            reads=[ptk, mkey], writes=[ptk], tag="bs_msk")
                if L1B_STAGE < 3:
                    continue
                for j in range(4):
                    hg = 4 * grp + j
                    for ki, kt in enumerate(kts):
                        pt = A.pt[ki]
                        ptk = r_("l1b_pt", ki)
                        st = pvi[0] == 0
                        sp = pvi[0] == n_pv - 1
                        pvi[0] += 1
                        S.op("tensor", lambda e, kt=kt, j=j, hg=hg, pt=pt,
                             st=st, sp=sp: e.matmul(
                                 B.pb[3][0:65, 128 * j:128 * j + 128],
                                 A.vt[kt][:, 65 * hg:65 * hg + 65],
                                 pt[:, 128 * j:128 * j + 128],
                                 start=st, stop=sp),
                            reads=[r_("l1b_vt", kt), ptk], writes=[r_("pb", 3)],
                            tag="bpv")
                rec = B.rowf[:, 3 * TJ:4 * TJ]
                rec_k = r_("rowf", 3)
                S.op("vector", lambda e, rec=rec: e.reciprocal(
                    rec, B.pb[3][64:65, 0:TJ]),
                    reads=[r_("pb", 3)], writes=[rec_k], tag="bn_rec")
                S.op("vector", lambda e, rec=rec: e.tensor_copy(B.rowh[:], rec),
                     reads=[rec_k], writes=[r_("rowh")], tag="bn_cast")
                S.op("tensor", lambda e: e.matmul(
                    B.pb[7][0:64, 0:TJ], B.ones_row[:, 0:64], B.rowh[:],
                    start=True, stop=True),
                    reads=[r_("rowh"), r_("ones_row")], writes=[r_("pb", 7)],
                    tag="bn_bcmm")
                S.op("scalar", lambda e: e.copy(A.recbc[:], B.pb[7][0:64, 0:TJ]),
                     reads=[r_("pb", 7)], writes=[r_("l1b_recbc")], tag="bn_bcev")
                for j in range(4):
                    ci, rs = 2 * grp + j // 2, 64 * (j % 2)
                    S.op("vector", lambda e, j=j, ci=ci, rs=rs, qt=qt, t0=t0:
                         e.tensor_mul(
                             A.ao[ci][rs:rs + 64,
                                      t0 + 128 * qt:t0 + 128 * qt + 128],
                             B.pb[3][0:64, 128 * j:128 * j + 128],
                             A.recbc[:, 128 * j:128 * j + 128]),
                        reads=[r_("pb", 3), r_("l1b_recbc")],
                        writes=[r_("l1b_ao", ci)], tag="bn_mul")
    if L1B_STAGE < 4:
        return
    for tj in range(NTJ):
        t0 = tj * TJ
        for oc in range(NCHUNK):
            S.dma("gpsimd", A.wo[:],
                  W["p_l1b_wox"][oc, :, 4 * half:4 * half + 4, :].rearrange(
                      "p a j -> p (a j)"),
                  writes=[r_("l1b_wo")], tag="bwo_ld")
            for i in range(4):
                S.op("tensor", lambda e, i=i, oc=oc, t0=t0: e.matmul(
                    B.pb[0 + oc % 2][:], A.wo[:, i * 128:(i + 1) * 128],
                    A.ao[i][:, t0:t0 + TJ], start=(i == 0), stop=(i == 3)),
                    reads=[r_("l1b_wo"), r_("l1b_ao", i)],
                    writes=[r_("pb", 0 + oc % 2)], tag="bwo_mm")
            if half == 0:
                S.op("vector", lambda e, oc=oc, t0=t0: e.scalar_tensor_tensor(
                    tok(B.x[oc], t0, TJ), B.pb[0 + oc % 2][:],
                    A.bo[:, oc:oc + 1], tok(B.x[oc], t0, TJ), ALU.add, ALU.add),
                    reads=[r_("pb", 0 + oc % 2), r_("l1b_bo"), r_("x", oc)],
                    writes=[r_("x", oc)], tag="bwo_res")
            else:
                S.op("vector", lambda e, oc=oc, t0=t0: e.tensor_add(
                    tok(B.x[oc], t0, TJ), tok(B.x[oc], t0, TJ),
                    B.pb[0 + oc % 2][:]),
                    reads=[r_("pb", 0 + oc % 2), r_("x", oc)],
                    writes=[r_("x", oc)], tag="bwo_res2")



C_L1F = 35.0
L1F_NAMES = ["L1_ffn_wqkv", "p_l1f_wqkvp", "L1_ffn_w1", "L1_ffn_w2",
             "p_l1f_wox", "p_l1f_qsv", "p_l1f_lg", "p_l1f_lb", "p_l1f_b1a",
             "p_l1f_b1g", "p_l1f_b2", "cst_xq_cos", "cst_xq_sin",
             "cst_xk_cos", "cst_xk_sin", "cst_l1_tri", "cst_geq"]


def alloc_l1f(nc, ctx):
    F = Bufs()
    ent = ctx.enter_context
    F.big = [ent(nc.sbuf_tensor(f"l1f_big{i}", [128, T], BF16)) for i in range(8)]
    F.vt = [ent(nc.sbuf_tensor(f"l1f_vt{t}", [128, 520], BF16)) for t in range(16)]
    F.aow = [ent(nc.sbuf_tensor(f"l1f_ao{i}", [128, TJ], BF16)) for i in range(4)]
    F.hp = [ent(nc.sbuf_tensor(f"l1f_hp{c}", [128, TJ], BF16)) for c in range(NCHUNK)]
    F.tc = ent(nc.sbuf_tensor("l1f_tc", [128, T], BF16))
    F.ts = ent(nc.sbuf_tensor("l1f_ts", [128, T], BF16))
    F.wa = [ent(nc.sbuf_tensor(f"l1f_wa{c}", [128, 128], BF16)) for c in range(NCHUNK)]
    F.wb = [ent(nc.sbuf_tensor(f"l1f_wb{c}", [128, 128], BF16)) for c in range(NCHUNK)]
    F.w2c = ent(nc.sbuf_tensor("l1f_w2c", [128, 2816], BF16))
    F.wo = ent(nc.sbuf_tensor("l1f_wo", [128, 512], BF16))
    F.praw = [ent(nc.sbuf_tensor(f"l1f_praw{i}", [128, TJ], BF16)) for i in range(2)]
    F.t = [ent(nc.sbuf_tensor(f"l1f_t{i}", [128, TJ], BF16)) for i in range(2)]
    F.pt = [ent(nc.sbuf_tensor(f"l1f_pt{i}", [128, TJ], BF16)) for i in range(2)]
    F.rnbc = ent(nc.sbuf_tensor("l1f_rnbc", [128, TJ], BF16))
    F.recbc = ent(nc.sbuf_tensor("l1f_recbc", [64, TJ], BF16))
    F.rh2 = ent(nc.sbuf_tensor("l1f_rh2", [1, 1024], BF16))
    F.geq = ent(nc.sbuf_tensor("l1f_geq", [128, 128], BF16))
    F.tri = ent(nc.sbuf_tensor("l1f_tri", [128, 128], BF16))
    F.qsv = ent(nc.sbuf_tensor("l1f_qsv", [128, 8], F32))
    F.lg = ent(nc.sbuf_tensor("l1f_lg", [128, 8], F32))
    F.lb = ent(nc.sbuf_tensor("l1f_lb", [128, 8], F32))
    F.b1a = ent(nc.sbuf_tensor("l1f_b1a", [128, 22], F32))
    F.b1g = ent(nc.sbuf_tensor("l1f_b1g", [128, 22], F32))
    F.b2 = ent(nc.sbuf_tensor("l1f_b2", [128, 8], F32))
    return F


def emit_l1f_setup(S, nc, B, F, W):
    S.dma("sync", F.geq[:], W["cst_geq"], writes=[r_("l1f_geq")])
    S.dma("sync", F.tri[:], W["cst_l1_tri"], writes=[r_("l1f_tri")])
    S.dma("sync", F.qsv[:], W["p_l1f_qsv"], writes=[r_("l1f_qsv")])
    F.scr_x = nc.dram_tensor("scr_xout", [NCHUNK, 128, T], F32,
                             kind="Internal").ap()
    F.scr_w1a = nc.dram_tensor("scr_w1a", [2, 22, 128, 8, 128], BF16,
                               kind="Internal").ap()
    F.scr_w1g = nc.dram_tensor("scr_w1g", [2, 22, 128, 8, 128], BF16,
                               kind="Internal").ap()
    F.scr_w2 = nc.dram_tensor("scr_w2f", [2, 8, 128, 22, 128], BF16,
                              kind="Internal").ap()
    w1 = W["L1_ffn_w1"]
    w2 = W["L1_ffn_w2"]
    for l in range(2):
        for part, scr in ((0, F.scr_w1a), (1, F.scr_w1g)):
            for ns in range(22):
                n0 = part * 2730 + ns * 128
                wcols = min(128, part * 2730 + 2730 - n0)
                stg = F.w2c[:, 0:1024]
                if wcols < 128:
                    S.op("vector", lambda e, stg=stg: e.memset(stg, 0.0),
                         writes=[r_("l1f_w2c", 0)], tag="sw_z")
                dst = stg.rearrange("p (c j) -> p c j", c=8)[:, :, 0:wcols]
                S.dma("gpsimd", dst,
                      w1[l, :, n0:n0 + wcols].rearrange("(c p) j -> p c j", p=128),
                      writes=[r_("l1f_w2c", 0)], tag="sw_ld")
                S.dma("sync", scr[l, ns].rearrange("p c j -> p (c j)"), stg,
                      reads=[r_("l1f_w2c", 0)], writes=[r_("scrw1", part, l, ns)],
                      tag="sw_st")
        for i in range(22):
            k0 = i * 128
            krows = min(128, 2730 - k0)
            stg = F.w2c[:, 1024:2048]
            if krows < 128:
                S.op("vector", lambda e, stg=stg: e.memset(stg, 0.0),
                     writes=[r_("l1f_w2c", 1)], tag="sw_z2")
            S.dma("gpsimd", stg[0:krows, 0:1024], w2[l, k0:k0 + krows, :],
                  writes=[r_("l1f_w2c", 1)], tag="sw_ld2")
            S.dma("sync",
                  bass.AP(F.scr_w2.tensor,
                          l * (8 * 128 * 22 * 128) + i * 128,
                          [[22 * 128, 128], [128 * 22 * 128, 8], [1, 128]]),
                  stg.rearrange("p (oc j) -> p oc j", oc=8),
                  reads=[r_("l1f_w2c", 1)], writes=[r_("scrw2", l, i)], tag="sw_st2")


def emit_l1_ffn(S, nc, B, F, W):
    emit_ln(S, B, "ln")
    for c in range(NCHUNK):
        S.dma("sync", F.scr_x[c], tok(B.x[c], 0, T), reads=[r_("x", c)],
              writes=[r_("scrx", c)], tag="xspill")
    for c in range(NCHUNK):
        S.op("vector", lambda e, c=c: e.tensor_mul(
            tok(B.x[c], 0, T), tok(B.x[c], 0, T), B.b1bc[:]),
            reads=[r_("x", c), r_("b1bc"), r_("scrx", c)], writes=[r_("x", c)],
            tag="xi1")
        S.op("vector", lambda e, c=c: e.tensor_add(
            tok(B.x[c], 0, T), tok(B.x[c], 0, T), B.b2bc[:]),
            reads=[r_("x", c), r_("b2bc")], writes=[r_("x", c)], tag="xi2")
        S.op("vector", lambda e, c=c: e.tensor_copy(
            tok(B.h[c], 0, T), tok(B.x[c], 0, T)),
            reads=[r_("x", c)], writes=[r_("h", c)], tag="xb")
    for l in range(2):
        emit_l1f_layer(S, nc, B, F, W, l)
    for c in range(NCHUNK):
        S.dma("gpsimd", tok(B.x[c], 0, T), F.scr_x[c],
              reads=[r_("scrx", c), r_("x", c)], writes=[r_("x", c)],
              tag="xmerge", accum_op=ALU.add)


def _xsh(B, c, t0, n):
    if c >= 4:
        return B.h[c][:, t0:t0 + n]
    return tok(B.h[c], t0, n)


def emit_l1f_layer(S, nc, B, F, W, l):
    negC, negC_k = cvec(S, B, -C_L1F)
    bigk = lambda i, blk: r_("l1f_big", i, blk)
    allblk = lambda i: [r_("l1f_big", i, b_) for b_ in range(4)]

    S.dma("sync", F.lg[:], W["p_l1f_lg"][l], writes=[r_("l1f_lg")])
    S.dma("sync", F.lb[:], W["p_l1f_lb"][l], writes=[r_("l1f_lb")])
    S.dma("sync", F.b1a[:], W["p_l1f_b1a"][l], writes=[r_("l1f_b1a")])
    S.dma("sync", F.b1g[:], W["p_l1f_b1g"][l], writes=[r_("l1f_b1g")])
    S.dma("sync", F.b2[:], W["p_l1f_b2"][l], writes=[r_("l1f_b2")])

    for c in range(NCHUNK):
        S.dma("gpsimd", F.big[c][:, 0:512],
              W["L1_ffn_wqkv"][l][c * 128:(c + 1) * 128, 1024:1536],
              writes=[bigk(c, 0)], tag="fv_wld")
    for tt in range(16):
        vt = F.vt[tt]
        S.op("vector", lambda e, vt=vt: e.memset(
            vt.ap().rearrange("p (a b) -> p a b", b=65)[:, :, 64:65], 1.0),
            writes=[r_("l1f_vt", tt)], tag="fvones")
        for c in range(NCHUNK):
            S.op("tensor", lambda e, c=c, tt=tt: e.matmul(
                B.pb[6][:, 0:512], _xsh(B, c, tt * 128, 128),
                F.big[c][:, 0:512],
                start=(c == 0), stop=(c == NCHUNK - 1)),
                reads=[bigk(c, 0), r_("h", c)], writes=[r_("pb", 6)],
                tag="fvproj")
        S.op("scalar", lambda e, vt=vt: e.copy(
            vt.ap().rearrange("p (a b) -> p a b", b=65)[:, :, 0:64],
            B.pb[6][:, 0:512].rearrange("p (a b) -> p a b", b=64)),
            reads=[r_("pb", 6)], writes=[r_("l1f_vt", tt)], tag="fvevac")

    for side in range(2):
        tc, ts_ = F.tc, F.ts
        tck, tsk = r_("l1f_tc"), r_("l1f_ts")
        S.dma("sync", F.tc[:], W["cst_xq_cos" if side == 0 else "cst_xk_cos"],
              writes=[tck], tag="ftab_c")
        S.dma("sync", F.ts[:], W["cst_xq_sin" if side == 0 else "cst_xk_sin"],
              writes=[tsk], tag="ftab_s")
        qv = F.qsv[:, 4 * l + 2 * side:4 * l + 2 * side + 1]
        qpv = F.qsv[:, 4 * l + 2 * side + 1:4 * l + 2 * side + 2]
        for qc in range(4):
            big_i = 4 * side + qc
            n0 = 512 * side + 128 * qc
            _load_w_tiles(S, F.wa, "l1f_wa", W["L1_ffn_wqkv"][l], n0, 128)
            _load_w_tiles(S, F.wb, "l1f_wb", W["p_l1f_wqkvp"][l], n0, 128)
            for tj in range(NTJ):
                t0 = tj * TJ
                for c in range(NCHUNK):
                    S.op("tensor", lambda e, c=c, t0=t0: e.matmul(
                        B.pb[0][:], F.wa[c][:, 0:128], _xsh(B, c, t0, TJ),
                        start=(c == 0), stop=(c == NCHUNK - 1)),
                        reads=[r_("l1f_wa", c), r_("h", c)], writes=[r_("pb", 0)],
                        tag="fq_mm")
                    S.op("tensor", lambda e, c=c, t0=t0: e.matmul(
                        B.pb[1][:], F.wb[c][:, 0:128], _xsh(B, c, t0, TJ),
                        start=(c == 0), stop=(c == NCHUNK - 1)),
                        reads=[r_("l1f_wb", c), r_("h", c)], writes=[r_("pb", 1)],
                        tag="fq_mmp")
                S.op("scalar", lambda e: e.copy(F.praw[0][:], B.pb[0][:]),
                     reads=[r_("pb", 0)], writes=[r_("l1f_praw", 0)], tag="fq_e1")
                S.op("scalar", lambda e: e.copy(F.praw[1][:], B.pb[1][:]),
                     reads=[r_("pb", 1)], writes=[r_("l1f_praw", 1)], tag="fq_e2")
                S.op("scalar", lambda e: e.activation(
                    B.sq[0][:], F.praw[0][:], AF.Square),
                    reads=[r_("l1f_praw", 0)], writes=[r_("sq", 0)], tag="fq_sq")
                S.op("tensor", lambda e: e.matmul(
                    B.pb[5][0:1, :], B.onecol[0:64, :], B.sq[0][0:64, :],
                    start=True, stop=True),
                    reads=[r_("sq", 0), r_("onecol")], writes=[r_("pb", 5)],
                    tag="fq_n1", pe_base=0)
                S.op("tensor", lambda e: e.matmul(
                    B.pb[6][0:1, :], B.onecol[64:128, :], B.sq[0][64:128, :],
                    start=True, stop=True),
                    reads=[r_("sq", 0), r_("onecol")], writes=[r_("pb", 6)],
                    tag="fq_n2", pe_base=64)
                eps0, eps0_k = cvec(S, B, 1e-24, 0, 1)
                mh_ap, mh_k = cvec(S, B, -0.5, 0, 1)
                for bi, bank in ((0, 5), (1, 6)):
                    S.op("scalar", lambda e, bank=bank, a=eps0: e.activation(
                        B.pb[bank][0:1, :], B.pb[bank][0:1, :], AF.Ln, bias=a),
                        reads=[r_("pb", bank), eps0_k], writes=[r_("pb", bank)],
                        tag="fq_ln")
                    S.op("scalar", lambda e, bank=bank, bi=bi, a=mh_ap:
                         e.activation(F.rh2[:, 512 * bi:512 * bi + 512],
                                      B.pb[bank][0:1, :], AF.Exp, scale=a),
                         reads=[r_("pb", bank), mh_k], writes=[r_("l1f_rh2", bi)],
                         tag="fq_ex")
                S.op("tensor", lambda e: e.matmul(
                    B.pb[7][0:64, :], B.ones_row[:, 0:64], F.rh2[:, 0:512],
                    start=True, stop=True),
                    reads=[r_("l1f_rh2", 0), r_("ones_row")], writes=[r_("pb", 7)],
                    tag="fq_b1")
                S.op("tensor", lambda e: e.matmul(
                    B.pb[7][64:128, :], B.ones_row[:, 0:64], F.rh2[:, 512:1024],
                    start=True, stop=True),
                    reads=[r_("l1f_rh2", 1), r_("ones_row")], writes=[r_("pb", 7)],
                    tag="fq_b2")
                S.op("scalar", lambda e: e.copy(F.rnbc[:], B.pb[7][:]),
                     reads=[r_("pb", 7)], writes=[r_("l1f_rnbc")], tag="fq_bc")
                S.op("vector", lambda e, t0=t0, qv=qv: e.scalar_tensor_tensor(
                    F.t[0][:], F.praw[0][:], qv, tc[:, t0:t0 + TJ],
                    ALU.mult, ALU.mult),
                    reads=[r_("l1f_praw", 0), r_("l1f_qsv"), tck],
                    writes=[r_("l1f_t", 0)], tag="fq_p1")
                S.op("vector", lambda e, t0=t0, qpv=qpv: e.scalar_tensor_tensor(
                    F.t[1][:], F.praw[1][:], qpv, ts_[:, t0:t0 + TJ],
                    ALU.mult, ALU.mult),
                    reads=[r_("l1f_praw", 1), r_("l1f_qsv"), tsk],
                    writes=[r_("l1f_t", 1)], tag="fq_p2")
                S.op("vector", lambda e: e.tensor_add(
                    F.t[0][:], F.t[0][:], F.t[1][:]),
                    reads=[r_("l1f_t", 0), r_("l1f_t", 1)], writes=[r_("l1f_t", 0)],
                    tag="fq_p3")
                S.op("vector", lambda e, big_i=big_i, t0=t0: e.tensor_mul(
                    F.big[big_i][:, t0:t0 + TJ], F.t[0][:], F.rnbc[:]),
                    reads=[r_("l1f_t", 0), r_("l1f_rnbc")],
                    writes=[bigk(big_i, t0 // TJ)], tag="fq_p4")

    for wb in range(4):
        q0 = wb * TJ
        for h in range(8):
            qc, rs = h // 2, 64 * (h % 2)
            qbig = F.big[qc]
            kbig = F.big[4 + qc]
            tiles = [(4 * wb + rp, 128 * rp, TJ, "cur") for rp in range(4)]
            if wb > 0:
                tiles += [(4 * (wb - 1) + rp, 0, 128 * (rp + 1), "prev")
                          for rp in range(4)]
            n_pv = len(tiles)
            for pi, (kt, qlo, qhi, kind) in enumerate(tiles):
                w = qhi - qlo
                pt = F.pt[pi % 2]
                ptk = r_("l1f_pt", pi % 2)
                ps_S = 2 + (pi % 2) * 2
                S.op("tensor", lambda e, kt=kt, qlo=qlo, w=w, rs=rs, ps_S=ps_S,
                     qbig=qbig, kbig=kbig, q0=q0: e.matmul(
                         B.pb[ps_S][:, 0:w],
                         kbig[rs:rs + 64, kt * 128:(kt + 1) * 128],
                         qbig[rs:rs + 64, q0 + qlo:q0 + qlo + w],
                         start=True, stop=True),
                    reads=[bigk(4 + qc, kt // 4), bigk(qc, wb)],
                    writes=[r_("pb", ps_S)], tag="fs_mm", pe_base=rs)
                S.op("scalar", lambda e, w=w, pt=pt, ps_S=ps_S: e.activation(
                    pt[:, 0:w], B.pb[ps_S][:, 0:w], AF.Exp, bias=negC),
                    reads=[r_("pb", ps_S), negC_k], writes=[ptk], tag="fs_exp")
                if kind == "cur":
                    S.op("vector", lambda e, pt=pt: e.tensor_tensor(
                        pt[:, 0:128], pt[:, 0:128], F.tri[:], ALU.min),
                        reads=[ptk, r_("l1f_tri")], writes=[ptk], tag="fs_mt")
                else:
                    S.op("vector", lambda e, pt=pt, w=w: e.tensor_tensor(
                        pt[:, w - 128:w], pt[:, w - 128:w], F.geq[:], ALU.min),
                        reads=[ptk, r_("l1f_geq")], writes=[ptk], tag="fs_mg")
                S.op("tensor", lambda e, kt=kt, h=h, pt=pt, qlo=qlo, w=w,
                     pi=pi, n_pv=n_pv: e.matmul(
                         B.pb[3][0:65, qlo:qlo + w],
                         F.vt[kt][:, 65 * h:65 * h + 65], pt[:, 0:w],
                         start=(pi == 0), stop=(pi == n_pv - 1)),
                    reads=[r_("l1f_vt", kt), ptk], writes=[r_("pb", 3)],
                    tag="fpv")
            rec = B.rowf[:, 3 * TJ:4 * TJ]
            rec_k = r_("rowf", 3)
            S.op("vector", lambda e, rec=rec: e.reciprocal(
                rec, B.pb[3][64:65, 0:TJ]),
                reads=[r_("pb", 3)], writes=[rec_k], tag="fn_rec")
            S.op("vector", lambda e, rec=rec: e.tensor_copy(B.rowh[:], rec),
                 reads=[rec_k], writes=[r_("rowh")], tag="fn_cast")
            S.op("tensor", lambda e: e.matmul(
                B.pb[7][0:64, 0:TJ], B.ones_row[:, 0:64], B.rowh[:],
                start=True, stop=True),
                reads=[r_("rowh"), r_("ones_row")], writes=[r_("pb", 7)],
                tag="fn_bcmm")
            S.op("scalar", lambda e: e.copy(F.recbc[:], B.pb[7][0:64, 0:TJ]),
                 reads=[r_("pb", 7)], writes=[r_("l1f_recbc")], tag="fn_bcev")
            S.op("vector", lambda e, qc=qc, rs=rs: e.tensor_mul(
                F.aow[qc][rs:rs + 64, :], B.pb[3][0:64, 0:TJ], F.recbc[:]),
                reads=[r_("pb", 3), r_("l1f_recbc")], writes=[r_("l1f_aow", qc)],
                tag="fn_mul")
        for oc in range(NCHUNK):
            S.dma("gpsimd", F.wo[:],
                  W["p_l1f_wox"][l, oc].rearrange("p a j -> p (a j)"),
                  writes=[r_("l1f_wo")], tag="fwo_ld")
            for i in range(4):
                S.op("tensor", lambda e, i=i, oc=oc: e.matmul(
                    B.pb[0 + oc % 2][:], F.wo[:, i * 128:(i + 1) * 128],
                    F.aow[i][:], start=(i == 0), stop=(i == 3)),
                    reads=[r_("l1f_wo"), r_("l1f_aow", i)],
                    writes=[r_("pb", 0 + oc % 2)], tag="fwo_mm")
            S.op("vector", lambda e, oc=oc, q0=q0: e.tensor_add(
                tok(B.x[oc], q0, TJ), tok(B.x[oc], q0, TJ),
                B.pb[0 + oc % 2][:]),
                reads=[r_("pb", 0 + oc % 2), r_("x", oc)], writes=[r_("x", oc)],
                tag="fwo_res")
    for c in range(NCHUNK):
        S.op("vector", lambda e, c=c: e.tensor_copy(
            tok(B.h[c], 0, T), tok(B.x[c], 0, T)),
            reads=[r_("x", c)], writes=[r_("h", c)], tag="fxb")

    emit_l1f_ffn(S, nc, B, F, W, l)


def emit_l1f_ffn(S, nc, B, F, W, l):
    pb_s0, pb_s1 = B.pb[5], B.pb[6]
    eps_ap, eps_k = cvec(S, B, EPS, 0, 1)
    mh_ap, mh_k = cvec(S, B, -0.5, 0, 1)
    for tj in range(NTJ):
        t0 = tj * TJ
        for c in range(NCHUNK):
            sqb = B.sq[c % 2]
            S.op("scalar", lambda e, c=c, t0=t0, sqb=sqb: e.activation(
                sqb[:], _xsh(B, c, t0, TJ), AF.Square),
                reads=[r_("h", c)], writes=[r_("sq", c % 2)], tag="fl_sq")
            S.op("tensor", lambda e, c=c, t0=t0: e.matmul(
                pb_s0[0:1, :], B.onecol[:], _xsh(B, c, t0, TJ),
                start=(c == 0), stop=(c == NCHUNK - 1)),
                reads=[r_("h", c), r_("onecol")], writes=[r_("pb", 5)],
                tag="fl_mm0")
            S.op("tensor", lambda e, sqb=sqb, c=c: e.matmul(
                pb_s1[0:1, :], B.onecol[:], sqb[:],
                start=(c == 0), stop=(c == NCHUNK - 1)),
                reads=[r_("sq", c % 2), r_("onecol")], writes=[r_("pb", 6)],
                tag="fl_mm1")

        def slot(i):
            return B.rowf[:, i * TJ:(i + 1) * TJ]

        mu, b1r, b2r = slot(0), slot(1), slot(2)
        tmp = slot(4)
        K_mu, K_b1, K_b2 = (r_("rowf", i) for i in (0, 1, 2))
        K_tmp = r_("rowf", 4)
        S.op("vector", lambda e, mu=mu: e.tensor_scalar(
            mu, pb_s0[0:1, :], 1.0 / D, None, ALU.mult),
            reads=[r_("pb", 5)], writes=[K_mu], tag="fl_mu")
        S.op("vector", lambda e: e.tensor_scalar(
            pb_s1[0:1, :], pb_s1[0:1, :], 1.0 / D, None, ALU.mult),
            reads=[r_("pb", 6)], writes=[r_("pb", 6)], tag="fl_ms")
        S.op("vector", lambda e, mu=mu, tmp=tmp: e.tensor_mul(tmp, mu, mu),
             reads=[K_mu], writes=[K_tmp], tag="fl_mu2")
        S.op("vector", lambda e, tmp=tmp: e.tensor_sub(
            pb_s1[0:1, :], pb_s1[0:1, :], tmp),
            reads=[K_tmp, r_("pb", 6)], writes=[r_("pb", 6)], tag="fl_var")
        S.op("scalar", lambda e, a=eps_ap: e.activation(
            pb_s1[0:1, :], pb_s1[0:1, :], AF.Ln, bias=a),
            reads=[r_("pb", 6), eps_k], writes=[r_("pb", 6)], tag="fl_ln")
        S.op("scalar", lambda e, b1r=b1r, a=mh_ap: e.activation(
            b1r, pb_s1[0:1, :], AF.Exp, scale=a),
            reads=[r_("pb", 6), mh_k], writes=[K_b1], tag="fl_ex")
        S.op("vector", lambda e, mu=mu, b1r=b1r, b2r=b2r: e.scalar_tensor_tensor(
            b2r, mu, -1.0, b1r, ALU.mult, ALU.mult),
            reads=[K_mu, K_b1], writes=[K_b2], tag="fl_b2")
        _bc_row(S, B, b1r, K_b1, B.b1bc, r_("b1bc"), t0)
        _bc_row(S, B, b2r, K_b2, B.b2bc, r_("b2bc"), t0)
    for tj in range(NTJ):
        t0 = tj * TJ
        for c in range(NCHUNK):
            g = F.lg[:, c:c + 1]
            bt = F.lb[:, c:c + 1]
            S.op("vector", lambda e, c=c, t0=t0, g=g: e.scalar_tensor_tensor(
                F.hp[c][:], _xsh(B, c, t0, TJ), g, B.b1bc[:, t0:t0 + TJ],
                ALU.mult, ALU.mult),
                reads=[r_("h", c), r_("b1bc"), r_("l1f_lg")],
                writes=[r_("l1f_hp", c)], tag="fh_1")
            S.op("vector", lambda e, c=c, t0=t0, g=g: e.scalar_tensor_tensor(
                F.hp[c][:], B.b2bc[:, t0:t0 + TJ], g, F.hp[c][:],
                ALU.mult, ALU.add),
                reads=[r_("l1f_hp", c), r_("b2bc"), r_("l1f_lg")],
                writes=[r_("l1f_hp", c)], tag="fh_2")
            S.op("vector", lambda e, c=c, bt=bt: e.tensor_scalar(
                F.hp[c][:], F.hp[c][:], bt, None, ALU.add),
                reads=[r_("l1f_hp", c), r_("l1f_lb")], writes=[r_("l1f_hp", c)],
                tag="fh_3")
        for ns in range(22):
            zb = F.big[ns // 4]
            zcol = 512 * (ns % 4)
            zkey = bigk_g(ns)
            w1a = F.w2c[:, 0:1024]
            w1g = F.w2c[:, 1024:2048]
            S.dma("sync", w1a, F.scr_w1a[l, ns].rearrange("p c j -> p (c j)"),
                  reads=[r_("scrw1", 0, l, ns)], writes=[r_("l1f_w2c", 0)],
                  tag="f1_lda")
            S.dma("sync", w1g, F.scr_w1g[l, ns].rearrange("p c j -> p (c j)"),
                  reads=[r_("scrw1", 1, l, ns)], writes=[r_("l1f_w2c", 1)],
                  tag="f1_ldg")
            for c in range(NCHUNK):
                S.op("tensor", lambda e, c=c, w1a=w1a: e.matmul(
                    B.pb[0][:], w1a[:, c * 128:(c + 1) * 128], F.hp[c][:],
                    start=(c == 0), stop=(c == NCHUNK - 1)),
                    reads=[r_("l1f_w2c", 0), r_("l1f_hp", c)], writes=[r_("pb", 0)],
                    tag="f1_mma")
                S.op("tensor", lambda e, c=c, w1g=w1g: e.matmul(
                    B.pb[1][:], w1g[:, c * 128:(c + 1) * 128], F.hp[c][:],
                    start=(c == 0), stop=(c == NCHUNK - 1)),
                    reads=[r_("l1f_w2c", 1), r_("l1f_hp", c)], writes=[r_("pb", 1)],
                    tag="f1_mmg")
            S.op("scalar", lambda e, ns=ns: e.activation(
                F.t[0][:], B.pb[1][:], AF.Gelu, bias=F.b1g[:, ns:ns + 1]),
                reads=[r_("pb", 1), r_("l1f_b1g")], writes=[r_("l1f_t", 0)],
                tag="f1_gelu")
            S.op("vector", lambda e, ns=ns, zb=zb, zcol=zcol:
                 e.scalar_tensor_tensor(
                     zb[:, zcol:zcol + 512], B.pb[0][:], F.b1a[:, ns:ns + 1],
                     F.t[0][:], ALU.add, ALU.mult),
                 reads=[r_("pb", 0), r_("l1f_b1a"), r_("l1f_t", 0)],
                 writes=[zkey], tag="f1_z")
        for oc in range(NCHUNK):
            S.dma("sync", F.w2c[:], F.scr_w2[l, oc].rearrange("p i j -> p (i j)"),
                  reads=[r_("scrw2", l, i_) for i_ in range(22)],
                  writes=[r_("l1f_w2c", 0), r_("l1f_w2c", 1), r_("l1f_w2c", 2)],
                  tag="f2_ld")
            for i in range(22):
                S.op("tensor", lambda e, i=i, oc=oc: e.matmul(
                    B.pb[2 + oc % 2][:], F.w2c[:, i * 128:(i + 1) * 128],
                    F.big[i // 4][:, 512 * (i % 4):512 * (i % 4) + 512],
                    start=(i == 0), stop=(i == 21)),
                    reads=[r_("l1f_w2c", min(i // 8, 2)), bigk_g(i)],
                    writes=[r_("pb", 2 + oc % 2)], tag="f2_mm")
            S.op("vector", lambda e, oc=oc, t0=t0: e.scalar_tensor_tensor(
                tok(B.x[oc], t0, TJ), B.pb[2 + oc % 2][:], F.b2[:, oc:oc + 1],
                tok(B.x[oc], t0, TJ), ALU.add, ALU.add),
                reads=[r_("pb", 2 + oc % 2), r_("l1f_b2"), r_("x", oc)],
                writes=[r_("x", oc)], tag="f2_res")
    for c in range(NCHUNK):
        S.op("vector", lambda e, c=c: e.tensor_copy(
            tok(B.h[c], 0, T), tok(B.x[c], 0, T)),
            reads=[r_("x", c)], writes=[r_("h", c)], tag="fxb2")


def bigk_g(ns):
    return r_("l1f_big", ns // 4, ns % 4)



def _sub_l0a(S, nc, B, A, W):
    emit_ln(S, B, "ln")
    emit_l0_attn_a(S, nc, B, A, W)


def _sub_l0f(S, nc, B, F, W):
    emit_l0_ffn(S, nc, B, F, W)


SUBLAYERS = {
    "l0a": (L0A_NAMES, alloc_l0a, emit_l0a_setup, _sub_l0a),
    "l0f": (L0F_NAMES, alloc_l0f, emit_l0f_setup, _sub_l0f),
    "l0b": (L0B_NAMES, alloc_l0b, emit_l0b_setup,
            lambda S, nc, B, A, W: emit_l0_attn_b(S, nc, B, A, W)),
    "l1a": (L1A_NAMES, alloc_l1a, emit_l1a_setup,
            lambda S, nc, B, A, W: emit_l1_attn_a(S, nc, B, A, W)),
    "l1b": (L1B_NAMES, alloc_l1b, emit_l1b_setup,
            lambda S, nc, B, A, W: emit_l1_attn_b(S, nc, B, A, W)),
    "l1f": (L1F_NAMES, alloc_l1f, emit_l1f_setup,
            lambda S, nc, B, A, W: emit_l1_ffn(S, nc, B, A, W)),
}


def build_chain_program(sub_ids, arrays, n_seq=NSEQ, strict=False):
    nc = bass.Bass("TRN2", target_bir_lowering=False)
    x_in, x_out = _declare_x_io(nc)
    names = []
    for sid in sub_ids:
        for n in SUBLAYERS[sid][0]:
            if n not in names:
                names.append(n)
    W = declare_inputs(nc, arrays, names)
    with ExitStack() as ctx:
        B = alloc_common(nc, ctx)
        S = Sched(nc, strict=strict)
        emit_init_consts(S, B)
        bufs = {}
        for sid in dict.fromkeys(sub_ids):
            _, alloc, setup, _ = SUBLAYERS[sid]
            bufs[sid] = alloc(nc, ctx)
            setup(S, nc, B, bufs[sid], W)
        for seq in range(n_seq):
            emit_load_x(S, B, x_in, seq)
            for sid in sub_ids:
                SUBLAYERS[sid][3](S, nc, B, bufs[sid], W)
            emit_store_x(S, B, x_out, seq)
        S.finalize()
    return nc, S, names


LAUNCHES = [["l0a"], ["l0f"], ["l0b"], ["l0f"], ["l1a"], ["l1f"], ["l1b"],
            ["l1f"]]


def kernel(**inputs):
    arrays = dict(inputs)
    arrays.update(host_prep(inputs))
    xsh = shard_x(np.asarray(inputs["L0_attn_a_x"], dtype=np.float32))
    built = {}
    for group in LAUNCHES:
        key = tuple(group)
        if key not in built:
            built[key] = build_chain_program(group, arrays)
        nc, S, names = built[key]
        in_maps = []
        for ci in range(8):
            m = {n: arrays[n] for n in names}
            m["x_io_in"] = xsh[ci]
            in_maps.append(m)
        res = run_bass_kernel_spmd(nc, in_maps, core_ids=list(range(8)))
        xsh = [np.asarray(res.results[ci]["x_io_out"]) for ci in range(8)]
    out = np.empty((32, T, D), np.float32)
    for ci in range(8):
        out[4 * ci:4 * ci + 4] = (
            xsh[ci].reshape(NSEQ, D, T).transpose(0, 2, 1))
    return out


# version 02
# speedup vs baseline: 1.0342x; 1.0342x over previous
import math
from contextlib import ExitStack

import numpy as np
import ml_dtypes
import concourse.bass as bass
import concourse.mybir as mybir
from concourse.bass_utils import run_bass_kernel_spmd

F32 = mybir.dt.float32
BF16 = mybir.dt.bfloat16
I32 = mybir.dt.int32
AF = mybir.ActivationFunctionType
ALU = mybir.AluOpType
AX = mybir.AxisListType
bf16 = ml_dtypes.bfloat16

T = 2048
TG = T + 4
D = 1024
NCHUNK = 8
NSEQ = 4
TJ = 512
NTJ = T // TJ
EPS = 1e-5

C_L0A = 20.0
C_L0B = 30.0
C_L1A = 25.0
C_L1B = 25.0



def _rope_flavA():
    nb = 16
    inv_freq = 1.0 / (10000.0 ** (np.arange(nb, dtype=np.float64) / nb))
    ang = np.arange(T, dtype=np.float64)[:, None] * inv_freq[None, :]
    sin_h = np.repeat(np.sin(ang), 2, axis=-1)
    cos_h = np.repeat(np.cos(ang), 2, axis=-1)
    return (np.concatenate([cos_h, cos_h], axis=-1),
            np.concatenate([sin_h, sin_h], axis=-1))


def _rope_halves(dim=64, theta=10000.0):
    inv_freq = 1.0 / (theta ** (np.arange(0, dim, 2, dtype=np.float64) / dim))
    freqs = np.arange(T, dtype=np.float64)[:, None] * inv_freq[None, :]
    emb = np.concatenate([freqs, freqs], axis=-1)
    return np.cos(emb), np.sin(emb)


def _rope_interleaved(d_k=64, theta=10000.0):
    half = d_k // 2
    inv_freq = theta ** (-np.arange(half, dtype=np.float64) * 2.0 / d_k)
    ang = np.arange(T, dtype=np.float64)[:, None] * inv_freq[None, :]
    return np.repeat(np.cos(ang), 2, axis=-1), np.repeat(np.sin(ang), 2, axis=-1)


def _xpos_tables():
    DIM_HEAD, WINDOW = 64, 512
    half_idx = np.arange(0, DIM_HEAD, 2, dtype=np.float64)
    inv_freq = 1.0 / (10000.0 ** (half_idx / DIM_HEAD))
    t = np.arange(T, dtype=np.float64)
    freqs = t[:, None] * inv_freq[None, :]
    freqs = np.concatenate([freqs, freqs], axis=-1)
    base = (half_idx + 0.4 * DIM_HEAD) / (1.4 * DIM_HEAD)
    power = (t - T // 2) / float(WINDOW // 2)
    scale = base[None, :] ** power[:, None]
    scale = np.concatenate([scale, scale], axis=-1)
    return np.cos(freqs), np.sin(freqs), scale


def _sign_halves(dim=64):
    s = np.ones(dim)
    s[dim // 2:] = 1.0
    s[: dim // 2] = -1.0
    return s


def _sign_interleaved(dim=64):
    s = np.ones(dim)
    s[0::2] = -1.0
    return s


def _permA_idx(n):
    h = n % 64
    return n - h + ((h + 32) % 64)


def _permI_idx(n):
    return n ^ 1


def _tile2(a_t64):
    at = np.ascontiguousarray(a_t64.T)
    return np.concatenate([at, at], axis=0)


def host_constants():
    C = {}
    cosA, sinA = _rope_flavA()
    sgA = _sign_halves()
    C["cst_cosA"] = _tile2(cosA).astype(bf16)
    C["cst_sinA"] = _tile2(sinA * sgA[None, :]).astype(bf16)
    cosB, sinB = _rope_halves(64)
    C["cst_cosB_q"] = np.ascontiguousarray(cosB.T).astype(bf16)
    C["cst_sinB_q"] = np.ascontiguousarray((sinB * sgA[None, :]).T).astype(bf16)
    C["cst_cosB_k"] = np.ascontiguousarray((cosB / 16.0).T).astype(bf16)
    C["cst_sinB_k"] = np.ascontiguousarray(
        (sinB * sgA[None, :] / 16.0).T).astype(bf16)
    cosI, sinI = _rope_interleaved()
    sgI = _sign_interleaved()
    C["cst_cosI_q"] = _tile2(cosI / 8.0).astype(bf16)
    C["cst_sinI_q"] = _tile2(sinI * sgI[None, :] / 8.0).astype(bf16)
    C["cst_cosI_k"] = _tile2(cosI).astype(bf16)
    C["cst_sinI_k"] = _tile2(sinI * sgI[None, :]).astype(bf16)
    C["cst_cosH_q"] = _tile2(cosB / 8.0).astype(bf16)
    C["cst_sinH_q"] = _tile2(sinB * sgA[None, :] / 8.0).astype(bf16)
    C["cst_cosH_k"] = _tile2(cosB).astype(bf16)
    C["cst_sinH_k"] = _tile2(sinB * sgA[None, :]).astype(bf16)
    cosX, sinX, scX = _xpos_tables()
    C["cst_xq_cos"] = _tile2(cosX * scX * 8.0).astype(bf16)
    C["cst_xq_sin"] = _tile2(sinX * scX * sgA[None, :] * 8.0).astype(bf16)
    C["cst_xk_cos"] = _tile2(cosX / scX).astype(bf16)
    C["cst_xk_sin"] = _tile2(sinX / scX * sgA[None, :]).astype(bf16)
    return C



class _Op:
    __slots__ = ("eng", "emit", "reads", "writes", "dma", "idx", "deps",
                 "marked", "inc_total", "tag", "pe_base", "dma_idx")

    def __init__(self, eng, emit, reads, writes, dma, idx, tag, pe_base=0):
        self.eng, self.emit, self.dma, self.idx = eng, emit, dma, idx
        self.reads, self.writes = list(reads), list(writes)
        self.deps = set()
        self.marked = False
        self.inc_total = None
        self.tag = tag
        self.pe_base = pe_base
        self.dma_idx = None


class Sched:
    def __init__(self, nc, strict=False):
        self.nc = nc
        self.strict = strict
        self.ops = []
        self.last_writer = {}
        self.readers = {}
        self.dma_counts = {}
        self.last_per_stream = {}
        self.frontier = None

    def op(self, eng, emit, reads=(), writes=(), dma=False, tag="", pe_base=0):
        o = _Op(eng, emit, reads, writes, dma, len(self.ops), tag, pe_base)
        if dma:
            i = self.dma_counts.get(eng, 0)
            self.dma_counts[eng] = i + 1
            o.dma_idx = i
            self.last_per_stream[(eng, True, i % self.NSEM_DMA)] = o.idx
        else:
            self.last_per_stream[(eng, False, 0)] = o.idx
        if self.frontier is not None:
            o.deps.update(self.frontier)
        for k in o.reads:
            if k in self.last_writer:
                o.deps.add(self.last_writer[k])
        for k in o.writes:
            if k in self.last_writer:
                o.deps.add(self.last_writer[k])
            for r in self.readers.get(k, ()):
                o.deps.add(r)
        for k in o.reads:
            self.readers.setdefault(k, []).append(o.idx)
        for k in o.writes:
            self.last_writer[k] = o.idx
            self.readers[k] = []
        self.ops.append(o)
        return o.idx

    def dma(self, eng, out, in_, reads=(), writes=(), tag="", **kw):
        return self.op(eng, (lambda e: e.dma_start(out=out, in_=in_, **kw)),
                       reads=reads, writes=writes, dma=True, tag=tag)

    NSEM_DMA = 6

    def barrier(self):
        self.frontier = set(self.last_per_stream.values())

    def finalize(self, final_wait_engine="sync"):
        nc = self.nc
        ops = self.ops
        N = self.NSEM_DMA
        if not self.strict:
            for o in ops:
                o.deps = {d for d in o.deps
                          if not (ops[d].eng == o.eng == "tensor"
                                  and not ops[d].dma and not o.dma)}
        prev_t = None
        for o in ops:
            if o.eng == "tensor" and not o.dma:
                if prev_t is not None and ops[prev_t].pe_base != o.pe_base:
                    o.deps.add(prev_t)
                prev_t = o.idx
        dma_idx = dict(self.dma_counts)
        for o in ops:
            if o.dma:
                o.inc_total = o.dma_idx
        for o in ops:
            best = {}
            for d in o.deps:
                od = ops[d]
                key = (od.eng, od.dma, od.inc_total % N if od.dma else 0)
                if key not in best or d > best[key]:
                    best[key] = d
            o.deps = set(best.values())
        for o in ops:
            for d in o.deps:
                ops[d].marked = True
        lastof = {}
        for o in ops:
            if not o.dma:
                lastof[o.eng] = o
        for o in lastof.values():
            o.marked = True
        counters = {}
        for o in ops:
            if o.dma:
                continue
            if o.marked:
                key = (o.eng, False)
                counters[key] = counters.get(key, 0) + 1
                o.inc_total = counters[key]

        sems = {}
        cms = []

        def get_sem(key, name):
            if key not in sems:
                cm = nc.semaphore(name)
                sems[key] = cm.__enter__()
                cms.append(cm)
            return sems[key]

        for key in counters:
            get_sem((key[0], False, 0), f"sem_{key[0]}_ex")
        for eng, cnt in dma_idx.items():
            for s in range(min(N, cnt)):
                get_sem((eng, True, s), f"sem_{eng}_dma{s}")

        watermark = {}
        blk_cm = nc.Block()
        block = blk_cm.__enter__()

        def wait_for(eng, eng_name, od):
            if od.dma:
                skey = (od.eng, True, od.inc_total % N)
                val = 16 * (od.inc_total // N + 1)
            else:
                skey = (od.eng, False, 0)
                val = od.inc_total
            w = (eng_name, skey)
            if watermark.get(w, -1) >= val:
                return
            eng.wait_ge(sems[skey], val)
            watermark[w] = val

        def mk_section(eng_name):
            def section(eng):
                for o in ops:
                    if o.eng != eng_name:
                        continue
                    for d in sorted(o.deps):
                        wait_for(eng, eng_name, ops[d])
                    if o.dma:
                        i = o.inc_total
                        if i >= N:
                            skey = (o.eng, True, i % N)
                            val = 16 * (i // N)
                            w = (eng_name, skey)
                            if watermark.get(w, -1) < val:
                                eng.wait_ge(sems[skey], val)
                                watermark[w] = val
                    inst = o.emit(eng)
                    if o.dma:
                        inst.then_inc(sems[(o.eng, True, o.inc_total % N)], 16)
                    elif o.marked:
                        inst.then_inc(sems[(o.eng, False, 0)], 1)
                if eng_name == final_wait_engine:
                    for (ename, isdma, slot), sem in sems.items():
                        if (ename, isdma) == (eng_name, False):
                            continue
                        if isdma:
                            cnt = dma_idx[ename]
                            rounds = (cnt - slot + N - 1) // N
                            eng.wait_ge(sem, 16 * rounds)
                        else:
                            eng.wait_ge(sem, counters[(ename, False)])
            return section

        for name in sorted({o.eng for o in ops}):
            getattr(block, name)(mk_section(name))
        blk_cm.__exit__(None, None, None)
        for cm in cms:
            cm.__exit__(None, None, None)
        self.n_ops = len(ops)


def r_(name, *idx):
    return (name,) + tuple(idx)


def tok(ap, t0, n):
    return ap[:, 1 + t0:1 + t0 + n]



class Bufs:
    pass


def alloc_common(nc, ctx):
    B = Bufs()
    ent = ctx.enter_context
    B.x = [ent(nc.sbuf_tensor(f"x{c}", [128, TG], F32)) for c in range(NCHUNK)]
    B.h = [ent(nc.sbuf_tensor(f"h{c}", [128, TG], BF16)) for c in range(NCHUNK)]
    B.b1bc = ent(nc.sbuf_tensor("b1bc", [128, T], BF16))
    B.b2bc = ent(nc.sbuf_tensor("b2bc", [128, T], BF16))
    B.b3bc = None
    B.b4bc = None
    B.rowf = ent(nc.sbuf_tensor("rowf", [1, 5 * TJ], F32))
    B.rowh = ent(nc.sbuf_tensor("rowh", [1, TJ], BF16))
    B.pb = [ent(nc.psum_tensor(f"pb{i}", [128, 512], F32)) for i in range(8)]
    B.ones_row = ent(nc.sbuf_tensor("ones_row", [1, 128], BF16))
    B.onecol = ent(nc.sbuf_tensor("onecol", [128, 1], BF16))
    B.sq = [ent(nc.sbuf_tensor(f"sq{i}", [128, TJ], BF16)) for i in range(2)]
    B.consts = ent(nc.sbuf_tensor("consts", [128, 16], F32))
    B.const_cols = {}
    return B


def cvec(S, B, val, p0=0, p=128):
    if val not in B.const_cols:
        col = len(B.const_cols)
        assert col < 16
        B.const_cols[val] = col
        S.op("vector", lambda e, col=col, val=val: e.memset(
            B.consts[:, col:col + 1], float(val)),
            writes=[r_("consts", col)], tag="cset")
    col = B.const_cols[val]
    return B.consts[p0:p0 + p, col:col + 1], r_("consts", col)


def emit_init_consts(S, B):
    S.op("vector", lambda e: e.memset(B.ones_row[:], 1.0), writes=[r_("ones_row")])
    S.op("vector", lambda e: e.memset(B.onecol[:], 1.0), writes=[r_("onecol")])
    for c in range(NCHUNK):
        S.op("vector", lambda e, c=c: e.memset(B.x[c][:, 0:1], 0.0),
             writes=[r_("x", c)])
        S.op("vector", lambda e, c=c: e.memset(B.h[c][:, 0:1], 0.0),
             writes=[r_("h", c)])



def emit_ln(S, B, st, src=None, src_key="x", out_h=None, out_key="h",
            gamma_vec=None, beta_vec=None, gb_key=None, l0ffn=False,
            shifted=False):
    src = src or B.x
    out_h = out_h or B.h
    pb_s0, pb_s1, pb_b = B.pb[5], B.pb[6], B.pb[7]

    def skey(c):
        return r_(src_key, c)

    def hkey(c):
        return r_(out_key, c)

    def sslice(c, t0, n):
        if shifted and c >= 4:
            return src[c][:, t0:t0 + n]
        return tok(src[c], t0, n)

    for c in range(NCHUNK):
        S.op("vector",
             lambda e, c=c: e.tensor_copy(tok(out_h[c], 0, T), sslice(c, 0, T)),
             reads=[skey(c)], writes=[hkey(c)], tag="ln_cast")

    for tj in range(NTJ):
        t0 = tj * TJ
        for c in range(NCHUNK):
            sqb = B.sq[c % 2]
            S.op("scalar",
                 lambda e, c=c, t0=t0, sqb=sqb: e.activation(
                     sqb[:], tok(out_h[c], t0, TJ), AF.Square),
                 reads=[hkey(c)], writes=[r_("sq", c % 2)], tag="ln_sq")
            S.op("tensor",
                 lambda e, c=c, t0=t0: e.matmul(
                     pb_s0[0:1, :], B.onecol[:], tok(out_h[c], t0, TJ),
                     start=(c == 0), stop=(c == NCHUNK - 1)),
                 reads=[hkey(c), r_("onecol")], writes=[r_("pb", 5)], tag="ln_mm0")
            S.op("tensor",
                 lambda e, sqb=sqb, c=c: e.matmul(
                     pb_s1[0:1, :], B.onecol[:], sqb[:],
                     start=(c == 0), stop=(c == NCHUNK - 1)),
                 reads=[r_("sq", c % 2), r_("onecol")], writes=[r_("pb", 6)],
                 tag="ln_mm1")
        def slot(i):
            return B.rowf[:, i * TJ:(i + 1) * TJ]

        mu, b1r, b2r, var_s = (slot(i) for i in range(4))
        ex = tmp = slot(4)
        K_mu, K_b1, K_b2, K_var = (r_("rowf", i) for i in range(4))
        K_ex = K_tmp = r_("rowf", 4)
        eps_ap, eps_k = cvec(S, B, EPS, 0, 1)
        mh_ap, mh_k = cvec(S, B, -0.5, 0, 1)
        S.op("vector", lambda e, mu=mu: e.tensor_scalar(
            mu, pb_s0[0:1, :], 1.0 / D, None, ALU.mult),
            reads=[r_("pb", 5)], writes=[K_mu], tag="ln_mu")
        S.op("vector", lambda e: e.tensor_scalar(
            pb_s1[0:1, :], pb_s1[0:1, :], 1.0 / D, None, ALU.mult),
            reads=[r_("pb", 6)], writes=[r_("pb", 6)], tag="ln_ms")
        S.op("vector", lambda e, mu=mu, tmp=tmp: e.tensor_mul(tmp, mu, mu),
             reads=[K_mu], writes=[K_tmp], tag="ln_mu2")
        S.op("vector", lambda e, tmp=tmp: e.tensor_sub(
            pb_s1[0:1, :], pb_s1[0:1, :], tmp),
            reads=[K_tmp, r_("pb", 6)], writes=[r_("pb", 6)], tag="ln_var")
        if l0ffn:
            S.op("vector", lambda e, var_s=var_s: e.tensor_copy(
                var_s, pb_s1[0:1, :]),
                reads=[r_("pb", 6)], writes=[K_var], tag="lf_var")
        S.op("scalar", lambda e, a=eps_ap: e.activation(
            pb_s1[0:1, :], pb_s1[0:1, :], AF.Ln, bias=a),
            reads=[r_("pb", 6), eps_k], writes=[r_("pb", 6)], tag="ln_ln")
        S.op("scalar", lambda e, b1r=b1r, a=mh_ap: e.activation(
            b1r, pb_s1[0:1, :], AF.Exp, scale=a),
            reads=[r_("pb", 6), mh_k], writes=[K_b1], tag="ln_exp")
        S.op("vector", lambda e, mu=mu, b1r=b1r, b2r=b2r: e.scalar_tensor_tensor(
            b2r, mu, -1.0, b1r, ALU.mult, ALU.mult),
            reads=[K_mu, K_b1], writes=[K_b2], tag="ln_b2")
        if l0ffn:
            S.op("vector", lambda e, var_s=var_s, b1r=b1r, ex=ex: e.tensor_mul(
                ex, var_s, b1r), reads=[K_var, K_b1], writes=[K_ex], tag="lf_t")
            S.op("vector", lambda e, ex=ex, b1r=b1r: e.tensor_mul(ex, ex, b1r),
                 reads=[K_ex, K_b1], writes=[K_ex], tag="lf_t2")
            S.op("scalar", lambda e, ex=ex, a=eps_ap: e.activation(
                ex, ex, AF.Ln, bias=a),
                reads=[K_ex, eps_k], writes=[K_ex], tag="lf_ln")
            S.op("scalar", lambda e, ex=ex, a=mh_ap: e.activation(
                ex, ex, AF.Exp, scale=a),
                reads=[K_ex, mh_k], writes=[K_ex], tag="lf_exp")
            _bc_row(S, B, b1r, K_b1, B.b3bc, r_("b3bc"), t0)
            _bc_row(S, B, b2r, K_b2, B.b4bc, r_("b4bc"), t0)
            S.op("vector", lambda e, b1r=b1r, ex=ex: e.tensor_mul(b1r, b1r, ex),
                 reads=[K_ex, K_b1], writes=[K_b1], tag="lf_f1")
            S.op("vector", lambda e, b2r=b2r, ex=ex: e.tensor_mul(b2r, b2r, ex),
                 reads=[K_ex, K_b2], writes=[K_b2], tag="lf_f2")
        _bc_row(S, B, b1r, K_b1, B.b1bc, r_("b1bc"), t0)
        _bc_row(S, B, b2r, K_b2, B.b2bc, r_("b2bc"), t0)

    for c in range(NCHUNK):
        g = gamma_vec[c] if gamma_vec is not None else None
        bta = beta_vec[c] if beta_vec is not None else None
        if g is None:
            S.op("vector", lambda e, c=c: e.tensor_mul(
                tok(out_h[c], 0, T), tok(out_h[c], 0, T), B.b1bc[:]),
                reads=[hkey(c), r_("b1bc")], writes=[hkey(c)], tag="ln_ap1")
            S.op("vector", lambda e, c=c: e.tensor_add(
                tok(out_h[c], 0, T), tok(out_h[c], 0, T), B.b2bc[:]),
                reads=[hkey(c), r_("b2bc")], writes=[hkey(c)], tag="ln_ap2")
        else:
            S.op("vector", lambda e, c=c, g=g: e.scalar_tensor_tensor(
                tok(out_h[c], 0, T), tok(out_h[c], 0, T), g, B.b1bc[:],
                ALU.mult, ALU.mult),
                reads=[hkey(c), r_("b1bc"), gb_key], writes=[hkey(c)],
                tag="ln_ap1g")
            S.op("vector", lambda e, c=c, g=g: e.scalar_tensor_tensor(
                tok(out_h[c], 0, T), B.b2bc[:], g, tok(out_h[c], 0, T),
                ALU.mult, ALU.add),
                reads=[hkey(c), r_("b2bc"), gb_key], writes=[hkey(c)],
                tag="ln_ap2g")
            if bta is not None:
                S.op("vector", lambda e, c=c, bta=bta: e.tensor_scalar(
                    tok(out_h[c], 0, T), tok(out_h[c], 0, T), bta, None, ALU.add),
                    reads=[hkey(c), gb_key], writes=[hkey(c)], tag="ln_ap3g")


def _bc_row(S, B, row_ap, row_key, dst, dst_key, t0):
    pb_b = B.pb[7]
    rh = B.rowh[:]
    K_rh = r_("rowh")
    S.op("vector", lambda e, rh=rh, row_ap=row_ap: e.tensor_copy(rh, row_ap),
         reads=[row_key], writes=[K_rh], tag="bc_cast")
    S.op("tensor", lambda e, rh=rh: e.matmul(
        pb_b[:, 0:TJ], B.ones_row[:], rh, start=True, stop=True),
        reads=[K_rh, r_("ones_row")], writes=[r_("pb", 7)], tag="bc_mm")
    S.op("scalar", lambda e, dst=dst, t0=t0: e.copy(dst[:, t0:t0 + TJ], pb_b[:, 0:TJ]),
         reads=[r_("pb", 7)], writes=[dst_key], tag="bc_evac")



def _l0a_ranges():
    out = []
    for r in (-1, 0, 1, 2, 3, 4):
        qlo = max(0, 128 * r - 12)
        qhi = min(512, 128 * r + 140)
        out.append((r, qlo, qhi))
    return out


def _l0a_masks():
    INF = np.float32(np.inf)
    packed = []
    offs = {}
    pos = 0
    for r, qlo, qhi in _l0a_ranges():
        w = qhi - qlo
        jj = np.arange(128)[:, None]
        qq = np.arange(qlo, qhi)[None, :]
        allowed = np.abs(-128 * r + qq - jj) <= 12
        packed.append(np.where(allowed, INF, 0.0).astype(bf16))
        offs[r] = (pos, w)
        pos += w
    band = np.concatenate(packed, axis=1)
    jj = np.arange(128)[:, None]
    qq = np.arange(512)[None, :]
    allowed0 = (np.abs(qq - jj) <= 12) | (jj < 8) | (qq < 8)
    m0 = np.where(allowed0, INF, 0.0).astype(bf16)
    return band, offs, m0


def alloc_l0a(nc, ctx, sfx=""):
    A = Bufs()
    ent = ctx.enter_context
    A.kh = [ent(nc.sbuf_tensor(f"l0a_kh{i}{sfx}", [128, T], BF16)) for i in range(4)]
    A.qh = [ent(nc.sbuf_tensor(f"l0a_qh0{sfx}", [128, T], BF16))]
    A.vt = [ent(nc.sbuf_tensor(f"l0a_vt{t}{sfx}", [128, 260], BF16)) for t in range(16)]
    A.ao = [ent(nc.sbuf_tensor(f"l0a_ao{c}{sfx}", [128, T], BF16)) for c in range(NCHUNK)]
    A.tmp = [ent(nc.sbuf_tensor(f"l0a_tmp{i}{sfx}", [128, TJ], BF16)) for i in range(2)]
    A.pt = [ent(nc.sbuf_tensor(f"l0a_pt{i}{sfx}", [128, TJ], BF16)) for i in range(3)]
    A.recbc = ent(nc.sbuf_tensor(f"l0a_recbc{sfx}", [64, TJ], BF16))
    A.wa = [ent(nc.sbuf_tensor(f"l0a_wa{c}{sfx}", [128, 128], BF16)) for c in range(NCHUNK)]
    A.wb = [ent(nc.sbuf_tensor(f"l0a_wb{c}{sfx}", [128, 128], BF16)) for c in range(NCHUNK)]
    A.wv = [ent(nc.sbuf_tensor(f"l0a_wv{c}{sfx}", [128, 256], BF16)) for c in range(NCHUNK)]
    A.bias = ent(nc.sbuf_tensor(f"l0a_bias{sfx}", [128, 8], F32))
    A.biasp = ent(nc.sbuf_tensor(f"l0a_biasp{sfx}", [128, 8], F32))
    A.bk = ent(nc.sbuf_tensor(f"l0a_bk{sfx}", [64, 4], F32))
    A.bkp = ent(nc.sbuf_tensor(f"l0a_bkp{sfx}", [64, 4], F32))
    A.bo = ent(nc.sbuf_tensor(f"l0a_bo{sfx}", [128, 8], F32))
    A.bvrow = ent(nc.sbuf_tensor(f"l0a_bvrow{sfx}", [1, 256], BF16))
    A.hw8 = ent(nc.sbuf_tensor(f"l0a_hw8{sfx}", [128, 8], F32))
    A.cos = ent(nc.sbuf_tensor(f"l0a_cos{sfx}", [128, T], BF16))
    A.sin = ent(nc.sbuf_tensor(f"l0a_sin{sfx}", [128, T], BF16))
    bw = sum(qhi - qlo for _, qlo, qhi in _l0a_ranges())
    A.band = ent(nc.sbuf_tensor(f"l0a_band{sfx}", [128, bw], BF16))
    A.m0 = ent(nc.sbuf_tensor(f"l0a_m0{sfx}", [128, 512], BF16))
    return A


def emit_l0a_setup(S, nc, B, A, W):
    S.dma("sync", A.cos[:], W["cst_cosA"], writes=[r_("l0a_cos")])
    S.dma("sync", A.sin[:], W["cst_sinA"], writes=[r_("l0a_sin")])
    S.dma("sync", A.band[:], W["cst_l0a_band"], writes=[r_("l0a_band")])
    S.dma("sync", A.m0[:], W["cst_l0a_m0"], writes=[r_("l0a_m0")])
    S.dma("sync", A.bias[:], W["p_l0a_bq8"], writes=[r_("l0a_bias")])
    S.dma("sync", A.biasp[:], W["p_l0a_bqp8"], writes=[r_("l0a_biasp")])
    S.dma("sync", A.bk[:], W["p_l0a_bk4"], writes=[r_("l0a_bk")])
    S.dma("sync", A.bkp[:], W["p_l0a_bkp4"], writes=[r_("l0a_bkp")])
    S.dma("sync", A.bo[:], W["p_l0a_bo8"], writes=[r_("l0a_bo")])
    S.dma("gpsimd", A.bvrow[:], W["p_l0a_bvrow"], writes=[r_("l0a_bvrow")])
    S.dma("sync", A.hw8[:], W["p_l0a_hw8"], writes=[r_("l0a_hw8")])
    S.op("vector", lambda e: e.tensor_scalar(
        A.hw8[:], A.hw8[:], 1.0 / 8.0, None, ALU.mult),
        reads=[r_("l0a_hw8")], writes=[r_("l0a_hw8")], tag="hw8")


def _load_w_tiles(S, tiles, physkey, dram, n0, w, nk=NCHUNK, queue="gpsimd",
                  coloff=0):
    for c in range(nk):
        S.dma(queue, tiles[c][:, coloff:coloff + w],
              dram[c * 128:(c + 1) * 128, n0:n0 + w],
              writes=[r_(physkey, c)], tag="wload")


def emit_dual_rope_proj(S, B, A, wA, wAk, wB, wBk, bA, bB, kbA, kbB,
                        cos, sin, kcos, ksin, psA, psB, out_tile, out_key,
                        hw_ap=None, hwkey=None, n_k=NCHUNK, hsrc=None):
    hsrc = hsrc or B.h
    for tj in range(NTJ):
        t0 = tj * TJ
        for c in range(n_k):
            S.op("tensor", lambda e, c=c, t0=t0: e.matmul(
                B.pb[psA][:], wA[c][:, 0:128], tok(hsrc[c], t0, TJ),
                start=(c == 0), stop=(c == n_k - 1)),
                reads=[r_(wAk, c), r_("h", c)], writes=[r_("pb", psA)],
                tag="qp_a")
            S.op("tensor", lambda e, c=c, t0=t0: e.matmul(
                B.pb[psB][:], wB[c][:, 0:128], tok(hsrc[c], t0, TJ),
                start=(c == 0), stop=(c == n_k - 1)),
                reads=[r_(wBk, c), r_("h", c)], writes=[r_("pb", psB)],
                tag="qp_b")
        t1, t2 = A.tmp[0], A.tmp[1]
        S.op("vector", lambda e, t0=t0, t1=t1: e.scalar_tensor_tensor(
            t1[:], B.pb[psA][:], bA, cos[:, t0:t0 + TJ], ALU.add, ALU.mult),
            reads=[r_("pb", psA), kbA, kcos], writes=[r_("l0a_t", 0)], tag="rope1")
        S.op("vector", lambda e, t0=t0, t2=t2: e.scalar_tensor_tensor(
            t2[:], B.pb[psB][:], bB, sin[:, t0:t0 + TJ], ALU.add, ALU.mult),
            reads=[r_("pb", psB), kbB, ksin], writes=[r_("l0a_t", 1)], tag="rope2")
        if hw_ap is None:
            S.op("vector", lambda e, t0=t0, t1=t1, t2=t2: e.tensor_add(
                out_tile[:, t0:t0 + TJ], t1[:], t2[:]),
                reads=[r_("l0a_t", 0), r_("l0a_t", 1)], writes=[out_key],
                tag="rope3")
        else:
            S.op("vector", lambda e, t1=t1, t2=t2: e.tensor_add(
                t1[:], t1[:], t2[:]),
                reads=[r_("l0a_t", 0), r_("l0a_t", 1)], writes=[r_("l0a_t", 0)],
                tag="rope3")
            S.op("vector", lambda e, t0=t0, t1=t1: e.tensor_scalar(
                out_tile[:, t0:t0 + TJ], t1[:], hw_ap, None, ALU.mult),
                reads=[r_("l0a_t", 0), hwkey], writes=[out_key], tag="rope4")


def emit_l0_attn_a(S, nc, B, A, W):
    band_offs = {r: None for r, _, _ in _l0a_ranges()}
    pos = 0
    for r, qlo, qhi in _l0a_ranges():
        band_offs[r] = (pos, qlo, qhi)
        pos += qhi - qlo
    negC, negC_k = cvec(S, B, -C_L0A)

    _load_w_tiles(S, A.wv, "l0a_wv", W["L0_attn_a_Wv"], 0, 256)
    for tt in range(16):
        vt = A.vt[tt]
        S.op("vector", lambda e, vt=vt: e.memset(
            vt.ap().rearrange("p (a b) -> p a b", b=65)[:, :, 64:65], 1.0),
            writes=[r_("l0a_vt", tt)], tag="vones")
        for c in range(NCHUNK):
            S.op("tensor", lambda e, c=c, tt=tt: e.matmul(
                B.pb[6][:, 0:256], tok(B.h[c], tt * 128, 128), A.wv[c][:, 0:256],
                start=(c == 0), stop=False),
                reads=[r_("l0a_wv", c), r_("h", c)], writes=[r_("pb", 6)],
                tag="vproj")
        S.op("tensor", lambda e: e.matmul(
            B.pb[6][:, 0:256], B.ones_row[:], A.bvrow[:], start=False, stop=True),
            reads=[r_("l0a_bvrow"), r_("ones_row")], writes=[r_("pb", 6)],
            tag="vbias")
        S.op("scalar", lambda e, vt=vt: e.copy(
            vt.ap().rearrange("p (a b) -> p a b", b=65)[:, :, 0:64],
            B.pb[6][:, 0:256].rearrange("p (a b) -> p a b", b=64)),
            reads=[r_("pb", 6)], writes=[r_("l0a_vt", tt)], tag="vevac")

    for kv in range(4):
        _load_w_tiles(S, A.wa, "l0a_wa", W["L0_attn_a_Wk"], kv * 64, 64)
        _load_w_tiles(S, A.wb, "l0a_wb", W["p_l0a_wkp"], kv * 64, 64)
        kt_tile = A.kh[kv]
        kkey = r_("l0a_kh", kv)
        for tj in range(NTJ):
            t0 = tj * TJ
            for c in range(NCHUNK):
                S.op("tensor", lambda e, c=c, t0=t0: e.matmul(
                    B.pb[0][0:64, :], A.wa[c][:, 0:64], tok(B.h[c], t0, TJ),
                    start=(c == 0), stop=(c == NCHUNK - 1)),
                    reads=[r_("l0a_wa", c), r_("h", c)], writes=[r_("pb", 0)],
                    tag="kp_a")
                S.op("tensor", lambda e, c=c, t0=t0: e.matmul(
                    B.pb[1][0:64, :], A.wb[c][:, 0:64], tok(B.h[c], t0, TJ),
                    start=(c == 0), stop=(c == NCHUNK - 1)),
                    reads=[r_("l0a_wb", c), r_("h", c)], writes=[r_("pb", 1)],
                    tag="kp_b")
            t1, t2 = A.tmp[0], A.tmp[1]
            S.op("vector", lambda e, t0=t0, kv=kv: e.scalar_tensor_tensor(
                t1[0:64, :], B.pb[0][0:64, :], A.bk[0:64, kv:kv + 1],
                A.cos[0:64, t0:t0 + TJ], ALU.add, ALU.mult),
                reads=[r_("pb", 0), r_("l0a_bk"), r_("l0a_cos")],
                writes=[r_("l0a_t", 0)], tag="krope1")
            S.op("vector", lambda e, t0=t0, kv=kv: e.scalar_tensor_tensor(
                t2[0:64, :], B.pb[1][0:64, :], A.bkp[0:64, kv:kv + 1],
                A.sin[0:64, t0:t0 + TJ], ALU.add, ALU.mult),
                reads=[r_("pb", 1), r_("l0a_bkp"), r_("l0a_sin")],
                writes=[r_("l0a_t", 1)], tag="krope2")
            S.op("vector", lambda e, t0=t0, kt_tile=kt_tile: e.tensor_add(
                kt_tile[0:64, t0:t0 + TJ], t1[0:64, :], t2[0:64, :]),
                reads=[r_("l0a_t", 0), r_("l0a_t", 1)], writes=[kkey],
                tag="krope3")
        S.dma("gpsimd", kt_tile[64:128, :], kt_tile[0:64, :],
              reads=[kkey], writes=[kkey], tag="kdup")

    for c in range(NCHUNK):
        qb = A.qh[0]
        qbk = r_("l0a_qh", 0)
        _load_w_tiles(S, A.wa, "l0a_wa", W["L0_attn_a_Wq"], c * 128, 128)
        _load_w_tiles(S, A.wb, "l0a_wb", W["p_l0a_wqp"], c * 128, 128)
        emit_dual_rope_proj(
            S, B, A, A.wa, "l0a_wa", A.wb, "l0a_wb",
            A.bias[:, c:c + 1], A.biasp[:, c:c + 1], r_("l0a_bias"), r_("l0a_biasp"),
            A.cos, A.sin, r_("l0a_cos"), r_("l0a_sin"),
            0, 1, qb, qbk,
            hw_ap=A.hw8[:, c:c + 1], hwkey=r_("l0a_hw8"))
        for hh in range(2):
            h = 2 * c + hh
            kv = h // 4
            krow = 64 * hh
            klhs = A.kh[kv]
            kkey = r_("l0a_kh", kv)
            qrows = qb[64 * hh:64 * hh + 64, :]
            for a in range(4):
                q0 = a * TJ
                emit_l0a_head_chunk(S, B, A, h, a, q0, qrows, qbk, klhs, kkey,
                                    krow, kv, band_offs, negC, negC_k, c, hh)

    for oc in range(NCHUNK):
        _load_w_tiles(S, A.wa, "l0a_wa", W["L0_attn_a_Wo"], oc * 128, 128)
        for tj in range(NTJ):
            t0 = tj * TJ
            for c in range(NCHUNK):
                S.op("tensor", lambda e, c=c, t0=t0: e.matmul(
                    B.pb[4][:], A.wa[c][:, 0:128], A.ao[c][:, t0:t0 + TJ],
                    start=(c == 0), stop=(c == NCHUNK - 1)),
                    reads=[r_("l0a_wa", c), r_("l0a_ao", c)],
                    writes=[r_("pb", 4)], tag="wo_mm")
            S.op("vector", lambda e, oc=oc, t0=t0: e.scalar_tensor_tensor(
                tok(B.x[oc], t0, TJ), B.pb[4][:], A.bo[:, oc:oc + 1],
                tok(B.x[oc], t0, TJ), ALU.add, ALU.add),
                reads=[r_("pb", 4), r_("l0a_bo"), r_("x", oc)],
                writes=[r_("x", oc)], tag="wo_res")


def emit_l0a_head_chunk(S, B, A, h, a, q0, qrows, qbk, klhs, kkey, krow, kv,
                        band_offs, negC, negC_k, c, hh):
    ps_S, ps_O, ps_G, ps_B = 2, 3, 5, 7
    ktiles = []
    if a == 0:
        for kt in range(0, 5):
            ktiles.append((kt, kt))
    else:
        for r in (-1, 0, 1, 2, 3, 4):
            kt = 4 * a + r
            if 0 <= kt < 16:
                ktiles.append((kt, r))
    n_pv = len(ktiles) + (11 if a == 0 else 1)
    pv_i = [0]

    def pv(lhsT_ap, lhs_key, rhs_ap, rhs_key, qlo, w):
        st = pv_i[0] == 0
        sp = pv_i[0] == n_pv - 1
        pv_i[0] += 1
        S.op("tensor", lambda e: e.matmul(
            B.pb[ps_O][0:65, qlo:qlo + w],
            lhsT_ap, rhs_ap, start=st, stop=sp),
            reads=[lhs_key, rhs_key], writes=[r_("pb", ps_O)], tag="pv")

    if a != 0:
        pt = A.pt[2]
        ptk = r_("l0a_pt", 2)
        S.op("tensor", lambda e: e.matmul(
            B.pb[ps_G][0:8, 0:TJ], klhs[krow:krow + 64, 0:8],
            qrows[:, q0:q0 + TJ], start=True, stop=True),
            reads=[kkey, qbk], writes=[r_("pb", ps_G)], tag="sk_mm",
            pe_base=krow)
        S.op("scalar", lambda e, pt=pt: e.activation(
            pt[0:8, 0:TJ], B.pb[ps_G][0:8, 0:TJ], AF.Exp, bias=negC[0:8, :]),
            reads=[r_("pb", ps_G), negC_k], writes=[ptk], tag="sk_exp")
        pv(A.vt[0][0:8, 65 * kv:65 * kv + 65], r_("l0a_vt", 0),
           pt[0:8, 0:TJ], ptk, 0, TJ)

    for i, (kt, r) in enumerate(ktiles):
        if a == 0:
            qlo, qhi = (0, 512) if kt == 0 else band_offs[r][1:]
        else:
            qlo, qhi = band_offs[r][1:]
        w = qhi - qlo
        pt = A.pt[i % 2]
        ptk = r_("l0a_pt", i % 2)
        S.op("tensor", lambda e, kt=kt, qlo=qlo, w=w, pt=pt: e.matmul(
            B.pb[ps_S][:, 0:w], klhs[krow:krow + 64, kt * 128:(kt + 1) * 128],
            qrows[:, q0 + qlo:q0 + qlo + w], start=True, stop=True),
            reads=[kkey, qbk], writes=[r_("pb", ps_S)], tag="s_mm",
            pe_base=krow)
        S.op("scalar", lambda e, w=w, pt=pt: e.activation(
            pt[:, 0:w], B.pb[ps_S][:, 0:w], AF.Exp, bias=negC),
            reads=[r_("pb", ps_S), negC_k], writes=[ptk], tag="s_exp")
        if a == 0 and kt == 0:
            S.op("vector", lambda e, w=w, pt=pt: e.tensor_tensor(
                pt[:, 0:w], pt[:, 0:w], A.m0[:, 0:w], ALU.min),
                reads=[ptk, r_("l0a_m0")], writes=[ptk], tag="s_mask")
        else:
            moff = band_offs[r][0]
            S.op("vector", lambda e, w=w, moff=moff, pt=pt: e.tensor_tensor(
                pt[:, 0:w], pt[:, 0:w], A.band[:, moff:moff + w], ALU.min),
                reads=[ptk, r_("l0a_band")], writes=[ptk], tag="s_mask")
        pv(A.vt[kt][:, 65 * kv:65 * kv + 65], r_("l0a_vt", kt),
           pt[:, 0:w], ptk, qlo, w)

    if a == 0:
        for kt in range(5, 16):
            pt = A.pt[2]
            ptk = r_("l0a_pt", 2)
            S.op("tensor", lambda e, kt=kt: e.matmul(
                B.pb[ps_G][:, 0:8], klhs[krow:krow + 64, kt * 128:(kt + 1) * 128],
                qrows[:, q0:q0 + 8], start=True, stop=True),
                reads=[kkey, qbk], writes=[r_("pb", ps_G)], tag="sg_mm",
                pe_base=krow)
            S.op("scalar", lambda e: e.activation(
                pt[:, 0:8], B.pb[ps_G][:, 0:8], AF.Exp, bias=negC),
                reads=[r_("pb", ps_G), negC_k], writes=[ptk], tag="sg_exp")
            pv(A.vt[kt][:, 65 * kv:65 * kv + 65], r_("l0a_vt", kt),
               pt[:, 0:8], ptk, 0, 8)

    rec = B.rowf[:, 3 * TJ:4 * TJ]
    rec_k = r_("rowf", 3)
    S.op("vector", lambda e, rec=rec: e.reciprocal(rec, B.pb[ps_O][64:65, 0:TJ]),
         reads=[r_("pb", ps_O)], writes=[rec_k], tag="n_rec")
    S.op("vector", lambda e, rec=rec: e.tensor_copy(B.rowh[:], rec),
         reads=[rec_k], writes=[r_("rowh")], tag="n_cast")
    S.op("tensor", lambda e: e.matmul(
        B.pb[ps_B][0:64, 0:TJ], B.ones_row[:, 0:64], B.rowh[:],
        start=True, stop=True),
        reads=[r_("rowh"), r_("ones_row")], writes=[r_("pb", ps_B)],
        tag="n_bcmm")
    S.op("scalar", lambda e: e.copy(A.recbc[:], B.pb[ps_B][0:64, 0:TJ]),
         reads=[r_("pb", ps_B)], writes=[r_("l0a_recbc")], tag="n_bcevac")
    S.op("vector", lambda e, c=c, hh=hh: e.tensor_tensor(
        A.ao[c][64 * hh:64 * hh + 64, q0:q0 + TJ], B.pb[ps_O][0:64, 0:TJ],
        A.recbc[:], ALU.mult),
        reads=[r_("pb", ps_O), r_("l0a_recbc")], writes=[r_("l0a_ao", c)],
        tag="n_mul")



def _declare_x_io(nc):
    x_in = nc.dram_tensor("x_io_in", [NSEQ, NCHUNK, 128, T], F32,
                          kind="ExternalInput").ap()
    x_out = nc.dram_tensor("x_io_out", [NSEQ, NCHUNK, 128, T], F32,
                           kind="ExternalOutput").ap()
    return x_in, x_out


def emit_load_x(S, B, x_in, seq):
    for c in range(NCHUNK):
        S.dma("sync", tok(B.x[c], 0, T), x_in[seq, c], writes=[r_("x", c)],
              tag="ldx")


def emit_store_x(S, B, x_out, seq):
    for c in range(NCHUNK):
        S.dma("sync", x_out[seq, c], tok(B.x[c], 0, T), reads=[r_("x", c)],
              tag="stx")


def _np_dt(a):
    if a.dtype == np.float32:
        return F32
    if a.dtype == np.int32:
        return I32
    if a.dtype == bf16:
        return BF16
    raise ValueError(a.dtype)


def declare_inputs(nc, arrays, names):
    W = {}
    for n in names:
        a = arrays[n]
        W[n] = nc.dram_tensor(n, list(a.shape), _np_dt(a), kind="ExternalInput").ap()
    return W


L0A_NAMES = ["L0_attn_a_Wq", "L0_attn_a_Wk", "L0_attn_a_Wv", "L0_attn_a_Wo",
             "p_l0a_bvrow", "p_l0a_wqp", "p_l0a_wkp", "p_l0a_bq8",
             "p_l0a_bqp8", "p_l0a_bk4", "p_l0a_bkp4", "p_l0a_bo8", "p_l0a_hw8",
             "cst_cosA", "cst_sinA", "cst_l0a_band", "cst_l0a_m0"]


def host_prep(inputs):
    P = dict(host_constants())
    permA1024 = np.array([_permA_idx(n) for n in range(1024)])
    permA256 = permA1024[:256]
    ii = inputs
    P["p_l0a_wqp"] = np.ascontiguousarray(ii["L0_attn_a_Wq"][:, permA1024])
    P["p_l0a_wkp"] = np.ascontiguousarray(ii["L0_attn_a_Wk"][:, permA256])
    P["p_l0a_bq8"] = np.ascontiguousarray(ii["L0_attn_a_bq"].reshape(8, 128).T)
    P["p_l0a_bqp8"] = np.ascontiguousarray(
        ii["L0_attn_a_bq"][permA1024].reshape(8, 128).T)
    P["p_l0a_bk4"] = np.ascontiguousarray(ii["L0_attn_a_bk"].reshape(4, 64).T)
    P["p_l0a_bkp4"] = np.ascontiguousarray(
        ii["L0_attn_a_bk"][permA256].reshape(4, 64).T)
    P["p_l0a_bo8"] = np.ascontiguousarray(ii["L0_attn_a_bo"].reshape(8, 128).T)
    P["p_l0a_bvrow"] = np.ascontiguousarray(ii["L0_attn_a_bv"].reshape(1, 256))
    hw = ii["L0_attn_a_head_weights"]
    P["p_l0a_hw8"] = np.ascontiguousarray(
        np.repeat(hw.reshape(8, 2), 64, axis=1).reshape(8, 128).T)
    band, offs, m0 = _l0a_masks()
    P["cst_l0a_band"] = band
    P["cst_l0a_m0"] = m0
    rot64 = np.array([((d + 32) % 64) for d in range(64)])
    qrot_cols = np.concatenate([256 * h + rot64 for h in range(8)])
    krot_cols = np.concatenate([256 * kv + rot64 for kv in range(2)])
    P["p_l0b_wqrot"] = np.ascontiguousarray(ii["L0_attn_b_Wq"][:, qrot_cols])
    P["p_l0b_wkrot"] = np.ascontiguousarray(ii["L0_attn_b_Wk"][:, krot_cols])
    P["p_l0b_wox"] = np.ascontiguousarray(
        ii["L0_attn_b_Wo"].reshape(16, 128, 8, 128).transpose(2, 1, 0, 3))
    permI1024 = np.arange(1024) ^ 1
    wqT = np.ascontiguousarray(ii["L1_attn_a_Wq"].T)
    wkT = np.ascontiguousarray(ii["L1_attn_a_Wk"].T)
    P["p_l1a_wqT"] = wqT
    P["p_l1a_wqTp"] = np.ascontiguousarray(wqT[:, permI1024])
    P["p_l1a_wkT"] = wkT
    P["p_l1a_wkTp"] = np.ascontiguousarray(wkT[:, permI1024])
    P["p_l1a_wvT"] = np.ascontiguousarray(ii["L1_attn_a_Wv"].T)
    P["p_l1a_wox"] = np.ascontiguousarray(
        ii["L1_attn_a_Wo"].T.reshape(8, 128, 8, 128).transpose(2, 1, 0, 3))
    INF = np.float32(np.inf)
    jj = np.arange(128)[:, None]
    qq = np.arange(128)[None, :]
    P["cst_l1_tri"] = np.where(jj <= qq, INF, 0.0).astype(bf16)
    tri = np.where(jj <= qq, INF, 0.0).astype(bf16)
    low = np.where(jj > qq, INF, 0.0).astype(bf16)
    P["cst_l1b_tri4"] = np.ascontiguousarray(np.tile(tri, (1, 4)))
    P["cst_l1b_low4"] = np.ascontiguousarray(np.tile(low, (1, 4)))
    permA1024b = np.array([_permA_idx(n) for n in range(1024)])
    wqTb = np.ascontiguousarray(ii["L1_attn_b_Wq"].T)
    wkTb = np.ascontiguousarray(ii["L1_attn_b_Wk"].T)
    P["p_l1b_wqT"] = wqTb
    P["p_l1b_wqTp"] = np.ascontiguousarray(wqTb[:, permA1024b])
    P["p_l1b_wkT"] = wkTb
    P["p_l1b_wkTp"] = np.ascontiguousarray(wkTb[:, permA1024b])
    P["p_l1b_wvT"] = np.ascontiguousarray(ii["L1_attn_b_Wv"].T)
    P["p_l1b_wox"] = np.ascontiguousarray(
        ii["L1_attn_b_Wo"].T.reshape(8, 128, 8, 128).transpose(2, 1, 0, 3))
    P["p_l1b_bo8"] = np.ascontiguousarray(ii["L1_attn_b_bo"].reshape(8, 128).T)
    rot64b = np.array([(d + 32) % 64 for d in range(64)])
    permqk = np.concatenate([64 * hh + rot64b for hh in range(16)])
    P["p_l1f_wqkvp"] = np.ascontiguousarray(ii["L1_ffn_wqkv"][:, :, permqk])
    P["p_l1f_wox"] = np.ascontiguousarray(
        ii["L1_ffn_wo"].reshape(2, 4, 128, 8, 128).transpose(0, 3, 2, 1, 4))
    qsv = np.zeros((128, 8), np.float32)
    for l in range(2):
        qsv[:, 4 * l + 0] = np.tile(ii["L1_ffn_q_scale"][l], 2)
        qsv[:, 4 * l + 1] = np.tile(ii["L1_ffn_q_scale"][l][rot64b], 2)
        qsv[:, 4 * l + 2] = np.tile(ii["L1_ffn_k_scale"][l], 2)
        qsv[:, 4 * l + 3] = np.tile(ii["L1_ffn_k_scale"][l][rot64b], 2)
    P["p_l1f_qsv"] = qsv
    P["p_l1f_lg"] = np.ascontiguousarray(
        ii["L1_ffn_ln_g"].reshape(2, 8, 128).transpose(0, 2, 1))
    P["p_l1f_lb"] = np.ascontiguousarray(
        ii["L1_ffn_ln_b"].reshape(2, 8, 128).transpose(0, 2, 1))
    b1p = np.zeros((2, 128, 22), np.float32)
    b1gp = np.zeros((2, 128, 22), np.float32)
    for l in range(2):
        a = np.zeros(22 * 128, np.float32)
        a[:2730] = ii["L1_ffn_b1"][l][:2730]
        b1p[l] = a.reshape(22, 128).T
        gpart = np.zeros(22 * 128, np.float32)
        gpart[:2730] = ii["L1_ffn_b1"][l][2730:]
        b1gp[l] = gpart.reshape(22, 128).T
    P["p_l1f_b1a"] = b1p
    P["p_l1f_b1g"] = b1gp
    P["p_l1f_b2"] = np.ascontiguousarray(
        ii["L1_ffn_b2"].reshape(2, 8, 128).transpose(0, 2, 1))
    P["cst_geq"] = np.where(jj >= qq, INF, 0.0).astype(bf16)
    P["p_l0f_g8"] = np.ascontiguousarray(ii["L0_ffn_ln_gamma"].reshape(8, 128).T)
    P["p_l0f_be8"] = np.ascontiguousarray(ii["L0_ffn_ln_beta"].reshape(8, 128).T)
    P["p_l0f_b1"] = np.ascontiguousarray(ii["L0_ffn_fc1_bias"].reshape(32, 128).T)
    P["p_l0f_b2"] = np.ascontiguousarray(ii["L0_ffn_fc2_bias"].reshape(8, 128).T)
    return P


def shard_x(x):
    shards = []
    for ci in range(8):
        xs = x[4 * ci:4 * ci + 4]
        xt = np.ascontiguousarray(xs.transpose(0, 2, 1))
        shards.append(xt.reshape(NSEQ, NCHUNK, 128, T))
    return shards



L0F_NAMES = ["L0_ffn_fc1_q", "L0_ffn_fc1_scales", "L0_ffn_fc2_q",
             "L0_ffn_fc2_scales", "p_l0f_g8", "p_l0f_be8", "p_l0f_b1",
             "p_l0f_b2"]


def alloc_l0f(nc, ctx, sfx=""):
    F = Bufs()
    ent = ctx.enter_context
    B3 = ent(nc.sbuf_tensor(f"b3bc{sfx}", [128, T], BF16))
    B4 = ent(nc.sbuf_tensor(f"b4bc{sfx}", [128, T], BF16))
    F.b3bc, F.b4bc = B3, B4
    F.qi = ent(nc.sbuf_tensor(f"l0f_qi{sfx}", [128, 512], I32))
    F.srow = ent(nc.sbuf_tensor(f"l0f_srow{sfx}", [1, 512], F32))
    F.srowh = ent(nc.sbuf_tensor(f"l0f_srowh{sfx}", [1, 512], BF16))
    F.scbc = ent(nc.sbuf_tensor(f"l0f_scbc{sfx}", [128, 512], F32))
    F.wout = ent(nc.sbuf_tensor(f"l0f_wout{sfx}", [128, 512], BF16))
    F.wacat = ent(nc.sbuf_tensor(f"l0f_wacat{sfx}", [128, 1024], BF16))
    F.w2cat = ent(nc.sbuf_tensor(f"l0f_w2cat{sfx}", [128, 4096], BF16))
    F.z = [ent(nc.sbuf_tensor(f"l0f_z{i}{sfx}", [128, 512], BF16)) for i in range(32)]
    F.u = ent(nc.sbuf_tensor(f"l0f_u{sfx}", [128, 512], F32))
    F.g8 = ent(nc.sbuf_tensor(f"l0f_g8{sfx}", [128, 8], F32))
    F.be8 = ent(nc.sbuf_tensor(f"l0f_be8{sfx}", [128, 8], F32))
    F.b1 = ent(nc.sbuf_tensor(f"l0f_b1{sfx}", [128, 32], F32))
    F.b2h = ent(nc.sbuf_tensor(f"l0f_b2h{sfx}", [128, 8], F32))
    return F


def emit_l0f_setup(S, nc, B, F, W, scratch=True, consts=True):
    if not consts:
        pass
    S.dma("sync", F.g8[:], W["p_l0f_g8"], writes=[r_("l0f_g8")])
    S.dma("sync", F.be8[:], W["p_l0f_be8"], writes=[r_("l0f_be8")])
    S.dma("sync", F.b1[:], W["p_l0f_b1"], writes=[r_("l0f_b1")])
    S.dma("sync", F.b2h[:], W["p_l0f_b2"], writes=[r_("l0f_b2h")])
    S.op("vector", lambda e: e.tensor_scalar(
        F.b2h[:], F.b2h[:], 0.5, None, ALU.mult),
        reads=[r_("l0f_b2h")], writes=[r_("l0f_b2h")], tag="b2half")
    if not scratch:
        F.scr_w1, F.scr_w2 = B.scr_l0f
        return
    scr_w1 = nc.dram_tensor("scr_w1", [32, 128, 8, 128], BF16, kind="Internal").ap()
    scr_w2 = nc.dram_tensor("scr_w2", [8, 128, 32, 128], BF16, kind="Internal").ap()
    F.scr_w1, F.scr_w2 = scr_w1, scr_w2
    B.scr_l0f = (scr_w1, scr_w2)
    for wi, (qd, sd, KK, NN) in enumerate([
            (W["L0_ffn_fc1_q"], W["L0_ffn_fc1_scales"], 1024, 4096),
            (W["L0_ffn_fc2_q"], W["L0_ffn_fc2_scales"], 4096, 1024)]):
        for c in range(KK // 128):
            for nb in range(NN // 512):
                n0 = nb * 512
                S.dma("sync", F.qi[:], qd[c * 128:(c + 1) * 128, n0:n0 + 512],
                      writes=[r_("l0f_qi")], tag="dq_q")
                S.dma("sync", F.srow[:], sd[c:c + 1, n0:n0 + 512],
                      writes=[r_("l0f_srow")], tag="dq_s")
                S.op("vector", lambda e: e.tensor_copy(F.srowh[:], F.srow[:]),
                     reads=[r_("l0f_srow")], writes=[r_("l0f_srowh")], tag="dq_c")
                S.op("tensor", lambda e: e.matmul(
                    B.pb[7][:], B.ones_row[:], F.srowh[:], start=True, stop=True),
                    reads=[r_("l0f_srowh"), r_("ones_row")], writes=[r_("pb", 7)],
                    tag="dq_mm")
                S.op("scalar", lambda e: e.copy(F.scbc[:], B.pb[7][:]),
                     reads=[r_("pb", 7)], writes=[r_("l0f_scbc")], tag="dq_ev")
                S.op("vector", lambda e: e.scalar_tensor_tensor(
                    F.wout[:], F.qi[:], -128.0, F.scbc[:], ALU.add, ALU.mult),
                    reads=[r_("l0f_qi"), r_("l0f_scbc")], writes=[r_("l0f_wout")],
                    tag="dq_stt")
                if wi == 0:
                    dst = scr_w1[4 * nb:4 * nb + 4, :, c, :].rearrange(
                        "a p j -> p a j")
                else:
                    dst = scr_w2[4 * nb:4 * nb + 4, :, c, :].rearrange(
                        "a p j -> p a j")
                S.dma("gpsimd", dst,
                      F.wout[:].rearrange("p (a j) -> p a j", a=4),
                      reads=[r_("l0f_wout")], writes=[r_("scr", wi, c, nb)],
                      tag="dq_st")


def emit_l0_ffn(S, nc, B, F, W):
    B.b3bc, B.b4bc = F.b3bc, F.b4bc
    gvec = [F.g8[:, c:c + 1] for c in range(NCHUNK)]
    bvec = [F.be8[:, c:c + 1] for c in range(NCHUNK)]
    emit_ln(S, B, "lf", gamma_vec=gvec, beta_vec=bvec, gb_key=r_("l0f_g8"),
            l0ffn=True)
    for tj in range(NTJ):
        t0 = tj * TJ
        for ns in range(32):
            zb = F.z[ns]
            S.dma("sync", F.wacat[:],
                  F.scr_w1[ns].rearrange("p c j -> p (c j)"),
                  reads=[r_("scr", 0, c_, ns // 4) for c_ in range(8)],
                  writes=[r_("l0f_wacat")], tag="w1load")
            for c in range(NCHUNK):
                S.op("tensor", lambda e, c=c, t0=t0, ns=ns: e.matmul(
                    B.pb[ns % 2][:], F.wacat[:, c * 128:(c + 1) * 128],
                    tok(B.h[c], t0, TJ),
                    start=(c == 0), stop=(c == NCHUNK - 1)),
                    reads=[r_("l0f_wacat"), r_("h", c)], writes=[r_("pb", ns % 2)],
                    tag="f1_mm")
            S.op("scalar", lambda e, ns=ns, zb=zb: e.activation(
                zb[:], B.pb[ns % 2][:], AF.Silu, bias=F.b1[:, ns:ns + 1]),
                reads=[r_("pb", ns % 2), r_("l0f_b1")], writes=[r_("l0f_z", ns)],
                tag="f1_silu")
        for oc in range(NCHUNK):
            S.dma("sync", F.w2cat[:],
                  F.scr_w2[oc].rearrange("p i j -> p (i j)"),
                  reads=[r_("scr", 1, c_, oc // 4) for c_ in range(32)],
                  writes=[r_("l0f_w2cat")], tag="w2load")
            for i in range(32):
                S.op("tensor", lambda e, i=i, oc=oc: e.matmul(
                    B.pb[2 + oc % 2][:], F.w2cat[:, i * 128:(i + 1) * 128],
                    F.z[i][:],
                    start=(i == 0), stop=(i == 31)),
                    reads=[r_("l0f_w2cat"), r_("l0f_z", i)],
                    writes=[r_("pb", 2 + oc % 2)], tag="f2_mm")
            xx = tok(B.x[oc], t0, TJ)
            S.op("vector", lambda e, oc=oc, t0=t0: e.tensor_mul(
                F.u[:], tok(B.x[oc], t0, TJ), F.b3bc[:, t0:t0 + TJ]),
                reads=[r_("x", oc), r_("b3bc")], writes=[r_("l0f_u")], tag="res_u")
            S.op("vector", lambda e, oc=oc, t0=t0: e.scalar_tensor_tensor(
                tok(B.x[oc], t0, TJ), B.pb[2 + oc % 2][:], 0.5,
                tok(B.x[oc], t0, TJ), ALU.mult, ALU.add),
                reads=[r_("pb", 2 + oc % 2), r_("x", oc)], writes=[r_("x", oc)],
                tag="res_z")
            S.op("vector", lambda e, oc=oc, t0=t0: e.tensor_add(
                tok(B.x[oc], t0, TJ), tok(B.x[oc], t0, TJ), F.u[:]),
                reads=[r_("x", oc), r_("l0f_u")], writes=[r_("x", oc)], tag="res_a")
            S.op("vector", lambda e, oc=oc, t0=t0: e.scalar_tensor_tensor(
                tok(B.x[oc], t0, TJ), F.b4bc[:, t0:t0 + TJ], F.b2h[:, oc:oc + 1],
                tok(B.x[oc], t0, TJ), ALU.add, ALU.add),
                reads=[r_("x", oc), r_("b4bc"), r_("l0f_b2h")],
                writes=[r_("x", oc)], tag="res_b")



L0B_NAMES = ["L0_attn_b_Wq", "L0_attn_b_Wk", "L0_attn_b_Wv",
             "p_l0b_wqrot", "p_l0b_wkrot", "p_l0b_wox",
             "cst_cosB_q", "cst_sinB_q"]


def alloc_l0b(nc, ctx, sfx=""):
    A = Bufs()
    ent = ctx.enter_context
    A.kh = [ent(nc.sbuf_tensor(f"l0b_kh{i}{sfx}", [128, T], BF16)) for i in range(4)]
    A.vt = [ent(nc.sbuf_tensor(f"l0b_vt{t}{sfx}", [128, 512], BF16)) for t in range(16)]
    A.qh = [ent(nc.sbuf_tensor(f"l0b_qh{i}{sfx}", [128, TJ], BF16)) for i in range(2)]
    A.gs = [ent(nc.sbuf_tensor(f"l0b_gs{i}{sfx}", [128, TJ], BF16)) for i in range(2)]
    A.pt = [ent(nc.sbuf_tensor(f"l0b_pt{i}{sfx}", [128, TJ], BF16)) for i in range(3)]
    A.recbc = ent(nc.sbuf_tensor(f"l0b_recbc{sfx}", [128, TJ], BF16))
    A.gm = ent(nc.sbuf_tensor(f"l0b_gm{sfx}", [128, TJ], BF16))
    A.ao = [ent(nc.sbuf_tensor(f"l0b_ao{i}{sfx}", [128, TJ], BF16)) for i in range(16)]
    A.wq = [ent(nc.sbuf_tensor(f"l0b_wq{c}{sfx}", [128, 256], BF16)) for c in range(NCHUNK)]
    A.wg = [ent(nc.sbuf_tensor(f"l0b_wg{c}{sfx}", [128, 256], BF16)) for c in range(NCHUNK)]
    A.wr = [ent(nc.sbuf_tensor(f"l0b_wr{c}{sfx}", [128, 64], BF16)) for c in range(NCHUNK)]
    A.wo = ent(nc.sbuf_tensor(f"l0b_wo{sfx}", [128, 2048], BF16))
    A.tmp = [ent(nc.sbuf_tensor(f"l0b_tmp{i}{sfx}", [128, TJ], BF16)) for i in range(2)]
    A.cq = ent(nc.sbuf_tensor(f"l0b_cq{sfx}", [64, T], BF16))
    A.sq_ = ent(nc.sbuf_tensor(f"l0b_sq{sfx}", [64, T], BF16))
    return A


def emit_l0b_setup(S, nc, B, A, W):
    S.dma("sync", A.cq[:], W["cst_cosB_q"], writes=[r_("l0b_cq")])
    S.dma("sync", A.sq_[:], W["cst_sinB_q"], writes=[r_("l0b_sq")])


def emit_l0_attn_b(S, nc, B, A, W):
    emit_ln(S, B, "ln")
    negC, negC_k = cvec(S, B, -C_L0B)

    _load_w_tiles(S, A.wq, "l0b_wq", W["L0_attn_b_Wv"], 0, 256)
    _load_w_tiles(S, A.wg, "l0b_wg", W["L0_attn_b_Wv"], 256, 256)
    for tt in range(16):
        vt = A.vt[tt]
        for c in range(NCHUNK):
            S.op("tensor", lambda e, c=c, tt=tt: e.matmul(
                B.pb[6][:, 0:256], tok(B.h[c], tt * 128, 128), A.wq[c][:, 0:256],
                start=(c == 0), stop=False),
                reads=[r_("l0b_wq", c), r_("h", c)], writes=[r_("pb", 6)],
                tag="bvproj")
            S.op("tensor", lambda e, c=c, tt=tt: e.matmul(
                B.pb[6][:, 256:512], tok(B.h[c], tt * 128, 128), A.wg[c][:, 0:256],
                start=False, stop=(c == NCHUNK - 1)),
                reads=[r_("l0b_wg", c), r_("h", c)], writes=[r_("pb", 6)],
                tag="bvproj2")
        S.op("scalar", lambda e, vt=vt: e.copy(vt[:], B.pb[6][:]),
             reads=[r_("pb", 6)], writes=[r_("l0b_vt", tt)], tag="bvevac")

    for m in range(4):
        kb = A.kh[m]
        kkey = r_("l0b_kh", m)
        _load_w_tiles(S, A.wq, "l0b_wq", W["L0_attn_b_Wk"], m * 128, 128)
        if m % 2 == 0:
            _load_w_tiles(S, A.wr, "l0b_wr", W["p_l0b_wkrot"], (m // 2) * 64, 64)
        for tj in range(NTJ):
            t0 = tj * TJ
            for c in range(NCHUNK):
                S.op("tensor", lambda e, c=c, t0=t0: e.matmul(
                    B.pb[0][:], A.wq[c][:, 0:128], tok(B.h[c], t0, TJ),
                    start=(c == 0), stop=(c == NCHUNK - 1)),
                    reads=[r_("l0b_wq", c), r_("h", c)], writes=[r_("pb", 0)],
                    tag="bk_mm")
                if m % 2 == 0:
                    S.op("tensor", lambda e, c=c, t0=t0: e.matmul(
                        B.pb[1][0:64, :], A.wr[c][:, 0:64], tok(B.h[c], t0, TJ),
                        start=(c == 0), stop=(c == NCHUNK - 1)),
                        reads=[r_("l0b_wr", c), r_("h", c)], writes=[r_("pb", 1)],
                        tag="bk_mmr")
            if m % 2 == 0:
                t1, t2 = A.tmp[0], A.tmp[1]
                S.op("vector", lambda e, t0=t0, t1=t1: e.scalar_tensor_tensor(
                    t1[0:64, :], B.pb[0][0:64, :], 1.0 / 16.0,
                    A.cq[:, t0:t0 + TJ], ALU.mult, ALU.mult),
                    reads=[r_("pb", 0), r_("l0b_cq")], writes=[r_("l0b_t", 0)],
                    tag="bk_r1")
                S.op("vector", lambda e, t0=t0, t2=t2: e.scalar_tensor_tensor(
                    t2[0:64, :], B.pb[1][0:64, :], 1.0 / 16.0,
                    A.sq_[:, t0:t0 + TJ], ALU.mult, ALU.mult),
                    reads=[r_("pb", 1), r_("l0b_sq")], writes=[r_("l0b_t", 1)],
                    tag="bk_r2")
                S.op("vector", lambda e, t0=t0, kb=kb, t1=t1, t2=t2: e.tensor_add(
                    kb[0:64, t0:t0 + TJ], t1[0:64, :], t2[0:64, :]),
                    reads=[r_("l0b_t", 0), r_("l0b_t", 1)], writes=[kkey],
                    tag="bk_r3")
                S.op("scalar", lambda e, t0=t0, kb=kb: e.activation(
                    kb[64:128, t0:t0 + TJ], B.pb[0][64:128, :], AF.Copy,
                    scale=1.0 / 16.0),
                    reads=[r_("pb", 0)], writes=[kkey], tag="bk_pass")
            else:
                S.op("scalar", lambda e, t0=t0, kb=kb: e.activation(
                    kb[:, t0:t0 + TJ], B.pb[0][:], AF.Copy, scale=1.0 / 16.0),
                    reads=[r_("pb", 0)], writes=[kkey], tag="bk_pass2")

    for tj in range(NTJ):
        t0 = tj * TJ
        for h in range(8):
            kv = h // 4
            _load_w_tiles(S, A.wq, "l0b_wq", W["L0_attn_b_Wq"], h * 256, 256)
            _load_w_tiles(S, A.wr, "l0b_wr", W["p_l0b_wqrot"], h * 64, 64)
            _load_w_tiles(S, A.wg, "l0b_wg", W["L0_attn_b_Wq"], 2048 + h * 256, 256)
            for cc in range(2):
                qb = A.qh[cc]
                for c in range(NCHUNK):
                    S.op("tensor", lambda e, c=c, cc=cc: e.matmul(
                        B.pb[0][:], A.wq[c][:, cc * 128:cc * 128 + 128],
                        tok(B.h[c], t0, TJ),
                        start=(c == 0), stop=(c == NCHUNK - 1)),
                        reads=[r_("l0b_wq", c), r_("h", c)], writes=[r_("pb", 0)],
                        tag="bq_mm")
                if cc == 0:
                    for c in range(NCHUNK):
                        S.op("tensor", lambda e, c=c: e.matmul(
                            B.pb[1][0:64, :], A.wr[c][:, 0:64],
                            tok(B.h[c], t0, TJ),
                            start=(c == 0), stop=(c == NCHUNK - 1)),
                            reads=[r_("l0b_wr", c), r_("h", c)],
                            writes=[r_("pb", 1)], tag="bq_mmr")
                    t1, t2 = A.tmp[0], A.tmp[1]
                    S.op("vector", lambda e, t1=t1: e.tensor_mul(
                        t1[0:64, :], B.pb[0][0:64, :], A.cq[:, t0:t0 + TJ]),
                        reads=[r_("pb", 0), r_("l0b_cq")], writes=[r_("l0b_t", 0)],
                        tag="bq_r1")
                    S.op("vector", lambda e, t2=t2: e.tensor_mul(
                        t2[0:64, :], B.pb[1][0:64, :], A.sq_[:, t0:t0 + TJ]),
                        reads=[r_("pb", 1), r_("l0b_sq")], writes=[r_("l0b_t", 1)],
                        tag="bq_r2")
                    S.op("vector", lambda e, qb=qb, t1=t1, t2=t2: e.tensor_add(
                        qb[0:64, :], t1[0:64, :], t2[0:64, :]),
                        reads=[r_("l0b_t", 0), r_("l0b_t", 1)],
                        writes=[r_("l0b_qh", cc)], tag="bq_r3")
                    S.op("scalar", lambda e, qb=qb: e.copy(
                        qb[64:128, :], B.pb[0][64:128, :]),
                        reads=[r_("pb", 0)], writes=[r_("l0b_qh", cc)],
                        tag="bq_pass")
                else:
                    S.op("scalar", lambda e, qb=qb: e.copy(qb[:], B.pb[0][:]),
                         reads=[r_("pb", 0)], writes=[r_("l0b_qh", cc)],
                         tag="bq_pass2")
                gb = A.gs[cc]
                for c in range(NCHUNK):
                    S.op("tensor", lambda e, c=c, cc=cc: e.matmul(
                        B.pb[1][:], A.wg[c][:, cc * 128:cc * 128 + 128],
                        tok(B.h[c], t0, TJ),
                        start=(c == 0), stop=(c == NCHUNK - 1)),
                        reads=[r_("l0b_wg", c), r_("h", c)], writes=[r_("pb", 1)],
                        tag="bg_mm")
                S.op("scalar", lambda e, gb=gb, cc=cc: e.activation(
                    gb[:], B.pb[1][:], AF.Silu),
                    reads=[r_("pb", 1)], writes=[r_("l0b_gs", cc)], tag="bg_silu")
            n_pv = 16 * 2
            for kt in range(16):
                pt = A.pt[kt % 2]
                ptk = r_("l0b_pt", kt % 2)
                ps_S = 2 + (kt % 2) * 2
                for cc in range(2):
                    S.op("tensor", lambda e, kt=kt, cc=cc, ps_S=ps_S: e.matmul(
                        B.pb[ps_S][:], A.kh[2 * kv + cc][:, kt * 128:(kt + 1) * 128],
                        A.qh[cc][:], start=(cc == 0), stop=(cc == 1)),
                        reads=[r_("l0b_kh", 2 * kv + cc), r_("l0b_qh", cc)],
                        writes=[r_("pb", ps_S)], tag="bs_mm")
                S.op("scalar", lambda e, pt=pt, ps_S=ps_S: e.activation(
                    pt[:], B.pb[ps_S][:], AF.Exp, bias=negC),
                    reads=[r_("pb", ps_S), negC_k], writes=[ptk], tag="bs_exp")
                for cc in range(2):
                    S.op("tensor", lambda e, kt=kt, cc=cc, pt=pt: e.matmul(
                        B.pb[3 + 2 * cc][:], A.vt[kt][:, kv * 256 + cc * 128:
                                                      kv * 256 + cc * 128 + 128],
                        pt[:], start=(kt == 0), stop=(kt == 15)),
                        reads=[r_("l0b_vt", kt), ptk],
                        writes=[r_("pb", 3 + 2 * cc)], tag="bpv")
                S.op("tensor", lambda e, pt=pt, kt=kt: e.matmul(
                    B.pb[6][0:1, :], B.onecol[:], pt[:],
                    start=(kt == 0), stop=(kt == 15)),
                    reads=[r_("onecol"), ptk], writes=[r_("pb", 6)], tag="bden")
            rec = B.rowf[:, 3 * TJ:4 * TJ]
            rec_k = r_("rowf", 3)
            S.op("vector", lambda e, rec=rec: e.reciprocal(rec, B.pb[6][0:1, :]),
                 reads=[r_("pb", 6)], writes=[rec_k], tag="bn_rec")
            S.op("vector", lambda e, rec=rec: e.tensor_copy(B.rowh[:], rec),
                 reads=[rec_k], writes=[r_("rowh")], tag="bn_cast")
            S.op("tensor", lambda e: e.matmul(
                B.pb[7][:], B.ones_row[:], B.rowh[:], start=True, stop=True),
                reads=[r_("rowh"), r_("ones_row")], writes=[r_("pb", 7)],
                tag="bn_bcmm")
            S.op("scalar", lambda e: e.copy(A.recbc[:], B.pb[7][:]),
                 reads=[r_("pb", 7)], writes=[r_("l0b_recbc")], tag="bn_bcev")
            for cc in range(2):
                S.op("vector", lambda e, cc=cc: e.tensor_mul(
                    A.gm[:], A.gs[cc][:], A.recbc[:]),
                    reads=[r_("l0b_gs", cc), r_("l0b_recbc")], writes=[r_("l0b_gm")],
                    tag="bn_gm")
                S.op("vector", lambda e, cc=cc, h=h: e.tensor_mul(
                    A.ao[2 * h + cc][:], B.pb[3 + 2 * cc][:], A.gm[:]),
                    reads=[r_("pb", 3 + 2 * cc), r_("l0b_gm")],
                    writes=[r_("l0b_ao", 2 * h + cc)], tag="bn_mul")
        for oc in range(NCHUNK):
            S.dma("gpsimd", A.wo[:],
                  W["p_l0b_wox"][oc].rearrange("p a j -> p (a j)"),
                  writes=[r_("l0b_wo")], tag="bwo_ld")
            for i in range(16):
                S.op("tensor", lambda e, i=i, oc=oc: e.matmul(
                    B.pb[0 + oc % 2][:], A.wo[:, i * 128:(i + 1) * 128],
                    A.ao[i][:], start=(i == 0), stop=(i == 15)),
                    reads=[r_("l0b_wo"), r_("l0b_ao", i)],
                    writes=[r_("pb", 0 + oc % 2)], tag="bwo_mm")
            S.op("vector", lambda e, oc=oc, t0=t0: e.tensor_add(
                tok(B.x[oc], t0, TJ), tok(B.x[oc], t0, TJ),
                B.pb[0 + oc % 2][:]),
                reads=[r_("pb", 0 + oc % 2), r_("x", oc)], writes=[r_("x", oc)],
                tag="bwo_res")



L1A_NAMES = ["p_l1a_wqT", "p_l1a_wqTp", "p_l1a_wkT", "p_l1a_wkTp",
             "p_l1a_wvT", "p_l1a_wox", "cst_cosI_q", "cst_sinI_q",
             "cst_l1_tri"]


def alloc_l1a(nc, ctx, sfx=""):
    A = Bufs()
    ent = ctx.enter_context
    A.kh = [ent(nc.sbuf_tensor(f"l1a_kh{c}{sfx}", [128, T], BF16)) for c in range(4)]
    A.vt = [ent(nc.sbuf_tensor(f"l1a_vt{t}{sfx}", [128, 520], BF16)) for t in range(16)]
    A.qh = ent(nc.sbuf_tensor(f"l1a_qh{sfx}", [128, TJ], BF16))
    A.pt = [ent(nc.sbuf_tensor(f"l1a_pt{i}{sfx}", [128, TJ], BF16)) for i in range(2)]
    A.recbc = ent(nc.sbuf_tensor(f"l1a_recbc{sfx}", [64, TJ], BF16))
    A.ao = [ent(nc.sbuf_tensor(f"l1a_ao{c}{sfx}", [128, TJ], BF16)) for c in range(4)]
    A.wv = [ent(nc.sbuf_tensor(f"l1a_wv{c}{sfx}", [128, 512], BF16)) for c in range(NCHUNK)]
    A.wo = ent(nc.sbuf_tensor(f"l1a_wo{sfx}", [128, 512], BF16))
    A.tmp = [ent(nc.sbuf_tensor(f"l1a_tmp{i}{sfx}", [128, TJ], BF16)) for i in range(2)]
    A.cq = ent(nc.sbuf_tensor(f"l1a_cq{sfx}", [128, T], BF16))
    A.sq_ = ent(nc.sbuf_tensor(f"l1a_sq{sfx}", [128, T], BF16))
    A.tri = ent(nc.sbuf_tensor(f"l1a_tri{sfx}", [128, 128], BF16))
    return A


def emit_l1a_setup(S, nc, B, A, W):
    S.dma("sync", A.cq[:], W["cst_cosI_q"], writes=[r_("l1a_cq")])
    S.dma("sync", A.sq_[:], W["cst_sinI_q"], writes=[r_("l1a_sq")])
    S.dma("sync", A.tri[:], W["cst_l1_tri"], writes=[r_("l1a_tri")])


def emit_l1_attn_a(S, nc, B, A, W):
    emit_ln(S, B, "ln")
    negC, negC_k = cvec(S, B, -C_L1A)
    for half in range(2):
        emit_l1a_half(S, nc, B, A, W, half, negC, negC_k)


def emit_l1a_half(S, nc, B, A, W, half, negC, negC_k):
    hbase = 4 * half

    for c in range(NCHUNK):
        S.dma("gpsimd", A.wv[c][:, 0:512],
              W["p_l1a_wvT"][c * 128:(c + 1) * 128, 512 * half:512 * half + 512],
              writes=[r_("l1a_wv", c), r_("l1a_wv2", c)], tag="wloadv")
    for tt in range(16):
        vt = A.vt[tt]
        S.op("vector", lambda e, vt=vt: e.memset(
            vt.ap().rearrange("p (a b) -> p a b", b=65)[:, :, 64:65], 1.0),
            writes=[r_("l1a_vt", tt)], tag="avones")
        for c in range(NCHUNK):
            S.op("tensor", lambda e, c=c, tt=tt: e.matmul(
                B.pb[6][:, 0:512], tok(B.h[c], tt * 128, 128),
                A.wv[c][:, 0:512],
                start=(c == 0), stop=(c == NCHUNK - 1)),
                reads=[r_("l1a_wv", c), r_("h", c)], writes=[r_("pb", 6)],
                tag="avproj")
        S.op("scalar", lambda e, vt=vt: e.copy(
            vt.ap().rearrange("p (a b) -> p a b", b=65)[:, :, 0:64],
            B.pb[6][:, 0:512].rearrange("p (a b) -> p a b", b=64)),
            reads=[r_("pb", 6)], writes=[r_("l1a_vt", tt)], tag="avevac")

    for ci in range(4):
        c = hbase + ci
        kb = A.kh[ci]
        kkey = r_("l1a_kh", ci)
        _load_w_tiles(S, A.wv, "l1a_wv", W["p_l1a_wkT"], c * 128, 128)
        _load_w_tiles(S, A.wv, "l1a_wv2", W["p_l1a_wkTp"], c * 128, 128,
                      coloff=128)
        for tj in range(NTJ):
            t0 = tj * TJ
            for cc in range(NCHUNK):
                S.op("tensor", lambda e, cc=cc, t0=t0: e.matmul(
                    B.pb[0][:], A.wv[cc][:, 0:128], tok(B.h[cc], t0, TJ),
                    start=(cc == 0), stop=(cc == NCHUNK - 1)),
                    reads=[r_("l1a_wv", cc), r_("h", cc)], writes=[r_("pb", 0)],
                    tag="ak_mm")
                S.op("tensor", lambda e, cc=cc, t0=t0: e.matmul(
                    B.pb[1][:], A.wv[cc][:, 128:256], tok(B.h[cc], t0, TJ),
                    start=(cc == 0), stop=(cc == NCHUNK - 1)),
                    reads=[r_("l1a_wv2", cc), r_("h", cc)], writes=[r_("pb", 1)],
                    tag="ak_mmp")
            t1, t2 = A.tmp[0], A.tmp[1]
            S.op("vector", lambda e, t0=t0, t1=t1: e.scalar_tensor_tensor(
                t1[:], B.pb[0][:], 8.0, A.cq[:, t0:t0 + TJ], ALU.mult, ALU.mult),
                reads=[r_("pb", 0), r_("l1a_cq")], writes=[r_("l1a_t", 0)],
                tag="ak_r1")
            S.op("vector", lambda e, t0=t0, t2=t2: e.scalar_tensor_tensor(
                t2[:], B.pb[1][:], 8.0, A.sq_[:, t0:t0 + TJ], ALU.mult, ALU.mult),
                reads=[r_("pb", 1), r_("l1a_sq")], writes=[r_("l1a_t", 1)],
                tag="ak_r2")
            S.op("vector", lambda e, t0=t0, kb=kb, t1=t1, t2=t2: e.tensor_add(
                kb[:, t0:t0 + TJ], t1[:], t2[:]),
                reads=[r_("l1a_t", 0), r_("l1a_t", 1)], writes=[kkey], tag="ak_r3")

    for tj in range(NTJ):
        t0 = tj * TJ
        for ci in range(4):
            c = hbase + ci
            qb = A.qh
            qbk = r_("l1a_qh")
            _load_w_tiles(S, A.wv, "l1a_wv", W["p_l1a_wqT"], c * 128, 128)
            _load_w_tiles(S, A.wv, "l1a_wv2", W["p_l1a_wqTp"], c * 128, 128,
                          coloff=128)
            for cc in range(NCHUNK):
                S.op("tensor", lambda e, cc=cc, t0=t0: e.matmul(
                    B.pb[0][:], A.wv[cc][:, 0:128], tok(B.h[cc], t0, TJ),
                    start=(cc == 0), stop=(cc == NCHUNK - 1)),
                    reads=[r_("l1a_wv", cc), r_("h", cc)], writes=[r_("pb", 0)],
                    tag="aq_mm")
                S.op("tensor", lambda e, cc=cc, t0=t0: e.matmul(
                    B.pb[1][:], A.wv[cc][:, 128:256], tok(B.h[cc], t0, TJ),
                    start=(cc == 0), stop=(cc == NCHUNK - 1)),
                    reads=[r_("l1a_wv2", cc), r_("h", cc)], writes=[r_("pb", 1)],
                    tag="aq_mmp")
            t1, t2 = A.tmp[0], A.tmp[1]
            S.op("vector", lambda e, t0=t0, t1=t1: e.tensor_mul(
                t1[:], B.pb[0][:], A.cq[:, t0:t0 + TJ]),
                reads=[r_("pb", 0), r_("l1a_cq")], writes=[r_("l1a_t", 0)],
                tag="aq_r1")
            S.op("vector", lambda e, t0=t0, t2=t2: e.tensor_mul(
                t2[:], B.pb[1][:], A.sq_[:, t0:t0 + TJ]),
                reads=[r_("pb", 1), r_("l1a_sq")], writes=[r_("l1a_t", 1)],
                tag="aq_r2")
            S.op("vector", lambda e, qb=qb, t1=t1, t2=t2: e.tensor_add(
                qb[:], t1[:], t2[:]),
                reads=[r_("l1a_t", 0), r_("l1a_t", 1)], writes=[qbk], tag="aq_r3")
            for hh in range(2):
                h = 8 * half + 2 * ci + hh
                rs = 64 * hh
                n_pv = 4 * tj + 4
                pvi = [0]

                def pv(lhs, lk, rhs, rk, qlo, w):
                    st = pvi[0] == 0
                    sp = pvi[0] == n_pv - 1
                    pvi[0] += 1
                    S.op("tensor", lambda e, lhs=lhs, rhs=rhs, qlo=qlo, w=w,
                         st=st, sp=sp: e.matmul(
                             B.pb[3][0:65, qlo:qlo + w], lhs, rhs,
                             start=st, stop=sp),
                         reads=[lk, rk], writes=[r_("pb", 3)], tag="apv")

                hg = h % 8
                for kt in range(4 * tj + 4):
                    r = kt - 4 * tj
                    qlo = max(0, 128 * r)
                    w = TJ - qlo
                    pt = A.pt[kt % 2]
                    ptk = r_("l1a_pt", kt % 2)
                    ps_S = 2 + (kt % 2) * 2
                    S.op("tensor", lambda e, kt=kt, qlo=qlo, w=w, rs=rs,
                         ps_S=ps_S, ci=ci: e.matmul(
                             B.pb[ps_S][:, 0:w],
                             A.kh[ci][rs:rs + 64, kt * 128:(kt + 1) * 128],
                             qb[rs:rs + 64, qlo:qlo + w], start=True, stop=True),
                        reads=[r_("l1a_kh", ci), qbk], writes=[r_("pb", ps_S)],
                        tag="as_mm", pe_base=rs)
                    S.op("scalar", lambda e, w=w, pt=pt, ps_S=ps_S: e.activation(
                        pt[:, 0:w], B.pb[ps_S][:, 0:w], AF.Exp, bias=negC),
                        reads=[r_("pb", ps_S), negC_k], writes=[ptk], tag="as_exp")
                    if r >= 0:
                        S.op("vector", lambda e, pt=pt: e.tensor_tensor(
                            pt[:, 0:128], pt[:, 0:128], A.tri[:], ALU.min),
                            reads=[ptk, r_("l1a_tri")], writes=[ptk], tag="as_msk")
                    pv(A.vt[kt][:, 65 * hg:65 * hg + 65], r_("l1a_vt", kt),
                       pt[:, 0:w], ptk, qlo, w)
                rec = B.rowf[:, 3 * TJ:4 * TJ]
                rec_k = r_("rowf", 3)
                S.op("vector", lambda e, rec=rec: e.reciprocal(
                    rec, B.pb[3][64:65, 0:TJ]),
                    reads=[r_("pb", 3)], writes=[rec_k], tag="an_rec")
                S.op("vector", lambda e, rec=rec: e.tensor_copy(B.rowh[:], rec),
                     reads=[rec_k], writes=[r_("rowh")], tag="an_cast")
                S.op("tensor", lambda e: e.matmul(
                    B.pb[7][0:64, 0:TJ], B.ones_row[:, 0:64], B.rowh[:],
                    start=True, stop=True),
                    reads=[r_("rowh"), r_("ones_row")], writes=[r_("pb", 7)],
                    tag="an_bcmm")
                S.op("scalar", lambda e: e.copy(A.recbc[:], B.pb[7][0:64, 0:TJ]),
                     reads=[r_("pb", 7)], writes=[r_("l1a_recbc")], tag="an_bcev")
                S.op("vector", lambda e, ci=ci, rs=rs: e.tensor_mul(
                    A.ao[ci][rs:rs + 64, :], B.pb[3][0:64, 0:TJ], A.recbc[:]),
                    reads=[r_("pb", 3), r_("l1a_recbc")], writes=[r_("l1a_ao", ci)],
                    tag="an_mul")
        for oc in range(NCHUNK):
            S.dma("gpsimd", A.wo[:],
                  W["p_l1a_wox"][oc, :, 4 * half:4 * half + 4, :].rearrange(
                      "p a j -> p (a j)"),
                  writes=[r_("l1a_wo")], tag="awo_ld")
            for i in range(4):
                S.op("tensor", lambda e, i=i, oc=oc: e.matmul(
                    B.pb[0 + oc % 2][:], A.wo[:, i * 128:(i + 1) * 128],
                    A.ao[i][:], start=(i == 0), stop=(i == 3)),
                    reads=[r_("l1a_wo"), r_("l1a_ao", i)],
                    writes=[r_("pb", 0 + oc % 2)], tag="awo_mm")
            S.op("vector", lambda e, oc=oc, t0=t0: e.tensor_add(
                tok(B.x[oc], t0, TJ), tok(B.x[oc], t0, TJ), B.pb[0 + oc % 2][:]),
                reads=[r_("pb", 0 + oc % 2), r_("x", oc)], writes=[r_("x", oc)],
                tag="awo_res")



L1B_NAMES = ["p_l1b_wqT", "p_l1b_wqTp", "p_l1b_wkT", "p_l1b_wkTp",
             "p_l1b_wvT", "p_l1b_wox", "p_l1b_bo8", "cst_cosH_q",
             "cst_sinH_q", "cst_l1b_tri4", "cst_l1b_low4"]


def alloc_l1b(nc, ctx, sfx=""):
    A = Bufs()
    ent = ctx.enter_context
    A.kh = [ent(nc.sbuf_tensor(f"l1b_kh{c}{sfx}", [128, T], BF16)) for c in range(4)]
    A.vt = [ent(nc.sbuf_tensor(f"l1b_vt{t}{sfx}", [128, 520], BF16)) for t in range(16)]
    A.qh2 = [ent(nc.sbuf_tensor(f"l1b_qh{i}{sfx}", [128, TJ], BF16)) for i in range(2)]
    A.pt = [ent(nc.sbuf_tensor(f"l1b_pt{i}{sfx}", [128, TJ], BF16)) for i in range(2)]
    A.recbc = ent(nc.sbuf_tensor(f"l1b_recbc{sfx}", [64, TJ], BF16))
    A.ao = [ent(nc.sbuf_tensor(f"l1b_ao{c}{sfx}", [128, T], BF16)) for c in range(4)]
    A.wv = [ent(nc.sbuf_tensor(f"l1b_wv{c}{sfx}", [128, 512], BF16)) for c in range(NCHUNK)]
    A.wo = ent(nc.sbuf_tensor(f"l1b_wo{sfx}", [128, 512], BF16))
    A.tmp = [ent(nc.sbuf_tensor(f"l1b_tmp{i}{sfx}", [128, TJ], BF16)) for i in range(2)]
    A.cq = ent(nc.sbuf_tensor(f"l1b_cq{sfx}", [128, T], BF16))
    A.sq_ = ent(nc.sbuf_tensor(f"l1b_sq{sfx}", [128, T], BF16))
    A.tri4 = ent(nc.sbuf_tensor(f"l1b_tri4{sfx}", [128, 512], BF16))
    A.low4 = ent(nc.sbuf_tensor(f"l1b_low4{sfx}", [128, 512], BF16))
    A.bo = ent(nc.sbuf_tensor(f"l1b_bo{sfx}", [128, 8], F32))
    return A


def emit_l1b_setup(S, nc, B, A, W):
    S.dma("sync", A.cq[:], W["cst_cosH_q"], writes=[r_("l1b_cq")])
    S.dma("sync", A.sq_[:], W["cst_sinH_q"], writes=[r_("l1b_sq")])
    S.dma("sync", A.tri4[:], W["cst_l1b_tri4"], writes=[r_("l1b_tri4")])
    S.dma("sync", A.low4[:], W["cst_l1b_low4"], writes=[r_("l1b_low4")])
    S.dma("sync", A.bo[:], W["p_l1b_bo8"], writes=[r_("l1b_bo")])


def emit_l1_attn_b(S, nc, B, A, W):
    emit_ln(S, B, "ln")
    negC, negC_k = cvec(S, B, -C_L1B)
    for half in range(2):
        emit_l1b_half(S, nc, B, A, W, half, negC, negC_k)


L1B_STAGE = 4


def emit_l1b_half(S, nc, B, A, W, half, negC, negC_k):
    hbase = 4 * half

    for c in range(NCHUNK):
        S.dma("gpsimd", A.wv[c][:, 0:512],
              W["p_l1b_wvT"][c * 128:(c + 1) * 128, 512 * half:512 * half + 512],
              writes=[r_("l1b_wv", c), r_("l1b_wv2", c)], tag="wloadv")
    for tt in range(16):
        vt = A.vt[tt]
        S.op("vector", lambda e, vt=vt: e.memset(
            vt.ap().rearrange("p (a b) -> p a b", b=65)[:, :, 64:65], 1.0),
            writes=[r_("l1b_vt", tt)], tag="bvones")
        for c in range(NCHUNK):
            S.op("tensor", lambda e, c=c, tt=tt: e.matmul(
                B.pb[6][:, 0:512], tok(B.h[c], tt * 128, 128), A.wv[c][:, 0:512],
                start=(c == 0), stop=(c == NCHUNK - 1)),
                reads=[r_("l1b_wv", c), r_("h", c)], writes=[r_("pb", 6)],
                tag="bvproj")
        S.op("scalar", lambda e, vt=vt: e.copy(
            vt.ap().rearrange("p (a b) -> p a b", b=65)[:, :, 0:64],
            B.pb[6][:, 0:512].rearrange("p (a b) -> p a b", b=64)),
            reads=[r_("pb", 6)], writes=[r_("l1b_vt", tt)], tag="bvevac")

    for ci in range(4):
        c = hbase + ci
        kb = A.kh[ci]
        kkey = r_("l1b_kh", ci)
        _load_w_tiles(S, A.wv, "l1b_wv", W["p_l1b_wkT"], c * 128, 128)
        _load_w_tiles(S, A.wv, "l1b_wv2", W["p_l1b_wkTp"], c * 128, 128,
                      coloff=128)
        for tj in range(NTJ):
            t0 = tj * TJ
            for cc in range(NCHUNK):
                S.op("tensor", lambda e, cc=cc, t0=t0: e.matmul(
                    B.pb[0][:], A.wv[cc][:, 0:128], tok(B.h[cc], t0, TJ),
                    start=(cc == 0), stop=(cc == NCHUNK - 1)),
                    reads=[r_("l1b_wv", cc), r_("h", cc)], writes=[r_("pb", 0)],
                    tag="bk_mm")
                S.op("tensor", lambda e, cc=cc, t0=t0: e.matmul(
                    B.pb[1][:], A.wv[cc][:, 128:256], tok(B.h[cc], t0, TJ),
                    start=(cc == 0), stop=(cc == NCHUNK - 1)),
                    reads=[r_("l1b_wv2", cc), r_("h", cc)], writes=[r_("pb", 1)],
                    tag="bk_mmp")
            t1, t2 = A.tmp[0], A.tmp[1]
            S.op("vector", lambda e, t0=t0, t1=t1: e.scalar_tensor_tensor(
                t1[:], B.pb[0][:], 8.0, A.cq[:, t0:t0 + TJ], ALU.mult, ALU.mult),
                reads=[r_("pb", 0), r_("l1b_cq")], writes=[r_("l1b_t", 0)],
                tag="bk_r1")
            S.op("vector", lambda e, t0=t0, t2=t2: e.scalar_tensor_tensor(
                t2[:], B.pb[1][:], 8.0, A.sq_[:, t0:t0 + TJ], ALU.mult, ALU.mult),
                reads=[r_("pb", 1), r_("l1b_sq")], writes=[r_("l1b_t", 1)],
                tag="bk_r2")
            S.op("vector", lambda e, t0=t0, kb=kb, t1=t1, t2=t2: e.tensor_add(
                kb[:, t0:t0 + TJ], t1[:], t2[:]),
                reads=[r_("l1b_t", 0), r_("l1b_t", 1)], writes=[kkey], tag="bk_r3")

    if L1B_STAGE < 1:
        return
    for tj in range(NTJ):
        t0 = tj * TJ
        for grp in range(2):
            for gg in range(2):
                ci = 2 * grp + gg
                c = hbase + ci
                qb = A.qh2[gg]
                qbk = r_("l1b_qh", gg)
                _load_w_tiles(S, A.wv, "l1b_wv", W["p_l1b_wqT"], c * 128, 128)
                _load_w_tiles(S, A.wv, "l1b_wv2", W["p_l1b_wqTp"], c * 128, 128,
                              coloff=128)
                for cc in range(NCHUNK):
                    S.op("tensor", lambda e, cc=cc, t0=t0: e.matmul(
                        B.pb[0][:], A.wv[cc][:, 0:128], tok(B.h[cc], t0, TJ),
                        start=(cc == 0), stop=(cc == NCHUNK - 1)),
                        reads=[r_("l1b_wv", cc), r_("h", cc)], writes=[r_("pb", 0)],
                        tag="bq_mm")
                    S.op("tensor", lambda e, cc=cc, t0=t0: e.matmul(
                        B.pb[1][:], A.wv[cc][:, 128:256], tok(B.h[cc], t0, TJ),
                        start=(cc == 0), stop=(cc == NCHUNK - 1)),
                        reads=[r_("l1b_wv2", cc), r_("h", cc)],
                        writes=[r_("pb", 1)], tag="bq_mmp")
                t1, t2 = A.tmp[0], A.tmp[1]
                S.op("vector", lambda e, t0=t0, t1=t1: e.tensor_mul(
                    t1[:], B.pb[0][:], A.cq[:, t0:t0 + TJ]),
                    reads=[r_("pb", 0), r_("l1b_cq")], writes=[r_("l1b_t", 0)],
                    tag="bq_r1")
                S.op("vector", lambda e, t0=t0, t2=t2: e.tensor_mul(
                    t2[:], B.pb[1][:], A.sq_[:, t0:t0 + TJ]),
                    reads=[r_("pb", 1), r_("l1b_sq")], writes=[r_("l1b_t", 1)],
                    tag="bq_r2")
                S.op("vector", lambda e, qb=qb, t1=t1, t2=t2: e.tensor_add(
                    qb[:], t1[:], t2[:]),
                    reads=[r_("l1b_t", 0), r_("l1b_t", 1)], writes=[qbk],
                    tag="bq_r3")
            if L1B_STAGE < 2:
                continue
            for qt in range(4):
                qt_abs = 4 * tj + qt
                kts = [qt_abs] + ([qt_abs - 1] if qt_abs > 0 else [])
                n_pv = 4 * len(kts)
                pvi = [0]
                for ki, kt in enumerate(kts):
                    pt = A.pt[ki]
                    ptk = r_("l1b_pt", ki)
                    ps_S = 2 + ki * 2
                    for j in range(4):
                        ci, rs = 2 * grp + j // 2, 64 * (j % 2)
                        S.op("tensor", lambda e, kt=kt, j=j, ci=ci, rs=rs,
                             qt=qt, ps_S=ps_S: e.matmul(
                                 B.pb[ps_S][:, 128 * j:128 * j + 128],
                                 A.kh[ci][rs:rs + 64, kt * 128:(kt + 1) * 128],
                                 A.qh2[j // 2][rs:rs + 64,
                                               128 * qt:128 * qt + 128],
                                 start=True, stop=True),
                            reads=[r_("l1b_kh", ci), r_("l1b_qh", j // 2)],
                            writes=[r_("pb", ps_S)], tag="bs_mm", pe_base=rs)
                    S.op("scalar", lambda e, pt=pt, ps_S=ps_S: e.activation(
                        pt[:], B.pb[ps_S][:], AF.Exp, bias=negC),
                        reads=[r_("pb", ps_S), negC_k], writes=[ptk], tag="bs_exp")
                    if L1B_STAGE != 9:
                        msk = A.tri4 if kt == qt_abs else A.low4
                        mkey = r_("l1b_tri4") if kt == qt_abs else r_("l1b_low4")
                        S.op("vector", lambda e, pt=pt, msk=msk: e.tensor_tensor(
                            pt[:], pt[:], msk[:], ALU.min),
                            reads=[ptk, mkey], writes=[ptk], tag="bs_msk")
                if L1B_STAGE < 3:
                    continue
                for j in range(4):
                    hg = 4 * grp + j
                    for ki, kt in enumerate(kts):
                        pt = A.pt[ki]
                        ptk = r_("l1b_pt", ki)
                        st = pvi[0] == 0
                        sp = pvi[0] == n_pv - 1
                        pvi[0] += 1
                        S.op("tensor", lambda e, kt=kt, j=j, hg=hg, pt=pt,
                             st=st, sp=sp: e.matmul(
                                 B.pb[3][0:65, 128 * j:128 * j + 128],
                                 A.vt[kt][:, 65 * hg:65 * hg + 65],
                                 pt[:, 128 * j:128 * j + 128],
                                 start=st, stop=sp),
                            reads=[r_("l1b_vt", kt), ptk], writes=[r_("pb", 3)],
                            tag="bpv")
                rec = B.rowf[:, 3 * TJ:4 * TJ]
                rec_k = r_("rowf", 3)
                S.op("vector", lambda e, rec=rec: e.reciprocal(
                    rec, B.pb[3][64:65, 0:TJ]),
                    reads=[r_("pb", 3)], writes=[rec_k], tag="bn_rec")
                S.op("vector", lambda e, rec=rec: e.tensor_copy(B.rowh[:], rec),
                     reads=[rec_k], writes=[r_("rowh")], tag="bn_cast")
                S.op("tensor", lambda e: e.matmul(
                    B.pb[7][0:64, 0:TJ], B.ones_row[:, 0:64], B.rowh[:],
                    start=True, stop=True),
                    reads=[r_("rowh"), r_("ones_row")], writes=[r_("pb", 7)],
                    tag="bn_bcmm")
                S.op("scalar", lambda e: e.copy(A.recbc[:], B.pb[7][0:64, 0:TJ]),
                     reads=[r_("pb", 7)], writes=[r_("l1b_recbc")], tag="bn_bcev")
                for j in range(4):
                    ci, rs = 2 * grp + j // 2, 64 * (j % 2)
                    S.op("vector", lambda e, j=j, ci=ci, rs=rs, qt=qt, t0=t0:
                         e.tensor_mul(
                             A.ao[ci][rs:rs + 64,
                                      t0 + 128 * qt:t0 + 128 * qt + 128],
                             B.pb[3][0:64, 128 * j:128 * j + 128],
                             A.recbc[:, 128 * j:128 * j + 128]),
                        reads=[r_("pb", 3), r_("l1b_recbc")],
                        writes=[r_("l1b_ao", ci)], tag="bn_mul")
    if L1B_STAGE < 4:
        return
    for tj in range(NTJ):
        t0 = tj * TJ
        for oc in range(NCHUNK):
            S.dma("gpsimd", A.wo[:],
                  W["p_l1b_wox"][oc, :, 4 * half:4 * half + 4, :].rearrange(
                      "p a j -> p (a j)"),
                  writes=[r_("l1b_wo")], tag="bwo_ld")
            for i in range(4):
                S.op("tensor", lambda e, i=i, oc=oc, t0=t0: e.matmul(
                    B.pb[0 + oc % 2][:], A.wo[:, i * 128:(i + 1) * 128],
                    A.ao[i][:, t0:t0 + TJ], start=(i == 0), stop=(i == 3)),
                    reads=[r_("l1b_wo"), r_("l1b_ao", i)],
                    writes=[r_("pb", 0 + oc % 2)], tag="bwo_mm")
            if half == 0:
                S.op("vector", lambda e, oc=oc, t0=t0: e.scalar_tensor_tensor(
                    tok(B.x[oc], t0, TJ), B.pb[0 + oc % 2][:],
                    A.bo[:, oc:oc + 1], tok(B.x[oc], t0, TJ), ALU.add, ALU.add),
                    reads=[r_("pb", 0 + oc % 2), r_("l1b_bo"), r_("x", oc)],
                    writes=[r_("x", oc)], tag="bwo_res")
            else:
                S.op("vector", lambda e, oc=oc, t0=t0: e.tensor_add(
                    tok(B.x[oc], t0, TJ), tok(B.x[oc], t0, TJ),
                    B.pb[0 + oc % 2][:]),
                    reads=[r_("pb", 0 + oc % 2), r_("x", oc)],
                    writes=[r_("x", oc)], tag="bwo_res2")



C_L1F = 35.0
L1F_NAMES = ["L1_ffn_wqkv", "p_l1f_wqkvp", "L1_ffn_w1", "L1_ffn_w2",
             "p_l1f_wox", "p_l1f_qsv", "p_l1f_lg", "p_l1f_lb", "p_l1f_b1a",
             "p_l1f_b1g", "p_l1f_b2", "cst_xq_cos", "cst_xq_sin",
             "cst_xk_cos", "cst_xk_sin", "cst_l1_tri", "cst_geq"]


def alloc_l1f(nc, ctx, sfx=""):
    F = Bufs()
    ent = ctx.enter_context
    F.big = [ent(nc.sbuf_tensor(f"l1f_big{i}{sfx}", [128, T], BF16)) for i in range(8)]
    F.vt = [ent(nc.sbuf_tensor(f"l1f_vt{t}{sfx}", [128, 520], BF16)) for t in range(16)]
    F.aow = [ent(nc.sbuf_tensor(f"l1f_ao{i}{sfx}", [128, TJ], BF16)) for i in range(4)]
    F.hp = [ent(nc.sbuf_tensor(f"l1f_hp{c}{sfx}", [128, TJ], BF16)) for c in range(NCHUNK)]
    F.tc = ent(nc.sbuf_tensor(f"l1f_tc{sfx}", [128, T], BF16))
    F.ts = ent(nc.sbuf_tensor(f"l1f_ts{sfx}", [128, T], BF16))
    F.wa = [ent(nc.sbuf_tensor(f"l1f_wa{c}{sfx}", [128, 128], BF16)) for c in range(NCHUNK)]
    F.wb = [ent(nc.sbuf_tensor(f"l1f_wb{c}{sfx}", [128, 128], BF16)) for c in range(NCHUNK)]
    F.w2c = ent(nc.sbuf_tensor(f"l1f_w2c{sfx}", [128, 2816], BF16))
    F.wo = ent(nc.sbuf_tensor(f"l1f_wo{sfx}", [128, 512], BF16))
    F.praw = [ent(nc.sbuf_tensor(f"l1f_praw{i}{sfx}", [128, TJ], BF16)) for i in range(2)]
    F.t = [ent(nc.sbuf_tensor(f"l1f_t{i}{sfx}", [128, TJ], BF16)) for i in range(2)]
    F.pt = [ent(nc.sbuf_tensor(f"l1f_pt{i}{sfx}", [128, TJ], BF16)) for i in range(2)]
    F.rnbc = ent(nc.sbuf_tensor(f"l1f_rnbc{sfx}", [128, TJ], BF16))
    F.recbc = ent(nc.sbuf_tensor(f"l1f_recbc{sfx}", [64, TJ], BF16))
    F.rh2 = ent(nc.sbuf_tensor(f"l1f_rh2{sfx}", [1, 1024], BF16))
    F.geq = ent(nc.sbuf_tensor(f"l1f_geq{sfx}", [128, 128], BF16))
    F.tri = ent(nc.sbuf_tensor(f"l1f_tri{sfx}", [128, 128], BF16))
    F.qsv = ent(nc.sbuf_tensor(f"l1f_qsv{sfx}", [128, 8], F32))
    F.lg = ent(nc.sbuf_tensor(f"l1f_lg{sfx}", [128, 8], F32))
    F.lb = ent(nc.sbuf_tensor(f"l1f_lb{sfx}", [128, 8], F32))
    F.b1a = ent(nc.sbuf_tensor(f"l1f_b1a{sfx}", [128, 22], F32))
    F.b1g = ent(nc.sbuf_tensor(f"l1f_b1g{sfx}", [128, 22], F32))
    F.b2 = ent(nc.sbuf_tensor(f"l1f_b2{sfx}", [128, 8], F32))
    return F


def emit_l1f_setup(S, nc, B, F, W, scratch=True, consts=True):
    if consts:
        S.dma("sync", F.geq[:], W["cst_geq"], writes=[r_("l1f_geq")])
        S.dma("sync", F.tri[:], W["cst_l1_tri"], writes=[r_("l1f_tri")])
        S.dma("sync", F.qsv[:], W["p_l1f_qsv"], writes=[r_("l1f_qsv")])
    if not scratch:
        F.scr_x, F.scr_w1a, F.scr_w1g, F.scr_w2 = B.scr_l1f
        return
    F.scr_x = nc.dram_tensor("scr_xout", [NCHUNK, 128, T], F32,
                             kind="Internal").ap()
    F.scr_w1a = nc.dram_tensor("scr_w1a", [2, 22, 128, 8, 128], BF16,
                               kind="Internal").ap()
    F.scr_w1g = nc.dram_tensor("scr_w1g", [2, 22, 128, 8, 128], BF16,
                               kind="Internal").ap()
    F.scr_w2 = nc.dram_tensor("scr_w2f", [2, 8, 128, 22, 128], BF16,
                              kind="Internal").ap()
    B.scr_l1f = (F.scr_x, F.scr_w1a, F.scr_w1g, F.scr_w2)
    w1 = W["L1_ffn_w1"]
    w2 = W["L1_ffn_w2"]
    for l in range(2):
        for part, scr in ((0, F.scr_w1a), (1, F.scr_w1g)):
            for ns in range(22):
                n0 = part * 2730 + ns * 128
                wcols = min(128, part * 2730 + 2730 - n0)
                stg = F.w2c[:, 0:1024]
                if wcols < 128:
                    S.op("vector", lambda e, stg=stg: e.memset(stg, 0.0),
                         writes=[r_("l1f_w2c", 0)], tag="sw_z")
                dst = stg.rearrange("p (c j) -> p c j", c=8)[:, :, 0:wcols]
                S.dma("gpsimd", dst,
                      w1[l, :, n0:n0 + wcols].rearrange("(c p) j -> p c j", p=128),
                      writes=[r_("l1f_w2c", 0)], tag="sw_ld")
                S.dma("sync", scr[l, ns].rearrange("p c j -> p (c j)"), stg,
                      reads=[r_("l1f_w2c", 0)], writes=[r_("scrw1", part, l, ns)],
                      tag="sw_st")
        for i in range(22):
            k0 = i * 128
            krows = min(128, 2730 - k0)
            stg = F.w2c[:, 1024:2048]
            if krows < 128:
                S.op("vector", lambda e, stg=stg: e.memset(stg, 0.0),
                     writes=[r_("l1f_w2c", 1)], tag="sw_z2")
            S.dma("gpsimd", stg[0:krows, 0:1024], w2[l, k0:k0 + krows, :],
                  writes=[r_("l1f_w2c", 1)], tag="sw_ld2")
            S.dma("sync",
                  bass.AP(F.scr_w2.tensor,
                          l * (8 * 128 * 22 * 128) + i * 128,
                          [[22 * 128, 128], [128 * 22 * 128, 8], [1, 128]]),
                  stg.rearrange("p (oc j) -> p oc j", oc=8),
                  reads=[r_("l1f_w2c", 1)], writes=[r_("scrw2", l, i)], tag="sw_st2")


def emit_l1_ffn(S, nc, B, F, W):
    emit_ln(S, B, "ln")
    for c in range(NCHUNK):
        S.dma("sync", F.scr_x[c], tok(B.x[c], 0, T), reads=[r_("x", c)],
              writes=[r_("scrx", c)], tag="xspill")
    for c in range(NCHUNK):
        S.op("vector", lambda e, c=c: e.tensor_mul(
            tok(B.x[c], 0, T), tok(B.x[c], 0, T), B.b1bc[:]),
            reads=[r_("x", c), r_("b1bc"), r_("scrx", c)], writes=[r_("x", c)],
            tag="xi1")
        S.op("vector", lambda e, c=c: e.tensor_add(
            tok(B.x[c], 0, T), tok(B.x[c], 0, T), B.b2bc[:]),
            reads=[r_("x", c), r_("b2bc")], writes=[r_("x", c)], tag="xi2")
        S.op("vector", lambda e, c=c: e.tensor_copy(
            tok(B.h[c], 0, T), tok(B.x[c], 0, T)),
            reads=[r_("x", c)], writes=[r_("h", c)], tag="xb")
    for l in range(2):
        emit_l1f_layer(S, nc, B, F, W, l)
    for c in range(NCHUNK):
        S.dma("gpsimd", tok(B.x[c], 0, T), F.scr_x[c],
              reads=[r_("scrx", c), r_("x", c)], writes=[r_("x", c)],
              tag="xmerge", accum_op=ALU.add)


def _xsh(B, c, t0, n):
    if c >= 4:
        return B.h[c][:, t0:t0 + n]
    return tok(B.h[c], t0, n)


def emit_l1f_layer(S, nc, B, F, W, l):
    negC, negC_k = cvec(S, B, -C_L1F)
    bigk = lambda i, blk: r_("l1f_big", i, blk)
    allblk = lambda i: [r_("l1f_big", i, b_) for b_ in range(4)]

    S.dma("sync", F.lg[:], W["p_l1f_lg"][l], writes=[r_("l1f_lg")])
    S.dma("sync", F.lb[:], W["p_l1f_lb"][l], writes=[r_("l1f_lb")])
    S.dma("sync", F.b1a[:], W["p_l1f_b1a"][l], writes=[r_("l1f_b1a")])
    S.dma("sync", F.b1g[:], W["p_l1f_b1g"][l], writes=[r_("l1f_b1g")])
    S.dma("sync", F.b2[:], W["p_l1f_b2"][l], writes=[r_("l1f_b2")])

    for c in range(NCHUNK):
        S.dma("gpsimd", F.big[c][:, 0:512],
              W["L1_ffn_wqkv"][l][c * 128:(c + 1) * 128, 1024:1536],
              writes=[bigk(c, 0)], tag="fv_wld")
    for tt in range(16):
        vt = F.vt[tt]
        S.op("vector", lambda e, vt=vt: e.memset(
            vt.ap().rearrange("p (a b) -> p a b", b=65)[:, :, 64:65], 1.0),
            writes=[r_("l1f_vt", tt)], tag="fvones")
        for c in range(NCHUNK):
            S.op("tensor", lambda e, c=c, tt=tt: e.matmul(
                B.pb[6][:, 0:512], _xsh(B, c, tt * 128, 128),
                F.big[c][:, 0:512],
                start=(c == 0), stop=(c == NCHUNK - 1)),
                reads=[bigk(c, 0), r_("h", c)], writes=[r_("pb", 6)],
                tag="fvproj")
        S.op("scalar", lambda e, vt=vt: e.copy(
            vt.ap().rearrange("p (a b) -> p a b", b=65)[:, :, 0:64],
            B.pb[6][:, 0:512].rearrange("p (a b) -> p a b", b=64)),
            reads=[r_("pb", 6)], writes=[r_("l1f_vt", tt)], tag="fvevac")

    for side in range(2):
        tc, ts_ = F.tc, F.ts
        tck, tsk = r_("l1f_tc"), r_("l1f_ts")
        S.dma("sync", F.tc[:], W["cst_xq_cos" if side == 0 else "cst_xk_cos"],
              writes=[tck], tag="ftab_c")
        S.dma("sync", F.ts[:], W["cst_xq_sin" if side == 0 else "cst_xk_sin"],
              writes=[tsk], tag="ftab_s")
        qv = F.qsv[:, 4 * l + 2 * side:4 * l + 2 * side + 1]
        qpv = F.qsv[:, 4 * l + 2 * side + 1:4 * l + 2 * side + 2]
        for qc in range(4):
            big_i = 4 * side + qc
            n0 = 512 * side + 128 * qc
            _load_w_tiles(S, F.wa, "l1f_wa", W["L1_ffn_wqkv"][l], n0, 128)
            _load_w_tiles(S, F.wb, "l1f_wb", W["p_l1f_wqkvp"][l], n0, 128)
            for tj in range(NTJ):
                t0 = tj * TJ
                for c in range(NCHUNK):
                    S.op("tensor", lambda e, c=c, t0=t0: e.matmul(
                        B.pb[0][:], F.wa[c][:, 0:128], _xsh(B, c, t0, TJ),
                        start=(c == 0), stop=(c == NCHUNK - 1)),
                        reads=[r_("l1f_wa", c), r_("h", c)], writes=[r_("pb", 0)],
                        tag="fq_mm")
                    S.op("tensor", lambda e, c=c, t0=t0: e.matmul(
                        B.pb[1][:], F.wb[c][:, 0:128], _xsh(B, c, t0, TJ),
                        start=(c == 0), stop=(c == NCHUNK - 1)),
                        reads=[r_("l1f_wb", c), r_("h", c)], writes=[r_("pb", 1)],
                        tag="fq_mmp")
                S.op("scalar", lambda e: e.copy(F.praw[0][:], B.pb[0][:]),
                     reads=[r_("pb", 0)], writes=[r_("l1f_praw", 0)], tag="fq_e1")
                S.op("scalar", lambda e: e.copy(F.praw[1][:], B.pb[1][:]),
                     reads=[r_("pb", 1)], writes=[r_("l1f_praw", 1)], tag="fq_e2")
                S.op("scalar", lambda e: e.activation(
                    B.sq[0][:], F.praw[0][:], AF.Square),
                    reads=[r_("l1f_praw", 0)], writes=[r_("sq", 0)], tag="fq_sq")
                S.op("tensor", lambda e: e.matmul(
                    B.pb[5][0:1, :], B.onecol[0:64, :], B.sq[0][0:64, :],
                    start=True, stop=True),
                    reads=[r_("sq", 0), r_("onecol")], writes=[r_("pb", 5)],
                    tag="fq_n1", pe_base=0)
                S.op("tensor", lambda e: e.matmul(
                    B.pb[6][0:1, :], B.onecol[64:128, :], B.sq[0][64:128, :],
                    start=True, stop=True),
                    reads=[r_("sq", 0), r_("onecol")], writes=[r_("pb", 6)],
                    tag="fq_n2", pe_base=64)
                eps0, eps0_k = cvec(S, B, 1e-24, 0, 1)
                mh_ap, mh_k = cvec(S, B, -0.5, 0, 1)
                for bi, bank in ((0, 5), (1, 6)):
                    S.op("scalar", lambda e, bank=bank, a=eps0: e.activation(
                        B.pb[bank][0:1, :], B.pb[bank][0:1, :], AF.Ln, bias=a),
                        reads=[r_("pb", bank), eps0_k], writes=[r_("pb", bank)],
                        tag="fq_ln")
                    S.op("scalar", lambda e, bank=bank, bi=bi, a=mh_ap:
                         e.activation(F.rh2[:, 512 * bi:512 * bi + 512],
                                      B.pb[bank][0:1, :], AF.Exp, scale=a),
                         reads=[r_("pb", bank), mh_k], writes=[r_("l1f_rh2", bi)],
                         tag="fq_ex")
                S.op("tensor", lambda e: e.matmul(
                    B.pb[7][0:64, :], B.ones_row[:, 0:64], F.rh2[:, 0:512],
                    start=True, stop=True),
                    reads=[r_("l1f_rh2", 0), r_("ones_row")], writes=[r_("pb", 7)],
                    tag="fq_b1")
                S.op("tensor", lambda e: e.matmul(
                    B.pb[7][64:128, :], B.ones_row[:, 0:64], F.rh2[:, 512:1024],
                    start=True, stop=True),
                    reads=[r_("l1f_rh2", 1), r_("ones_row")], writes=[r_("pb", 7)],
                    tag="fq_b2")
                S.op("scalar", lambda e: e.copy(F.rnbc[:], B.pb[7][:]),
                     reads=[r_("pb", 7)], writes=[r_("l1f_rnbc")], tag="fq_bc")
                S.op("vector", lambda e, t0=t0, qv=qv: e.scalar_tensor_tensor(
                    F.t[0][:], F.praw[0][:], qv, tc[:, t0:t0 + TJ],
                    ALU.mult, ALU.mult),
                    reads=[r_("l1f_praw", 0), r_("l1f_qsv"), tck],
                    writes=[r_("l1f_t", 0)], tag="fq_p1")
                S.op("vector", lambda e, t0=t0, qpv=qpv: e.scalar_tensor_tensor(
                    F.t[1][:], F.praw[1][:], qpv, ts_[:, t0:t0 + TJ],
                    ALU.mult, ALU.mult),
                    reads=[r_("l1f_praw", 1), r_("l1f_qsv"), tsk],
                    writes=[r_("l1f_t", 1)], tag="fq_p2")
                S.op("vector", lambda e: e.tensor_add(
                    F.t[0][:], F.t[0][:], F.t[1][:]),
                    reads=[r_("l1f_t", 0), r_("l1f_t", 1)], writes=[r_("l1f_t", 0)],
                    tag="fq_p3")
                S.op("vector", lambda e, big_i=big_i, t0=t0: e.tensor_mul(
                    F.big[big_i][:, t0:t0 + TJ], F.t[0][:], F.rnbc[:]),
                    reads=[r_("l1f_t", 0), r_("l1f_rnbc")],
                    writes=[bigk(big_i, t0 // TJ)], tag="fq_p4")

    for wb in range(4):
        q0 = wb * TJ
        for h in range(8):
            qc, rs = h // 2, 64 * (h % 2)
            qbig = F.big[qc]
            kbig = F.big[4 + qc]
            tiles = [(4 * wb + rp, 128 * rp, TJ, "cur") for rp in range(4)]
            if wb > 0:
                tiles += [(4 * (wb - 1) + rp, 0, 128 * (rp + 1), "prev")
                          for rp in range(4)]
            n_pv = len(tiles)
            for pi, (kt, qlo, qhi, kind) in enumerate(tiles):
                w = qhi - qlo
                pt = F.pt[pi % 2]
                ptk = r_("l1f_pt", pi % 2)
                ps_S = 2 + (pi % 2) * 2
                S.op("tensor", lambda e, kt=kt, qlo=qlo, w=w, rs=rs, ps_S=ps_S,
                     qbig=qbig, kbig=kbig, q0=q0: e.matmul(
                         B.pb[ps_S][:, 0:w],
                         kbig[rs:rs + 64, kt * 128:(kt + 1) * 128],
                         qbig[rs:rs + 64, q0 + qlo:q0 + qlo + w],
                         start=True, stop=True),
                    reads=[bigk(4 + qc, kt // 4), bigk(qc, wb)],
                    writes=[r_("pb", ps_S)], tag="fs_mm", pe_base=rs)
                S.op("scalar", lambda e, w=w, pt=pt, ps_S=ps_S: e.activation(
                    pt[:, 0:w], B.pb[ps_S][:, 0:w], AF.Exp, bias=negC),
                    reads=[r_("pb", ps_S), negC_k], writes=[ptk], tag="fs_exp")
                if kind == "cur":
                    S.op("vector", lambda e, pt=pt: e.tensor_tensor(
                        pt[:, 0:128], pt[:, 0:128], F.tri[:], ALU.min),
                        reads=[ptk, r_("l1f_tri")], writes=[ptk], tag="fs_mt")
                else:
                    S.op("vector", lambda e, pt=pt, w=w: e.tensor_tensor(
                        pt[:, w - 128:w], pt[:, w - 128:w], F.geq[:], ALU.min),
                        reads=[ptk, r_("l1f_geq")], writes=[ptk], tag="fs_mg")
                S.op("tensor", lambda e, kt=kt, h=h, pt=pt, qlo=qlo, w=w,
                     pi=pi, n_pv=n_pv: e.matmul(
                         B.pb[3][0:65, qlo:qlo + w],
                         F.vt[kt][:, 65 * h:65 * h + 65], pt[:, 0:w],
                         start=(pi == 0), stop=(pi == n_pv - 1)),
                    reads=[r_("l1f_vt", kt), ptk], writes=[r_("pb", 3)],
                    tag="fpv")
            rec = B.rowf[:, 3 * TJ:4 * TJ]
            rec_k = r_("rowf", 3)
            S.op("vector", lambda e, rec=rec: e.reciprocal(
                rec, B.pb[3][64:65, 0:TJ]),
                reads=[r_("pb", 3)], writes=[rec_k], tag="fn_rec")
            S.op("vector", lambda e, rec=rec: e.tensor_copy(B.rowh[:], rec),
                 reads=[rec_k], writes=[r_("rowh")], tag="fn_cast")
            S.op("tensor", lambda e: e.matmul(
                B.pb[7][0:64, 0:TJ], B.ones_row[:, 0:64], B.rowh[:],
                start=True, stop=True),
                reads=[r_("rowh"), r_("ones_row")], writes=[r_("pb", 7)],
                tag="fn_bcmm")
            S.op("scalar", lambda e: e.copy(F.recbc[:], B.pb[7][0:64, 0:TJ]),
                 reads=[r_("pb", 7)], writes=[r_("l1f_recbc")], tag="fn_bcev")
            S.op("vector", lambda e, qc=qc, rs=rs: e.tensor_mul(
                F.aow[qc][rs:rs + 64, :], B.pb[3][0:64, 0:TJ], F.recbc[:]),
                reads=[r_("pb", 3), r_("l1f_recbc")], writes=[r_("l1f_aow", qc)],
                tag="fn_mul")
        for oc in range(NCHUNK):
            S.dma("gpsimd", F.wo[:],
                  W["p_l1f_wox"][l, oc].rearrange("p a j -> p (a j)"),
                  writes=[r_("l1f_wo")], tag="fwo_ld")
            for i in range(4):
                S.op("tensor", lambda e, i=i, oc=oc: e.matmul(
                    B.pb[0 + oc % 2][:], F.wo[:, i * 128:(i + 1) * 128],
                    F.aow[i][:], start=(i == 0), stop=(i == 3)),
                    reads=[r_("l1f_wo"), r_("l1f_aow", i)],
                    writes=[r_("pb", 0 + oc % 2)], tag="fwo_mm")
            S.op("vector", lambda e, oc=oc, q0=q0: e.tensor_add(
                tok(B.x[oc], q0, TJ), tok(B.x[oc], q0, TJ),
                B.pb[0 + oc % 2][:]),
                reads=[r_("pb", 0 + oc % 2), r_("x", oc)], writes=[r_("x", oc)],
                tag="fwo_res")
    for c in range(NCHUNK):
        S.op("vector", lambda e, c=c: e.tensor_copy(
            tok(B.h[c], 0, T), tok(B.x[c], 0, T)),
            reads=[r_("x", c)], writes=[r_("h", c)], tag="fxb")

    emit_l1f_ffn(S, nc, B, F, W, l)


def emit_l1f_ffn(S, nc, B, F, W, l):
    pb_s0, pb_s1 = B.pb[5], B.pb[6]
    eps_ap, eps_k = cvec(S, B, EPS, 0, 1)
    mh_ap, mh_k = cvec(S, B, -0.5, 0, 1)
    for tj in range(NTJ):
        t0 = tj * TJ
        for c in range(NCHUNK):
            sqb = B.sq[c % 2]
            S.op("scalar", lambda e, c=c, t0=t0, sqb=sqb: e.activation(
                sqb[:], _xsh(B, c, t0, TJ), AF.Square),
                reads=[r_("h", c)], writes=[r_("sq", c % 2)], tag="fl_sq")
            S.op("tensor", lambda e, c=c, t0=t0: e.matmul(
                pb_s0[0:1, :], B.onecol[:], _xsh(B, c, t0, TJ),
                start=(c == 0), stop=(c == NCHUNK - 1)),
                reads=[r_("h", c), r_("onecol")], writes=[r_("pb", 5)],
                tag="fl_mm0")
            S.op("tensor", lambda e, sqb=sqb, c=c: e.matmul(
                pb_s1[0:1, :], B.onecol[:], sqb[:],
                start=(c == 0), stop=(c == NCHUNK - 1)),
                reads=[r_("sq", c % 2), r_("onecol")], writes=[r_("pb", 6)],
                tag="fl_mm1")

        def slot(i):
            return B.rowf[:, i * TJ:(i + 1) * TJ]

        mu, b1r, b2r = slot(0), slot(1), slot(2)
        tmp = slot(4)
        K_mu, K_b1, K_b2 = (r_("rowf", i) for i in (0, 1, 2))
        K_tmp = r_("rowf", 4)
        S.op("vector", lambda e, mu=mu: e.tensor_scalar(
            mu, pb_s0[0:1, :], 1.0 / D, None, ALU.mult),
            reads=[r_("pb", 5)], writes=[K_mu], tag="fl_mu")
        S.op("vector", lambda e: e.tensor_scalar(
            pb_s1[0:1, :], pb_s1[0:1, :], 1.0 / D, None, ALU.mult),
            reads=[r_("pb", 6)], writes=[r_("pb", 6)], tag="fl_ms")
        S.op("vector", lambda e, mu=mu, tmp=tmp: e.tensor_mul(tmp, mu, mu),
             reads=[K_mu], writes=[K_tmp], tag="fl_mu2")
        S.op("vector", lambda e, tmp=tmp: e.tensor_sub(
            pb_s1[0:1, :], pb_s1[0:1, :], tmp),
            reads=[K_tmp, r_("pb", 6)], writes=[r_("pb", 6)], tag="fl_var")
        S.op("scalar", lambda e, a=eps_ap: e.activation(
            pb_s1[0:1, :], pb_s1[0:1, :], AF.Ln, bias=a),
            reads=[r_("pb", 6), eps_k], writes=[r_("pb", 6)], tag="fl_ln")
        S.op("scalar", lambda e, b1r=b1r, a=mh_ap: e.activation(
            b1r, pb_s1[0:1, :], AF.Exp, scale=a),
            reads=[r_("pb", 6), mh_k], writes=[K_b1], tag="fl_ex")
        S.op("vector", lambda e, mu=mu, b1r=b1r, b2r=b2r: e.scalar_tensor_tensor(
            b2r, mu, -1.0, b1r, ALU.mult, ALU.mult),
            reads=[K_mu, K_b1], writes=[K_b2], tag="fl_b2")
        _bc_row(S, B, b1r, K_b1, B.b1bc, r_("b1bc"), t0)
        _bc_row(S, B, b2r, K_b2, B.b2bc, r_("b2bc"), t0)
    for tj in range(NTJ):
        t0 = tj * TJ
        for c in range(NCHUNK):
            g = F.lg[:, c:c + 1]
            bt = F.lb[:, c:c + 1]
            S.op("vector", lambda e, c=c, t0=t0, g=g: e.scalar_tensor_tensor(
                F.hp[c][:], _xsh(B, c, t0, TJ), g, B.b1bc[:, t0:t0 + TJ],
                ALU.mult, ALU.mult),
                reads=[r_("h", c), r_("b1bc"), r_("l1f_lg")],
                writes=[r_("l1f_hp", c)], tag="fh_1")
            S.op("vector", lambda e, c=c, t0=t0, g=g: e.scalar_tensor_tensor(
                F.hp[c][:], B.b2bc[:, t0:t0 + TJ], g, F.hp[c][:],
                ALU.mult, ALU.add),
                reads=[r_("l1f_hp", c), r_("b2bc"), r_("l1f_lg")],
                writes=[r_("l1f_hp", c)], tag="fh_2")
            S.op("vector", lambda e, c=c, bt=bt: e.tensor_scalar(
                F.hp[c][:], F.hp[c][:], bt, None, ALU.add),
                reads=[r_("l1f_hp", c), r_("l1f_lb")], writes=[r_("l1f_hp", c)],
                tag="fh_3")
        for ns in range(22):
            zb = F.big[ns // 4]
            zcol = 512 * (ns % 4)
            zkey = bigk_g(ns)
            w1a = F.w2c[:, 0:1024]
            w1g = F.w2c[:, 1024:2048]
            S.dma("sync", w1a, F.scr_w1a[l, ns].rearrange("p c j -> p (c j)"),
                  reads=[r_("scrw1", 0, l, ns)], writes=[r_("l1f_w2c", 0)],
                  tag="f1_lda")
            S.dma("sync", w1g, F.scr_w1g[l, ns].rearrange("p c j -> p (c j)"),
                  reads=[r_("scrw1", 1, l, ns)], writes=[r_("l1f_w2c", 1)],
                  tag="f1_ldg")
            for c in range(NCHUNK):
                S.op("tensor", lambda e, c=c, w1a=w1a: e.matmul(
                    B.pb[0][:], w1a[:, c * 128:(c + 1) * 128], F.hp[c][:],
                    start=(c == 0), stop=(c == NCHUNK - 1)),
                    reads=[r_("l1f_w2c", 0), r_("l1f_hp", c)], writes=[r_("pb", 0)],
                    tag="f1_mma")
                S.op("tensor", lambda e, c=c, w1g=w1g: e.matmul(
                    B.pb[1][:], w1g[:, c * 128:(c + 1) * 128], F.hp[c][:],
                    start=(c == 0), stop=(c == NCHUNK - 1)),
                    reads=[r_("l1f_w2c", 1), r_("l1f_hp", c)], writes=[r_("pb", 1)],
                    tag="f1_mmg")
            S.op("scalar", lambda e, ns=ns: e.activation(
                F.t[0][:], B.pb[1][:], AF.Gelu, bias=F.b1g[:, ns:ns + 1]),
                reads=[r_("pb", 1), r_("l1f_b1g")], writes=[r_("l1f_t", 0)],
                tag="f1_gelu")
            S.op("vector", lambda e, ns=ns, zb=zb, zcol=zcol:
                 e.scalar_tensor_tensor(
                     zb[:, zcol:zcol + 512], B.pb[0][:], F.b1a[:, ns:ns + 1],
                     F.t[0][:], ALU.add, ALU.mult),
                 reads=[r_("pb", 0), r_("l1f_b1a"), r_("l1f_t", 0)],
                 writes=[zkey], tag="f1_z")
        for oc in range(NCHUNK):
            S.dma("sync", F.w2c[:], F.scr_w2[l, oc].rearrange("p i j -> p (i j)"),
                  reads=[r_("scrw2", l, i_) for i_ in range(22)],
                  writes=[r_("l1f_w2c", 0), r_("l1f_w2c", 1), r_("l1f_w2c", 2)],
                  tag="f2_ld")
            for i in range(22):
                S.op("tensor", lambda e, i=i, oc=oc: e.matmul(
                    B.pb[2 + oc % 2][:], F.w2c[:, i * 128:(i + 1) * 128],
                    F.big[i // 4][:, 512 * (i % 4):512 * (i % 4) + 512],
                    start=(i == 0), stop=(i == 21)),
                    reads=[r_("l1f_w2c", min(i // 8, 2)), bigk_g(i)],
                    writes=[r_("pb", 2 + oc % 2)], tag="f2_mm")
            S.op("vector", lambda e, oc=oc, t0=t0: e.scalar_tensor_tensor(
                tok(B.x[oc], t0, TJ), B.pb[2 + oc % 2][:], F.b2[:, oc:oc + 1],
                tok(B.x[oc], t0, TJ), ALU.add, ALU.add),
                reads=[r_("pb", 2 + oc % 2), r_("l1f_b2"), r_("x", oc)],
                writes=[r_("x", oc)], tag="f2_res")
    for c in range(NCHUNK):
        S.op("vector", lambda e, c=c: e.tensor_copy(
            tok(B.h[c], 0, T), tok(B.x[c], 0, T)),
            reads=[r_("x", c)], writes=[r_("h", c)], tag="fxb2")


def bigk_g(ns):
    return r_("l1f_big", ns // 4, ns % 4)



def _sub_l0a(S, nc, B, A, W):
    emit_ln(S, B, "ln")
    emit_l0_attn_a(S, nc, B, A, W)


def _sub_l0f(S, nc, B, F, W):
    emit_l0_ffn(S, nc, B, F, W)


SUBLAYERS = {
    "l0a": (L0A_NAMES, alloc_l0a, emit_l0a_setup, _sub_l0a),
    "l0f": (L0F_NAMES, alloc_l0f, emit_l0f_setup, _sub_l0f),
    "l0b": (L0B_NAMES, alloc_l0b, emit_l0b_setup,
            lambda S, nc, B, A, W: emit_l0_attn_b(S, nc, B, A, W)),
    "l1a": (L1A_NAMES, alloc_l1a, emit_l1a_setup,
            lambda S, nc, B, A, W: emit_l1_attn_a(S, nc, B, A, W)),
    "l1b": (L1B_NAMES, alloc_l1b, emit_l1b_setup,
            lambda S, nc, B, A, W: emit_l1_attn_b(S, nc, B, A, W)),
    "l1f": (L1F_NAMES, alloc_l1f, emit_l1f_setup,
            lambda S, nc, B, A, W: emit_l1_ffn(S, nc, B, A, W)),
}


def build_chain_program(sub_ids, arrays, n_seq=NSEQ, strict=False):
    nc = bass.Bass("TRN2", target_bir_lowering=False)
    x_in, x_out = _declare_x_io(nc)
    names = []
    for sid in sub_ids:
        for n in SUBLAYERS[sid][0]:
            if n not in names:
                names.append(n)
    W = declare_inputs(nc, arrays, names)
    with ExitStack() as ctx:
        B = alloc_common(nc, ctx)
        S = Sched(nc, strict=strict)
        emit_init_consts(S, B)
        bufs = {}
        for sid in dict.fromkeys(sub_ids):
            _, alloc, setup, _ = SUBLAYERS[sid]
            bufs[sid] = alloc(nc, ctx)
            setup(S, nc, B, bufs[sid], W)
        for seq in range(n_seq):
            emit_load_x(S, B, x_in, seq)
            for sid in sub_ids:
                SUBLAYERS[sid][3](S, nc, B, bufs[sid], W)
            emit_store_x(S, B, x_out, seq)
        S.finalize()
    return nc, S, names


CHAIN = ["l0a", "l0f", "l0b", "l0f", "l1a", "l1f", "l1b", "l1f"]
LAUNCHES = [[sid] for sid in CHAIN]
FUSED = True


def build_fused_program(arrays, n_seq=NSEQ, strict=False):
    nc = bass.Bass("TRN2", target_bir_lowering=False)
    x_in, x_out = _declare_x_io(nc)
    names = []
    for sid in CHAIN:
        for n in SUBLAYERS[sid][0]:
            if n not in names:
                names.append(n)
    W = declare_inputs(nc, arrays, names)
    with ExitStack() as ctx:
        B = alloc_common(nc, ctx)
        S = Sched(nc, strict=strict)
        emit_init_consts(S, B)
        with ExitStack() as pctx:
            Fp0 = alloc_l0f(nc, pctx, "_p0")
            emit_l0f_setup(S, nc, B, Fp0, W, scratch=True, consts=False)
        with ExitStack() as pctx:
            Fp1 = alloc_l1f(nc, pctx, "_p1")
            emit_l1f_setup(S, nc, B, Fp1, W, scratch=True, consts=False)
        S.barrier()
        inst = 0
        for seq in range(n_seq):
            emit_load_x(S, B, x_in, seq)
            for sid in CHAIN:
                S.barrier()
                with ExitStack() as ictx:
                    _, alloc, setup, body = SUBLAYERS[sid]
                    A = alloc(nc, ictx, f"_i{inst}")
                    if sid in ("l0f", "l1f"):
                        setup(S, nc, B, A, W, scratch=False, consts=True)
                    else:
                        setup(S, nc, B, A, W)
                    body(S, nc, B, A, W)
                inst += 1
            emit_store_x(S, B, x_out, seq)
        S.finalize()
    return nc, S, names


def kernel(**inputs):
    arrays = dict(inputs)
    arrays.update(host_prep(inputs))
    xsh = shard_x(np.asarray(inputs["L0_attn_a_x"], dtype=np.float32))
    if FUSED:
        nc, S, names = build_fused_program(arrays)
        in_maps = []
        for ci in range(8):
            m = {n: arrays[n] for n in names}
            m["x_io_in"] = xsh[ci]
            in_maps.append(m)
        res = run_bass_kernel_spmd(nc, in_maps, core_ids=list(range(8)))
        out = np.empty((32, T, D), np.float32)
        for ci in range(8):
            out[4 * ci:4 * ci + 4] = (np.asarray(res.results[ci]["x_io_out"])
                                      .reshape(NSEQ, D, T).transpose(0, 2, 1))
        return out
    built = {}
    for group in LAUNCHES:
        key = tuple(group)
        if key not in built:
            built[key] = build_chain_program(group, arrays)
        nc, S, names = built[key]
        in_maps = []
        for ci in range(8):
            m = {n: arrays[n] for n in names}
            m["x_io_in"] = xsh[ci]
            in_maps.append(m)
        res = run_bass_kernel_spmd(nc, in_maps, core_ids=list(range(8)))
        xsh = [np.asarray(res.results[ci]["x_io_out"]) for ci in range(8)]
    out = np.empty((32, T, D), np.float32)
    for ci in range(8):
        out[4 * ci:4 * ci + 4] = (
            xsh[ci].reshape(NSEQ, D, T).transpose(0, 2, 1))
    return out


# version 04
# speedup vs baseline: 1.2727x; 1.2307x over previous
import math
from contextlib import ExitStack

import numpy as np
import ml_dtypes
import concourse.bass as bass
import concourse.mybir as mybir
from concourse.bass_utils import run_bass_kernel_spmd

F32 = mybir.dt.float32
BF16 = mybir.dt.bfloat16
I32 = mybir.dt.int32
AF = mybir.ActivationFunctionType
ALU = mybir.AluOpType
AX = mybir.AxisListType
bf16 = ml_dtypes.bfloat16

T = 2048
TG = T + 4
D = 1024
NCHUNK = 8
NSEQ = 4
TJ = 512
NTJ = T // TJ
EPS = 1e-5

C_L0A = 20.0
C_L0B = 30.0
C_L1A = 25.0
C_L1B = 25.0



def _rope_flavA():
    nb = 16
    inv_freq = 1.0 / (10000.0 ** (np.arange(nb, dtype=np.float64) / nb))
    ang = np.arange(T, dtype=np.float64)[:, None] * inv_freq[None, :]
    sin_h = np.repeat(np.sin(ang), 2, axis=-1)
    cos_h = np.repeat(np.cos(ang), 2, axis=-1)
    return (np.concatenate([cos_h, cos_h], axis=-1),
            np.concatenate([sin_h, sin_h], axis=-1))


def _rope_halves(dim=64, theta=10000.0):
    inv_freq = 1.0 / (theta ** (np.arange(0, dim, 2, dtype=np.float64) / dim))
    freqs = np.arange(T, dtype=np.float64)[:, None] * inv_freq[None, :]
    emb = np.concatenate([freqs, freqs], axis=-1)
    return np.cos(emb), np.sin(emb)


def _rope_interleaved(d_k=64, theta=10000.0):
    half = d_k // 2
    inv_freq = theta ** (-np.arange(half, dtype=np.float64) * 2.0 / d_k)
    ang = np.arange(T, dtype=np.float64)[:, None] * inv_freq[None, :]
    return np.repeat(np.cos(ang), 2, axis=-1), np.repeat(np.sin(ang), 2, axis=-1)


def _xpos_tables():
    DIM_HEAD, WINDOW = 64, 512
    half_idx = np.arange(0, DIM_HEAD, 2, dtype=np.float64)
    inv_freq = 1.0 / (10000.0 ** (half_idx / DIM_HEAD))
    t = np.arange(T, dtype=np.float64)
    freqs = t[:, None] * inv_freq[None, :]
    freqs = np.concatenate([freqs, freqs], axis=-1)
    base = (half_idx + 0.4 * DIM_HEAD) / (1.4 * DIM_HEAD)
    power = (t - T // 2) / float(WINDOW // 2)
    scale = base[None, :] ** power[:, None]
    scale = np.concatenate([scale, scale], axis=-1)
    return np.cos(freqs), np.sin(freqs), scale


def _sign_halves(dim=64):
    s = np.ones(dim)
    s[dim // 2:] = 1.0
    s[: dim // 2] = -1.0
    return s


def _sign_interleaved(dim=64):
    s = np.ones(dim)
    s[0::2] = -1.0
    return s


def _permA_idx(n):
    h = n % 64
    return n - h + ((h + 32) % 64)


def _permI_idx(n):
    return n ^ 1


def _tile2(a_t64):
    at = np.ascontiguousarray(a_t64.T)
    return np.concatenate([at, at], axis=0)


def host_constants():
    C = {}
    cosA, sinA = _rope_flavA()
    sgA = _sign_halves()
    C["cst_cosA"] = _tile2(cosA).astype(bf16)
    C["cst_sinA"] = _tile2(sinA * sgA[None, :]).astype(bf16)
    cosB, sinB = _rope_halves(64)
    C["cst_cosB_q"] = np.ascontiguousarray(cosB.T).astype(bf16)
    C["cst_sinB_q"] = np.ascontiguousarray((sinB * sgA[None, :]).T).astype(bf16)
    C["cst_cosB_k"] = np.ascontiguousarray((cosB / 16.0).T).astype(bf16)
    C["cst_sinB_k"] = np.ascontiguousarray(
        (sinB * sgA[None, :] / 16.0).T).astype(bf16)
    cosI, sinI = _rope_interleaved()
    sgI = _sign_interleaved()
    C["cst_cosI_q"] = _tile2(cosI / 8.0).astype(bf16)
    C["cst_sinI_q"] = _tile2(sinI * sgI[None, :] / 8.0).astype(bf16)
    C["cst_cosI_k"] = _tile2(cosI).astype(bf16)
    C["cst_sinI_k"] = _tile2(sinI * sgI[None, :]).astype(bf16)
    C["cst_cosH_q"] = _tile2(cosB / 8.0).astype(bf16)
    C["cst_sinH_q"] = _tile2(sinB * sgA[None, :] / 8.0).astype(bf16)
    C["cst_cosH_k"] = _tile2(cosB).astype(bf16)
    C["cst_sinH_k"] = _tile2(sinB * sgA[None, :]).astype(bf16)
    cosX, sinX, scX = _xpos_tables()
    C["cst_xq_cos"] = _tile2(cosX * scX * 8.0).astype(bf16)
    C["cst_xq_sin"] = _tile2(sinX * scX * sgA[None, :] * 8.0).astype(bf16)
    C["cst_xk_cos"] = _tile2(cosX / scX).astype(bf16)
    C["cst_xk_sin"] = _tile2(sinX / scX * sgA[None, :]).astype(bf16)
    return C



class _Op:
    __slots__ = ("eng", "emit", "reads", "writes", "dma", "idx", "deps",
                 "marked", "inc_total", "tag", "pe_base", "dma_idx")

    def __init__(self, eng, emit, reads, writes, dma, idx, tag, pe_base=0):
        self.eng, self.emit, self.dma, self.idx = eng, emit, dma, idx
        self.reads, self.writes = list(reads), list(writes)
        self.deps = set()
        self.marked = False
        self.inc_total = None
        self.tag = tag
        self.pe_base = pe_base
        self.dma_idx = None


class Sched:
    def __init__(self, nc, strict=False):
        self.nc = nc
        self.strict = strict
        self.ops = []
        self.last_writer = {}
        self.readers = {}
        self.dma_counts = {}
        self.last_per_stream = {}
        self.frontier = None

    def op(self, eng, emit, reads=(), writes=(), dma=False, tag="", pe_base=0):
        o = _Op(eng, emit, reads, writes, dma, len(self.ops), tag, pe_base)
        if dma:
            i = self.dma_counts.get(eng, 0)
            self.dma_counts[eng] = i + 1
            o.dma_idx = i
            self.last_per_stream[(eng, True, i % self.NSEM_DMA)] = o.idx
        else:
            self.last_per_stream[(eng, False, 0)] = o.idx
        if self.frontier is not None:
            o.deps.update(self.frontier)
        for k in o.reads:
            if k in self.last_writer:
                o.deps.add(self.last_writer[k])
        for k in o.writes:
            if k in self.last_writer:
                o.deps.add(self.last_writer[k])
            for r in self.readers.get(k, ()):
                o.deps.add(r)
        for k in o.reads:
            self.readers.setdefault(k, []).append(o.idx)
        for k in o.writes:
            self.last_writer[k] = o.idx
            self.readers[k] = []
        self.ops.append(o)
        return o.idx

    def dma(self, eng, out, in_, reads=(), writes=(), tag="", **kw):
        return self.op(eng, (lambda e: e.dma_start(out=out, in_=in_, **kw)),
                       reads=reads, writes=writes, dma=True, tag=tag)

    NSEM_DMA = 6

    def barrier(self):
        self.frontier = set(self.last_per_stream.values())

    def finalize(self, final_wait_engine="sync"):
        nc = self.nc
        ops = self.ops
        N = self.NSEM_DMA
        if not self.strict:
            for o in ops:
                o.deps = {d for d in o.deps
                          if not (ops[d].eng == o.eng == "tensor"
                                  and not ops[d].dma and not o.dma)}
        prev_t = None
        for o in ops:
            if o.eng == "tensor" and not o.dma:
                if prev_t is not None and ops[prev_t].pe_base != o.pe_base:
                    o.deps.add(prev_t)
                prev_t = o.idx
        dma_idx = dict(self.dma_counts)
        for o in ops:
            if o.dma:
                o.inc_total = o.dma_idx
        for o in ops:
            best = {}
            for d in o.deps:
                od = ops[d]
                key = (od.eng, od.dma, od.inc_total % N if od.dma else 0)
                if key not in best or d > best[key]:
                    best[key] = d
            o.deps = set(best.values())
        for o in ops:
            for d in o.deps:
                ops[d].marked = True
        lastof = {}
        for o in ops:
            if not o.dma:
                lastof[o.eng] = o
        for o in lastof.values():
            o.marked = True
        counters = {}
        for o in ops:
            if o.dma:
                continue
            if o.marked:
                key = (o.eng, False)
                counters[key] = counters.get(key, 0) + 1
                o.inc_total = counters[key]

        sems = {}
        cms = []

        def get_sem(key, name):
            if key not in sems:
                cm = nc.semaphore(name)
                sems[key] = cm.__enter__()
                cms.append(cm)
            return sems[key]

        for key in counters:
            get_sem((key[0], False, 0), f"sem_{key[0]}_ex")
        for eng, cnt in dma_idx.items():
            for s in range(min(N, cnt)):
                get_sem((eng, True, s), f"sem_{eng}_dma{s}")

        watermark = {}
        blk_cm = nc.Block()
        block = blk_cm.__enter__()

        def wait_for(eng, eng_name, od):
            if od.dma:
                skey = (od.eng, True, od.inc_total % N)
                val = 16 * (od.inc_total // N + 1)
            else:
                skey = (od.eng, False, 0)
                val = od.inc_total
            w = (eng_name, skey)
            if watermark.get(w, -1) >= val:
                return
            eng.wait_ge(sems[skey], val)
            watermark[w] = val

        def mk_section(eng_name):
            def section(eng):
                for o in ops:
                    if o.eng != eng_name:
                        continue
                    for d in sorted(o.deps):
                        wait_for(eng, eng_name, ops[d])
                    if o.dma:
                        i = o.inc_total
                        if i >= N:
                            skey = (o.eng, True, i % N)
                            val = 16 * (i // N)
                            w = (eng_name, skey)
                            if watermark.get(w, -1) < val:
                                eng.wait_ge(sems[skey], val)
                                watermark[w] = val
                    inst = o.emit(eng)
                    if o.dma:
                        inst.then_inc(sems[(o.eng, True, o.inc_total % N)], 16)
                    elif o.marked:
                        inst.then_inc(sems[(o.eng, False, 0)], 1)
                if eng_name == final_wait_engine:
                    for (ename, isdma, slot), sem in sems.items():
                        if (ename, isdma) == (eng_name, False):
                            continue
                        if isdma:
                            cnt = dma_idx[ename]
                            rounds = (cnt - slot + N - 1) // N
                            eng.wait_ge(sem, 16 * rounds)
                        else:
                            eng.wait_ge(sem, counters[(ename, False)])
            return section

        for name in sorted({o.eng for o in ops}):
            getattr(block, name)(mk_section(name))
        blk_cm.__exit__(None, None, None)
        for cm in cms:
            cm.__exit__(None, None, None)
        self.n_ops = len(ops)


def r_(name, *idx):
    return (name,) + tuple(idx)


def tok(ap, t0, n):
    return ap[:, 1 + t0:1 + t0 + n]



class Bufs:
    pass


def alloc_common(nc, ctx):
    B = Bufs()
    ent = ctx.enter_context
    B.x = [ent(nc.sbuf_tensor(f"x{c}", [128, TG], F32)) for c in range(NCHUNK)]
    B.h = [ent(nc.sbuf_tensor(f"h{c}", [128, TG], BF16)) for c in range(NCHUNK)]
    B.b1bc = ent(nc.sbuf_tensor("b1bc", [128, T], BF16))
    B.b2bc = ent(nc.sbuf_tensor("b2bc", [128, T], BF16))
    B.b3bc = None
    B.b4bc = None
    B.rowf = ent(nc.sbuf_tensor("rowf", [1, 5 * TJ], F32))
    B.rowh = ent(nc.sbuf_tensor("rowh", [1, TJ], BF16))
    B.pb = [ent(nc.psum_tensor(f"pb{i}", [128, 512], F32)) for i in range(8)]
    B.ones_row = ent(nc.sbuf_tensor("ones_row", [1, 128], BF16))
    B.onecol = ent(nc.sbuf_tensor("onecol", [128, 1], BF16))
    B.sq = [ent(nc.sbuf_tensor(f"sq{i}", [128, TJ], BF16)) for i in range(2)]
    B.consts = ent(nc.sbuf_tensor("consts", [128, 16], F32))
    B.const_cols = {}
    return B


def cvec(S, B, val, p0=0, p=128):
    if val not in B.const_cols:
        col = len(B.const_cols)
        assert col < 16
        B.const_cols[val] = col
        S.op("vector", lambda e, col=col, val=val: e.memset(
            B.consts[:, col:col + 1], float(val)),
            writes=[r_("consts", col)], tag="cset")
    col = B.const_cols[val]
    return B.consts[p0:p0 + p, col:col + 1], r_("consts", col)


def emit_init_consts(S, B):
    S.op("vector", lambda e: e.memset(B.ones_row[:], 1.0), writes=[r_("ones_row")])
    S.op("vector", lambda e: e.memset(B.onecol[:], 1.0), writes=[r_("onecol")])
    for c in range(NCHUNK):
        S.op("vector", lambda e, c=c: e.memset(B.x[c][:, 0:1], 0.0),
             writes=[r_("x", c)])
        S.op("vector", lambda e, c=c: e.memset(B.h[c][:, 0:1], 0.0),
             writes=[r_("h", c)])



def emit_ln(S, B, st, src=None, src_key="x", out_h=None, out_key="h",
            gamma_vec=None, beta_vec=None, gb_key=None, l0ffn=False,
            shifted=False):
    src = src or B.x
    out_h = out_h or B.h
    pb_s0, pb_s1, pb_b = B.pb[5], B.pb[6], B.pb[7]

    def skey(c):
        return r_(src_key, c)

    def hkey(c):
        return r_(out_key, c)

    def sslice(c, t0, n):
        if shifted and c >= 4:
            return src[c][:, t0:t0 + n]
        return tok(src[c], t0, n)

    for c in range(NCHUNK):
        S.op("vector",
             lambda e, c=c: e.tensor_copy(tok(out_h[c], 0, T), sslice(c, 0, T)),
             reads=[skey(c)], writes=[hkey(c)], tag="ln_cast")

    for tj in range(NTJ):
        t0 = tj * TJ
        for c in range(NCHUNK):
            sqb = B.sq[c % 2]
            S.op("scalar",
                 lambda e, c=c, t0=t0, sqb=sqb: e.activation(
                     sqb[:], tok(out_h[c], t0, TJ), AF.Square),
                 reads=[hkey(c)], writes=[r_("sq", c % 2)], tag="ln_sq")
            S.op("tensor",
                 lambda e, c=c, t0=t0: e.matmul(
                     pb_s0[0:1, :], B.onecol[:], tok(out_h[c], t0, TJ),
                     start=(c == 0), stop=(c == NCHUNK - 1)),
                 reads=[hkey(c), r_("onecol")], writes=[r_("pb", 5)], tag="ln_mm0")
            S.op("tensor",
                 lambda e, sqb=sqb, c=c: e.matmul(
                     pb_s1[0:1, :], B.onecol[:], sqb[:],
                     start=(c == 0), stop=(c == NCHUNK - 1)),
                 reads=[r_("sq", c % 2), r_("onecol")], writes=[r_("pb", 6)],
                 tag="ln_mm1")
        def slot(i):
            return B.rowf[:, i * TJ:(i + 1) * TJ]

        mu, b1r, b2r, var_s = (slot(i) for i in range(4))
        ex = tmp = slot(4)
        K_mu, K_b1, K_b2, K_var = (r_("rowf", i) for i in range(4))
        K_ex = K_tmp = r_("rowf", 4)
        eps_ap, eps_k = cvec(S, B, EPS, 0, 1)
        mh_ap, mh_k = cvec(S, B, -0.5, 0, 1)
        S.op("vector", lambda e, mu=mu: e.tensor_scalar(
            mu, pb_s0[0:1, :], 1.0 / D, None, ALU.mult),
            reads=[r_("pb", 5)], writes=[K_mu], tag="ln_mu")
        S.op("vector", lambda e: e.tensor_scalar(
            pb_s1[0:1, :], pb_s1[0:1, :], 1.0 / D, None, ALU.mult),
            reads=[r_("pb", 6)], writes=[r_("pb", 6)], tag="ln_ms")
        S.op("vector", lambda e, mu=mu, tmp=tmp: e.tensor_mul(tmp, mu, mu),
             reads=[K_mu], writes=[K_tmp], tag="ln_mu2")
        S.op("vector", lambda e, tmp=tmp: e.tensor_sub(
            pb_s1[0:1, :], pb_s1[0:1, :], tmp),
            reads=[K_tmp, r_("pb", 6)], writes=[r_("pb", 6)], tag="ln_var")
        if l0ffn:
            S.op("vector", lambda e, var_s=var_s: e.tensor_copy(
                var_s, pb_s1[0:1, :]),
                reads=[r_("pb", 6)], writes=[K_var], tag="lf_var")
        S.op("scalar", lambda e, a=eps_ap: e.activation(
            pb_s1[0:1, :], pb_s1[0:1, :], AF.Ln, bias=a),
            reads=[r_("pb", 6), eps_k], writes=[r_("pb", 6)], tag="ln_ln")
        S.op("scalar", lambda e, b1r=b1r, a=mh_ap: e.activation(
            b1r, pb_s1[0:1, :], AF.Exp, scale=a),
            reads=[r_("pb", 6), mh_k], writes=[K_b1], tag="ln_exp")
        S.op("vector", lambda e, mu=mu, b1r=b1r, b2r=b2r: e.scalar_tensor_tensor(
            b2r, mu, -1.0, b1r, ALU.mult, ALU.mult),
            reads=[K_mu, K_b1], writes=[K_b2], tag="ln_b2")
        if l0ffn:
            S.op("vector", lambda e, var_s=var_s, b1r=b1r, ex=ex: e.tensor_mul(
                ex, var_s, b1r), reads=[K_var, K_b1], writes=[K_ex], tag="lf_t")
            S.op("vector", lambda e, ex=ex, b1r=b1r: e.tensor_mul(ex, ex, b1r),
                 reads=[K_ex, K_b1], writes=[K_ex], tag="lf_t2")
            S.op("scalar", lambda e, ex=ex, a=eps_ap: e.activation(
                ex, ex, AF.Ln, bias=a),
                reads=[K_ex, eps_k], writes=[K_ex], tag="lf_ln")
            S.op("scalar", lambda e, ex=ex, a=mh_ap: e.activation(
                ex, ex, AF.Exp, scale=a),
                reads=[K_ex, mh_k], writes=[K_ex], tag="lf_exp")
            _bc_row(S, B, b1r, K_b1, B.b3bc, r_("b3bc"), t0)
            _bc_row(S, B, b2r, K_b2, B.b4bc, r_("b4bc"), t0)
            S.op("vector", lambda e, b1r=b1r, ex=ex: e.tensor_mul(b1r, b1r, ex),
                 reads=[K_ex, K_b1], writes=[K_b1], tag="lf_f1")
            S.op("vector", lambda e, b2r=b2r, ex=ex: e.tensor_mul(b2r, b2r, ex),
                 reads=[K_ex, K_b2], writes=[K_b2], tag="lf_f2")
        _bc_row(S, B, b1r, K_b1, B.b1bc, r_("b1bc"), t0)
        _bc_row(S, B, b2r, K_b2, B.b2bc, r_("b2bc"), t0)

    for c in range(NCHUNK):
        g = gamma_vec[c] if gamma_vec is not None else None
        bta = beta_vec[c] if beta_vec is not None else None
        if g is None:
            S.op("vector", lambda e, c=c: e.tensor_mul(
                tok(out_h[c], 0, T), tok(out_h[c], 0, T), B.b1bc[:]),
                reads=[hkey(c), r_("b1bc")], writes=[hkey(c)], tag="ln_ap1")
            S.op("vector", lambda e, c=c: e.tensor_add(
                tok(out_h[c], 0, T), tok(out_h[c], 0, T), B.b2bc[:]),
                reads=[hkey(c), r_("b2bc")], writes=[hkey(c)], tag="ln_ap2")
        else:
            S.op("vector", lambda e, c=c, g=g: e.scalar_tensor_tensor(
                tok(out_h[c], 0, T), tok(out_h[c], 0, T), g, B.b1bc[:],
                ALU.mult, ALU.mult),
                reads=[hkey(c), r_("b1bc"), gb_key], writes=[hkey(c)],
                tag="ln_ap1g")
            S.op("vector", lambda e, c=c, g=g: e.scalar_tensor_tensor(
                tok(out_h[c], 0, T), B.b2bc[:], g, tok(out_h[c], 0, T),
                ALU.mult, ALU.add),
                reads=[hkey(c), r_("b2bc"), gb_key], writes=[hkey(c)],
                tag="ln_ap2g")
            if bta is not None:
                S.op("vector", lambda e, c=c, bta=bta: e.tensor_scalar(
                    tok(out_h[c], 0, T), tok(out_h[c], 0, T), bta, None, ALU.add),
                    reads=[hkey(c), gb_key], writes=[hkey(c)], tag="ln_ap3g")


def _bc_row(S, B, row_ap, row_key, dst, dst_key, t0):
    pb_b = B.pb[7]
    rh = B.rowh[:]
    K_rh = r_("rowh")
    S.op("vector", lambda e, rh=rh, row_ap=row_ap: e.tensor_copy(rh, row_ap),
         reads=[row_key], writes=[K_rh], tag="bc_cast")
    S.op("tensor", lambda e, rh=rh: e.matmul(
        pb_b[:, 0:TJ], B.ones_row[:], rh, start=True, stop=True),
        reads=[K_rh, r_("ones_row")], writes=[r_("pb", 7)], tag="bc_mm")
    S.op("scalar", lambda e, dst=dst, t0=t0: e.copy(dst[:, t0:t0 + TJ], pb_b[:, 0:TJ]),
         reads=[r_("pb", 7)], writes=[dst_key], tag="bc_evac")



def _l0a_ranges():
    out = []
    for r in (-1, 0, 1, 2, 3, 4):
        qlo = max(0, 128 * r - 12)
        qhi = min(512, 128 * r + 140)
        out.append((r, qlo, qhi))
    return out


def _l0a_masks():
    INF = np.float32(np.inf)
    packed = []
    offs = {}
    pos = 0
    for r, qlo, qhi in _l0a_ranges():
        w = qhi - qlo
        jj = np.arange(128)[:, None]
        qq = np.arange(qlo, qhi)[None, :]
        allowed = np.abs(-128 * r + qq - jj) <= 12
        packed.append(np.where(allowed, INF, 0.0).astype(bf16))
        offs[r] = (pos, w)
        pos += w
    band = np.concatenate(packed, axis=1)
    jj = np.arange(128)[:, None]
    qq = np.arange(512)[None, :]
    allowed0 = (np.abs(qq - jj) <= 12) | (jj < 8) | (qq < 8)
    m0 = np.where(allowed0, INF, 0.0).astype(bf16)
    return band, offs, m0


def alloc_l0a(nc, ctx, sfx=""):
    A = Bufs()
    ent = ctx.enter_context
    A.kh = [ent(nc.sbuf_tensor(f"l0a_kh{i}{sfx}", [128, T], BF16)) for i in range(4)]
    A.qh = [ent(nc.sbuf_tensor(f"l0a_qh0{sfx}", [128, T], BF16))]
    A.vt = [ent(nc.sbuf_tensor(f"l0a_vt{t}{sfx}", [128, 260], BF16)) for t in range(16)]
    A.ao = [ent(nc.sbuf_tensor(f"l0a_ao{c}{sfx}", [128, T], BF16)) for c in range(NCHUNK)]
    A.tmp = [ent(nc.sbuf_tensor(f"l0a_tmp{i}{sfx}", [128, TJ], BF16)) for i in range(2)]
    A.pt = [ent(nc.sbuf_tensor(f"l0a_pt{i}{sfx}", [128, TJ], BF16)) for i in range(3)]
    A.recbc = ent(nc.sbuf_tensor(f"l0a_recbc{sfx}", [64, TJ], BF16))
    A.wa = [ent(nc.sbuf_tensor(f"l0a_wa{c}{sfx}", [128, 128], BF16)) for c in range(NCHUNK)]
    A.wb = [ent(nc.sbuf_tensor(f"l0a_wb{c}{sfx}", [128, 128], BF16)) for c in range(NCHUNK)]
    A.wv = [ent(nc.sbuf_tensor(f"l0a_wv{c}{sfx}", [128, 256], BF16)) for c in range(NCHUNK)]
    A.bias = ent(nc.sbuf_tensor(f"l0a_bias{sfx}", [128, 8], F32))
    A.biasp = ent(nc.sbuf_tensor(f"l0a_biasp{sfx}", [128, 8], F32))
    A.bk = ent(nc.sbuf_tensor(f"l0a_bk{sfx}", [64, 4], F32))
    A.bkp = ent(nc.sbuf_tensor(f"l0a_bkp{sfx}", [64, 4], F32))
    A.bo = ent(nc.sbuf_tensor(f"l0a_bo{sfx}", [128, 8], F32))
    A.bvrow = ent(nc.sbuf_tensor(f"l0a_bvrow{sfx}", [1, 256], BF16))
    A.hw8 = ent(nc.sbuf_tensor(f"l0a_hw8{sfx}", [128, 8], F32))
    A.cos = ent(nc.sbuf_tensor(f"l0a_cos{sfx}", [128, T], BF16))
    A.sin = ent(nc.sbuf_tensor(f"l0a_sin{sfx}", [128, T], BF16))
    bw = sum(qhi - qlo for _, qlo, qhi in _l0a_ranges())
    A.band = ent(nc.sbuf_tensor(f"l0a_band{sfx}", [128, bw], BF16))
    A.m0 = ent(nc.sbuf_tensor(f"l0a_m0{sfx}", [128, 512], BF16))
    return A


def emit_l0a_setup(S, nc, B, A, W):
    S.dma("sync", A.cos[:], W["cst_cosA"], writes=[r_("l0a_cos")])
    S.dma("sync", A.sin[:], W["cst_sinA"], writes=[r_("l0a_sin")])
    S.dma("sync", A.band[:], W["cst_l0a_band"], writes=[r_("l0a_band")])
    S.dma("sync", A.m0[:], W["cst_l0a_m0"], writes=[r_("l0a_m0")])
    S.dma("sync", A.bias[:], W["p_l0a_bq8"], writes=[r_("l0a_bias")])
    S.dma("sync", A.biasp[:], W["p_l0a_bqp8"], writes=[r_("l0a_biasp")])
    S.dma("sync", A.bk[:], W["p_l0a_bk4"], writes=[r_("l0a_bk")])
    S.dma("sync", A.bkp[:], W["p_l0a_bkp4"], writes=[r_("l0a_bkp")])
    S.dma("sync", A.bo[:], W["p_l0a_bo8"], writes=[r_("l0a_bo")])
    S.dma("gpsimd", A.bvrow[:], W["p_l0a_bvrow"], writes=[r_("l0a_bvrow")])
    S.dma("sync", A.hw8[:], W["p_l0a_hw8"], writes=[r_("l0a_hw8")])
    S.op("vector", lambda e: e.tensor_scalar(
        A.hw8[:], A.hw8[:], 1.0 / 8.0, None, ALU.mult),
        reads=[r_("l0a_hw8")], writes=[r_("l0a_hw8")], tag="hw8")


def _load_w_tiles(S, tiles, physkey, dram, n0, w, nk=NCHUNK, queue="gpsimd",
                  coloff=0):
    for c in range(nk):
        S.dma(queue, tiles[c][:, coloff:coloff + w],
              dram[c * 128:(c + 1) * 128, n0:n0 + w],
              writes=[r_(physkey, c)], tag="wload")


def emit_dual_rope_proj(S, B, A, wA, wAk, wB, wBk, bA, bB, kbA, kbB,
                        cos, sin, kcos, ksin, psA, psB, out_tile, out_key,
                        hw_ap=None, hwkey=None, n_k=NCHUNK, hsrc=None):
    hsrc = hsrc or B.h
    for tj in range(NTJ):
        t0 = tj * TJ
        for c in range(n_k):
            S.op("tensor", lambda e, c=c, t0=t0: e.matmul(
                B.pb[psA][:], wA[c][:, 0:128], tok(hsrc[c], t0, TJ),
                start=(c == 0), stop=(c == n_k - 1)),
                reads=[r_(wAk, c), r_("h", c)], writes=[r_("pb", psA)],
                tag="qp_a")
            S.op("tensor", lambda e, c=c, t0=t0: e.matmul(
                B.pb[psB][:], wB[c][:, 0:128], tok(hsrc[c], t0, TJ),
                start=(c == 0), stop=(c == n_k - 1)),
                reads=[r_(wBk, c), r_("h", c)], writes=[r_("pb", psB)],
                tag="qp_b")
        t1, t2 = A.tmp[0], A.tmp[1]
        S.op("vector", lambda e, t0=t0, t1=t1: e.scalar_tensor_tensor(
            t1[:], B.pb[psA][:], bA, cos[:, t0:t0 + TJ], ALU.add, ALU.mult),
            reads=[r_("pb", psA), kbA, kcos], writes=[r_("l0a_t", 0)], tag="rope1")
        S.op("vector", lambda e, t0=t0, t2=t2: e.scalar_tensor_tensor(
            t2[:], B.pb[psB][:], bB, sin[:, t0:t0 + TJ], ALU.add, ALU.mult),
            reads=[r_("pb", psB), kbB, ksin], writes=[r_("l0a_t", 1)], tag="rope2")
        if hw_ap is None:
            S.op("vector", lambda e, t0=t0, t1=t1, t2=t2: e.tensor_add(
                out_tile[:, t0:t0 + TJ], t1[:], t2[:]),
                reads=[r_("l0a_t", 0), r_("l0a_t", 1)], writes=[out_key],
                tag="rope3")
        else:
            S.op("vector", lambda e, t1=t1, t2=t2: e.tensor_add(
                t1[:], t1[:], t2[:]),
                reads=[r_("l0a_t", 0), r_("l0a_t", 1)], writes=[r_("l0a_t", 0)],
                tag="rope3")
            S.op("vector", lambda e, t0=t0, t1=t1: e.tensor_scalar(
                out_tile[:, t0:t0 + TJ], t1[:], hw_ap, None, ALU.mult),
                reads=[r_("l0a_t", 0), hwkey], writes=[out_key], tag="rope4")


def emit_l0_attn_a(S, nc, B, A, W):
    band_offs = {r: None for r, _, _ in _l0a_ranges()}
    pos = 0
    for r, qlo, qhi in _l0a_ranges():
        band_offs[r] = (pos, qlo, qhi)
        pos += qhi - qlo
    negC, negC_k = cvec(S, B, -C_L0A)

    _load_w_tiles(S, A.wv, "l0a_wv", W["L0_attn_a_Wv"], 0, 256)
    for tt in range(16):
        vt = A.vt[tt]
        S.op("vector", lambda e, vt=vt: e.memset(
            vt.ap().rearrange("p (a b) -> p a b", b=65)[:, :, 64:65], 1.0),
            writes=[r_("l0a_vt", tt)], tag="vones")
        for c in range(NCHUNK):
            S.op("tensor", lambda e, c=c, tt=tt: e.matmul(
                B.pb[6][:, 0:256], tok(B.h[c], tt * 128, 128), A.wv[c][:, 0:256],
                start=(c == 0), stop=False),
                reads=[r_("l0a_wv", c), r_("h", c)], writes=[r_("pb", 6)],
                tag="vproj")
        S.op("tensor", lambda e: e.matmul(
            B.pb[6][:, 0:256], B.ones_row[:], A.bvrow[:], start=False, stop=True),
            reads=[r_("l0a_bvrow"), r_("ones_row")], writes=[r_("pb", 6)],
            tag="vbias")
        S.op("scalar", lambda e, vt=vt: e.copy(
            vt.ap().rearrange("p (a b) -> p a b", b=65)[:, :, 0:64],
            B.pb[6][:, 0:256].rearrange("p (a b) -> p a b", b=64)),
            reads=[r_("pb", 6)], writes=[r_("l0a_vt", tt)], tag="vevac")

    for kv in range(4):
        _load_w_tiles(S, A.wa, "l0a_wa", W["L0_attn_a_Wk"], kv * 64, 64)
        _load_w_tiles(S, A.wb, "l0a_wb", W["p_l0a_wkp"], kv * 64, 64)
        kt_tile = A.kh[kv]
        kkey = r_("l0a_kh", kv)
        for tj in range(NTJ):
            t0 = tj * TJ
            for c in range(NCHUNK):
                S.op("tensor", lambda e, c=c, t0=t0: e.matmul(
                    B.pb[0][0:64, :], A.wa[c][:, 0:64], tok(B.h[c], t0, TJ),
                    start=(c == 0), stop=(c == NCHUNK - 1)),
                    reads=[r_("l0a_wa", c), r_("h", c)], writes=[r_("pb", 0)],
                    tag="kp_a")
                S.op("tensor", lambda e, c=c, t0=t0: e.matmul(
                    B.pb[1][0:64, :], A.wb[c][:, 0:64], tok(B.h[c], t0, TJ),
                    start=(c == 0), stop=(c == NCHUNK - 1)),
                    reads=[r_("l0a_wb", c), r_("h", c)], writes=[r_("pb", 1)],
                    tag="kp_b")
            t1, t2 = A.tmp[0], A.tmp[1]
            S.op("vector", lambda e, t0=t0, kv=kv: e.scalar_tensor_tensor(
                t1[0:64, :], B.pb[0][0:64, :], A.bk[0:64, kv:kv + 1],
                A.cos[0:64, t0:t0 + TJ], ALU.add, ALU.mult),
                reads=[r_("pb", 0), r_("l0a_bk"), r_("l0a_cos")],
                writes=[r_("l0a_t", 0)], tag="krope1")
            S.op("vector", lambda e, t0=t0, kv=kv: e.scalar_tensor_tensor(
                t2[0:64, :], B.pb[1][0:64, :], A.bkp[0:64, kv:kv + 1],
                A.sin[0:64, t0:t0 + TJ], ALU.add, ALU.mult),
                reads=[r_("pb", 1), r_("l0a_bkp"), r_("l0a_sin")],
                writes=[r_("l0a_t", 1)], tag="krope2")
            S.op("vector", lambda e, t0=t0, kt_tile=kt_tile: e.tensor_add(
                kt_tile[0:64, t0:t0 + TJ], t1[0:64, :], t2[0:64, :]),
                reads=[r_("l0a_t", 0), r_("l0a_t", 1)], writes=[kkey],
                tag="krope3")
        S.dma("gpsimd", kt_tile[64:128, :], kt_tile[0:64, :],
              reads=[kkey], writes=[kkey], tag="kdup")

    for c in range(NCHUNK):
        qb = A.qh[0]
        qbk = r_("l0a_qh", 0)
        _load_w_tiles(S, A.wa, "l0a_wa", W["L0_attn_a_Wq"], c * 128, 128)
        _load_w_tiles(S, A.wb, "l0a_wb", W["p_l0a_wqp"], c * 128, 128)
        emit_dual_rope_proj(
            S, B, A, A.wa, "l0a_wa", A.wb, "l0a_wb",
            A.bias[:, c:c + 1], A.biasp[:, c:c + 1], r_("l0a_bias"), r_("l0a_biasp"),
            A.cos, A.sin, r_("l0a_cos"), r_("l0a_sin"),
            0, 1, qb, qbk,
            hw_ap=A.hw8[:, c:c + 1], hwkey=r_("l0a_hw8"))
        for hh in range(2):
            h = 2 * c + hh
            kv = h // 4
            krow = 64 * hh
            klhs = A.kh[kv]
            kkey = r_("l0a_kh", kv)
            qrows = qb[64 * hh:64 * hh + 64, :]
            for a in range(4):
                q0 = a * TJ
                emit_l0a_head_chunk(S, B, A, h, a, q0, qrows, qbk, klhs, kkey,
                                    krow, kv, band_offs, negC, negC_k, c, hh)

    for oc in range(NCHUNK):
        _load_w_tiles(S, A.wa, "l0a_wa", W["L0_attn_a_Wo"], oc * 128, 128)
        for tj in range(NTJ):
            t0 = tj * TJ
            for c in range(NCHUNK):
                S.op("tensor", lambda e, c=c, t0=t0: e.matmul(
                    B.pb[4][:], A.wa[c][:, 0:128], A.ao[c][:, t0:t0 + TJ],
                    start=(c == 0), stop=(c == NCHUNK - 1)),
                    reads=[r_("l0a_wa", c), r_("l0a_ao", c)],
                    writes=[r_("pb", 4)], tag="wo_mm")
            S.op("vector", lambda e, oc=oc, t0=t0: e.scalar_tensor_tensor(
                tok(B.x[oc], t0, TJ), B.pb[4][:], A.bo[:, oc:oc + 1],
                tok(B.x[oc], t0, TJ), ALU.add, ALU.add),
                reads=[r_("pb", 4), r_("l0a_bo"), r_("x", oc)],
                writes=[r_("x", oc)], tag="wo_res")


def emit_l0a_head_chunk(S, B, A, h, a, q0, qrows, qbk, klhs, kkey, krow, kv,
                        band_offs, negC, negC_k, c, hh):
    ps_S, ps_O, ps_G, ps_B = 2, 3, 5, 7
    ktiles = []
    if a == 0:
        for kt in range(0, 5):
            ktiles.append((kt, kt))
    else:
        for r in (-1, 0, 1, 2, 3, 4):
            kt = 4 * a + r
            if 0 <= kt < 16:
                ktiles.append((kt, r))
    n_pv = len(ktiles) + (11 if a == 0 else 1)
    pv_i = [0]

    def pv(lhsT_ap, lhs_key, rhs_ap, rhs_key, qlo, w):
        st = pv_i[0] == 0
        sp = pv_i[0] == n_pv - 1
        pv_i[0] += 1
        S.op("tensor", lambda e: e.matmul(
            B.pb[ps_O][0:65, qlo:qlo + w],
            lhsT_ap, rhs_ap, start=st, stop=sp),
            reads=[lhs_key, rhs_key], writes=[r_("pb", ps_O)], tag="pv")

    if a != 0:
        pt = A.pt[2]
        ptk = r_("l0a_pt", 2)
        S.op("tensor", lambda e: e.matmul(
            B.pb[ps_G][0:8, 0:TJ], klhs[krow:krow + 64, 0:8],
            qrows[:, q0:q0 + TJ], start=True, stop=True),
            reads=[kkey, qbk], writes=[r_("pb", ps_G)], tag="sk_mm",
            pe_base=krow)
        S.op("scalar", lambda e, pt=pt: e.activation(
            pt[0:8, 0:TJ], B.pb[ps_G][0:8, 0:TJ], AF.Exp, bias=negC[0:8, :]),
            reads=[r_("pb", ps_G), negC_k], writes=[ptk], tag="sk_exp")
        pv(A.vt[0][0:8, 65 * kv:65 * kv + 65], r_("l0a_vt", 0),
           pt[0:8, 0:TJ], ptk, 0, TJ)

    for i, (kt, r) in enumerate(ktiles):
        if a == 0:
            qlo, qhi = (0, 512) if kt == 0 else band_offs[r][1:]
        else:
            qlo, qhi = band_offs[r][1:]
        w = qhi - qlo
        pt = A.pt[i % 2]
        ptk = r_("l0a_pt", i % 2)
        S.op("tensor", lambda e, kt=kt, qlo=qlo, w=w, pt=pt: e.matmul(
            B.pb[ps_S][:, 0:w], klhs[krow:krow + 64, kt * 128:(kt + 1) * 128],
            qrows[:, q0 + qlo:q0 + qlo + w], start=True, stop=True),
            reads=[kkey, qbk], writes=[r_("pb", ps_S)], tag="s_mm",
            pe_base=krow)
        S.op("scalar", lambda e, w=w, pt=pt: e.activation(
            pt[:, 0:w], B.pb[ps_S][:, 0:w], AF.Exp, bias=negC),
            reads=[r_("pb", ps_S), negC_k], writes=[ptk], tag="s_exp")
        if a == 0 and kt == 0:
            S.op("vector", lambda e, w=w, pt=pt: e.tensor_tensor(
                pt[:, 0:w], pt[:, 0:w], A.m0[:, 0:w], ALU.min),
                reads=[ptk, r_("l0a_m0")], writes=[ptk], tag="s_mask")
        else:
            moff = band_offs[r][0]
            S.op("vector", lambda e, w=w, moff=moff, pt=pt: e.tensor_tensor(
                pt[:, 0:w], pt[:, 0:w], A.band[:, moff:moff + w], ALU.min),
                reads=[ptk, r_("l0a_band")], writes=[ptk], tag="s_mask")
        pv(A.vt[kt][:, 65 * kv:65 * kv + 65], r_("l0a_vt", kt),
           pt[:, 0:w], ptk, qlo, w)

    if a == 0:
        for kt in range(5, 16):
            pt = A.pt[2]
            ptk = r_("l0a_pt", 2)
            S.op("tensor", lambda e, kt=kt: e.matmul(
                B.pb[ps_G][:, 0:8], klhs[krow:krow + 64, kt * 128:(kt + 1) * 128],
                qrows[:, q0:q0 + 8], start=True, stop=True),
                reads=[kkey, qbk], writes=[r_("pb", ps_G)], tag="sg_mm",
                pe_base=krow)
            S.op("scalar", lambda e: e.activation(
                pt[:, 0:8], B.pb[ps_G][:, 0:8], AF.Exp, bias=negC),
                reads=[r_("pb", ps_G), negC_k], writes=[ptk], tag="sg_exp")
            pv(A.vt[kt][:, 65 * kv:65 * kv + 65], r_("l0a_vt", kt),
               pt[:, 0:8], ptk, 0, 8)

    rec = B.rowf[:, 3 * TJ:4 * TJ]
    rec_k = r_("rowf", 3)
    S.op("vector", lambda e, rec=rec: e.reciprocal(rec, B.pb[ps_O][64:65, 0:TJ]),
         reads=[r_("pb", ps_O)], writes=[rec_k], tag="n_rec")
    S.op("vector", lambda e, rec=rec: e.tensor_copy(B.rowh[:], rec),
         reads=[rec_k], writes=[r_("rowh")], tag="n_cast")
    S.op("tensor", lambda e: e.matmul(
        B.pb[ps_B][0:64, 0:TJ], B.ones_row[:, 0:64], B.rowh[:],
        start=True, stop=True),
        reads=[r_("rowh"), r_("ones_row")], writes=[r_("pb", ps_B)],
        tag="n_bcmm")
    S.op("scalar", lambda e: e.copy(A.recbc[:], B.pb[ps_B][0:64, 0:TJ]),
         reads=[r_("pb", ps_B)], writes=[r_("l0a_recbc")], tag="n_bcevac")
    S.op("vector", lambda e, c=c, hh=hh: e.tensor_tensor(
        A.ao[c][64 * hh:64 * hh + 64, q0:q0 + TJ], B.pb[ps_O][0:64, 0:TJ],
        A.recbc[:], ALU.mult),
        reads=[r_("pb", ps_O), r_("l0a_recbc")], writes=[r_("l0a_ao", c)],
        tag="n_mul")



def _declare_x_io(nc):
    x_in = nc.dram_tensor("x_io_in", [NSEQ, NCHUNK, 128, T], F32,
                          kind="ExternalInput").ap()
    x_out = nc.dram_tensor("x_io_out", [NSEQ, NCHUNK, 128, T], F32,
                           kind="ExternalOutput").ap()
    return x_in, x_out


def emit_load_x(S, B, x_in, seq):
    for c in range(NCHUNK):
        S.dma("sync", tok(B.x[c], 0, T), x_in[seq, c], writes=[r_("x", c)],
              tag="ldx")


def emit_store_x(S, B, x_out, seq):
    for c in range(NCHUNK):
        S.dma("sync", x_out[seq, c], tok(B.x[c], 0, T), reads=[r_("x", c)],
              tag="stx")


def _np_dt(a):
    if a.dtype == np.float32:
        return F32
    if a.dtype == np.int32:
        return I32
    if a.dtype == bf16:
        return BF16
    raise ValueError(a.dtype)


def declare_inputs(nc, arrays, names):
    W = {}
    for n in names:
        a = arrays[n]
        W[n] = nc.dram_tensor(n, list(a.shape), _np_dt(a), kind="ExternalInput").ap()
    return W


L0A_NAMES = ["L0_attn_a_Wq", "L0_attn_a_Wk", "L0_attn_a_Wv", "L0_attn_a_Wo",
             "p_l0a_bvrow", "p_l0a_wqp", "p_l0a_wkp", "p_l0a_bq8",
             "p_l0a_bqp8", "p_l0a_bk4", "p_l0a_bkp4", "p_l0a_bo8", "p_l0a_hw8",
             "cst_cosA", "cst_sinA", "cst_l0a_band", "cst_l0a_m0"]


def host_prep(inputs):
    P = dict(host_constants())
    permA1024 = np.array([_permA_idx(n) for n in range(1024)])
    permA256 = permA1024[:256]
    ii = inputs
    P["p_l0a_wqp"] = np.ascontiguousarray(ii["L0_attn_a_Wq"][:, permA1024])
    P["p_l0a_wkp"] = np.ascontiguousarray(ii["L0_attn_a_Wk"][:, permA256])
    P["p_l0a_bq8"] = np.ascontiguousarray(ii["L0_attn_a_bq"].reshape(8, 128).T)
    P["p_l0a_bqp8"] = np.ascontiguousarray(
        ii["L0_attn_a_bq"][permA1024].reshape(8, 128).T)
    P["p_l0a_bk4"] = np.ascontiguousarray(ii["L0_attn_a_bk"].reshape(4, 64).T)
    P["p_l0a_bkp4"] = np.ascontiguousarray(
        ii["L0_attn_a_bk"][permA256].reshape(4, 64).T)
    P["p_l0a_bo8"] = np.ascontiguousarray(ii["L0_attn_a_bo"].reshape(8, 128).T)
    P["p_l0a_bvrow"] = np.ascontiguousarray(ii["L0_attn_a_bv"].reshape(1, 256))
    hw = ii["L0_attn_a_head_weights"]
    P["p_l0a_hw8"] = np.ascontiguousarray(
        np.repeat(hw.reshape(8, 2), 64, axis=1).reshape(8, 128).T)
    band, offs, m0 = _l0a_masks()
    P["cst_l0a_band"] = band
    P["cst_l0a_m0"] = m0
    rot64 = np.array([((d + 32) % 64) for d in range(64)])
    qrot_cols = np.concatenate([256 * h + rot64 for h in range(8)])
    krot_cols = np.concatenate([256 * kv + rot64 for kv in range(2)])
    P["p_l0b_wqrot"] = np.ascontiguousarray(ii["L0_attn_b_Wq"][:, qrot_cols])
    P["p_l0b_wkrot"] = np.ascontiguousarray(ii["L0_attn_b_Wk"][:, krot_cols])
    P["p_l0b_wox"] = np.ascontiguousarray(
        ii["L0_attn_b_Wo"].reshape(16, 128, 8, 128).transpose(2, 1, 0, 3))
    permI1024 = np.arange(1024) ^ 1
    wqT = np.ascontiguousarray(ii["L1_attn_a_Wq"].T)
    wkT = np.ascontiguousarray(ii["L1_attn_a_Wk"].T)
    P["p_l1a_wqT"] = wqT
    P["p_l1a_wqTp"] = np.ascontiguousarray(wqT[:, permI1024])
    P["p_l1a_wkT"] = wkT
    P["p_l1a_wkTp"] = np.ascontiguousarray(wkT[:, permI1024])
    P["p_l1a_wvT"] = np.ascontiguousarray(ii["L1_attn_a_Wv"].T)
    P["p_l1a_wox"] = np.ascontiguousarray(
        ii["L1_attn_a_Wo"].T.reshape(8, 128, 8, 128).transpose(2, 1, 0, 3))
    INF = np.float32(np.inf)
    jj = np.arange(128)[:, None]
    qq = np.arange(128)[None, :]
    P["cst_l1_tri"] = np.where(jj <= qq, INF, 0.0).astype(bf16)
    tri = np.where(jj <= qq, INF, 0.0).astype(bf16)
    low = np.where(jj > qq, INF, 0.0).astype(bf16)
    P["cst_l1b_tri4"] = np.ascontiguousarray(np.tile(tri, (1, 4)))
    P["cst_l1b_low4"] = np.ascontiguousarray(np.tile(low, (1, 4)))
    permA1024b = np.array([_permA_idx(n) for n in range(1024)])
    wqTb = np.ascontiguousarray(ii["L1_attn_b_Wq"].T)
    wkTb = np.ascontiguousarray(ii["L1_attn_b_Wk"].T)
    P["p_l1b_wqT"] = wqTb
    P["p_l1b_wqTp"] = np.ascontiguousarray(wqTb[:, permA1024b])
    P["p_l1b_wkT"] = wkTb
    P["p_l1b_wkTp"] = np.ascontiguousarray(wkTb[:, permA1024b])
    P["p_l1b_wvT"] = np.ascontiguousarray(ii["L1_attn_b_Wv"].T)
    P["p_l1b_wox"] = np.ascontiguousarray(
        ii["L1_attn_b_Wo"].T.reshape(8, 128, 8, 128).transpose(2, 1, 0, 3))
    P["p_l1b_bo8"] = np.ascontiguousarray(ii["L1_attn_b_bo"].reshape(8, 128).T)
    rot64b = np.array([(d + 32) % 64 for d in range(64)])
    permqk = np.concatenate([64 * hh + rot64b for hh in range(16)])
    P["p_l1f_wqkvp"] = np.ascontiguousarray(ii["L1_ffn_wqkv"][:, :, permqk])
    P["p_l1f_wox"] = np.ascontiguousarray(
        ii["L1_ffn_wo"].reshape(2, 4, 128, 8, 128).transpose(0, 3, 2, 1, 4))
    qsv = np.zeros((128, 8), np.float32)
    for l in range(2):
        qsv[:, 4 * l + 0] = np.tile(ii["L1_ffn_q_scale"][l], 2)
        qsv[:, 4 * l + 1] = np.tile(ii["L1_ffn_q_scale"][l][rot64b], 2)
        qsv[:, 4 * l + 2] = np.tile(ii["L1_ffn_k_scale"][l], 2)
        qsv[:, 4 * l + 3] = np.tile(ii["L1_ffn_k_scale"][l][rot64b], 2)
    P["p_l1f_qsv"] = qsv
    P["p_l1f_lg"] = np.ascontiguousarray(
        ii["L1_ffn_ln_g"].reshape(2, 8, 128).transpose(0, 2, 1))
    P["p_l1f_lb"] = np.ascontiguousarray(
        ii["L1_ffn_ln_b"].reshape(2, 8, 128).transpose(0, 2, 1))
    b1p = np.zeros((2, 128, 22), np.float32)
    b1gp = np.zeros((2, 128, 22), np.float32)
    for l in range(2):
        a = np.zeros(22 * 128, np.float32)
        a[:2730] = ii["L1_ffn_b1"][l][:2730]
        b1p[l] = a.reshape(22, 128).T
        gpart = np.zeros(22 * 128, np.float32)
        gpart[:2730] = ii["L1_ffn_b1"][l][2730:]
        b1gp[l] = gpart.reshape(22, 128).T
    P["p_l1f_b1a"] = b1p
    P["p_l1f_b1g"] = b1gp
    P["p_l1f_b2"] = np.ascontiguousarray(
        ii["L1_ffn_b2"].reshape(2, 8, 128).transpose(0, 2, 1))
    P["cst_geq"] = np.where(jj >= qq, INF, 0.0).astype(bf16)
    oc2 = np.zeros((128, 2), np.float32)
    oc2[0:64, 0] = 1.0
    oc2[64:128, 1] = 1.0
    P["cst_oc2"] = oc2.astype(bf16)
    bc2 = np.zeros((2, 128), np.float32)
    bc2[0, 0:64] = 1.0
    bc2[1, 64:128] = 1.0
    P["cst_bc2"] = np.ascontiguousarray(bc2.astype(bf16))
    P["p_l0f_g8"] = np.ascontiguousarray(ii["L0_ffn_ln_gamma"].reshape(8, 128).T)
    P["p_l0f_be8"] = np.ascontiguousarray(ii["L0_ffn_ln_beta"].reshape(8, 128).T)
    P["p_l0f_b1"] = np.ascontiguousarray(ii["L0_ffn_fc1_bias"].reshape(32, 128).T)
    P["p_l0f_b2"] = np.ascontiguousarray(ii["L0_ffn_fc2_bias"].reshape(8, 128).T)
    return P


def shard_x(x):
    shards = []
    for ci in range(8):
        xs = x[4 * ci:4 * ci + 4]
        xt = np.ascontiguousarray(xs.transpose(0, 2, 1))
        shards.append(xt.reshape(NSEQ, NCHUNK, 128, T))
    return shards



L0F_NAMES = ["L0_ffn_fc1_q", "L0_ffn_fc1_scales", "L0_ffn_fc2_q",
             "L0_ffn_fc2_scales", "p_l0f_g8", "p_l0f_be8", "p_l0f_b1",
             "p_l0f_b2"]


def alloc_l0f(nc, ctx, sfx=""):
    F = Bufs()
    ent = ctx.enter_context
    B3 = ent(nc.sbuf_tensor(f"b3bc{sfx}", [128, T], BF16))
    B4 = ent(nc.sbuf_tensor(f"b4bc{sfx}", [128, T], BF16))
    F.b3bc, F.b4bc = B3, B4
    F.qi = ent(nc.sbuf_tensor(f"l0f_qi{sfx}", [128, 512], I32))
    F.srow = ent(nc.sbuf_tensor(f"l0f_srow{sfx}", [1, 512], F32))
    F.srowh = ent(nc.sbuf_tensor(f"l0f_srowh{sfx}", [1, 512], BF16))
    F.scbc = ent(nc.sbuf_tensor(f"l0f_scbc{sfx}", [128, 512], F32))
    F.wout = ent(nc.sbuf_tensor(f"l0f_wout{sfx}", [128, 512], BF16))
    F.wacat = ent(nc.sbuf_tensor(f"l0f_wacat{sfx}", [128, 1024], BF16))
    F.wacatb = ent(nc.sbuf_tensor(f"l0f_wacatb{sfx}", [128, 1024], BF16))
    F.w2cat = ent(nc.sbuf_tensor(f"l0f_w2cat{sfx}", [128, 4096], BF16))
    F.w2catb = ent(nc.sbuf_tensor(f"l0f_w2catb{sfx}", [128, 4096], BF16))
    F.z = [ent(nc.sbuf_tensor(f"l0f_z{i}{sfx}", [128, 512], BF16)) for i in range(32)]
    F.b3p1 = ent(nc.sbuf_tensor(f"l0f_b3p1{sfx}", [128, T], F32))
    F.g8 = ent(nc.sbuf_tensor(f"l0f_g8{sfx}", [128, 8], F32))
    F.be8 = ent(nc.sbuf_tensor(f"l0f_be8{sfx}", [128, 8], F32))
    F.b1 = ent(nc.sbuf_tensor(f"l0f_b1{sfx}", [128, 32], F32))
    F.b2h = ent(nc.sbuf_tensor(f"l0f_b2h{sfx}", [128, 8], F32))
    return F


def emit_l0f_setup(S, nc, B, F, W, scratch=True, consts=True):
    if not consts:
        pass
    S.dma("sync", F.g8[:], W["p_l0f_g8"], writes=[r_("l0f_g8")])
    S.dma("sync", F.be8[:], W["p_l0f_be8"], writes=[r_("l0f_be8")])
    S.dma("sync", F.b1[:], W["p_l0f_b1"], writes=[r_("l0f_b1")])
    S.dma("sync", F.b2h[:], W["p_l0f_b2"], writes=[r_("l0f_b2h")])
    S.op("vector", lambda e: e.tensor_scalar(
        F.b2h[:], F.b2h[:], 0.5, None, ALU.mult),
        reads=[r_("l0f_b2h")], writes=[r_("l0f_b2h")], tag="b2half")
    if not scratch:
        F.scr_w1, F.scr_w2 = B.scr_l0f
        return
    scr_w1 = nc.dram_tensor("scr_w1", [32, 128, 8, 128], BF16, kind="Internal").ap()
    scr_w2 = nc.dram_tensor("scr_w2", [8, 128, 32, 128], BF16, kind="Internal").ap()
    F.scr_w1, F.scr_w2 = scr_w1, scr_w2
    B.scr_l0f = (scr_w1, scr_w2)
    for wi, (qd, sd, KK, NN) in enumerate([
            (W["L0_ffn_fc1_q"], W["L0_ffn_fc1_scales"], 1024, 4096),
            (W["L0_ffn_fc2_q"], W["L0_ffn_fc2_scales"], 4096, 1024)]):
        for c in range(KK // 128):
            for nb in range(NN // 512):
                n0 = nb * 512
                S.dma("sync", F.qi[:], qd[c * 128:(c + 1) * 128, n0:n0 + 512],
                      writes=[r_("l0f_qi")], tag="dq_q")
                S.dma("sync", F.srow[:], sd[c:c + 1, n0:n0 + 512],
                      writes=[r_("l0f_srow")], tag="dq_s")
                S.op("vector", lambda e: e.tensor_copy(F.srowh[:], F.srow[:]),
                     reads=[r_("l0f_srow")], writes=[r_("l0f_srowh")], tag="dq_c")
                S.op("tensor", lambda e: e.matmul(
                    B.pb[7][:], B.ones_row[:], F.srowh[:], start=True, stop=True),
                    reads=[r_("l0f_srowh"), r_("ones_row")], writes=[r_("pb", 7)],
                    tag="dq_mm")
                S.op("scalar", lambda e: e.copy(F.scbc[:], B.pb[7][:]),
                     reads=[r_("pb", 7)], writes=[r_("l0f_scbc")], tag="dq_ev")
                S.op("vector", lambda e: e.scalar_tensor_tensor(
                    F.wout[:], F.qi[:], -128.0, F.scbc[:], ALU.add, ALU.mult),
                    reads=[r_("l0f_qi"), r_("l0f_scbc")], writes=[r_("l0f_wout")],
                    tag="dq_stt")
                if wi == 0:
                    dst = scr_w1[4 * nb:4 * nb + 4, :, c, :].rearrange(
                        "a p j -> p a j")
                else:
                    dst = scr_w2[4 * nb:4 * nb + 4, :, c, :].rearrange(
                        "a p j -> p a j")
                S.dma("gpsimd", dst,
                      F.wout[:].rearrange("p (a j) -> p a j", a=4),
                      reads=[r_("l0f_wout")], writes=[r_("scr", wi, c, nb)],
                      tag="dq_st")


def emit_l0_ffn(S, nc, B, F, W):
    B.b3bc, B.b4bc = F.b3bc, F.b4bc
    gvec = [F.g8[:, c:c + 1] for c in range(NCHUNK)]
    bvec = [F.be8[:, c:c + 1] for c in range(NCHUNK)]
    emit_ln(S, B, "lf", gamma_vec=gvec, beta_vec=bvec, gb_key=r_("l0f_g8"),
            l0ffn=True)
    S.op("vector", lambda e: e.tensor_scalar(
        F.b3p1[:], F.b3bc[:], 1.0, None, ALU.add),
        reads=[r_("b3bc")], writes=[r_("l0f_b3p1")], tag="b3p1")
    for tj in range(NTJ):
        t0 = tj * TJ
        for ns in range(32):
            zb = F.z[ns]
            par = ns % 2
            wt = F.wacat if par == 0 else F.wacatb
            kw = r_("l0f_wacat", par)
            S.dma("sync" if par == 0 else "gpsimd", wt[:],
                  F.scr_w1[ns].rearrange("p c j -> p (c j)"),
                  reads=[r_("scr", 0, c_, ns // 4) for c_ in range(8)],
                  writes=[kw], tag="w1load")
            for c in range(NCHUNK):
                S.op("tensor", lambda e, c=c, t0=t0, ns=ns, wt=wt: e.matmul(
                    B.pb[ns % 2][:], wt[:, c * 128:(c + 1) * 128],
                    tok(B.h[c], t0, TJ),
                    start=(c == 0), stop=(c == NCHUNK - 1)),
                    reads=[kw, r_("h", c)], writes=[r_("pb", ns % 2)],
                    tag="f1_mm")
            S.op("scalar", lambda e, ns=ns, zb=zb: e.activation(
                zb[:], B.pb[ns % 2][:], AF.Silu, bias=F.b1[:, ns:ns + 1]),
                reads=[r_("pb", ns % 2), r_("l0f_b1")], writes=[r_("l0f_z", ns)],
                tag="f1_silu")
        for oc in range(NCHUNK):
            par = oc % 2
            wt = F.w2cat if par == 0 else F.w2catb
            kw = "l0f_w2cat" if par == 0 else "l0f_w2catb"
            S.dma("sync", wt[:, 0:2048],
                  F.scr_w2[oc][:, 0:16, :].rearrange("p i j -> p (i j)"),
                  reads=[r_("scr", 1, c_, oc // 4) for c_ in range(16)],
                  writes=[r_(kw, 0)], tag="w2load")
            S.dma("gpsimd", wt[:, 2048:4096],
                  F.scr_w2[oc][:, 16:32, :].rearrange("p i j -> p (i j)"),
                  reads=[r_("scr", 1, c_, oc // 4) for c_ in range(16, 32)],
                  writes=[r_(kw, 1)], tag="w2load2")
            for i in range(32):
                S.op("tensor", lambda e, i=i, oc=oc, wt=wt: e.matmul(
                    B.pb[2 + oc % 2][:], wt[:, i * 128:(i + 1) * 128],
                    F.z[i][:],
                    start=(i == 0), stop=(i == 31)),
                    reads=[r_(kw, 0 if i < 16 else 1), r_("l0f_z", i)],
                    writes=[r_("pb", 2 + oc % 2)], tag="f2_mm")
            S.op("vector", lambda e, oc=oc, t0=t0: e.tensor_mul(
                tok(B.x[oc], t0, TJ), tok(B.x[oc], t0, TJ),
                F.b3p1[:, t0:t0 + TJ]),
                reads=[r_("x", oc), r_("l0f_b3p1")], writes=[r_("x", oc)],
                tag="res_u")
            S.op("vector", lambda e, oc=oc, t0=t0: e.scalar_tensor_tensor(
                tok(B.x[oc], t0, TJ), B.pb[2 + oc % 2][:], 0.5,
                tok(B.x[oc], t0, TJ), ALU.mult, ALU.add),
                reads=[r_("pb", 2 + oc % 2), r_("x", oc)], writes=[r_("x", oc)],
                tag="res_z")
            S.op("vector", lambda e, oc=oc, t0=t0: e.scalar_tensor_tensor(
                tok(B.x[oc], t0, TJ), F.b4bc[:, t0:t0 + TJ], F.b2h[:, oc:oc + 1],
                tok(B.x[oc], t0, TJ), ALU.add, ALU.add),
                reads=[r_("x", oc), r_("b4bc"), r_("l0f_b2h")],
                writes=[r_("x", oc)], tag="res_b")



L0B_NAMES = ["L0_attn_b_Wq", "L0_attn_b_Wk", "L0_attn_b_Wv",
             "p_l0b_wqrot", "p_l0b_wkrot", "p_l0b_wox",
             "cst_cosB_q", "cst_sinB_q"]


def alloc_l0b(nc, ctx, sfx=""):
    A = Bufs()
    ent = ctx.enter_context
    A.kh = [ent(nc.sbuf_tensor(f"l0b_kh{i}{sfx}", [128, T], BF16)) for i in range(4)]
    A.vt = [ent(nc.sbuf_tensor(f"l0b_vt{t}{sfx}", [128, 512], BF16)) for t in range(16)]
    A.qh = [ent(nc.sbuf_tensor(f"l0b_qh{i}{sfx}", [128, TJ], BF16)) for i in range(2)]
    A.gs = [ent(nc.sbuf_tensor(f"l0b_gs{i}{sfx}", [128, TJ], BF16)) for i in range(2)]
    A.pt = [ent(nc.sbuf_tensor(f"l0b_pt{i}{sfx}", [128, TJ], BF16)) for i in range(3)]
    A.recbc = ent(nc.sbuf_tensor(f"l0b_recbc{sfx}", [128, TJ], BF16))
    A.gm = ent(nc.sbuf_tensor(f"l0b_gm{sfx}", [128, TJ], BF16))
    A.ao = [ent(nc.sbuf_tensor(f"l0b_ao{i}{sfx}", [128, TJ], BF16)) for i in range(16)]
    A.wq = [ent(nc.sbuf_tensor(f"l0b_wq{c}{sfx}", [128, 256], BF16)) for c in range(NCHUNK)]
    A.wg = [ent(nc.sbuf_tensor(f"l0b_wg{c}{sfx}", [128, 256], BF16)) for c in range(NCHUNK)]
    A.wr = [ent(nc.sbuf_tensor(f"l0b_wr{c}{sfx}", [128, 64], BF16)) for c in range(NCHUNK)]
    A.wo = ent(nc.sbuf_tensor(f"l0b_wo{sfx}", [128, 2048], BF16))
    A.tmp = [ent(nc.sbuf_tensor(f"l0b_tmp{i}{sfx}", [128, TJ], BF16)) for i in range(2)]
    A.cq = ent(nc.sbuf_tensor(f"l0b_cq{sfx}", [64, T], BF16))
    A.sq_ = ent(nc.sbuf_tensor(f"l0b_sq{sfx}", [64, T], BF16))
    return A


def emit_l0b_setup(S, nc, B, A, W):
    S.dma("sync", A.cq[:], W["cst_cosB_q"], writes=[r_("l0b_cq")])
    S.dma("sync", A.sq_[:], W["cst_sinB_q"], writes=[r_("l0b_sq")])


def emit_l0_attn_b(S, nc, B, A, W):
    emit_ln(S, B, "ln")
    negC, negC_k = cvec(S, B, -C_L0B)

    _load_w_tiles(S, A.wq, "l0b_wq", W["L0_attn_b_Wv"], 0, 256)
    _load_w_tiles(S, A.wg, "l0b_wg", W["L0_attn_b_Wv"], 256, 256)
    for tt in range(16):
        vt = A.vt[tt]
        for c in range(NCHUNK):
            S.op("tensor", lambda e, c=c, tt=tt: e.matmul(
                B.pb[6][:, 0:256], tok(B.h[c], tt * 128, 128), A.wq[c][:, 0:256],
                start=(c == 0), stop=False),
                reads=[r_("l0b_wq", c), r_("h", c)], writes=[r_("pb", 6)],
                tag="bvproj")
            S.op("tensor", lambda e, c=c, tt=tt: e.matmul(
                B.pb[6][:, 256:512], tok(B.h[c], tt * 128, 128), A.wg[c][:, 0:256],
                start=False, stop=(c == NCHUNK - 1)),
                reads=[r_("l0b_wg", c), r_("h", c)], writes=[r_("pb", 6)],
                tag="bvproj2")
        S.op("scalar", lambda e, vt=vt: e.copy(vt[:], B.pb[6][:]),
             reads=[r_("pb", 6)], writes=[r_("l0b_vt", tt)], tag="bvevac")

    for m in range(4):
        kb = A.kh[m]
        kkey = r_("l0b_kh", m)
        _load_w_tiles(S, A.wq, "l0b_wq", W["L0_attn_b_Wk"], m * 128, 128)
        if m % 2 == 0:
            _load_w_tiles(S, A.wr, "l0b_wr", W["p_l0b_wkrot"], (m // 2) * 64, 64)
        for tj in range(NTJ):
            t0 = tj * TJ
            for c in range(NCHUNK):
                S.op("tensor", lambda e, c=c, t0=t0: e.matmul(
                    B.pb[0][:], A.wq[c][:, 0:128], tok(B.h[c], t0, TJ),
                    start=(c == 0), stop=(c == NCHUNK - 1)),
                    reads=[r_("l0b_wq", c), r_("h", c)], writes=[r_("pb", 0)],
                    tag="bk_mm")
                if m % 2 == 0:
                    S.op("tensor", lambda e, c=c, t0=t0: e.matmul(
                        B.pb[1][0:64, :], A.wr[c][:, 0:64], tok(B.h[c], t0, TJ),
                        start=(c == 0), stop=(c == NCHUNK - 1)),
                        reads=[r_("l0b_wr", c), r_("h", c)], writes=[r_("pb", 1)],
                        tag="bk_mmr")
            if m % 2 == 0:
                t1, t2 = A.tmp[0], A.tmp[1]
                S.op("vector", lambda e, t0=t0, t1=t1: e.scalar_tensor_tensor(
                    t1[0:64, :], B.pb[0][0:64, :], 1.0 / 16.0,
                    A.cq[:, t0:t0 + TJ], ALU.mult, ALU.mult),
                    reads=[r_("pb", 0), r_("l0b_cq")], writes=[r_("l0b_t", 0)],
                    tag="bk_r1")
                S.op("vector", lambda e, t0=t0, t2=t2: e.scalar_tensor_tensor(
                    t2[0:64, :], B.pb[1][0:64, :], 1.0 / 16.0,
                    A.sq_[:, t0:t0 + TJ], ALU.mult, ALU.mult),
                    reads=[r_("pb", 1), r_("l0b_sq")], writes=[r_("l0b_t", 1)],
                    tag="bk_r2")
                S.op("vector", lambda e, t0=t0, kb=kb, t1=t1, t2=t2: e.tensor_add(
                    kb[0:64, t0:t0 + TJ], t1[0:64, :], t2[0:64, :]),
                    reads=[r_("l0b_t", 0), r_("l0b_t", 1)], writes=[kkey],
                    tag="bk_r3")
                S.op("scalar", lambda e, t0=t0, kb=kb: e.activation(
                    kb[64:128, t0:t0 + TJ], B.pb[0][64:128, :], AF.Copy,
                    scale=1.0 / 16.0),
                    reads=[r_("pb", 0)], writes=[kkey], tag="bk_pass")
            else:
                S.op("scalar", lambda e, t0=t0, kb=kb: e.activation(
                    kb[:, t0:t0 + TJ], B.pb[0][:], AF.Copy, scale=1.0 / 16.0),
                    reads=[r_("pb", 0)], writes=[kkey], tag="bk_pass2")

    for tj in range(NTJ):
        t0 = tj * TJ
        for h in range(8):
            kv = h // 4
            _load_w_tiles(S, A.wq, "l0b_wq", W["L0_attn_b_Wq"], h * 256, 256)
            _load_w_tiles(S, A.wr, "l0b_wr", W["p_l0b_wqrot"], h * 64, 64)
            _load_w_tiles(S, A.wg, "l0b_wg", W["L0_attn_b_Wq"], 2048 + h * 256, 256)
            for cc in range(2):
                qb = A.qh[cc]
                for c in range(NCHUNK):
                    S.op("tensor", lambda e, c=c, cc=cc: e.matmul(
                        B.pb[0][:], A.wq[c][:, cc * 128:cc * 128 + 128],
                        tok(B.h[c], t0, TJ),
                        start=(c == 0), stop=(c == NCHUNK - 1)),
                        reads=[r_("l0b_wq", c), r_("h", c)], writes=[r_("pb", 0)],
                        tag="bq_mm")
                if cc == 0:
                    for c in range(NCHUNK):
                        S.op("tensor", lambda e, c=c: e.matmul(
                            B.pb[1][0:64, :], A.wr[c][:, 0:64],
                            tok(B.h[c], t0, TJ),
                            start=(c == 0), stop=(c == NCHUNK - 1)),
                            reads=[r_("l0b_wr", c), r_("h", c)],
                            writes=[r_("pb", 1)], tag="bq_mmr")
                    t1, t2 = A.tmp[0], A.tmp[1]
                    S.op("vector", lambda e, t1=t1: e.tensor_mul(
                        t1[0:64, :], B.pb[0][0:64, :], A.cq[:, t0:t0 + TJ]),
                        reads=[r_("pb", 0), r_("l0b_cq")], writes=[r_("l0b_t", 0)],
                        tag="bq_r1")
                    S.op("vector", lambda e, t2=t2: e.tensor_mul(
                        t2[0:64, :], B.pb[1][0:64, :], A.sq_[:, t0:t0 + TJ]),
                        reads=[r_("pb", 1), r_("l0b_sq")], writes=[r_("l0b_t", 1)],
                        tag="bq_r2")
                    S.op("vector", lambda e, qb=qb, t1=t1, t2=t2: e.tensor_add(
                        qb[0:64, :], t1[0:64, :], t2[0:64, :]),
                        reads=[r_("l0b_t", 0), r_("l0b_t", 1)],
                        writes=[r_("l0b_qh", cc)], tag="bq_r3")
                    S.op("scalar", lambda e, qb=qb: e.copy(
                        qb[64:128, :], B.pb[0][64:128, :]),
                        reads=[r_("pb", 0)], writes=[r_("l0b_qh", cc)],
                        tag="bq_pass")
                else:
                    S.op("scalar", lambda e, qb=qb: e.copy(qb[:], B.pb[0][:]),
                         reads=[r_("pb", 0)], writes=[r_("l0b_qh", cc)],
                         tag="bq_pass2")
                gb = A.gs[cc]
                for c in range(NCHUNK):
                    S.op("tensor", lambda e, c=c, cc=cc: e.matmul(
                        B.pb[1][:], A.wg[c][:, cc * 128:cc * 128 + 128],
                        tok(B.h[c], t0, TJ),
                        start=(c == 0), stop=(c == NCHUNK - 1)),
                        reads=[r_("l0b_wg", c), r_("h", c)], writes=[r_("pb", 1)],
                        tag="bg_mm")
                S.op("scalar", lambda e, gb=gb, cc=cc: e.activation(
                    gb[:], B.pb[1][:], AF.Silu),
                    reads=[r_("pb", 1)], writes=[r_("l0b_gs", cc)], tag="bg_silu")
            n_pv = 16 * 2
            for kt in range(16):
                pt = A.pt[kt % 2]
                ptk = r_("l0b_pt", kt % 2)
                ps_S = 2 + (kt % 2) * 2
                for cc in range(2):
                    S.op("tensor", lambda e, kt=kt, cc=cc, ps_S=ps_S: e.matmul(
                        B.pb[ps_S][:], A.kh[2 * kv + cc][:, kt * 128:(kt + 1) * 128],
                        A.qh[cc][:], start=(cc == 0), stop=(cc == 1)),
                        reads=[r_("l0b_kh", 2 * kv + cc), r_("l0b_qh", cc)],
                        writes=[r_("pb", ps_S)], tag="bs_mm")
                S.op("scalar", lambda e, pt=pt, ps_S=ps_S: e.activation(
                    pt[:], B.pb[ps_S][:], AF.Exp, bias=negC),
                    reads=[r_("pb", ps_S), negC_k], writes=[ptk], tag="bs_exp")
                for cc in range(2):
                    S.op("tensor", lambda e, kt=kt, cc=cc, pt=pt: e.matmul(
                        B.pb[3 + 2 * cc][:], A.vt[kt][:, kv * 256 + cc * 128:
                                                      kv * 256 + cc * 128 + 128],
                        pt[:], start=(kt == 0), stop=(kt == 15)),
                        reads=[r_("l0b_vt", kt), ptk],
                        writes=[r_("pb", 3 + 2 * cc)], tag="bpv")
                S.op("tensor", lambda e, pt=pt, kt=kt: e.matmul(
                    B.pb[6][0:1, :], B.onecol[:], pt[:],
                    start=(kt == 0), stop=(kt == 15)),
                    reads=[r_("onecol"), ptk], writes=[r_("pb", 6)], tag="bden")
            rec = B.rowf[:, 3 * TJ:4 * TJ]
            rec_k = r_("rowf", 3)
            S.op("vector", lambda e, rec=rec: e.reciprocal(rec, B.pb[6][0:1, :]),
                 reads=[r_("pb", 6)], writes=[rec_k], tag="bn_rec")
            S.op("vector", lambda e, rec=rec: e.tensor_copy(B.rowh[:], rec),
                 reads=[rec_k], writes=[r_("rowh")], tag="bn_cast")
            S.op("tensor", lambda e: e.matmul(
                B.pb[7][:], B.ones_row[:], B.rowh[:], start=True, stop=True),
                reads=[r_("rowh"), r_("ones_row")], writes=[r_("pb", 7)],
                tag="bn_bcmm")
            S.op("scalar", lambda e: e.copy(A.recbc[:], B.pb[7][:]),
                 reads=[r_("pb", 7)], writes=[r_("l0b_recbc")], tag="bn_bcev")
            for cc in range(2):
                S.op("vector", lambda e, cc=cc: e.tensor_mul(
                    A.gm[:], A.gs[cc][:], A.recbc[:]),
                    reads=[r_("l0b_gs", cc), r_("l0b_recbc")], writes=[r_("l0b_gm")],
                    tag="bn_gm")
                S.op("vector", lambda e, cc=cc, h=h: e.tensor_mul(
                    A.ao[2 * h + cc][:], B.pb[3 + 2 * cc][:], A.gm[:]),
                    reads=[r_("pb", 3 + 2 * cc), r_("l0b_gm")],
                    writes=[r_("l0b_ao", 2 * h + cc)], tag="bn_mul")
        for oc in range(NCHUNK):
            S.dma("gpsimd", A.wo[:],
                  W["p_l0b_wox"][oc].rearrange("p a j -> p (a j)"),
                  writes=[r_("l0b_wo")], tag="bwo_ld")
            for i in range(16):
                S.op("tensor", lambda e, i=i, oc=oc: e.matmul(
                    B.pb[0 + oc % 2][:], A.wo[:, i * 128:(i + 1) * 128],
                    A.ao[i][:], start=(i == 0), stop=(i == 15)),
                    reads=[r_("l0b_wo"), r_("l0b_ao", i)],
                    writes=[r_("pb", 0 + oc % 2)], tag="bwo_mm")
            S.op("vector", lambda e, oc=oc, t0=t0: e.tensor_add(
                tok(B.x[oc], t0, TJ), tok(B.x[oc], t0, TJ),
                B.pb[0 + oc % 2][:]),
                reads=[r_("pb", 0 + oc % 2), r_("x", oc)], writes=[r_("x", oc)],
                tag="bwo_res")



L1A_NAMES = ["p_l1a_wqT", "p_l1a_wqTp", "p_l1a_wkT", "p_l1a_wkTp",
             "p_l1a_wvT", "p_l1a_wox", "cst_cosI_q", "cst_sinI_q",
             "cst_l1_tri"]


def alloc_l1a(nc, ctx, sfx=""):
    A = Bufs()
    ent = ctx.enter_context
    A.kh = [ent(nc.sbuf_tensor(f"l1a_kh{c}{sfx}", [128, T], BF16)) for c in range(4)]
    A.vt = [ent(nc.sbuf_tensor(f"l1a_vt{t}{sfx}", [128, 520], BF16)) for t in range(16)]
    A.qh = ent(nc.sbuf_tensor(f"l1a_qh{sfx}", [128, TJ], BF16))
    A.pt = [ent(nc.sbuf_tensor(f"l1a_pt{i}{sfx}", [128, TJ], BF16)) for i in range(2)]
    A.recbc = ent(nc.sbuf_tensor(f"l1a_recbc{sfx}", [64, TJ], BF16))
    A.ao = [ent(nc.sbuf_tensor(f"l1a_ao{c}{sfx}", [128, TJ], BF16)) for c in range(4)]
    A.wv = [ent(nc.sbuf_tensor(f"l1a_wv{c}{sfx}", [128, 512], BF16)) for c in range(NCHUNK)]
    A.wo = ent(nc.sbuf_tensor(f"l1a_wo{sfx}", [128, 512], BF16))
    A.tmp = [ent(nc.sbuf_tensor(f"l1a_tmp{i}{sfx}", [128, TJ], BF16)) for i in range(2)]
    A.cq = ent(nc.sbuf_tensor(f"l1a_cq{sfx}", [128, T], BF16))
    A.sq_ = ent(nc.sbuf_tensor(f"l1a_sq{sfx}", [128, T], BF16))
    A.tri = ent(nc.sbuf_tensor(f"l1a_tri{sfx}", [128, 128], BF16))
    return A


def emit_l1a_setup(S, nc, B, A, W):
    S.dma("sync", A.cq[:], W["cst_cosI_q"], writes=[r_("l1a_cq")])
    S.dma("sync", A.sq_[:], W["cst_sinI_q"], writes=[r_("l1a_sq")])
    S.dma("sync", A.tri[:], W["cst_l1_tri"], writes=[r_("l1a_tri")])


def emit_l1_attn_a(S, nc, B, A, W):
    emit_ln(S, B, "ln")
    negC, negC_k = cvec(S, B, -C_L1A)
    for half in range(2):
        emit_l1a_half(S, nc, B, A, W, half, negC, negC_k)


def emit_l1a_half(S, nc, B, A, W, half, negC, negC_k):
    hbase = 4 * half

    for c in range(NCHUNK):
        S.dma("gpsimd", A.wv[c][:, 0:512],
              W["p_l1a_wvT"][c * 128:(c + 1) * 128, 512 * half:512 * half + 512],
              writes=[r_("l1a_wv", c), r_("l1a_wv2", c)], tag="wloadv")
    for tt in range(16):
        vt = A.vt[tt]
        S.op("vector", lambda e, vt=vt: e.memset(
            vt.ap().rearrange("p (a b) -> p a b", b=65)[:, :, 64:65], 1.0),
            writes=[r_("l1a_vt", tt)], tag="avones")
        for c in range(NCHUNK):
            S.op("tensor", lambda e, c=c, tt=tt: e.matmul(
                B.pb[6][:, 0:512], tok(B.h[c], tt * 128, 128),
                A.wv[c][:, 0:512],
                start=(c == 0), stop=(c == NCHUNK - 1)),
                reads=[r_("l1a_wv", c), r_("h", c)], writes=[r_("pb", 6)],
                tag="avproj")
        S.op("scalar", lambda e, vt=vt: e.copy(
            vt.ap().rearrange("p (a b) -> p a b", b=65)[:, :, 0:64],
            B.pb[6][:, 0:512].rearrange("p (a b) -> p a b", b=64)),
            reads=[r_("pb", 6)], writes=[r_("l1a_vt", tt)], tag="avevac")

    for ci in range(4):
        c = hbase + ci
        kb = A.kh[ci]
        kkey = r_("l1a_kh", ci)
        _load_w_tiles(S, A.wv, "l1a_wv", W["p_l1a_wkT"], c * 128, 128)
        _load_w_tiles(S, A.wv, "l1a_wv2", W["p_l1a_wkTp"], c * 128, 128,
                      coloff=128)
        for tj in range(NTJ):
            t0 = tj * TJ
            for cc in range(NCHUNK):
                S.op("tensor", lambda e, cc=cc, t0=t0: e.matmul(
                    B.pb[0][:], A.wv[cc][:, 0:128], tok(B.h[cc], t0, TJ),
                    start=(cc == 0), stop=(cc == NCHUNK - 1)),
                    reads=[r_("l1a_wv", cc), r_("h", cc)], writes=[r_("pb", 0)],
                    tag="ak_mm")
                S.op("tensor", lambda e, cc=cc, t0=t0: e.matmul(
                    B.pb[1][:], A.wv[cc][:, 128:256], tok(B.h[cc], t0, TJ),
                    start=(cc == 0), stop=(cc == NCHUNK - 1)),
                    reads=[r_("l1a_wv2", cc), r_("h", cc)], writes=[r_("pb", 1)],
                    tag="ak_mmp")
            t1, t2 = A.tmp[0], A.tmp[1]
            S.op("vector", lambda e, t0=t0, t1=t1: e.scalar_tensor_tensor(
                t1[:], B.pb[0][:], 8.0, A.cq[:, t0:t0 + TJ], ALU.mult, ALU.mult),
                reads=[r_("pb", 0), r_("l1a_cq")], writes=[r_("l1a_t", 0)],
                tag="ak_r1")
            S.op("vector", lambda e, t0=t0, t2=t2: e.scalar_tensor_tensor(
                t2[:], B.pb[1][:], 8.0, A.sq_[:, t0:t0 + TJ], ALU.mult, ALU.mult),
                reads=[r_("pb", 1), r_("l1a_sq")], writes=[r_("l1a_t", 1)],
                tag="ak_r2")
            S.op("vector", lambda e, t0=t0, kb=kb, t1=t1, t2=t2: e.tensor_add(
                kb[:, t0:t0 + TJ], t1[:], t2[:]),
                reads=[r_("l1a_t", 0), r_("l1a_t", 1)], writes=[kkey], tag="ak_r3")

    for tj in range(NTJ):
        t0 = tj * TJ
        for ci in range(4):
            c = hbase + ci
            qb = A.qh
            qbk = r_("l1a_qh")
            _load_w_tiles(S, A.wv, "l1a_wv", W["p_l1a_wqT"], c * 128, 128)
            _load_w_tiles(S, A.wv, "l1a_wv2", W["p_l1a_wqTp"], c * 128, 128,
                          coloff=128)
            for cc in range(NCHUNK):
                S.op("tensor", lambda e, cc=cc, t0=t0: e.matmul(
                    B.pb[0][:], A.wv[cc][:, 0:128], tok(B.h[cc], t0, TJ),
                    start=(cc == 0), stop=(cc == NCHUNK - 1)),
                    reads=[r_("l1a_wv", cc), r_("h", cc)], writes=[r_("pb", 0)],
                    tag="aq_mm")
                S.op("tensor", lambda e, cc=cc, t0=t0: e.matmul(
                    B.pb[1][:], A.wv[cc][:, 128:256], tok(B.h[cc], t0, TJ),
                    start=(cc == 0), stop=(cc == NCHUNK - 1)),
                    reads=[r_("l1a_wv2", cc), r_("h", cc)], writes=[r_("pb", 1)],
                    tag="aq_mmp")
            t1, t2 = A.tmp[0], A.tmp[1]
            S.op("vector", lambda e, t0=t0, t1=t1: e.tensor_mul(
                t1[:], B.pb[0][:], A.cq[:, t0:t0 + TJ]),
                reads=[r_("pb", 0), r_("l1a_cq")], writes=[r_("l1a_t", 0)],
                tag="aq_r1")
            S.op("vector", lambda e, t0=t0, t2=t2: e.tensor_mul(
                t2[:], B.pb[1][:], A.sq_[:, t0:t0 + TJ]),
                reads=[r_("pb", 1), r_("l1a_sq")], writes=[r_("l1a_t", 1)],
                tag="aq_r2")
            S.op("vector", lambda e, qb=qb, t1=t1, t2=t2: e.tensor_add(
                qb[:], t1[:], t2[:]),
                reads=[r_("l1a_t", 0), r_("l1a_t", 1)], writes=[qbk], tag="aq_r3")
            for hh in range(2):
                h = 8 * half + 2 * ci + hh
                rs = 64 * hh
                n_pv = 4 * tj + 4
                pvi = [0]

                def pv(lhs, lk, rhs, rk, qlo, w):
                    st = pvi[0] == 0
                    sp = pvi[0] == n_pv - 1
                    pvi[0] += 1
                    S.op("tensor", lambda e, lhs=lhs, rhs=rhs, qlo=qlo, w=w,
                         st=st, sp=sp: e.matmul(
                             B.pb[3][0:65, qlo:qlo + w], lhs, rhs,
                             start=st, stop=sp),
                         reads=[lk, rk], writes=[r_("pb", 3)], tag="apv")

                hg = h % 8
                for kt in range(4 * tj + 4):
                    r = kt - 4 * tj
                    qlo = max(0, 128 * r)
                    w = TJ - qlo
                    pt = A.pt[kt % 2]
                    ptk = r_("l1a_pt", kt % 2)
                    ps_S = 2 + (kt % 2) * 2
                    S.op("tensor", lambda e, kt=kt, qlo=qlo, w=w, rs=rs,
                         ps_S=ps_S, ci=ci: e.matmul(
                             B.pb[ps_S][:, 0:w],
                             A.kh[ci][rs:rs + 64, kt * 128:(kt + 1) * 128],
                             qb[rs:rs + 64, qlo:qlo + w], start=True, stop=True),
                        reads=[r_("l1a_kh", ci), qbk], writes=[r_("pb", ps_S)],
                        tag="as_mm", pe_base=rs)
                    S.op("scalar", lambda e, w=w, pt=pt, ps_S=ps_S: e.activation(
                        pt[:, 0:w], B.pb[ps_S][:, 0:w], AF.Exp, bias=negC),
                        reads=[r_("pb", ps_S), negC_k], writes=[ptk], tag="as_exp")
                    if r >= 0:
                        S.op("vector", lambda e, pt=pt: e.tensor_tensor(
                            pt[:, 0:128], pt[:, 0:128], A.tri[:], ALU.min),
                            reads=[ptk, r_("l1a_tri")], writes=[ptk], tag="as_msk")
                    pv(A.vt[kt][:, 65 * hg:65 * hg + 65], r_("l1a_vt", kt),
                       pt[:, 0:w], ptk, qlo, w)
                rec = B.rowf[:, 3 * TJ:4 * TJ]
                rec_k = r_("rowf", 3)
                S.op("vector", lambda e, rec=rec: e.reciprocal(
                    rec, B.pb[3][64:65, 0:TJ]),
                    reads=[r_("pb", 3)], writes=[rec_k], tag="an_rec")
                S.op("vector", lambda e, rec=rec: e.tensor_copy(B.rowh[:], rec),
                     reads=[rec_k], writes=[r_("rowh")], tag="an_cast")
                S.op("tensor", lambda e: e.matmul(
                    B.pb[7][0:64, 0:TJ], B.ones_row[:, 0:64], B.rowh[:],
                    start=True, stop=True),
                    reads=[r_("rowh"), r_("ones_row")], writes=[r_("pb", 7)],
                    tag="an_bcmm")
                S.op("scalar", lambda e: e.copy(A.recbc[:], B.pb[7][0:64, 0:TJ]),
                     reads=[r_("pb", 7)], writes=[r_("l1a_recbc")], tag="an_bcev")
                S.op("vector", lambda e, ci=ci, rs=rs: e.tensor_mul(
                    A.ao[ci][rs:rs + 64, :], B.pb[3][0:64, 0:TJ], A.recbc[:]),
                    reads=[r_("pb", 3), r_("l1a_recbc")], writes=[r_("l1a_ao", ci)],
                    tag="an_mul")
        for oc in range(NCHUNK):
            S.dma("gpsimd", A.wo[:],
                  W["p_l1a_wox"][oc, :, 4 * half:4 * half + 4, :].rearrange(
                      "p a j -> p (a j)"),
                  writes=[r_("l1a_wo")], tag="awo_ld")
            for i in range(4):
                S.op("tensor", lambda e, i=i, oc=oc: e.matmul(
                    B.pb[0 + oc % 2][:], A.wo[:, i * 128:(i + 1) * 128],
                    A.ao[i][:], start=(i == 0), stop=(i == 3)),
                    reads=[r_("l1a_wo"), r_("l1a_ao", i)],
                    writes=[r_("pb", 0 + oc % 2)], tag="awo_mm")
            S.op("vector", lambda e, oc=oc, t0=t0: e.tensor_add(
                tok(B.x[oc], t0, TJ), tok(B.x[oc], t0, TJ), B.pb[0 + oc % 2][:]),
                reads=[r_("pb", 0 + oc % 2), r_("x", oc)], writes=[r_("x", oc)],
                tag="awo_res")



L1B_NAMES = ["p_l1b_wqT", "p_l1b_wqTp", "p_l1b_wkT", "p_l1b_wkTp",
             "p_l1b_wvT", "p_l1b_wox", "p_l1b_bo8", "cst_cosH_q",
             "cst_sinH_q", "cst_l1b_tri4", "cst_l1b_low4"]


def alloc_l1b(nc, ctx, sfx=""):
    A = Bufs()
    ent = ctx.enter_context
    A.kh = [ent(nc.sbuf_tensor(f"l1b_kh{c}{sfx}", [128, T], BF16)) for c in range(4)]
    A.vt = [ent(nc.sbuf_tensor(f"l1b_vt{t}{sfx}", [128, 520], BF16)) for t in range(16)]
    A.qh2 = [ent(nc.sbuf_tensor(f"l1b_qh{i}{sfx}", [128, TJ], BF16)) for i in range(2)]
    A.pt = [ent(nc.sbuf_tensor(f"l1b_pt{i}{sfx}", [128, TJ], BF16)) for i in range(2)]
    A.recbc = ent(nc.sbuf_tensor(f"l1b_recbc{sfx}", [64, TJ], BF16))
    A.ao = [ent(nc.sbuf_tensor(f"l1b_ao{c}{sfx}", [128, T], BF16)) for c in range(4)]
    A.wv = [ent(nc.sbuf_tensor(f"l1b_wv{c}{sfx}", [128, 512], BF16)) for c in range(NCHUNK)]
    A.wo = ent(nc.sbuf_tensor(f"l1b_wo{sfx}", [128, 512], BF16))
    A.tmp = [ent(nc.sbuf_tensor(f"l1b_tmp{i}{sfx}", [128, TJ], BF16)) for i in range(2)]
    A.cq = ent(nc.sbuf_tensor(f"l1b_cq{sfx}", [128, T], BF16))
    A.sq_ = ent(nc.sbuf_tensor(f"l1b_sq{sfx}", [128, T], BF16))
    A.tri4 = ent(nc.sbuf_tensor(f"l1b_tri4{sfx}", [128, 512], BF16))
    A.low4 = ent(nc.sbuf_tensor(f"l1b_low4{sfx}", [128, 512], BF16))
    A.bo = ent(nc.sbuf_tensor(f"l1b_bo{sfx}", [128, 8], F32))
    return A


def emit_l1b_setup(S, nc, B, A, W):
    S.dma("sync", A.cq[:], W["cst_cosH_q"], writes=[r_("l1b_cq")])
    S.dma("sync", A.sq_[:], W["cst_sinH_q"], writes=[r_("l1b_sq")])
    S.dma("sync", A.tri4[:], W["cst_l1b_tri4"], writes=[r_("l1b_tri4")])
    S.dma("sync", A.low4[:], W["cst_l1b_low4"], writes=[r_("l1b_low4")])
    S.dma("sync", A.bo[:], W["p_l1b_bo8"], writes=[r_("l1b_bo")])


def emit_l1_attn_b(S, nc, B, A, W):
    emit_ln(S, B, "ln")
    negC, negC_k = cvec(S, B, -C_L1B)
    for half in range(2):
        emit_l1b_half(S, nc, B, A, W, half, negC, negC_k)


L1B_STAGE = 4


def emit_l1b_half(S, nc, B, A, W, half, negC, negC_k):
    hbase = 4 * half

    for c in range(NCHUNK):
        S.dma("gpsimd", A.wv[c][:, 0:512],
              W["p_l1b_wvT"][c * 128:(c + 1) * 128, 512 * half:512 * half + 512],
              writes=[r_("l1b_wv", c), r_("l1b_wv2", c)], tag="wloadv")
    for tt in range(16):
        vt = A.vt[tt]
        S.op("vector", lambda e, vt=vt: e.memset(
            vt.ap().rearrange("p (a b) -> p a b", b=65)[:, :, 64:65], 1.0),
            writes=[r_("l1b_vt", tt)], tag="bvones")
        for c in range(NCHUNK):
            S.op("tensor", lambda e, c=c, tt=tt: e.matmul(
                B.pb[6][:, 0:512], tok(B.h[c], tt * 128, 128), A.wv[c][:, 0:512],
                start=(c == 0), stop=(c == NCHUNK - 1)),
                reads=[r_("l1b_wv", c), r_("h", c)], writes=[r_("pb", 6)],
                tag="bvproj")
        S.op("scalar", lambda e, vt=vt: e.copy(
            vt.ap().rearrange("p (a b) -> p a b", b=65)[:, :, 0:64],
            B.pb[6][:, 0:512].rearrange("p (a b) -> p a b", b=64)),
            reads=[r_("pb", 6)], writes=[r_("l1b_vt", tt)], tag="bvevac")

    for ci in range(4):
        c = hbase + ci
        kb = A.kh[ci]
        kkey = r_("l1b_kh", ci)
        _load_w_tiles(S, A.wv, "l1b_wv", W["p_l1b_wkT"], c * 128, 128)
        _load_w_tiles(S, A.wv, "l1b_wv2", W["p_l1b_wkTp"], c * 128, 128,
                      coloff=128)
        for tj in range(NTJ):
            t0 = tj * TJ
            for cc in range(NCHUNK):
                S.op("tensor", lambda e, cc=cc, t0=t0: e.matmul(
                    B.pb[0][:], A.wv[cc][:, 0:128], tok(B.h[cc], t0, TJ),
                    start=(cc == 0), stop=(cc == NCHUNK - 1)),
                    reads=[r_("l1b_wv", cc), r_("h", cc)], writes=[r_("pb", 0)],
                    tag="bk_mm")
                S.op("tensor", lambda e, cc=cc, t0=t0: e.matmul(
                    B.pb[1][:], A.wv[cc][:, 128:256], tok(B.h[cc], t0, TJ),
                    start=(cc == 0), stop=(cc == NCHUNK - 1)),
                    reads=[r_("l1b_wv2", cc), r_("h", cc)], writes=[r_("pb", 1)],
                    tag="bk_mmp")
            t1, t2 = A.tmp[0], A.tmp[1]
            S.op("vector", lambda e, t0=t0, t1=t1: e.scalar_tensor_tensor(
                t1[:], B.pb[0][:], 8.0, A.cq[:, t0:t0 + TJ], ALU.mult, ALU.mult),
                reads=[r_("pb", 0), r_("l1b_cq")], writes=[r_("l1b_t", 0)],
                tag="bk_r1")
            S.op("vector", lambda e, t0=t0, t2=t2: e.scalar_tensor_tensor(
                t2[:], B.pb[1][:], 8.0, A.sq_[:, t0:t0 + TJ], ALU.mult, ALU.mult),
                reads=[r_("pb", 1), r_("l1b_sq")], writes=[r_("l1b_t", 1)],
                tag="bk_r2")
            S.op("vector", lambda e, t0=t0, kb=kb, t1=t1, t2=t2: e.tensor_add(
                kb[:, t0:t0 + TJ], t1[:], t2[:]),
                reads=[r_("l1b_t", 0), r_("l1b_t", 1)], writes=[kkey], tag="bk_r3")

    if L1B_STAGE < 1:
        return
    for tj in range(NTJ):
        t0 = tj * TJ
        for grp in range(2):
            for gg in range(2):
                ci = 2 * grp + gg
                c = hbase + ci
                qb = A.qh2[gg]
                qbk = r_("l1b_qh", gg)
                _load_w_tiles(S, A.wv, "l1b_wv", W["p_l1b_wqT"], c * 128, 128)
                _load_w_tiles(S, A.wv, "l1b_wv2", W["p_l1b_wqTp"], c * 128, 128,
                              coloff=128)
                for cc in range(NCHUNK):
                    S.op("tensor", lambda e, cc=cc, t0=t0: e.matmul(
                        B.pb[0][:], A.wv[cc][:, 0:128], tok(B.h[cc], t0, TJ),
                        start=(cc == 0), stop=(cc == NCHUNK - 1)),
                        reads=[r_("l1b_wv", cc), r_("h", cc)], writes=[r_("pb", 0)],
                        tag="bq_mm")
                    S.op("tensor", lambda e, cc=cc, t0=t0: e.matmul(
                        B.pb[1][:], A.wv[cc][:, 128:256], tok(B.h[cc], t0, TJ),
                        start=(cc == 0), stop=(cc == NCHUNK - 1)),
                        reads=[r_("l1b_wv2", cc), r_("h", cc)],
                        writes=[r_("pb", 1)], tag="bq_mmp")
                t1, t2 = A.tmp[0], A.tmp[1]
                S.op("vector", lambda e, t0=t0, t1=t1: e.tensor_mul(
                    t1[:], B.pb[0][:], A.cq[:, t0:t0 + TJ]),
                    reads=[r_("pb", 0), r_("l1b_cq")], writes=[r_("l1b_t", 0)],
                    tag="bq_r1")
                S.op("vector", lambda e, t0=t0, t2=t2: e.tensor_mul(
                    t2[:], B.pb[1][:], A.sq_[:, t0:t0 + TJ]),
                    reads=[r_("pb", 1), r_("l1b_sq")], writes=[r_("l1b_t", 1)],
                    tag="bq_r2")
                S.op("vector", lambda e, qb=qb, t1=t1, t2=t2: e.tensor_add(
                    qb[:], t1[:], t2[:]),
                    reads=[r_("l1b_t", 0), r_("l1b_t", 1)], writes=[qbk],
                    tag="bq_r3")
            if L1B_STAGE < 2:
                continue
            for qt in range(4):
                qt_abs = 4 * tj + qt
                kts = [qt_abs] + ([qt_abs - 1] if qt_abs > 0 else [])
                n_pv = 4 * len(kts)
                pvi = [0]
                for ki, kt in enumerate(kts):
                    pt = A.pt[ki]
                    ptk = r_("l1b_pt", ki)
                    ps_S = 2 + ki * 2
                    for j in range(4):
                        ci, rs = 2 * grp + j // 2, 64 * (j % 2)
                        S.op("tensor", lambda e, kt=kt, j=j, ci=ci, rs=rs,
                             qt=qt, ps_S=ps_S: e.matmul(
                                 B.pb[ps_S][:, 128 * j:128 * j + 128],
                                 A.kh[ci][rs:rs + 64, kt * 128:(kt + 1) * 128],
                                 A.qh2[j // 2][rs:rs + 64,
                                               128 * qt:128 * qt + 128],
                                 start=True, stop=True),
                            reads=[r_("l1b_kh", ci), r_("l1b_qh", j // 2)],
                            writes=[r_("pb", ps_S)], tag="bs_mm", pe_base=rs)
                    S.op("scalar", lambda e, pt=pt, ps_S=ps_S: e.activation(
                        pt[:], B.pb[ps_S][:], AF.Exp, bias=negC),
                        reads=[r_("pb", ps_S), negC_k], writes=[ptk], tag="bs_exp")
                    if L1B_STAGE != 9:
                        msk = A.tri4 if kt == qt_abs else A.low4
                        mkey = r_("l1b_tri4") if kt == qt_abs else r_("l1b_low4")
                        S.op("vector", lambda e, pt=pt, msk=msk: e.tensor_tensor(
                            pt[:], pt[:], msk[:], ALU.min),
                            reads=[ptk, mkey], writes=[ptk], tag="bs_msk")
                if L1B_STAGE < 3:
                    continue
                for j in range(4):
                    hg = 4 * grp + j
                    for ki, kt in enumerate(kts):
                        pt = A.pt[ki]
                        ptk = r_("l1b_pt", ki)
                        st = pvi[0] == 0
                        sp = pvi[0] == n_pv - 1
                        pvi[0] += 1
                        S.op("tensor", lambda e, kt=kt, j=j, hg=hg, pt=pt,
                             st=st, sp=sp: e.matmul(
                                 B.pb[3][0:65, 128 * j:128 * j + 128],
                                 A.vt[kt][:, 65 * hg:65 * hg + 65],
                                 pt[:, 128 * j:128 * j + 128],
                                 start=st, stop=sp),
                            reads=[r_("l1b_vt", kt), ptk], writes=[r_("pb", 3)],
                            tag="bpv")
                rec = B.rowf[:, 3 * TJ:4 * TJ]
                rec_k = r_("rowf", 3)
                S.op("vector", lambda e, rec=rec: e.reciprocal(
                    rec, B.pb[3][64:65, 0:TJ]),
                    reads=[r_("pb", 3)], writes=[rec_k], tag="bn_rec")
                S.op("vector", lambda e, rec=rec: e.tensor_copy(B.rowh[:], rec),
                     reads=[rec_k], writes=[r_("rowh")], tag="bn_cast")
                S.op("tensor", lambda e: e.matmul(
                    B.pb[7][0:64, 0:TJ], B.ones_row[:, 0:64], B.rowh[:],
                    start=True, stop=True),
                    reads=[r_("rowh"), r_("ones_row")], writes=[r_("pb", 7)],
                    tag="bn_bcmm")
                S.op("scalar", lambda e: e.copy(A.recbc[:], B.pb[7][0:64, 0:TJ]),
                     reads=[r_("pb", 7)], writes=[r_("l1b_recbc")], tag="bn_bcev")
                for j in range(4):
                    ci, rs = 2 * grp + j // 2, 64 * (j % 2)
                    S.op("vector", lambda e, j=j, ci=ci, rs=rs, qt=qt, t0=t0:
                         e.tensor_mul(
                             A.ao[ci][rs:rs + 64,
                                      t0 + 128 * qt:t0 + 128 * qt + 128],
                             B.pb[3][0:64, 128 * j:128 * j + 128],
                             A.recbc[:, 128 * j:128 * j + 128]),
                        reads=[r_("pb", 3), r_("l1b_recbc")],
                        writes=[r_("l1b_ao", ci)], tag="bn_mul")
    if L1B_STAGE < 4:
        return
    for tj in range(NTJ):
        t0 = tj * TJ
        for oc in range(NCHUNK):
            S.dma("gpsimd", A.wo[:],
                  W["p_l1b_wox"][oc, :, 4 * half:4 * half + 4, :].rearrange(
                      "p a j -> p (a j)"),
                  writes=[r_("l1b_wo")], tag="bwo_ld")
            for i in range(4):
                S.op("tensor", lambda e, i=i, oc=oc, t0=t0: e.matmul(
                    B.pb[0 + oc % 2][:], A.wo[:, i * 128:(i + 1) * 128],
                    A.ao[i][:, t0:t0 + TJ], start=(i == 0), stop=(i == 3)),
                    reads=[r_("l1b_wo"), r_("l1b_ao", i)],
                    writes=[r_("pb", 0 + oc % 2)], tag="bwo_mm")
            if half == 0:
                S.op("vector", lambda e, oc=oc, t0=t0: e.scalar_tensor_tensor(
                    tok(B.x[oc], t0, TJ), B.pb[0 + oc % 2][:],
                    A.bo[:, oc:oc + 1], tok(B.x[oc], t0, TJ), ALU.add, ALU.add),
                    reads=[r_("pb", 0 + oc % 2), r_("l1b_bo"), r_("x", oc)],
                    writes=[r_("x", oc)], tag="bwo_res")
            else:
                S.op("vector", lambda e, oc=oc, t0=t0: e.tensor_add(
                    tok(B.x[oc], t0, TJ), tok(B.x[oc], t0, TJ),
                    B.pb[0 + oc % 2][:]),
                    reads=[r_("pb", 0 + oc % 2), r_("x", oc)],
                    writes=[r_("x", oc)], tag="bwo_res2")



C_L1F = 35.0
L1F_NAMES = ["L1_ffn_wqkv", "p_l1f_wqkvp", "L1_ffn_w1", "L1_ffn_w2",
             "p_l1f_wox", "p_l1f_qsv", "p_l1f_lg", "p_l1f_lb", "p_l1f_b1a",
             "p_l1f_b1g", "p_l1f_b2", "cst_xq_cos", "cst_xq_sin",
             "cst_xk_cos", "cst_xk_sin", "cst_l1_tri", "cst_geq",
             "cst_oc2", "cst_bc2"]


def alloc_l1f(nc, ctx, sfx=""):
    F = Bufs()
    ent = ctx.enter_context
    F.big = [ent(nc.sbuf_tensor(f"l1f_big{i}{sfx}", [128, T], BF16)) for i in range(8)]
    F.vt = [ent(nc.sbuf_tensor(f"l1f_vt{t}{sfx}", [128, 520], BF16)) for t in range(16)]
    F.aow = [ent(nc.sbuf_tensor(f"l1f_ao{i}{sfx}", [128, TJ], BF16)) for i in range(4)]
    F.hp = [ent(nc.sbuf_tensor(f"l1f_hp{c}{sfx}", [128, TJ], BF16)) for c in range(NCHUNK)]
    F.tc = ent(nc.sbuf_tensor(f"l1f_tc{sfx}", [128, T], BF16))
    F.ts = ent(nc.sbuf_tensor(f"l1f_ts{sfx}", [128, T], BF16))
    F.wa = [ent(nc.sbuf_tensor(f"l1f_wa{c}{sfx}", [128, 128], BF16)) for c in range(NCHUNK)]
    F.wb = [ent(nc.sbuf_tensor(f"l1f_wb{c}{sfx}", [128, 128], BF16)) for c in range(NCHUNK)]
    F.w2c = ent(nc.sbuf_tensor(f"l1f_w2c{sfx}", [128, 2816], BF16))
    F.wo = ent(nc.sbuf_tensor(f"l1f_wo{sfx}", [128, 512], BF16))
    F.praw = [ent(nc.sbuf_tensor(f"l1f_praw{i}{sfx}", [128, TJ], BF16)) for i in range(2)]
    F.t = [ent(nc.sbuf_tensor(f"l1f_t{i}{sfx}", [128, TJ], BF16)) for i in range(2)]
    F.pt = [ent(nc.sbuf_tensor(f"l1f_pt{i}{sfx}", [128, TJ], BF16)) for i in range(2)]
    F.rnbc = ent(nc.sbuf_tensor(f"l1f_rnbc{sfx}", [128, TJ], BF16))
    F.recbc = ent(nc.sbuf_tensor(f"l1f_recbc{sfx}", [64, TJ], BF16))
    F.rh2 = ent(nc.sbuf_tensor(f"l1f_rh2{sfx}", [1, TJ], BF16))
    F.oc2 = ent(nc.sbuf_tensor(f"l1f_oc2{sfx}", [128, 2], BF16))
    F.bc2 = ent(nc.sbuf_tensor(f"l1f_bc2{sfx}", [2, 128], BF16))
    F.geq = ent(nc.sbuf_tensor(f"l1f_geq{sfx}", [128, 128], BF16))
    F.tri = ent(nc.sbuf_tensor(f"l1f_tri{sfx}", [128, 128], BF16))
    F.qsv = ent(nc.sbuf_tensor(f"l1f_qsv{sfx}", [128, 8], F32))
    F.lg = ent(nc.sbuf_tensor(f"l1f_lg{sfx}", [128, 8], F32))
    F.lb = ent(nc.sbuf_tensor(f"l1f_lb{sfx}", [128, 8], F32))
    F.b1a = ent(nc.sbuf_tensor(f"l1f_b1a{sfx}", [128, 22], F32))
    F.b1g = ent(nc.sbuf_tensor(f"l1f_b1g{sfx}", [128, 22], F32))
    F.b2 = ent(nc.sbuf_tensor(f"l1f_b2{sfx}", [128, 8], F32))
    return F


def emit_l1f_setup(S, nc, B, F, W, scratch=True, consts=True):
    if consts:
        S.dma("sync", F.geq[:], W["cst_geq"], writes=[r_("l1f_geq")])
        S.dma("sync", F.tri[:], W["cst_l1_tri"], writes=[r_("l1f_tri")])
        S.dma("sync", F.qsv[:], W["p_l1f_qsv"], writes=[r_("l1f_qsv")])
        S.dma("sync", F.oc2[:], W["cst_oc2"], writes=[r_("l1f_oc2")])
        S.dma("sync", F.bc2[:], W["cst_bc2"], writes=[r_("l1f_bc2")])
    if not scratch:
        F.scr_x, F.scr_w1a, F.scr_w1g, F.scr_w2 = B.scr_l1f
        return
    F.scr_x = nc.dram_tensor("scr_xout", [NCHUNK, 128, T], F32,
                             kind="Internal").ap()
    F.scr_w1a = nc.dram_tensor("scr_w1a", [2, 22, 128, 8, 128], BF16,
                               kind="Internal").ap()
    F.scr_w1g = nc.dram_tensor("scr_w1g", [2, 22, 128, 8, 128], BF16,
                               kind="Internal").ap()
    F.scr_w2 = nc.dram_tensor("scr_w2f", [2, 8, 128, 22, 128], BF16,
                              kind="Internal").ap()
    B.scr_l1f = (F.scr_x, F.scr_w1a, F.scr_w1g, F.scr_w2)
    w1 = W["L1_ffn_w1"]
    w2 = W["L1_ffn_w2"]
    for l in range(2):
        for part, scr in ((0, F.scr_w1a), (1, F.scr_w1g)):
            for ns in range(22):
                n0 = part * 2730 + ns * 128
                wcols = min(128, part * 2730 + 2730 - n0)
                stg = F.w2c[:, 0:1024]
                if wcols < 128:
                    S.op("vector", lambda e, stg=stg: e.memset(stg, 0.0),
                         writes=[r_("l1f_w2c", 0)], tag="sw_z")
                dst = stg.rearrange("p (c j) -> p c j", c=8)[:, :, 0:wcols]
                S.dma("gpsimd", dst,
                      w1[l, :, n0:n0 + wcols].rearrange("(c p) j -> p c j", p=128),
                      writes=[r_("l1f_w2c", 0)], tag="sw_ld")
                S.dma("sync", scr[l, ns].rearrange("p c j -> p (c j)"), stg,
                      reads=[r_("l1f_w2c", 0)], writes=[r_("scrw1", part, l, ns)],
                      tag="sw_st")
        for i in range(22):
            k0 = i * 128
            krows = min(128, 2730 - k0)
            stg = F.w2c[:, 1024:2048]
            if krows < 128:
                S.op("vector", lambda e, stg=stg: e.memset(stg, 0.0),
                     writes=[r_("l1f_w2c", 1)], tag="sw_z2")
            S.dma("gpsimd", stg[0:krows, 0:1024], w2[l, k0:k0 + krows, :],
                  writes=[r_("l1f_w2c", 1)], tag="sw_ld2")
            S.dma("sync",
                  bass.AP(F.scr_w2.tensor,
                          l * (8 * 128 * 22 * 128) + i * 128,
                          [[22 * 128, 128], [128 * 22 * 128, 8], [1, 128]]),
                  stg.rearrange("p (oc j) -> p oc j", oc=8),
                  reads=[r_("l1f_w2c", 1)], writes=[r_("scrw2", l, i)], tag="sw_st2")


def emit_l1_ffn(S, nc, B, F, W):
    emit_ln(S, B, "ln")
    for c in range(NCHUNK):
        S.dma("sync", F.scr_x[c], tok(B.x[c], 0, T), reads=[r_("x", c)],
              writes=[r_("scrx", c)], tag="xspill")
    for c in range(NCHUNK):
        S.op("vector", lambda e, c=c: e.tensor_mul(
            tok(B.x[c], 0, T), tok(B.x[c], 0, T), B.b1bc[:]),
            reads=[r_("x", c), r_("b1bc"), r_("scrx", c)], writes=[r_("x", c)],
            tag="xi1")
        S.op("vector", lambda e, c=c: e.tensor_add(
            tok(B.x[c], 0, T), tok(B.x[c], 0, T), B.b2bc[:]),
            reads=[r_("x", c), r_("b2bc")], writes=[r_("x", c)], tag="xi2")
        S.op("vector", lambda e, c=c: e.tensor_copy(
            tok(B.h[c], 0, T), tok(B.x[c], 0, T)),
            reads=[r_("x", c)], writes=[r_("h", c)], tag="xb")
    for l in range(2):
        emit_l1f_layer(S, nc, B, F, W, l)
    for c in range(NCHUNK):
        S.dma("gpsimd", tok(B.x[c], 0, T), F.scr_x[c],
              reads=[r_("scrx", c), r_("x", c)], writes=[r_("x", c)],
              tag="xmerge", accum_op=ALU.add)


def _xsh(B, c, t0, n):
    if c >= 4:
        return B.h[c][:, t0:t0 + n]
    return tok(B.h[c], t0, n)


def emit_l1f_layer(S, nc, B, F, W, l):
    negC, negC_k = cvec(S, B, -C_L1F)
    bigk = lambda i, blk: r_("l1f_big", i, blk)
    allblk = lambda i: [r_("l1f_big", i, b_) for b_ in range(4)]

    S.dma("sync", F.lg[:], W["p_l1f_lg"][l], writes=[r_("l1f_lg")])
    S.dma("sync", F.lb[:], W["p_l1f_lb"][l], writes=[r_("l1f_lb")])
    S.dma("sync", F.b1a[:], W["p_l1f_b1a"][l], writes=[r_("l1f_b1a")])
    S.dma("sync", F.b1g[:], W["p_l1f_b1g"][l], writes=[r_("l1f_b1g")])
    S.dma("sync", F.b2[:], W["p_l1f_b2"][l], writes=[r_("l1f_b2")])

    for c in range(NCHUNK):
        S.dma("gpsimd", F.big[c][:, 0:512],
              W["L1_ffn_wqkv"][l][c * 128:(c + 1) * 128, 1024:1536],
              writes=[bigk(c, 0)], tag="fv_wld")
    for tt in range(16):
        vt = F.vt[tt]
        vb = 6 + tt % 2
        S.op("vector", lambda e, vt=vt: e.memset(
            vt.ap().rearrange("p (a b) -> p a b", b=65)[:, :, 64:65], 1.0),
            writes=[r_("l1f_vt", tt)], tag="fvones")
        for c in range(NCHUNK):
            S.op("tensor", lambda e, c=c, tt=tt, vb=vb: e.matmul(
                B.pb[vb][:, 0:512], _xsh(B, c, tt * 128, 128),
                F.big[c][:, 0:512],
                start=(c == 0), stop=(c == NCHUNK - 1)),
                reads=[bigk(c, 0), r_("h", c)], writes=[r_("pb", vb)],
                tag="fvproj")
        S.op("scalar", lambda e, vt=vt, vb=vb: e.copy(
            vt.ap().rearrange("p (a b) -> p a b", b=65)[:, :, 0:64],
            B.pb[vb][:, 0:512].rearrange("p (a b) -> p a b", b=64)),
            reads=[r_("pb", vb)], writes=[r_("l1f_vt", tt)], tag="fvevac")

    for side in range(2):
        tc, ts_ = F.tc, F.ts
        tck, tsk = r_("l1f_tc"), r_("l1f_ts")
        S.dma("sync", F.tc[:], W["cst_xq_cos" if side == 0 else "cst_xk_cos"],
              writes=[tck], tag="ftab_c")
        S.dma("sync", F.ts[:], W["cst_xq_sin" if side == 0 else "cst_xk_sin"],
              writes=[tsk], tag="ftab_s")
        qv = F.qsv[:, 4 * l + 2 * side:4 * l + 2 * side + 1]
        qpv = F.qsv[:, 4 * l + 2 * side + 1:4 * l + 2 * side + 2]
        for qc in range(4):
            big_i = 4 * side + qc
            n0 = 512 * side + 128 * qc
            _load_w_tiles(S, F.wa, "l1f_wa", W["L1_ffn_wqkv"][l], n0, 128)
            _load_w_tiles(S, F.wb, "l1f_wb", W["p_l1f_wqkvp"][l], n0, 128)
            for tj in range(NTJ):
                t0 = tj * TJ
                for c in range(NCHUNK):
                    S.op("tensor", lambda e, c=c, t0=t0: e.matmul(
                        B.pb[0][:], F.wa[c][:, 0:128], _xsh(B, c, t0, TJ),
                        start=(c == 0), stop=(c == NCHUNK - 1)),
                        reads=[r_("l1f_wa", c), r_("h", c)], writes=[r_("pb", 0)],
                        tag="fq_mm")
                    S.op("tensor", lambda e, c=c, t0=t0: e.matmul(
                        B.pb[1][:], F.wb[c][:, 0:128], _xsh(B, c, t0, TJ),
                        start=(c == 0), stop=(c == NCHUNK - 1)),
                        reads=[r_("l1f_wb", c), r_("h", c)], writes=[r_("pb", 1)],
                        tag="fq_mmp")
                S.op("vector", lambda e: e.tensor_copy(F.praw[0][:], B.pb[0][:]),
                     reads=[r_("pb", 0)], writes=[r_("l1f_praw", 0)], tag="fq_e1")
                S.op("vector", lambda e: e.tensor_copy(F.praw[1][:], B.pb[1][:]),
                     reads=[r_("pb", 1)], writes=[r_("l1f_praw", 1)], tag="fq_e2")
                S.op("scalar", lambda e: e.activation(
                    B.sq[0][:], F.praw[0][:], AF.Square),
                    reads=[r_("l1f_praw", 0)], writes=[r_("sq", 0)], tag="fq_sq")
                S.op("tensor", lambda e: e.matmul(
                    B.pb[5][0:2, :], F.oc2[:], B.sq[0][:],
                    start=True, stop=True),
                    reads=[r_("sq", 0), r_("l1f_oc2")], writes=[r_("pb", 5)],
                    tag="fq_n1", pe_base=0)
                eps0, eps0_k = cvec(S, B, 1e-24, 0, 2)
                mh_ap, mh_k = cvec(S, B, -0.5, 0, 2)
                S.op("scalar", lambda e, a=eps0: e.activation(
                    B.pb[5][0:2, :], B.pb[5][0:2, :], AF.Ln, bias=a),
                    reads=[r_("pb", 5), eps0_k], writes=[r_("pb", 5)],
                    tag="fq_ln")
                S.op("scalar", lambda e, a=mh_ap: e.activation(
                    F.hp[2][0:2, :], B.pb[5][0:2, :], AF.Exp, scale=a),
                    reads=[r_("pb", 5), mh_k], writes=[r_("l1f_hp", 2)],
                    tag="fq_ex")
                S.op("tensor", lambda e: e.matmul(
                    B.pb[7][:], F.bc2[:], F.hp[2][0:2, :],
                    start=True, stop=True),
                    reads=[r_("l1f_hp", 2), r_("l1f_bc2")], writes=[r_("pb", 7)],
                    tag="fq_b1", pe_base=0)
                S.op("vector", lambda e: e.tensor_copy(F.rnbc[:], B.pb[7][:]),
                     reads=[r_("pb", 7)], writes=[r_("l1f_rnbc")], tag="fq_bc")
                S.op("vector", lambda e, t0=t0, qv=qv: e.scalar_tensor_tensor(
                    F.t[0][:], F.praw[0][:], qv, tc[:, t0:t0 + TJ],
                    ALU.mult, ALU.mult),
                    reads=[r_("l1f_praw", 0), r_("l1f_qsv"), tck],
                    writes=[r_("l1f_t", 0)], tag="fq_p1")
                S.op("vector", lambda e, t0=t0, qpv=qpv: e.scalar_tensor_tensor(
                    F.t[1][:], F.praw[1][:], qpv, ts_[:, t0:t0 + TJ],
                    ALU.mult, ALU.mult),
                    reads=[r_("l1f_praw", 1), r_("l1f_qsv"), tsk],
                    writes=[r_("l1f_t", 1)], tag="fq_p2")
                S.op("vector", lambda e: e.tensor_add(
                    F.t[0][:], F.t[0][:], F.t[1][:]),
                    reads=[r_("l1f_t", 0), r_("l1f_t", 1)], writes=[r_("l1f_t", 0)],
                    tag="fq_p3")
                S.op("vector", lambda e, big_i=big_i, t0=t0: e.tensor_mul(
                    F.big[big_i][:, t0:t0 + TJ], F.t[0][:], F.rnbc[:]),
                    reads=[r_("l1f_t", 0), r_("l1f_rnbc")],
                    writes=[bigk(big_i, t0 // TJ)], tag="fq_p4")

    ptpool = ([(F.pt[i], r_("l1f_pt", i)) for i in range(2)]
              + [(F.praw[i], r_("l1f_praw", i)) for i in range(2)]
              + [(F.t[i], r_("l1f_t", i)) for i in range(2)]
              + [(F.rnbc, r_("l1f_rnbc")), (F.hp[0], r_("l1f_hp", 0))])
    sc_banks = (2, 4, 6)
    pending = []
    hix = 0
    for wb in range(4):
        q0 = wb * TJ
        for h in range(8):
            qc, rs = h // 2, 64 * (h % 2)
            qbig = F.big[qc]
            kbig = F.big[4 + qc]
            tiles = [(4 * wb + rp, 128 * rp, TJ, "cur") for rp in range(4)]
            if wb > 0:
                tiles += [(4 * (wb - 1) + rp, 0, 128 * (rp + 1), "prev")
                          for rp in range(4)]
            n_pv = len(tiles)
            par = hix % 2
            pvb = 3 if par == 0 else 5
            for pi, (kt, qlo, qhi, kind) in enumerate(tiles):
                w = qhi - qlo
                pt, ptk = ptpool[pi]
                ps_S = sc_banks[pi % 3]
                S.op("tensor", lambda e, kt=kt, qlo=qlo, w=w, rs=rs, ps_S=ps_S,
                     qbig=qbig, kbig=kbig, q0=q0: e.matmul(
                         B.pb[ps_S][:, 0:w],
                         kbig[rs:rs + 64, kt * 128:(kt + 1) * 128],
                         qbig[rs:rs + 64, q0 + qlo:q0 + qlo + w],
                         start=True, stop=True),
                    reads=[bigk(4 + qc, kt // 4), bigk(qc, wb)],
                    writes=[r_("pb", ps_S)], tag="fs_mm", pe_base=rs)
                S.op("scalar", lambda e, w=w, pt=pt, ps_S=ps_S: e.activation(
                    pt[:, 0:w], B.pb[ps_S][:, 0:w], AF.Exp, bias=negC),
                    reads=[r_("pb", ps_S), negC_k], writes=[ptk], tag="fs_exp")
                if kind == "cur":
                    S.op("vector", lambda e, pt=pt: e.tensor_tensor(
                        pt[:, 0:128], pt[:, 0:128], F.tri[:], ALU.min),
                        reads=[ptk, r_("l1f_tri")], writes=[ptk], tag="fs_mt")
                else:
                    S.op("vector", lambda e, pt=pt, w=w: e.tensor_tensor(
                        pt[:, w - 128:w], pt[:, w - 128:w], F.geq[:], ALU.min),
                        reads=[ptk, r_("l1f_geq")], writes=[ptk], tag="fs_mg")
            for fn_ in pending:
                fn_()
            pending = []
            for pi, (kt, qlo, qhi, kind) in enumerate(tiles):
                w = qhi - qlo
                pt, ptk = ptpool[pi]
                S.op("tensor", lambda e, kt=kt, h=h, pt=pt, qlo=qlo, w=w,
                     pi=pi, n_pv=n_pv, pvb=pvb: e.matmul(
                         B.pb[pvb][0:65, qlo:qlo + w],
                         F.vt[kt][:, 65 * h:65 * h + 65], pt[:, 0:w],
                         start=(pi == 0), stop=(pi == n_pv - 1)),
                    reads=[r_("l1f_vt", kt), ptk], writes=[r_("pb", pvb)],
                    tag="fpv")

            def mk_tail(h=h, qc=qc, rs=rs, pvb=pvb, par=par):
                rec = B.rowf[:, (3 + par) * TJ:(4 + par) * TJ]
                rec_k = r_("rowf", 3 + par)
                rh = B.rowh[:] if par == 0 else F.rh2[0:1, 0:TJ]
                rh_k = r_("rowh") if par == 0 else r_("l1f_rh2")
                rb = F.recbc[:] if par == 0 else F.hp[1][0:64, :]
                rb_k = r_("l1f_recbc") if par == 0 else r_("l1f_hp", 1)

                def emit():
                    S.op("vector", lambda e: e.reciprocal(
                        rec, B.pb[pvb][64:65, 0:TJ]),
                        reads=[r_("pb", pvb)], writes=[rec_k], tag="fn_rec")
                    S.op("vector", lambda e: e.tensor_copy(rh, rec),
                         reads=[rec_k], writes=[rh_k], tag="fn_cast")
                    S.op("tensor", lambda e: e.matmul(
                        B.pb[7][0:64, 0:TJ], B.ones_row[:, 0:64], rh,
                        start=True, stop=True),
                        reads=[rh_k, r_("ones_row")], writes=[r_("pb", 7)],
                        tag="fn_bcmm", pe_base=0)
                    S.op("vector", lambda e: e.tensor_copy(
                        rb, B.pb[7][0:64, 0:TJ]),
                        reads=[r_("pb", 7)], writes=[rb_k], tag="fn_bcev")
                    S.op("vector", lambda e: e.tensor_mul(
                        F.aow[qc][rs:rs + 64, :], B.pb[pvb][0:64, 0:TJ], rb),
                        reads=[r_("pb", pvb), rb_k], writes=[r_("l1f_aow", qc)],
                        tag="fn_mul")
                return emit
            pending.append(mk_tail())
            hix += 1
        wodst = lambda p: F.wo[:, 0:512] if p == 0 else F.hp[3][:, 0:512]
        wosl = (lambda p, i: F.wo[:, i * 128:(i + 1) * 128] if p == 0
                else F.hp[3][:, i * 128:(i + 1) * 128])
        wok = lambda p: r_("l1f_wo") if p == 0 else r_("l1f_hp", 3)
        for oc in range(2):
            S.dma("gpsimd", wodst(oc % 2),
                  W["p_l1f_wox"][l, oc].rearrange("p a j -> p (a j)"),
                  writes=[wok(oc % 2)], tag="fwo_ld")
        for fn_ in pending:
            fn_()
        pending = []
        for oc in range(NCHUNK):
            wop = oc % 2
            for i in range(4):
                S.op("tensor", lambda e, i=i, oc=oc, wop=wop, w=wosl(oc % 2, i):
                     e.matmul(
                         B.pb[0 + wop][:], w,
                         F.aow[i][:], start=(i == 0), stop=(i == 3)),
                    reads=[wok(wop), r_("l1f_aow", i)],
                    writes=[r_("pb", 0 + wop)], tag="fwo_mm")
            if oc + 2 < NCHUNK:
                S.dma("gpsimd", wodst(wop),
                      W["p_l1f_wox"][l, oc + 2].rearrange("p a j -> p (a j)"),
                      writes=[wok(wop)], tag="fwo_ld")
            S.op("vector", lambda e, oc=oc, q0=q0, wop=wop: e.tensor_add(
                tok(B.x[oc], q0, TJ), tok(B.x[oc], q0, TJ),
                B.pb[0 + wop][:]),
                reads=[r_("pb", 0 + wop), r_("x", oc)], writes=[r_("x", oc)],
                tag="fwo_res")
    for c in range(NCHUNK):
        S.op("vector", lambda e, c=c: e.tensor_copy(
            tok(B.h[c], 0, T), tok(B.x[c], 0, T)),
            reads=[r_("x", c)], writes=[r_("h", c)], tag="fxb")

    emit_l1f_ffn(S, nc, B, F, W, l)


def emit_l1f_ffn(S, nc, B, F, W, l):
    pb_s0, pb_s1 = B.pb[5], B.pb[6]
    eps_ap, eps_k = cvec(S, B, EPS, 0, 1)
    mh_ap, mh_k = cvec(S, B, -0.5, 0, 1)
    for tj in range(NTJ):
        t0 = tj * TJ
        for c in range(NCHUNK):
            sqb = B.sq[c % 2]
            S.op("scalar", lambda e, c=c, t0=t0, sqb=sqb: e.activation(
                sqb[:], _xsh(B, c, t0, TJ), AF.Square),
                reads=[r_("h", c)], writes=[r_("sq", c % 2)], tag="fl_sq")
            S.op("tensor", lambda e, c=c, t0=t0: e.matmul(
                pb_s0[0:1, :], B.onecol[:], _xsh(B, c, t0, TJ),
                start=(c == 0), stop=(c == NCHUNK - 1)),
                reads=[r_("h", c), r_("onecol")], writes=[r_("pb", 5)],
                tag="fl_mm0")
            S.op("tensor", lambda e, sqb=sqb, c=c: e.matmul(
                pb_s1[0:1, :], B.onecol[:], sqb[:],
                start=(c == 0), stop=(c == NCHUNK - 1)),
                reads=[r_("sq", c % 2), r_("onecol")], writes=[r_("pb", 6)],
                tag="fl_mm1")

        def slot(i):
            return B.rowf[:, i * TJ:(i + 1) * TJ]

        mu, b1r, b2r = slot(0), slot(1), slot(2)
        tmp = slot(4)
        K_mu, K_b1, K_b2 = (r_("rowf", i) for i in (0, 1, 2))
        K_tmp = r_("rowf", 4)
        S.op("vector", lambda e, mu=mu: e.tensor_scalar(
            mu, pb_s0[0:1, :], 1.0 / D, None, ALU.mult),
            reads=[r_("pb", 5)], writes=[K_mu], tag="fl_mu")
        S.op("vector", lambda e: e.tensor_scalar(
            pb_s1[0:1, :], pb_s1[0:1, :], 1.0 / D, None, ALU.mult),
            reads=[r_("pb", 6)], writes=[r_("pb", 6)], tag="fl_ms")
        S.op("vector", lambda e, mu=mu, tmp=tmp: e.tensor_mul(tmp, mu, mu),
             reads=[K_mu], writes=[K_tmp], tag="fl_mu2")
        S.op("vector", lambda e, tmp=tmp: e.tensor_sub(
            pb_s1[0:1, :], pb_s1[0:1, :], tmp),
            reads=[K_tmp, r_("pb", 6)], writes=[r_("pb", 6)], tag="fl_var")
        S.op("scalar", lambda e, a=eps_ap: e.activation(
            pb_s1[0:1, :], pb_s1[0:1, :], AF.Ln, bias=a),
            reads=[r_("pb", 6), eps_k], writes=[r_("pb", 6)], tag="fl_ln")
        S.op("scalar", lambda e, b1r=b1r, a=mh_ap: e.activation(
            b1r, pb_s1[0:1, :], AF.Exp, scale=a),
            reads=[r_("pb", 6), mh_k], writes=[K_b1], tag="fl_ex")
        S.op("vector", lambda e, mu=mu, b1r=b1r, b2r=b2r: e.scalar_tensor_tensor(
            b2r, mu, -1.0, b1r, ALU.mult, ALU.mult),
            reads=[K_mu, K_b1], writes=[K_b2], tag="fl_b2")
        _bc_row(S, B, b1r, K_b1, B.b1bc, r_("b1bc"), t0)
        _bc_row(S, B, b2r, K_b2, B.b2bc, r_("b2bc"), t0)
    for tj in range(NTJ):
        t0 = tj * TJ
        for c in range(NCHUNK):
            g = F.lg[:, c:c + 1]
            bt = F.lb[:, c:c + 1]
            S.op("vector", lambda e, c=c, t0=t0, g=g: e.scalar_tensor_tensor(
                F.hp[c][:], _xsh(B, c, t0, TJ), g, B.b1bc[:, t0:t0 + TJ],
                ALU.mult, ALU.mult),
                reads=[r_("h", c), r_("b1bc"), r_("l1f_lg")],
                writes=[r_("l1f_hp", c)], tag="fh_1")
            S.op("vector", lambda e, c=c, t0=t0, g=g: e.scalar_tensor_tensor(
                F.hp[c][:], B.b2bc[:, t0:t0 + TJ], g, F.hp[c][:],
                ALU.mult, ALU.add),
                reads=[r_("l1f_hp", c), r_("b2bc"), r_("l1f_lg")],
                writes=[r_("l1f_hp", c)], tag="fh_2")
            S.op("vector", lambda e, c=c, bt=bt: e.tensor_scalar(
                F.hp[c][:], F.hp[c][:], bt, None, ALU.add),
                reads=[r_("l1f_hp", c), r_("l1f_lb")], writes=[r_("l1f_hp", c)],
                tag="fh_3")
        for ns in range(22):
            par = ns % 2
            zb = F.big[ns // 4]
            zcol = 512 * (ns % 4)
            zkey = bigk_g(ns)
            if par == 0:
                wsa = lambda c: F.w2c[:, c * 128:(c + 1) * 128]
                wsg = lambda c: F.w2c[:, 1024 + c * 128:1024 + (c + 1) * 128]
                ksa = lambda c: r_("l1f_w2c", 0)
                ksg = lambda c: r_("l1f_w2c", 1)
                ba, bg = 0, 1
                tb, tk = F.t[0], r_("l1f_t", 0)
                S.dma("sync", F.w2c[:, 0:1024],
                      F.scr_w1a[l, ns].rearrange("p c j -> p (c j)"),
                      reads=[r_("scrw1", 0, l, ns)], writes=[r_("l1f_w2c", 0)],
                      tag="f1_lda")
                S.dma("gpsimd", F.w2c[:, 1024:2048],
                      F.scr_w1g[l, ns].rearrange("p c j -> p (c j)"),
                      reads=[r_("scrw1", 1, l, ns)], writes=[r_("l1f_w2c", 1)],
                      tag="f1_ldg")
            else:
                wsa = lambda c: F.vt[6 + c // 4][:, (c % 4) * 128:(c % 4 + 1) * 128]
                wsg = lambda c: F.vt[8 + c // 4][:, (c % 4) * 128:(c % 4 + 1) * 128]
                ksa = lambda c: r_("l1f_vt", 6 + c // 4)
                ksg = lambda c: r_("l1f_vt", 8 + c // 4)
                ba, bg = 4, 5
                tb, tk = F.t[1], r_("l1f_t", 1)
                for q, base, scr, part in (("sync", 6, F.scr_w1a, 0),
                                           ("gpsimd", 8, F.scr_w1g, 1)):
                    for hh in range(2):
                        S.dma(q, F.vt[base + hh][:, 0:512],
                              scr[l, ns][:, 4 * hh:4 * hh + 4, :].rearrange(
                                  "p c j -> p (c j)"),
                              reads=[r_("scrw1", part, l, ns)],
                              writes=[r_("l1f_vt", base + hh)], tag="f1_ldv")
            for c in range(NCHUNK):
                S.op("tensor", lambda e, ba=ba, w=wsa(c), c=c: e.matmul(
                    B.pb[ba][:], w, F.hp[c][:],
                    start=(c == 0), stop=(c == NCHUNK - 1)),
                    reads=[ksa(c), r_("l1f_hp", c)], writes=[r_("pb", ba)],
                    tag="f1_mma")
                S.op("tensor", lambda e, bg=bg, w=wsg(c), c=c: e.matmul(
                    B.pb[bg][:], w, F.hp[c][:],
                    start=(c == 0), stop=(c == NCHUNK - 1)),
                    reads=[ksg(c), r_("l1f_hp", c)], writes=[r_("pb", bg)],
                    tag="f1_mmg")
            S.op("scalar", lambda e, ns=ns, bg=bg, tb=tb: e.activation(
                tb[:], B.pb[bg][:], AF.Gelu, bias=F.b1g[:, ns:ns + 1]),
                reads=[r_("pb", bg), r_("l1f_b1g")], writes=[tk],
                tag="f1_gelu")
            S.op("vector", lambda e, ns=ns, zb=zb, zcol=zcol, ba=ba, tb=tb:
                 e.scalar_tensor_tensor(
                     zb[:, zcol:zcol + 512], B.pb[ba][:], F.b1a[:, ns:ns + 1],
                     tb[:], ALU.add, ALU.mult),
                 reads=[r_("pb", ba), r_("l1f_b1a"), tk],
                 writes=[zkey], tag="f1_z")
        for oc in range(NCHUNK):
            wpar = oc % 2
            if wpar == 0:
                w2sl = lambda i: F.w2c[:, i * 128:(i + 1) * 128]
                w2k = lambda i: r_("l1f_w2c", 0 if i < 8 else 1)
                S.dma("sync", F.w2c[:, 0:1024],
                      F.scr_w2[l, oc][:, 0:8, :].rearrange("p i j -> p (i j)"),
                      reads=[r_("scrw2", l, i_) for i_ in range(8)],
                      writes=[r_("l1f_w2c", 0)], tag="f2_ld")
                S.dma("gpsimd", F.w2c[:, 1024:2816],
                      F.scr_w2[l, oc][:, 8:22, :].rearrange("p i j -> p (i j)"),
                      reads=[r_("scrw2", l, i_) for i_ in range(8, 22)],
                      writes=[r_("l1f_w2c", 1), r_("l1f_w2c", 2)], tag="f2_ld2")
            else:
                w2sl = lambda i: F.vt[i // 4][:, (i % 4) * 128:(i % 4 + 1) * 128]
                w2k = lambda i: r_("l1f_vt", i // 4)
                for pc in range(6):
                    nsl = 4 if pc < 5 else 2
                    S.dma("sync" if pc % 2 == 0 else "gpsimd",
                          F.vt[pc][:, 0:128 * nsl],
                          F.scr_w2[l, oc][:, 4 * pc:4 * pc + nsl, :].rearrange(
                              "p i j -> p (i j)"),
                          reads=[r_("scrw2", l, i_)
                                 for i_ in range(4 * pc, 4 * pc + nsl)],
                          writes=[r_("l1f_vt", pc)], tag="f2_ldv")
            for i in range(22):
                S.op("tensor", lambda e, i=i, oc=oc, w=w2sl(i): e.matmul(
                    B.pb[2 + oc % 2][:], w,
                    F.big[i // 4][:, 512 * (i % 4):512 * (i % 4) + 512],
                    start=(i == 0), stop=(i == 21)),
                    reads=[w2k(i), bigk_g(i)],
                    writes=[r_("pb", 2 + oc % 2)], tag="f2_mm")
            S.op("vector", lambda e, oc=oc, t0=t0: e.scalar_tensor_tensor(
                tok(B.x[oc], t0, TJ), B.pb[2 + oc % 2][:], F.b2[:, oc:oc + 1],
                tok(B.x[oc], t0, TJ), ALU.add, ALU.add),
                reads=[r_("pb", 2 + oc % 2), r_("l1f_b2"), r_("x", oc)],
                writes=[r_("x", oc)], tag="f2_res")
    for c in range(NCHUNK):
        S.op("vector", lambda e, c=c: e.tensor_copy(
            tok(B.h[c], 0, T), tok(B.x[c], 0, T)),
            reads=[r_("x", c)], writes=[r_("h", c)], tag="fxb2")


def bigk_g(ns):
    return r_("l1f_big", ns // 4, ns % 4)



def _sub_l0a(S, nc, B, A, W):
    emit_ln(S, B, "ln")
    emit_l0_attn_a(S, nc, B, A, W)


def _sub_l0f(S, nc, B, F, W):
    emit_l0_ffn(S, nc, B, F, W)


SUBLAYERS = {
    "l0a": (L0A_NAMES, alloc_l0a, emit_l0a_setup, _sub_l0a),
    "l0f": (L0F_NAMES, alloc_l0f, emit_l0f_setup, _sub_l0f),
    "l0b": (L0B_NAMES, alloc_l0b, emit_l0b_setup,
            lambda S, nc, B, A, W: emit_l0_attn_b(S, nc, B, A, W)),
    "l1a": (L1A_NAMES, alloc_l1a, emit_l1a_setup,
            lambda S, nc, B, A, W: emit_l1_attn_a(S, nc, B, A, W)),
    "l1b": (L1B_NAMES, alloc_l1b, emit_l1b_setup,
            lambda S, nc, B, A, W: emit_l1_attn_b(S, nc, B, A, W)),
    "l1f": (L1F_NAMES, alloc_l1f, emit_l1f_setup,
            lambda S, nc, B, A, W: emit_l1_ffn(S, nc, B, A, W)),
}


def build_chain_program(sub_ids, arrays, n_seq=NSEQ, strict=False):
    nc = bass.Bass("TRN2", target_bir_lowering=False)
    x_in, x_out = _declare_x_io(nc)
    names = []
    for sid in sub_ids:
        for n in SUBLAYERS[sid][0]:
            if n not in names:
                names.append(n)
    W = declare_inputs(nc, arrays, names)
    with ExitStack() as ctx:
        B = alloc_common(nc, ctx)
        S = Sched(nc, strict=strict)
        emit_init_consts(S, B)
        bufs = {}
        for sid in dict.fromkeys(sub_ids):
            _, alloc, setup, _ = SUBLAYERS[sid]
            bufs[sid] = alloc(nc, ctx)
            setup(S, nc, B, bufs[sid], W)
        for seq in range(n_seq):
            emit_load_x(S, B, x_in, seq)
            for sid in sub_ids:
                SUBLAYERS[sid][3](S, nc, B, bufs[sid], W)
            emit_store_x(S, B, x_out, seq)
        S.finalize()
    return nc, S, names


CHAIN = ["l0a", "l0f", "l0b", "l0f", "l1a", "l1f", "l1b", "l1f"]
LAUNCHES = [[sid] for sid in CHAIN]
FUSED = True


def build_fused_program(arrays, n_seq=NSEQ, strict=False):
    nc = bass.Bass("TRN2", target_bir_lowering=False)
    x_in, x_out = _declare_x_io(nc)
    names = []
    for sid in CHAIN:
        for n in SUBLAYERS[sid][0]:
            if n not in names:
                names.append(n)
    W = declare_inputs(nc, arrays, names)
    with ExitStack() as ctx:
        B = alloc_common(nc, ctx)
        S = Sched(nc, strict=strict)
        emit_init_consts(S, B)
        with ExitStack() as pctx:
            Fp0 = alloc_l0f(nc, pctx, "_p0")
            emit_l0f_setup(S, nc, B, Fp0, W, scratch=True, consts=False)
        with ExitStack() as pctx:
            Fp1 = alloc_l1f(nc, pctx, "_p1")
            emit_l1f_setup(S, nc, B, Fp1, W, scratch=True, consts=False)
        S.barrier()
        inst = 0
        for seq in range(n_seq):
            emit_load_x(S, B, x_in, seq)
            for sid in CHAIN:
                S.barrier()
                with ExitStack() as ictx:
                    _, alloc, setup, body = SUBLAYERS[sid]
                    A = alloc(nc, ictx, f"_i{inst}")
                    if sid in ("l0f", "l1f"):
                        setup(S, nc, B, A, W, scratch=False, consts=True)
                    else:
                        setup(S, nc, B, A, W)
                    body(S, nc, B, A, W)
                inst += 1
            emit_store_x(S, B, x_out, seq)
        S.finalize()
    return nc, S, names


def kernel(**inputs):
    arrays = dict(inputs)
    arrays.update(host_prep(inputs))
    xsh = shard_x(np.asarray(inputs["L0_attn_a_x"], dtype=np.float32))
    if FUSED:
        nc, S, names = build_fused_program(arrays)
        in_maps = []
        for ci in range(8):
            m = {n: arrays[n] for n in names}
            m["x_io_in"] = xsh[ci]
            in_maps.append(m)
        res = run_bass_kernel_spmd(nc, in_maps, core_ids=list(range(8)))
        out = np.empty((32, T, D), np.float32)
        for ci in range(8):
            out[4 * ci:4 * ci + 4] = (np.asarray(res.results[ci]["x_io_out"])
                                      .reshape(NSEQ, D, T).transpose(0, 2, 1))
        return out
    built = {}
    for group in LAUNCHES:
        key = tuple(group)
        if key not in built:
            built[key] = build_chain_program(group, arrays)
        nc, S, names = built[key]
        in_maps = []
        for ci in range(8):
            m = {n: arrays[n] for n in names}
            m["x_io_in"] = xsh[ci]
            in_maps.append(m)
        res = run_bass_kernel_spmd(nc, in_maps, core_ids=list(range(8)))
        xsh = [np.asarray(res.results[ci]["x_io_out"]) for ci in range(8)]
    out = np.empty((32, T, D), np.float32)
    for ci in range(8):
        out[4 * ci:4 * ci + 4] = (
            xsh[ci].reshape(NSEQ, D, T).transpose(0, 2, 1))
    return out
